# version 01
# baseline (speedup 1.0000x reference)
import math
import numpy as np
import ml_dtypes
import sys
if '/opt/trn_rl_repo' not in sys.path:
    sys.path.insert(0, '/opt/trn_rl_repo')
import concourse.bass as bass
import concourse.mybir as mybir
from concourse.bass_utils import run_bass_kernel_spmd

F32 = mybir.dt.float32
BF16 = mybir.dt.bfloat16
AF = mybir.ActivationFunctionType
ALU = mybir.AluOpType
AX = mybir.AxisListType

S = 4096
D = 1024
HFF = 4096
NH = 16
DK = 64
NKV = 4
SCW = 512
NSC = S // SCW
NST = S // 128
EPS = 1e-5
CAP = 50.0
THETA = 1e4

ENGINES = ("tensor", "vector", "scalar", "gpsimd", "sync")

PERM1 = [0, 4, 1, 5, 2, 6, 3, 7, 8, 12, 9, 13, 10, 14, 11, 15]


class Sem:
    __slots__ = ("h", "n", "name")
    def __init__(self, h, name):
        self.h = h
        self.n = 0
        self.name = name


class Planner:
    def __init__(self, nc):
        self.nc = nc
        self.items = {e: [] for e in ENGINES}
        self.pending = {e: [] for e in ENGINES}
        self.bar_idx = {e: 0 for e in ENGINES}
        self.nsem = 0

    def sem(self, name=None):
        name = name or f"s{self.nsem}"
        self.nsem += 1
        h = self.nc.alloc_semaphore(name)
        return Sem(h, name)

    def add(self, e, fn=None, wait=(), inc=(), is_dma=False):
        waits = list(self.pending[e]) + list(wait)
        self.pending[e] = []
        for s, n in inc:
            s.n += n
        self.items[e].append({"waits": waits, "fn": fn, "incs": list(inc),
                              "is_dma": is_dma})

    def dma(self, e, out, in_, sem, wait=(), transpose=False, inc=()):
        if transpose:
            fn = lambda eng, o=out, i=in_: eng.dma_start_transpose(out=o, in_=i)
        else:
            fn = lambda eng, o=out, i=in_: eng.dma_start(out=o, in_=i)
        self.add(e, fn, wait, [(sem, 16), *inc], is_dma=True)

    def barrier(self, name="bar"):
        b = self.sem(name)
        total = 0
        for e in ENGINES:
            its = self.items[e]
            if len(its) > self.bar_idx[e]:
                last = its[-1]
                n = 16 if (last["is_dma"] and last["fn"] is not None) else 1
                if last["fn"] is not None and not last["incs"]:
                    last["incs"].append((b, n))
                else:
                    self.items[e].append({"waits": [], "fn": None,
                                          "incs": [(b, 1)], "is_dma": False})
                    n = 1
                b.n += n
                total += n
            self.bar_idx[e] = len(self.items[e])
        for e in ENGINES:
            self.pending[e].append((b, total))
        return b

    def replay(self, ename, eng):
        for it in self.items[ename]:
            for (s, v) in it["waits"]:
                eng.wait_ge(s.h, v)
            ins = it["fn"](eng) if it["fn"] is not None else None
            incs = it["incs"]
            if ins is not None and incs:
                ins.then_inc(incs[0][0].h, incs[0][1])
                for s, n in incs[1:]:
                    eng.sem_inc(s.h, n)
            else:
                for s, n in incs:
                    eng.sem_inc(s.h, n)


class _SBWrap:
    __slots__ = ("t", "_sb_range")
    def __init__(self, t, rng):
        self.t = t
        self._sb_range = rng
    def ap(self):
        return self.t.ap()


class SB:
    def __init__(self, nc, base=16640, top=229376):
        self.nc = nc
        self.free = [(base, top)]
        self.cnt = 0

    @staticmethod
    def _sz(dt):
        return {F32: 4, BF16: 2, mybir.dt.int32: 4, mybir.dt.uint8: 1}[dt]

    def alloc(self, name, shape, dtype, align=128):
        bpp = int(np.prod(shape[1:])) * self._sz(dtype)
        for i, (a, b) in enumerate(self.free):
            start = (a + align - 1) // align * align
            if start + bpp <= b:
                pre = (a, start)
                post = (start + bpp, b)
                repl = []
                if pre[1] > pre[0]:
                    repl.append(pre)
                if post[1] > post[0]:
                    repl.append(post)
                self.free[i:i + 1] = repl
                self.cnt += 1
                t = self.nc.alloc_sbuf_tensor_at(f"{name}_{self.cnt}", list(shape),
                                                 dtype, offset=start)
                return _SBWrap(t, (start, start + bpp))
        raise MemoryError(f"SBUF alloc failed for {name} {shape} ({bpp} B/part); "
                          f"free={self.free}")

    def freeit(self, t):
        a, b = t._sb_range
        self.free.append((a, b))
        self.free.sort()
        merged = []
        for seg in self.free:
            if merged and seg[0] <= merged[-1][1]:
                merged[-1] = (merged[-1][0], max(merged[-1][1], seg[1]))
            else:
                merged.append(seg)
        self.free = merged


_CONST_CACHE = {}

def host_consts():
    if _CONST_CACHE:
        return _CONST_CACHE
    inv = 1.0 / THETA ** (np.arange(0, DK, 2, dtype=np.float64) / DK)
    freqs = np.arange(S, dtype=np.float64)[:, None] * inv[None, :]
    _CONST_CACHE["cos"] = np.cos(freqs).astype(np.float32)
    _CONST_CACHE["sin"] = np.sin(freqs).astype(np.float32)
    jk = np.arange(128)[:, None]
    jq = np.arange(128)[None, :]
    _CONST_CACHE["tri"] = (jq >= jk).astype(ml_dtypes.bfloat16)
    _CONST_CACHE["eye"] = np.eye(128, dtype=ml_dtypes.bfloat16)
    return _CONST_CACHE


class Ctx:
    pass


def make_ctx(nc, taps=False):
    c = Ctx()
    c.nc = nc
    c.P = Planner(nc)
    c.sb = SB(nc)
    c.taps = taps
    T = lambda name, shape, dt, kind: nc.dram_tensor(name, shape, dt, kind=kind).ap()
    I = lambda name, shape, dt=F32: T(name, shape, dt, "ExternalInput")
    scratch_kind = "ExternalOutput" if taps else "Internal"
    N = lambda name, shape, dt: T(name, shape, dt, scratch_kind)

    c.x = I("x", [S, D])
    c.wq0 = I("wq0", [D, D]); c.wk0 = I("wk0", [D, D])
    c.wv0 = I("wv0", [D, D]); c.wo0 = I("wo0", [D, D])
    c.bo0 = I("bo0", [D])
    c.W1 = I("W1", [HFF, D]); c.b1 = I("b1", [128, HFF // 128])
    c.W2 = I("W2", [D, HFF]); c.b2 = I("b2", [D])
    c.scale0 = I("scale0", [1, 1])
    c.wq1 = I("wq1", [D, D]); c.wk1 = I("wk1", [D, 256])
    c.wv1 = I("wv1", [D, 256]); c.wo1 = I("wo1", [D, D])
    c.w1w = I("w1w", [HFF, D]); c.w1s = I("w1s", [128, 32])
    c.w2w = I("w2w", [HFF, D]); c.w2s = I("w2s", [128, 32])
    c.h1w = I("h1w", [HFF, HFF]); c.h1s = I("h1s", [128, 32])
    c.h2w = I("h2w", [HFF, HFF]); c.h2s = I("h2s", [128, 32])
    c.w3w = I("w3w", [D, HFF]); c.w3s = I("w3s", [D])
    c.cos = I("cos", [S, 32]); c.sin = I("sin", [S, 32])
    c.tri = I("tri", [128, 128], BF16)
    c.eye = I("eye", [128, 128], BF16)

    c.y = T("y", [S, D], F32, "ExternalOutput")

    ST = lambda name, shape: T(name, shape, BF16, "Internal")
    c.st_wq0 = ST("st_wq0", [D, D]); c.st_wk0 = ST("st_wk0", [D, D])
    c.st_wv0 = ST("st_wv0", [D, D]); c.st_wo0 = ST("st_wo0", [D, D])
    c.st_W1 = ST("st_W1", [HFF, D]); c.st_W2 = ST("st_W2", [D, HFF])
    c.st_w1w = ST("st_w1w", [HFF, D]); c.st_w2w = ST("st_w2w", [HFF, D])
    c.st_h1 = ST("st_h1", [HFF, HFF]); c.st_h2 = ST("st_h2", [HFF, HFF])
    c.st_w3 = ST("st_w3", [D, HFF])

    c.x0b = T("x0b", [S, D], BF16, "Internal")
    c.x1f = N("x1f", [S, D], F32); c.x1b = T("x1b", [S, D], BF16, "Internal")
    c.x2f = N("x2f", [S, D], F32); c.x2b = T("x2b", [S, D], BF16, "Internal")
    c.x3f = N("x3f", [S, D], F32); c.x3b = T("x3b", [S, D], BF16, "Internal")
    c.haT = T("haT", [HFF, S], BF16, scratch_kind)
    c.hbT = T("hbT", [HFF, S], BF16, scratch_kind)

    P = c.P
    c.s_stg_x = P.sem("stg_x")
    c.s_stg_a0 = P.sem("stg_a0")
    c.s_stg_f0 = P.sem("stg_f0")
    c.s_stg_f1 = P.sem("stg_f1")
    c.s_ldw = P.sem("ldw")
    c.s_ldx = P.sem("ldx")
    c.s_res = P.sem("res")
    c.s_st = P.sem("st")
    c.s_stg_o = P.sem("stg_o")
    c.s_pe = P.sem("pe")
    c.s_act = P.sem("act")
    c.s_dve = P.sem("dve")

    c.psF_h = nc.alloc_psum_tensor("psF", [128, 6, 512], F32)
    c.psT_h = nc.alloc_psum_tensor("psT", [128, 2, 1024], BF16)
    c.psF = c.psF_h.ap()
    c.psT = c.psT_h.ap()
    return c


def tview(ap):
    return ap.rearrange("(a p) d -> p a d", p=128)


def emit_stage(c, groups=("x", "a0", "f0", "f1")):
    P = c.P
    if "x" in groups:
        P.dma("gpsimd", c.x0b, c.x, c.s_stg_x)
    if "a0" in groups:
        for dst, src in ((c.st_wq0, c.wq0), (c.st_wk0, c.wk0),
                         (c.st_wv0, c.wv0), (c.st_wo0, c.wo0)):
            P.dma("gpsimd", dst, src, c.s_stg_a0)
    if "f0" in groups:
        P.dma("gpsimd", c.st_W1, c.W1, c.s_stg_f0)
        P.dma("gpsimd", c.st_W2, c.W2, c.s_stg_f0)
    if "f1" in groups:
        for dst, src in ((c.st_w1w, c.w1w), (c.st_w2w, c.w2w), (c.st_w3, c.w3w)):
            P.dma("gpsimd", dst, src, c.s_stg_f1)
        for dst, src in ((c.st_h1, c.h1w), (c.st_h2, c.h2w)):
            for r in range(4):
                P.dma("gpsimd", dst[r * 1024:(r + 1) * 1024, :],
                      src[r * 1024:(r + 1) * 1024, :], c.s_stg_f1)


def load_wT(c, dst_sb, st_ap, nch, wait):
    P = c.P
    for dc in range(nch):
        P.dma("sync", dst_sb.ap()[:, dc, :], st_ap[:, dc * 128:(dc + 1) * 128],
              c.s_ldw, wait=wait if dc == 0 else (), transpose=True)


def load_xT(c, dst_ap, src_bf, s0, width, wait):
    P = c.P
    for dc in range(8):
        P.dma("sync", dst_ap[:, dc, :], src_bf[s0:s0 + width, dc * 128:(dc + 1) * 128],
              c.s_ldx, wait=wait if dc == 0 else (), transpose=True)


def emit_l0kv(c):
    P, sb = c.P, c.sb
    c.kT = sb.alloc("kT", [128, 8, S], BF16)
    c.v0 = sb.alloc("v0", [128, NST, NH, 65], BF16)
    wkT = sb.alloc("wkT", [128, 8, D], BF16)
    wvT = sb.alloc("wvT", [128, 8, D], BF16)
    xt = [sb.alloc(f"xt{i}", [128, 8, SCW], BF16) for i in range(2)]

    wait_stg = ((c.s_stg_a0, c.s_stg_a0.n),)
    load_wT(c, wkT, c.st_wk0, 8, wait_stg)
    load_wT(c, wvT, c.st_wv0, 8, wait_stg)
    ldw_done = c.s_ldw.n

    P.add("vector", lambda v: v.memset(c.v0.ap()[:, :, :, 64:65], 1.0),
          inc=[(c.s_dve, 1)])

    xt_free_at = [None, None]
    for sc in range(NSC):
        b = sc % 2
        w = [(c.s_stg_x, c.s_stg_x.n)] if sc == 0 else ()
        if xt_free_at[b] is not None:
            w = ((c.s_pe, xt_free_at[b]),)
        load_xT(c, xt[b].ap(), c.x0b, sc * SCW, SCW, w)
        ldx_done = c.s_ldx.n

        first_wait = [(c.s_ldx, ldx_done), (c.s_ldw, ldw_done)]
        for ec in range(8):
            bank = ec % 2
            for dc in range(8):
                wt = list(first_wait) if (ec == 0 and dc == 0) else []
                if dc == 0:
                    wt.append((c.s_act, max(0, c.s_act.n - 1)))
                P.add("tensor",
                      lambda t, o=c.psF[:, bank, :], l=wkT.ap()[:, dc, ec * 128:(ec + 1) * 128],
                             r=xt[b].ap()[:, dc, :], st=(dc == 0), sp=(dc == 7):
                      t.matmul(o, l, r, start=st, stop=sp),
                      wait=wt, inc=[(c.s_pe, 1)] if dc == 7 else ())
            P.add("scalar",
                  lambda s, o=c.kT.ap()[:, ec, sc * SCW:(sc + 1) * SCW], i=c.psF[:, bank, :]:
                  s.activation(o, i, AF.Copy),
                  wait=[(c.s_pe, c.s_pe.n)], inc=[(c.s_act, 1)])
        for st4 in range(4):
            for half in range(2):
                bank = 2 + (st4 * 2 + half) % 2
                for dc in range(8):
                    wt = []
                    if dc == 0:
                        wt.append((c.s_dve, max(0, c.s_dve.n - 1)))
                    P.add("tensor",
                          lambda t, o=c.psF[:, bank, :],
                                 l=xt[b].ap()[:, dc, st4 * 128:(st4 + 1) * 128],
                                 r=wvT.ap()[:, dc, half * 512:(half + 1) * 512],
                                 st=(dc == 0), sp=(dc == 7):
                          t.matmul(o, l, r, start=st, stop=sp),
                          wait=wt, inc=[(c.s_pe, 1)] if dc == 7 else ())
                src = c.psF[:, bank, :].rearrange("p (h d) -> p h d", d=64)
                dst = c.v0.ap()[:, sc * 4 + st4, half * 8:(half + 1) * 8, 0:64]
                P.add("vector",
                      lambda v, o=dst, i=src: v.tensor_copy(o, i),
                      wait=[(c.s_pe, c.s_pe.n)], inc=[(c.s_dve, 1)])
        xt_free_at[b] = c.s_pe.n

    if "l0kv" in DBG:
        dbg_dump(c, "kT", c.kT.ap(), [128, 8, S], BF16)
        dbg_dump(c, "v0", c.v0.ap(), [128, NST, NH, 65], BF16)
    for t in (wkT, wvT, xt[0], xt[1]):
        sb.freeit(t)
    P.barrier("l0kv_done")

def emit_finalize_tile(c, t_ap, scratch, stats, xn, row0, outf, outb):
    P = c.P
    sumsq = stats[:, 0:1]; sumsq2 = stats[:, 1:2]; ssum = stats[:, 2:3]
    m = stats[:, 3:4]; e2 = stats[:, 4:5]; veps = stats[:, 5:6]
    rstd = stats[:, 6:7]; nmr = stats[:, 7:8]; lnv = stats[:, 8:9]
    sqw = [(c.s_dve, c.s_dve.n)]
    P.add("scalar", lambda s: s.activation(scratch, t_ap[:, 0:512], AF.Square,
                                           accum_out=sumsq),
          wait=sqw)
    P.add("scalar", lambda s: s.activation(scratch, t_ap[:, 512:1024], AF.Square,
                                           accum_out=sumsq2),
          inc=[(c.s_act, 1)])
    sd = c.s_dve
    P.add("vector", lambda v: v.reduce_sum(ssum, t_ap, axis=AX.X), inc=[(sd, 1)])
    P.add("vector", lambda v: v.tensor_scalar(m, ssum, 1.0 / D, None, ALU.mult),
          wait=[(sd, sd.n)], inc=[(sd, 1)])
    P.add("vector", lambda v: v.tensor_add(sumsq, sumsq, sumsq2),
          wait=[(c.s_act, c.s_act.n)], inc=[(sd, 1)])
    P.add("vector", lambda v: v.tensor_scalar(e2, sumsq, 1.0 / D, EPS, ALU.mult, ALU.add),
          wait=[(sd, sd.n)], inc=[(sd, 1)])
    P.add("vector", lambda v: v.scalar_tensor_tensor(veps, m, -1.0, m, ALU.mult, ALU.mult),
          wait=[(sd, sd.n)], inc=[(sd, 1)])
    P.add("vector", lambda v: v.tensor_add(veps, veps, e2),
          wait=[(sd, sd.n)], inc=[(sd, 1)])
    P.add("scalar", lambda s: s.activation(lnv, veps, AF.Ln),
          wait=[(sd, sd.n)])
    P.add("scalar", lambda s: s.activation(rstd, lnv, AF.Exp, scale=-0.5),
          inc=[(c.s_act, 1)])
    P.add("vector", lambda v: v.scalar_tensor_tensor(nmr, m, -1.0, rstd, ALU.mult, ALU.mult),
          wait=[(c.s_act, c.s_act.n)], inc=[(sd, 1)])
    aw = [(sd, sd.n), (c.s_st, max(0, c.s_st.n - 16))]
    if outb is not None:
        aw.append((c.s_stg_o, max(0, c.s_stg_o.n - 16)))
    P.add("vector", lambda v: v.tensor_scalar(xn, t_ap, rstd, nmr, ALU.mult, ALU.add),
          wait=aw, inc=[(sd, 1)])
    w = [(c.s_dve, c.s_dve.n)]
    if "fin" in DBG and getattr(c, "_fin_n", 0) == 0:
        c._fin_n = 1
        dbg_dump(c, "stats", stats, [128, 12], F32)
        dbg_dump(c, "xn", xn, [128, D], F32)
    P.dma("scalar", outf[row0:row0 + 128, :], xn, c.s_st, wait=w)
    if outb is not None:
        P.dma("gpsimd", outb[row0:row0 + 128, :], xn, c.s_stg_o, wait=w)


def emit_l0attn(c):
    P, sb = c.P, c.sb
    wqT = sb.alloc("wqT", [128, 8, D], BF16)
    woT = sb.alloc("woT", [128, 8, D], BF16)
    bocast = sb.alloc("bocast", [128, D], BF16)
    ones64 = sb.alloc("ones64", [128, 64], F32)
    qT = sb.alloc("qT", [128, 8, SCW], BF16)
    xtq = sb.alloc("xtq", [128, 8, SCW], BF16)
    Pp = sb.alloc("Pp", [128, 2, SCW], BF16)
    ynT = sb.alloc("ynT", [128, 8, SCW], BF16)
    stat = sb.alloc("stat", [128, SCW], F32)
    x0r = sb.alloc("x0r", [128, D], F32)
    tt = sb.alloc("tt", [128, D], F32)
    xn = [sb.alloc(f"xn{i}", [128, D], F32) for i in range(2)]
    stats = sb.alloc("stats", [128, 12], F32)

    wait_stg = ((c.s_stg_a0, c.s_stg_a0.n),)
    load_wT(c, wqT, c.st_wq0, 8, wait_stg)
    load_wT(c, woT, c.st_wo0, 8, wait_stg)
    P.dma("gpsimd", bocast.ap(), c.bo0[None, :].broadcast_to([128, D]), c.s_ldw)
    ldw_done = c.s_ldw.n
    P.add("vector", lambda v: v.memset(ones64.ap(), 1.0), inc=[(c.s_dve, 1)])

    xtq_free = None
    for qc in range(NSC):
        w = ((c.s_stg_x, c.s_stg_x.n),) if qc == 0 else             ((c.s_pe, xtq_free), (c.s_act, c.s_act.n))
        load_xT(c, xtq.ap(), c.x0b, qc * SCW, SCW, w)
        ldx_done = c.s_ldx.n
        for ec in range(8):
            qb = 4 + ec % 2
            for dc in range(8):
                wt = []
                if ec == 0 and dc == 0:
                    wt = [(c.s_ldx, ldx_done), (c.s_ldw, ldw_done),
                          (c.s_act, c.s_act.n)]
                elif dc == 0:
                    wt.append((c.s_act, max(0, c.s_act.n - 1)))
                P.add("tensor",
                      lambda t, o=c.psF[:, qb, :], l=wqT.ap()[:, dc, ec * 128:(ec + 1) * 128],
                             r=xtq.ap()[:, dc, :], st=(dc == 0), sp=(dc == 7):
                      t.matmul(o, l, r, start=st, stop=sp),
                      wait=wt, inc=[(c.s_pe, 1)] if dc == 7 else ())
            P.add("scalar",
                  lambda s, o=qT.ap()[:, ec, :], i=c.psF[:, qb, :]:
                  s.activation(o, i, AF.Copy),
                  wait=[(c.s_pe, c.s_pe.n)], inc=[(c.s_act, 1)])
        xtq_free = c.s_pe.n
        qT_ready = c.s_act.n

        for h in range(NH):
            p0 = (h % 2) * 64
            chunk = h // 2
            avb = 2 + h % 2
            av_args = None
            for kt in range(NST):
                sbank = kt % 2
                wt = []
                if kt == 0 and h == 0:
                    wt.append((c.s_act, qT_ready))
                    wt.append((c.s_dve, c.s_dve.n))
                v_exp2ago = c.s_act.n - 1
                if v_exp2ago > 0:
                    wt.append((c.s_act, v_exp2ago))
                P.add("tensor",
                      lambda t, o=c.psF[:, sbank, :],
                             l=c.kT.ap()[p0:p0 + 64, chunk, kt * 128:(kt + 1) * 128],
                             r=qT.ap()[p0:p0 + 64, chunk, :]:
                      t.matmul(o, l, r, start=True, stop=True),
                      wait=wt, inc=[(c.s_pe, 1)])
                P.add("scalar",
                      lambda s, o=Pp.ap()[:, kt % 2, :], i=c.psF[:, sbank, :]:
                      s.activation(o, i, AF.Exp, scale=0.125),
                      wait=[(c.s_pe, c.s_pe.n)], inc=[(c.s_act, 1)])
                if av_args is not None:
                    P.add("tensor", av_args[0], wait=av_args[1], inc=av_args[2])
                avw = [(c.s_act, c.s_act.n)]
                if kt == 0:
                    avw.append((c.s_dve, max(0, c.s_dve.n - 3)))
                av_args = (
                    (lambda t, o=c.psF[0:65, avb, :], l=c.v0.ap()[:, kt, h, :],
                            r=Pp.ap()[:, kt % 2, :], st=(kt == 0), sp=(kt == NST - 1):
                     t.matmul(o, l, r, start=st, stop=sp)),
                    avw, [(c.s_pe, 1)] if kt == NST - 1 else ())
            P.add("tensor", av_args[0], wait=av_args[1], inc=av_args[2])
            P.add("vector", lambda v, o=stat.ap()[64:65, :], i=c.psF[64:65, avb, :]:
                  v.reciprocal(o, i),
                  wait=[(c.s_pe, c.s_pe.n)], inc=[(c.s_dve, 1)])
            P.add("tensor", lambda t, o=c.psF[0:64, 5, :], l=ones64.ap()[64:65, 0:64],
                           r=stat.ap()[64:65, :]:
                  t.matmul(o, l, r, start=True, stop=True),
                  wait=[(c.s_dve, c.s_dve.n), (c.s_dve, max(0, c.s_dve.n - 2))],
                  inc=[(c.s_pe, 1)])
            P.add("vector", lambda v, o=stat.ap()[0:64, :], i=c.psF[0:64, 5, :]:
                  v.tensor_copy(o, i),
                  wait=[(c.s_pe, c.s_pe.n)], inc=[(c.s_dve, 1)])
            P.add("vector", lambda v, o=ynT.ap()[p0:p0 + 64, chunk, :],
                           a=c.psF[0:64, avb, :], b2=stat.ap()[0:64, :]:
                  v.tensor_mul(o, a, b2),
                  wait=[(c.s_dve, c.s_dve.n)], inc=[(c.s_dve, 1)])
        heads_done = c.s_dve.n

        for st4 in range(4):
            row0 = qc * SCW + st4 * 128
            P.dma("sync", x0r.ap(), c.x[row0:row0 + 128, :], c.s_res,
                  wait=[(c.s_dve, max(0, c.s_dve.n - 1))])
            for half in range(2):
                bank = half
                for hc in range(8):
                    wt = []
                    if hc == 0:
                        wt.append((c.s_dve, heads_done))
                        wt.append((c.s_dve, max(0, c.s_dve.n - 2)))
                    P.add("tensor",
                          lambda t, o=c.psF[:, bank, :],
                                 l=ynT.ap()[:, hc, st4 * 128:(st4 + 1) * 128],
                                 r=woT.ap()[:, hc, half * 512:(half + 1) * 512],
                                 st=(hc == 0), sp=(hc == 7):
                          t.matmul(o, l, r, start=st, stop=sp),
                          wait=wt, inc=[(c.s_pe, 1)] if hc == 7 else ())
            pe_done = c.s_pe.n
            b = st4 % 2
            t_ap = tt.ap()
            P.add("vector", lambda v, o=t_ap[:, 0:512], i=c.psF[:, 0, :],
                           r=x0r.ap()[:, 0:512]:
                  v.tensor_add(o, i, r),
                  wait=[(c.s_pe, pe_done), (c.s_res, c.s_res.n),
                        (c.s_act, c.s_act.n)])
            P.add("vector", lambda v, o=t_ap[:, 512:1024], i=c.psF[:, 1, :],
                           r=x0r.ap()[:, 512:1024]:
                  v.tensor_add(o, i, r), inc=[(c.s_dve, 1)])
            P.add("vector", lambda v, o=t_ap, i=bocast.ap():
                  v.tensor_add(o, o, i),
                  wait=[(c.s_dve, c.s_dve.n)], inc=[(c.s_dve, 1)])
            if "l0attn" in DBG and qc == 0 and st4 == 0:
                dbg_dump(c, "t0", t_ap, [128, D], F32)
            emit_finalize_tile(c, t_ap, stat.ap(), stats.ap(), xn[b].ap(),
                               row0, c.x1f, c.x1b)
        if "l0attn" in DBG and qc == 0:
            dbg_dump(c, "qT", qT.ap(), [128, 8, SCW], BF16)
            dbg_dump(c, "ynT", ynT.ap(), [128, 8, SCW], BF16)

    for t in (wqT, woT, bocast, ones64, qT, xtq, Pp, ynT, stat, x0r,
              tt, xn[0], xn[1], stats, c.kT, c.v0):
        sb.freeit(t)
    P.barrier("l0attn_done")


def emit_finish(c):
    P = c.P
    P.add("scalar", wait=[(c.s_st, c.s_st.n)])
    P.add("gpsimd", wait=[(c.s_stg_o, c.s_stg_o.n)])
    P.add("sync", wait=[(c.s_st, c.s_st.n), (c.s_stg_o, c.s_stg_o.n)])


def build_nc(phases, taps=False):
    nc = bass.Bass("TRN2", target_bir_lowering=False, debug=False)
    c = make_ctx(nc, taps=taps)
    for ph in phases:
        ph(c)
    emit_finish(c)
    with nc.Block() as block:
        @block.tensor
        def _(eng):
            c.P.replay("tensor", eng)
        @block.vector
        def _(eng):
            c.P.replay("vector", eng)
        @block.scalar
        def _(eng):
            c.P.replay("scalar", eng)
        @block.gpsimd
        def _(eng):
            c.P.replay("gpsimd", eng)
        @block.sync
        def _(eng):
            c.P.replay("sync", eng)
    return nc, c


def make_inmap(inputs, core):
    hc = host_consts()
    a = lambda k: np.ascontiguousarray(inputs[k])
    rs = lambda v: np.ascontiguousarray(np.asarray(v).reshape(-1, 128).T)
    m = dict(
        x=np.ascontiguousarray(inputs["L0_attn_x"][core]),
        wq0=a("L0_attn_w_q"), wk0=a("L0_attn_w_k"), wv0=a("L0_attn_w_v"),
        wo0=a("L0_attn_w_o"), bo0=a("L0_attn_b_o"),
        W1=a("L0_ffn_W1"),
        b1=np.ascontiguousarray(np.asarray(inputs["L0_ffn_b1"]).reshape(32, 128).T),
        W2=a("L0_ffn_W2"), b2=a("L0_ffn_b2"),
        scale0=np.ascontiguousarray(inputs["L0_ffn_scale"]).reshape(1, 1),
        wq1=a("L1_attn_w_q"), wk1=a("L1_attn_w_k"), wv1=a("L1_attn_w_v"),
        wo1=a("L1_attn_w_o"),
        w1w=a("L1_ffn_w1_w"), w1s=rs(inputs["L1_ffn_w1_s"]),
        w2w=a("L1_ffn_w2_w"), w2s=rs(inputs["L1_ffn_w2_s"]),
        h1w=a("L1_ffn_h1_w"), h1s=rs(inputs["L1_ffn_h1_s"]),
        h2w=a("L1_ffn_h2_w"), h2s=rs(inputs["L1_ffn_h2_s"]),
        w3w=a("L1_ffn_w3_w"), w3s=a("L1_ffn_w3_s"),
        cos=hc["cos"], sin=hc["sin"], tri=hc["tri"], eye=hc["eye"],
    )
    return m


DBG = set()


def dbg_dump(c, name, ap, shape, dtype):
    t = c.nc.dram_tensor("dbg_" + name, list(shape), dtype, kind="ExternalOutput").ap()
    P = c.P
    P.dma("sync", t, ap, c.s_ldx,
          wait=[(c.s_pe, c.s_pe.n), (c.s_act, c.s_act.n), (c.s_dve, c.s_dve.n)])
    for e in ENGINES:
        P.pending[e].append((c.s_ldx, c.s_ldx.n))


def kernel(**inputs):
    phases = [emit_stage, emit_l0kv, emit_l0attn, emit_l0ffnw, emit_l0ffn,
              emit_l1prep, emit_l1attn, emit_l1ffnA, emit_l1ffnB, emit_l1ffnC,
              emit_l1ffnD]
    nc, c = build_nc(phases, taps=False)
    in_maps = [make_inmap(inputs, core) for core in range(8)]
    res = run_bass_kernel_spmd(nc, in_maps, core_ids=list(range(8)))
    out = np.stack([res.results[i]["y"] for i in range(8)], axis=0)
    return out.astype(np.float32)


def emit_l0ffnw(c):
    P, sb = c.P, c.sb
    c.W1T = sb.alloc("W1T", [128, 8, HFF], BF16)
    c.W2T = sb.alloc("W2T", [128, 32, D], BF16)
    c.b1t = sb.alloc("b1t", [128, 32], F32)
    c.b2cast = sb.alloc("b2cast", [128, D], F32)
    xre = sb.alloc("xre", [1, D], F32)
    xro = sb.alloc("xro", [1, D], F32)
    obf = sb.alloc("obf", [1, D], BF16)
    sev = sb.alloc("sev", [1, D], BF16)
    ev4 = sb.alloc("ev4", [1, 256], BF16)
    scs = sb.alloc("scs", [1, 1], F32)

    wait_stg = ((c.s_stg_f0, c.s_stg_f0.n),)
    load_wT(c, c.W1T, c.st_W1, 8, wait_stg)
    load_wT(c, c.W2T, c.st_W2, 32, wait_stg)
    ldw_done = c.s_ldw.n
    P.dma("sync", c.b1t.ap(), c.b1, c.s_ldw)
    P.dma("sync", c.b2cast.ap(), c.b2[None, :].broadcast_to([128, D]), c.s_ldw)
    P.dma("sync", scs.ap(), c.scale0, c.s_ldw)
    sd = c.s_dve

    def rows_prep(tok_e, tok_o):
        P.dma("sync", xre.ap(), c.x1f[tok_e:tok_e + 1, :], c.s_res,
              wait=[(sd, sd.n)])
        P.dma("sync", xro.ap(), c.x1f[tok_o:tok_o + 1, :], c.s_res)
        P.add("vector", lambda v: v.tensor_copy(obf.ap(), xro.ap()),
              wait=[(c.s_res, c.s_res.n), (c.s_ldw, ldw_done + 48),
                    (c.s_pe, c.s_pe.n)], inc=[(sd, 1)])
        P.add("vector", lambda v: v.tensor_scalar(sev.ap(), xre.ap(),
                                                  scs.ap()[0:1, 0:1], None, ALU.mult),
              inc=[(sd, 1)])

    for p in range(4):
        rows_prep(1 + 2 * p, 2 + 2 * p)
        rows_ready = sd.n
        for dc in range(8):
            for half in range(2):
                bank = (dc * 2 + half) % 2
                wt = [(sd, rows_ready)]
                wt.append((sd, max(0, sd.n - 1)))
                P.add("tensor",
                      lambda t, o=c.psF[:, bank, :], l=obf.ap()[0:1, dc * 128:(dc + 1) * 128],
                             r=sev.ap()[0:1, half * 512:(half + 1) * 512]:
                      t.matmul(o, l, r, start=True, stop=True),
                      wait=wt, inc=[(c.s_pe, 1)])
                dst = c.W1T.ap()[:, dc, p * 1024 + half * 512: p * 1024 + (half + 1) * 512]
                P.add("vector",
                      lambda v, o=dst, i=c.psF[:, bank, :]:
                      v.tensor_add(o, o, i),
                      wait=[(c.s_pe, c.s_pe.n)], inc=[(sd, 1)])
    for p in range(4):
        rows_prep(9 + 2 * p, 10 + 2 * p)
        for k in range(4):
            P.add("vector",
                  lambda v, o=ev4.ap(), i=sev.ap().rearrange("a (q f) -> a q f", f=4)[0:1, :, k]:
                  v.tensor_copy(o, i),
                  wait=[(sd, sd.n), (c.s_pe, c.s_pe.n)], inc=[(sd, 1)])
            ev_ready = sd.n
            for jc in range(8):
                bank = jc % 2
                wt = [(sd, ev_ready), (sd, max(0, sd.n - 1))]
                P.add("tensor",
                      lambda t, o=c.psF[:, bank, 0:256], l=obf.ap()[0:1, jc * 128:(jc + 1) * 128],
                             r=ev4.ap():
                      t.matmul(o, l, r, start=True, stop=True),
                      wait=wt, inc=[(c.s_pe, 1)])
                dst = c.W2T.ap()[:, 8 * k + jc, 256 * p: 256 * (p + 1)]
                P.add("vector",
                      lambda v, o=dst, i=c.psF[:, bank, 0:256]:
                      v.tensor_add(o, o, i),
                      wait=[(c.s_pe, c.s_pe.n)], inc=[(sd, 1)])

    for t in (xre, xro, obf, sev, ev4, scs):
        sb.freeit(t)
    P.barrier("l0ffnw_done")


def emit_l0ffn(c):
    P, sb = c.P, c.sb
    h_sb = sb.alloc("h0", [128, 32, SCW], BF16)
    xt = [sb.alloc(f"xtf{i}", [128, 8, SCW], BF16) for i in range(2)]
    x1r = sb.alloc("x1r", [128, D], F32)
    tt = sb.alloc("ttf", [128, D], F32)
    xn = [sb.alloc(f"xnf{i}", [128, D], F32) for i in range(2)]
    stats = sb.alloc("statsf", [128, 12], F32)
    sqs = sb.alloc("sqsf", [128, 512], F32)

    xt_free = [None, None]
    w2_done = None
    for sc8 in range(NSC):
        b = sc8 % 2
        w = [] if xt_free[b] is None else [(c.s_pe, xt_free[b])]
        load_xT(c, xt[b].ap(), c.x1b, sc8 * SCW, SCW, w)
        ldx_done = c.s_ldx.n
        for hc in range(32):
            bank = hc % 2
            for dc in range(8):
                wt = []
                if hc == 0 and dc == 0:
                    wt.append((c.s_ldx, ldx_done))
                if dc == 0:
                    wt.append((c.s_act, max(0, c.s_act.n - 1)))
                P.add("tensor",
                      lambda t, o=c.psF[:, bank, :], l=c.W1T.ap()[:, dc, hc * 128:(hc + 1) * 128],
                             r=xt[b].ap()[:, dc, :], st=(dc == 0), sp=(dc == 7):
                      t.matmul(o, l, r, start=st, stop=sp),
                      wait=wt, inc=[(c.s_pe, 1)] if dc == 7 else ())
            gw = [(c.s_pe, c.s_pe.n)]
            if w2_done is not None:
                gw.append((c.s_pe, w2_done))
            P.add("scalar",
                  lambda s, o=h_sb.ap()[:, hc, :], i=c.psF[:, bank, :],
                         bb=c.b1t.ap()[:, hc:hc + 1]:
                  s.activation(o, i, AF.Gelu, bias=bb),
                  wait=gw, inc=[(c.s_act, 1)])
        h_done = c.s_act.n
        xt_free[b] = c.s_pe.n
        for st4 in range(4):
            row0 = sc8 * SCW + st4 * 128
            P.dma("sync", x1r.ap(), c.x1f[row0:row0 + 128, :], c.s_res,
                  wait=[(c.s_dve, max(0, c.s_dve.n - 1))])
            for half in range(2):
                bank = 2 + half
                for hc in range(32):
                    wt = []
                    if hc == 0:
                        wt.append((c.s_act, h_done))
                        wt.append((c.s_dve, max(0, c.s_dve.n - 9)))
                    P.add("tensor",
                          lambda t, o=c.psF[:, bank, :],
                                 l=h_sb.ap()[:, hc, st4 * 128:(st4 + 1) * 128],
                                 r=c.W2T.ap()[:, hc, half * 512:(half + 1) * 512],
                                 st=(hc == 0), sp=(hc == 31):
                          t.matmul(o, l, r, start=st, stop=sp),
                          wait=wt, inc=[(c.s_pe, 1)] if hc == 31 else ())
            pe_done = c.s_pe.n
            bx = st4 % 2
            t_ap = tt.ap()
            P.add("vector", lambda v, o=t_ap[:, 0:512], i=c.psF[:, 2, :],
                           r=x1r.ap()[:, 0:512]:
                  v.tensor_add(o, i, r),
                  wait=[(c.s_pe, pe_done), (c.s_res, c.s_res.n),
                        (c.s_act, c.s_act.n)])
            P.add("vector", lambda v, o=t_ap[:, 512:1024], i=c.psF[:, 3, :],
                           r=x1r.ap()[:, 512:1024]:
                  v.tensor_add(o, i, r), inc=[(c.s_dve, 1)])
            P.add("vector", lambda v, o=t_ap, i=c.b2cast.ap():
                  v.tensor_add(o, o, i),
                  wait=[(c.s_dve, c.s_dve.n)], inc=[(c.s_dve, 1)])
            emit_finalize_tile(c, t_ap, sqs.ap(), stats.ap(), xn[bx].ap(),
                               row0, c.x2f, c.x2b)
        w2_done = c.s_pe.n

    for t in (h_sb, xt[0], xt[1], x1r, tt, xn[0], xn[1], stats, sqs,
              c.W1T, c.W2T, c.b1t, c.b2cast):
        sb.freeit(t)
    P.barrier("l0ffn_done")


def emit_l1prep(c):
    P, sb = c.P, c.sb
    c.q1T = sb.alloc("q1T", [128, 8, S], BF16)
    c.k1T = sb.alloc("k1T", [128, 2, S], BF16)
    c.v1 = sb.alloc("v1", [128, NST, NKV, 65], BF16)
    wq1s = sb.alloc("wq1s", [128, 8, D], BF16)
    wk1s = sb.alloc("wk1s", [128, 8, 256], BF16)
    wv1s = sb.alloc("wv1s", [128, 8, 256], BF16)
    eyes = sb.alloc("eyes", [128, 128], BF16)
    xt = [sb.alloc(f"xtp{i}", [128, 8, SCW], BF16) for i in range(2)]
    qf = sb.alloc("qf", [128, NH, DK], F32)
    kf = sb.alloc("kf", [128, NKV, DK], F32)
    qsq = sb.alloc("qsq", [128, D], F32)
    qn = sb.alloc("qn", [128, NH, DK], F32)
    kn = sb.alloc("kn", [128, NKV, DK], F32)
    qrot = sb.alloc("qrot", [128, D], BF16)
    krot = sb.alloc("krot", [128, 256], BF16)
    cosb = [sb.alloc(f"cosb{i}", [128, 32], F32) for i in range(2)]
    sinb = [sb.alloc(f"sinb{i}", [128, 32], F32) for i in range(2)]
    rstat = sb.alloc("rstat", [128, NH * 4], F32)

    sd = c.s_dve
    P.dma("gpsimd", wq1s.ap(), c.wq1.rearrange("(a p) n -> p a n", p=128), c.s_ldw)
    P.dma("gpsimd", wk1s.ap(), c.wk1.rearrange("(a p) n -> p a n", p=128), c.s_ldw)
    P.dma("gpsimd", wv1s.ap(), c.wv1.rearrange("(a p) n -> p a n", p=128), c.s_ldw)
    P.dma("sync", eyes.ap(), c.eye, c.s_ldw)
    ldw_done = c.s_ldw.n
    P.add("vector", lambda v: v.memset(c.v1.ap()[:, :, :, 64:65], 1.0), inc=[(sd, 1)])

    sumq = rstat.ap()[:, 0:16]; vq = rstat.ap()[:, 16:32]
    sumk = rstat.ap()[:, 32:36]; vk = rstat.ap()[:, 36:40]
    rq = rstat.ap()[:, 40:56]; rk = rstat.ap()[:, 56:60]

    xt_free = [None, None]
    prev_tr_evac = 0
    for st in range(NST):
        sc8, off = st // 4, (st % 4) * 128
        cb = st % 2
        if st % 4 == 0:
            fa = xt_free[(st // 4) % 2]
            w = [(c.s_ldw, ldw_done)] if st == 0 else                 ([(c.s_pe, fa)] if fa is not None else [])
            load_xT(c, xt[(st // 4) % 2].ap(), c.x2b, sc8 * SCW, SCW, w)
        xb = xt[(st // 4) % 2]
        ldx_done = c.s_ldx.n
        P.dma("sync", cosb[cb].ap(), c.cos[st * 128:(st + 1) * 128, :], c.s_res,
              wait=[(sd, sd.n)])
        P.dma("sync", sinb[cb].ap(), c.sin[st * 128:(st + 1) * 128, :], c.s_res)
        res_done = c.s_res.n

        for half in range(2):
            for dc in range(8):
                wt = []
                if half == 0 and dc == 0:
                    wt = [(c.s_ldx, ldx_done), (sd, max(0, sd.n - 2))]
                P.add("tensor",
                      lambda t, o=c.psF[:, half, :], l=xb.ap()[:, dc, off:off + 128],
                             r=wq1s.ap()[:, dc, half * 512:(half + 1) * 512],
                             st_=(dc == 0), sp=(dc == 7):
                      t.matmul(o, l, r, start=st_, stop=sp),
                      wait=wt, inc=[(c.s_pe, 1)] if dc == 7 else ())
        for (bank, wsb) in ((2, wk1s), (3, wv1s)):
            for dc in range(8):
                P.add("tensor",
                      lambda t, o=c.psF[:, bank, 0:256], l=xb.ap()[:, dc, off:off + 128],
                             r=wsb.ap()[:, dc, :], st_=(dc == 0), sp=(dc == 7):
                      t.matmul(o, l, r, start=st_, stop=sp),
                      wait=(), inc=[(c.s_pe, 1)] if dc == 7 else ())
        proj_done = c.s_pe.n
        qv = qf.ap().rearrange("p (a b) d -> p a b d", b=2)
        pv0 = c.psF[:, 0, :].rearrange("p (h d) -> p h d", d=DK)
        pv1 = c.psF[:, 1, :].rearrange("p (h d) -> p h d", d=DK)
        P.add("vector", lambda v, o=qv[:, 0:4, 0, :], i=pv0[:, 0:4, :]: v.tensor_copy(o, i),
              wait=[(c.s_pe, proj_done)])
        P.add("vector", lambda v, o=qv[:, 0:4, 1, :], i=pv0[:, 4:8, :]: v.tensor_copy(o, i))
        P.add("vector", lambda v, o=qv[:, 4:8, 0, :], i=pv1[:, 0:4, :]: v.tensor_copy(o, i))
        P.add("vector", lambda v, o=qv[:, 4:8, 1, :], i=pv1[:, 4:8, :]: v.tensor_copy(o, i))
        P.add("vector", lambda v, o=kf.ap(), i=c.psF[:, 2, 0:256].rearrange("p (h d) -> p h d", d=DK):
              v.tensor_copy(o, i))
        P.add("vector", lambda v, o=c.v1.ap()[:, st, :, 0:64],
                       i=c.psF[:, 3, 0:256].rearrange("p (h d) -> p h d", d=DK):
              v.tensor_copy(o, i), inc=[(sd, 1)])
        evac_done = sd.n

        P.add("scalar", lambda s: s.activation(qsq.ap(), qf.ap().rearrange("p a b -> p (a b)"),
                                               AF.Square),
              wait=[(sd, evac_done)], inc=[(c.s_act, 1)])
        P.add("vector", lambda v: v.reduce_sum(sumq.rearrange("p (a b) -> p a b", b=1),
                                               qsq.ap().rearrange("p (a b) -> p a b", b=DK),
                                               axis=AX.X),
              wait=[(c.s_act, c.s_act.n)], inc=[(sd, 1)])
        P.add("scalar", lambda s: s.activation(qsq.ap()[:, 0:256],
                                               kf.ap().rearrange("p a b -> p (a b)"),
                                               AF.Square),
              wait=[(sd, sd.n)], inc=[(c.s_act, 1)])
        P.add("vector", lambda v: v.reduce_sum(sumk.rearrange("p (a b) -> p a b", b=1),
                                               qsq.ap()[:, 0:256].rearrange("p (a b) -> p a b", b=DK),
                                               axis=AX.X),
              wait=[(c.s_act, c.s_act.n)], inc=[(sd, 1)])
        P.add("vector", lambda v: v.tensor_scalar(vq, sumq, 1.0 / DK, EPS, ALU.mult, ALU.add),
              wait=[(sd, sd.n)])
        P.add("vector", lambda v: v.tensor_scalar(vk, sumk, 1.0 / DK, EPS, ALU.mult, ALU.add),
              inc=[(sd, 1)])
        P.add("scalar", lambda s: s.activation(vq, vq, AF.Ln), wait=[(sd, sd.n)])
        P.add("scalar", lambda s: s.activation(rq, vq, AF.Exp, scale=-0.5))
        P.add("scalar", lambda s: s.activation(vk, vk, AF.Ln))
        P.add("scalar", lambda s: s.activation(rk, vk, AF.Exp, scale=-0.5),
              inc=[(c.s_act, 1)])
        P.add("vector", lambda v: v.tensor_tensor(
              qn.ap(), qf.ap(),
              rq.rearrange("p (a b) -> p a b", b=1).broadcast_to([128, NH, DK]), ALU.mult),
              wait=[(c.s_act, c.s_act.n)])
        P.add("vector", lambda v: v.tensor_tensor(
              kn.ap(), kf.ap(),
              rk.rearrange("p (a b) -> p a b", b=1).broadcast_to([128, NKV, DK]), ALU.mult),
              inc=[(sd, 1)])
        t1q = qsq.ap()[:, 0:512].rearrange("p (a b) -> p a b", b=32)
        t2q = qsq.ap()[:, 512:1024].rearrange("p (a b) -> p a b", b=32)
        cq = cosb[cb].ap().rearrange("p (a b) -> p a b", a=1).broadcast_to([128, NH, 32])
        sq_ = sinb[cb].ap().rearrange("p (a b) -> p a b", a=1).broadcast_to([128, NH, 32])
        ck = cosb[cb].ap().rearrange("p (a b) -> p a b", a=1).broadcast_to([128, NKV, 32])
        sk = sinb[cb].ap().rearrange("p (a b) -> p a b", a=1).broadcast_to([128, NKV, 32])
        qnv = qn.ap(); knv = kn.ap()
        qrv = qrot.ap().rearrange("p (a b) -> p a b", b=DK)
        krv = krot.ap().rearrange("p (a b) -> p a b", b=DK)
        P.add("vector", lambda v, a=t1q, b=cq: v.tensor_tensor(a, qnv[:, :, 0:32], b, ALU.mult),
              wait=[(sd, sd.n), (c.s_res, res_done)])
        P.add("vector", lambda v, a=t2q, b=sq_: v.tensor_tensor(a, qnv[:, :, 32:64], b, ALU.mult),
              inc=[(sd, 1)])
        P.add("vector", lambda v, a=t1q, b=t2q: v.tensor_sub(qrv[:, :, 0:32], a, b),
              wait=[(sd, sd.n)])
        P.add("vector", lambda v, a=t1q, b=cq: v.tensor_tensor(a, qnv[:, :, 32:64], b, ALU.mult))
        P.add("vector", lambda v, a=t2q, b=sq_: v.tensor_tensor(a, qnv[:, :, 0:32], b, ALU.mult),
              inc=[(sd, 1)])
        P.add("vector", lambda v, a=t1q, b=t2q: v.tensor_add(qrv[:, :, 32:64], a, b),
              wait=[(sd, sd.n)])
        t1k = qsq.ap()[:, 0:128].rearrange("p (a b) -> p a b", b=32)
        t2k = qsq.ap()[:, 128:256].rearrange("p (a b) -> p a b", b=32)
        P.add("vector", lambda v, a=t1k, b=ck: v.tensor_tensor(a, knv[:, :, 0:32], b, ALU.mult))
        P.add("vector", lambda v, a=t2k, b=sk: v.tensor_tensor(a, knv[:, :, 32:64], b, ALU.mult),
              inc=[(sd, 1)])
        P.add("vector", lambda v, a=t1k, b=t2k: v.tensor_sub(krv[:, :, 0:32], a, b),
              wait=[(sd, sd.n)])
        P.add("vector", lambda v, a=t1k, b=ck: v.tensor_tensor(a, knv[:, :, 32:64], b, ALU.mult))
        P.add("vector", lambda v, a=t2k, b=sk: v.tensor_tensor(a, knv[:, :, 0:32], b, ALU.mult),
              inc=[(sd, 1)])
        P.add("vector", lambda v, a=t1k, b=t2k: v.tensor_add(krv[:, :, 32:64], a, b),
              wait=[(sd, sd.n)], inc=[(sd, 1)])
        rope_done = sd.n

        a_ = st % 2
        for ec in range(8):
            wt = [(sd, rope_done)] if ec == 0 else []
            if ec == 0:
                wt.append((sd, prev_tr_evac))
            P.add("tensor",
                  lambda t, o=c.psT[:, a_, ec * 128:(ec + 1) * 128],
                         i=qrot.ap()[:, ec * 128:(ec + 1) * 128], e=eyes.ap():
                  t.transpose(o, i, e),
                  wait=wt, inc=[(c.s_pe, 1)] if ec == 7 else ())
        for ec in range(2):
            P.add("tensor",
                  lambda t, o=c.psT[:, 1 - a_, ec * 128:(ec + 1) * 128],
                         i=krot.ap()[:, ec * 128:(ec + 1) * 128], e=eyes.ap():
                  t.transpose(o, i, e),
                  wait=(), inc=[(c.s_pe, 1)] if ec == 1 else ())
        P.add("vector",
              lambda v, o=c.q1T.ap()[:, :, st * 128:(st + 1) * 128],
                     i=c.psT[:, a_, :].rearrange("p (a b) -> p a b", b=128):
              v.tensor_copy(o, i),
              wait=[(c.s_pe, c.s_pe.n)])
        P.add("vector",
              lambda v, o=c.k1T.ap()[:, :, st * 128:(st + 1) * 128],
                     i=c.psT[:, 1 - a_, 0:256].rearrange("p (a b) -> p a b", b=128):
              v.tensor_copy(o, i),
              inc=[(sd, 1)])
        if "l1prepd" in DBG and st == 0:
            dbg_dump(c, "qf", qf.ap(), [128, NH, DK], F32)
            dbg_dump(c, "qn", qn.ap(), [128, NH, DK], F32)
            dbg_dump(c, "qrot", qrot.ap(), [128, D], BF16)
            dbg_dump(c, "kf", kf.ap(), [128, NKV, DK], F32)
            dbg_dump(c, "krot", krot.ap(), [128, 256], BF16)
            dbg_dump(c, "rstat", rstat.ap(), [128, NH * 4], F32)
        prev_tr_evac = sd.n
        xt_free[(st // 4) % 2] = c.s_pe.n

    if "l1prep" in DBG:
        dbg_dump(c, "q1T", c.q1T.ap(), [128, 8, S], BF16)
        dbg_dump(c, "k1T", c.k1T.ap(), [128, 2, S], BF16)
        dbg_dump(c, "v1", c.v1.ap(), [128, NST, NKV, 65], BF16)
    for t in (wq1s, wk1s, wv1s, eyes, xt[0], xt[1], qf, kf, qsq, qn, kn,
              qrot, krot, cosb[0], cosb[1], sinb[0], sinb[1], rstat):
        sb.freeit(t)
    P.barrier("l1prep_done")


def emit_l1attn(c):
    P, sb = c.P, c.sb
    wo1p = sb.alloc("wo1p", [128, 8, D], BF16)
    trim = sb.alloc("trim", [128, 128], BF16)
    Pp = sb.alloc("Pp1", [128, 2, SCW], BF16)
    ynT = sb.alloc("ynT1", [128, 8, SCW], BF16)
    stat = sb.alloc("stat1", [128, SCW], F32)
    tcap = sb.alloc("tcap", [128, SCW], F32)
    ones64 = sb.alloc("ones64b", [128, 64], F32)
    x2r = sb.alloc("x2r", [128, D], F32)
    tt = sb.alloc("tt1", [128, D], F32)
    xn = [sb.alloc(f"xn1{i}", [128, D], F32) for i in range(2)]
    stats = sb.alloc("stats1", [128, 12], F32)
    sqs = sb.alloc("sqs1", [128, 512], F32)

    sd = c.s_dve
    for p in range(NH):
        h = PERM1[p]
        b0 = (p % 2) * 64
        P.dma("gpsimd", wo1p.ap()[b0:b0 + 64, p // 2, :],
              c.wo1[h * 64:(h + 1) * 64, :], c.s_ldw)
    P.dma("sync", trim.ap(), c.tri, c.s_ldw)
    ldw_done = c.s_ldw.n
    P.add("vector", lambda v: v.memset(ones64.ap(), 1.0), inc=[(sd, 1)])

    ynmul_done = {0: 0, 1: 0}
    prev_bcs = 0
    for qc in range(NSC):
        for pos in range(NH):
            h = PERM1[pos]
            p0 = (pos % 2) * 64
            chunk = pos // 2
            j = h // 4
            jchunk = j // 2
            avb = 2 + pos % 2
            kt_last = min(NST - 1, 4 * qc + 3)
            av_args = None
            for kt in range(kt_last + 1):
                m = kt - 4 * qc
                col0 = 128 * m if m > 0 else 0
                ncol = 512 - col0
                sbank = kt % 2
                wt = []
                if kt == 0 and pos == 0 and qc == 0:
                    wt.append((c.s_ldw, ldw_done))
                if kt == 0 and pos == 0:
                    wt.append((sd, sd.n))
                v_prev = c.s_act.n - 3
                if v_prev > 0:
                    wt.append((c.s_act, v_prev))
                P.add("tensor",
                      lambda t, o=c.psF[:, sbank, col0:512],
                             l=c.k1T.ap()[p0:p0 + 64, jchunk, kt * 128:(kt + 1) * 128],
                             r=c.q1T.ap()[p0:p0 + 64, chunk, qc * 512 + col0: (qc + 1) * 512]:
                      t.matmul(o, l, r, start=True, stop=True),
                      wait=wt, inc=[(c.s_pe, 1)])
                P.add("scalar",
                      lambda s, o=tcap.ap()[:, col0:512], i=c.psF[:, sbank, col0:512]:
                      s.activation(o, i, AF.Tanh, scale=1.0 / (8.0 * CAP)),
                      wait=[(c.s_pe, c.s_pe.n)], inc=[(c.s_act, 1)])
                P.add("scalar",
                      lambda s, o=Pp.ap()[:, kt % 2, col0:512], i=tcap.ap()[:, col0:512]:
                      s.activation(o, i, AF.Exp, scale=CAP),
                      inc=[(c.s_act, 1)])
                masked = 0 <= m <= 3
                if masked:
                    P.add("vector",
                          lambda v, o=Pp.ap()[:, kt % 2, col0:col0 + 128], tm=trim.ap():
                          v.tensor_mul(o, o, tm),
                          wait=[(c.s_act, c.s_act.n)], inc=[(sd, 1)])
                if av_args is not None:
                    P.add("tensor", av_args[0], wait=av_args[1], inc=av_args[2])
                avw = [(sd, sd.n)] if masked else [(c.s_act, c.s_act.n)]
                if kt == 0:
                    avw.append((sd, ynmul_done[pos % 2]))
                av_args = (
                    (lambda t, o=c.psF[0:65, avb, col0:512], l=c.v1.ap()[:, kt, j, :],
                            r=Pp.ap()[:, kt % 2, col0:512], st=(kt == 0), sp=(kt == kt_last):
                     t.matmul(o, l, r, start=st, stop=sp)),
                    avw, [(c.s_pe, 1)] if kt == kt_last else ())
            P.add("tensor", av_args[0], wait=av_args[1], inc=av_args[2])
            P.add("vector", lambda v, o=stat.ap()[64:65, :], i=c.psF[64:65, avb, :]:
                  v.reciprocal(o, i),
                  wait=[(c.s_pe, c.s_pe.n), (c.s_act, c.s_act.n)], inc=[(sd, 1)])
            P.add("tensor", lambda t, o=c.psF[0:64, 5, :], l=ones64.ap()[64:65, 0:64],
                           r=stat.ap()[64:65, :]:
                  t.matmul(o, l, r, start=True, stop=True),
                  wait=[(sd, sd.n), (sd, prev_bcs)],
                  inc=[(c.s_pe, 1)])
            P.add("vector", lambda v, o=stat.ap()[0:64, :], i=c.psF[0:64, 5, :]:
                  v.tensor_copy(o, i),
                  wait=[(c.s_pe, c.s_pe.n)], inc=[(sd, 1)])
            prev_bcs = sd.n
            P.add("vector", lambda v, o=ynT.ap()[p0:p0 + 64, chunk, :],
                           a=c.psF[0:64, avb, :], b2=stat.ap()[0:64, :]:
                  v.tensor_mul(o, a, b2),
                  wait=[(sd, sd.n)], inc=[(sd, 1)])
            ynmul_done[pos % 2] = sd.n
        heads_done = sd.n

        for st4 in range(4):
            row0 = qc * SCW + st4 * 128
            P.dma("sync", x2r.ap(), c.x2f[row0:row0 + 128, :], c.s_res,
                  wait=[(sd, max(0, sd.n - 1))])
            for half in range(2):
                bank = half
                for hc in range(8):
                    wt = []
                    if hc == 0:
                        wt.append((sd, heads_done))
                        wt.append((sd, max(0, sd.n - 2)))
                    P.add("tensor",
                          lambda t, o=c.psF[:, bank, :],
                                 l=ynT.ap()[:, hc, st4 * 128:(st4 + 1) * 128],
                                 r=wo1p.ap()[:, hc, half * 512:(half + 1) * 512],
                                 st=(hc == 0), sp=(hc == 7):
                          t.matmul(o, l, r, start=st, stop=sp),
                          wait=wt, inc=[(c.s_pe, 1)] if hc == 7 else ())
            pe_done = c.s_pe.n
            bx = st4 % 2
            t_ap = tt.ap()
            P.add("vector", lambda v, o=t_ap[:, 0:512], i=c.psF[:, 0, :],
                           r=x2r.ap()[:, 0:512]:
                  v.tensor_add(o, i, r),
                  wait=[(c.s_pe, pe_done), (c.s_res, c.s_res.n),
                        (c.s_act, c.s_act.n)])
            P.add("vector", lambda v, o=t_ap[:, 512:1024], i=c.psF[:, 1, :],
                           r=x2r.ap()[:, 512:1024]:
                  v.tensor_add(o, i, r), inc=[(sd, 1)])
            if "l1attn" in DBG and qc == 1 and st4 == 0:
                dbg_dump(c, "t1", t_ap, [128, D], F32)
            emit_finalize_tile(c, t_ap, sqs.ap(), stats.ap(), xn[bx].ap(),
                               row0, c.x3f, c.x3b)
        if "l1attn" in DBG and qc == 1:
            dbg_dump(c, "ynT1", ynT.ap(), [128, 8, SCW], BF16)

    for t in (wo1p, trim, Pp, ynT, stat, tcap, ones64, x2r, tt, xn[0], xn[1],
              stats, sqs, c.q1T, c.k1T, c.v1):
        sb.freeit(t)
    P.barrier("l1attn_done")


def emit_l1ffnA(c):
    P, sb = c.P, c.sb
    w1T = sb.alloc("w1T1", [128, 8, HFF], BF16)
    w2T = sb.alloc("w2T1", [128, 8, HFF], BF16)
    s1t = sb.alloc("s1t", [128, 32], F32)
    s2t = sb.alloc("s2t", [128, 32], F32)
    xt = [sb.alloc(f"xta{i}", [128, 8, SCW], BF16) for i in range(2)]
    u_sb = [sb.alloc(f"usb{i}", [128, SCW], F32) for i in range(2)]
    ha = sb.alloc("ha", [128, 32, SCW], BF16)

    sd = c.s_dve
    wait_stg = ((c.s_stg_f1, c.s_stg_f1.n),)
    load_wT(c, w1T, c.st_w1w, 8, wait_stg)
    load_wT(c, w2T, c.st_w2w, 8, wait_stg)
    P.dma("sync", s1t.ap(), c.w1s, c.s_ldw)
    P.dma("sync", s2t.ap(), c.w2s, c.s_ldw)
    ldw_done = c.s_ldw.n

    xt_free = [None, None]
    ha_store_done = 0
    for sc8 in range(NSC):
        b = sc8 % 2
        w = [] if xt_free[b] is None else [(c.s_pe, xt_free[b])]
        load_xT(c, xt[b].ap(), c.x3b, sc8 * SCW, SCW, w)
        ldx_done = c.s_ldx.n
        for hc in range(32):
            ub = hc % 2
            gb = 2 + hc % 2
            for dc in range(8):
                wt = []
                if hc == 0 and dc == 0:
                    wt = [(c.s_ldx, ldx_done), (c.s_ldw, ldw_done)]
                if dc == 0:
                    wt.append((c.s_act, max(0, c.s_act.n - 1)))
                P.add("tensor",
                      lambda t, o=c.psF[:, ub, :], l=w1T.ap()[:, dc, hc * 128:(hc + 1) * 128],
                             r=xt[b].ap()[:, dc, :], st=(dc == 0), sp=(dc == 7):
                      t.matmul(o, l, r, start=st, stop=sp),
                      wait=wt, inc=[(c.s_pe, 1)] if dc == 7 else ())
            for dc in range(8):
                wt = []
                if dc == 0:
                    wt.append((sd, max(0, sd.n - 1)))
                P.add("tensor",
                      lambda t, o=c.psF[:, gb, :], l=w2T.ap()[:, dc, hc * 128:(hc + 1) * 128],
                             r=xt[b].ap()[:, dc, :], st=(dc == 0), sp=(dc == 7):
                      t.matmul(o, l, r, start=st, stop=sp),
                      wait=wt, inc=[(c.s_pe, 1)] if dc == 7 else ())
            P.add("scalar",
                  lambda s, o=u_sb[hc % 2].ap(), i=c.psF[:, ub, :],
                         sc_=s1t.ap()[:, hc:hc + 1]:
                  s.activation(o, i, AF.Silu, scale=sc_),
                  wait=[(c.s_pe, c.s_pe.n), (sd, max(0, sd.n - 1))],
                  inc=[(c.s_act, 1)])
            stw = [(c.s_act, c.s_act.n)]
            if hc == 0:
                stw.append((c.s_st, ha_store_done))
            P.add("vector",
                  lambda v, o=ha.ap()[:, hc, :], i=c.psF[:, gb, :],
                         sc_=s2t.ap()[:, hc:hc + 1], u=u_sb[hc % 2].ap():
                  v.scalar_tensor_tensor(o, i, sc_, u, ALU.mult, ALU.mult),
                  wait=stw, inc=[(sd, 1)])
        xt_free[b] = c.s_pe.n
        if "l1ffnA" in DBG and sc8 == 0:
            dbg_dump(c, "xta", xt[b].ap(), [128, 8, SCW], BF16)
            dbg_dump(c, "w1Tc", w1T.ap()[:, :, 0:512], [128, 8, 512], BF16)
            dbg_dump(c, "hasc", ha.ap(), [128, 32, SCW], BF16)
            dbg_dump(c, "s1td", s1t.ap(), [128, 32], F32)
        P.dma("scalar",
              c.haT.rearrange("(a p) s -> p a s", p=128)[:, :, sc8 * SCW:(sc8 + 1) * SCW],
              ha.ap(), c.s_st, wait=[(sd, sd.n)])
        ha_store_done = c.s_st.n

    for t in (w1T, w2T, s1t, s2t, xt[0], xt[1], u_sb[0], u_sb[1], ha):
        sb.freeit(t)
    P.barrier("l1ffnA_done")


def emit_l1ffnBC(c, st_w, scales_in, in_d, out_d, tag):
    P, sb = c.P, c.sb
    hw = sb.alloc(f"hw{tag}", [128, 32, 1024], BF16)
    hst = sb.alloc(f"hst{tag}", [128, 32], F32)
    hin = [sb.alloc(f"hin{tag}{i}", [128, 32, SCW], BF16) for i in range(2)]
    ho = sb.alloc(f"ho{tag}", [128, 8, SCW], BF16)

    P.dma("sync", hst.ap(), scales_in, c.s_ldw)
    in_v = in_d.rearrange("(a p) s -> p a s", p=128)
    out_v = out_d.rearrange("(a p) s -> p a s", p=128)

    hin_free = [None, None]
    ho_store_done = 0
    for stripe in range(4):
        for ic in range(32):
            wt = [(c.s_stg_f1, c.s_stg_f1.n)] if (stripe == 0 and ic == 0) else ()
            if ic == 0 and stripe > 0:
                wt = [(c.s_pe, c.s_pe.n)]
            P.dma("sync", hw.ap()[:, ic, :],
                  st_w[stripe * 1024:(stripe + 1) * 1024, ic * 128:(ic + 1) * 128],
                  c.s_ldw, wait=wt, transpose=True)
        ldw_done = c.s_ldw.n
        for sc8 in range(NSC):
            b = sc8 % 2
            w = [(c.s_ldx, c.s_ldx.n)]
            if hin_free[b] is not None:
                w.append((c.s_pe, hin_free[b]))
            P.dma("sync", hin[b].ap(), in_v[:, :, sc8 * SCW:(sc8 + 1) * SCW],
                  c.s_ldx, wait=w)
            ldx_done = c.s_ldx.n
            for oc in range(8):
                bank = oc % 2
                for ic in range(32):
                    wt = []
                    if oc == 0 and ic == 0:
                        wt = [(c.s_ldx, ldx_done), (c.s_ldw, ldw_done)]
                    if ic == 0:
                        wt.append((c.s_act, max(0, c.s_act.n - 1)))
                    P.add("tensor",
                          lambda t, o=c.psF[:, bank, :], l=hw.ap()[:, ic, oc * 128:(oc + 1) * 128],
                                 r=hin[b].ap()[:, ic, :], st=(ic == 0), sp=(ic == 31):
                          t.matmul(o, l, r, start=st, stop=sp),
                          wait=wt, inc=[(c.s_pe, 1)] if ic == 31 else ())
                gw = [(c.s_pe, c.s_pe.n)]
                if oc == 0:
                    gw.append((c.s_st, ho_store_done))
                P.add("scalar",
                      lambda s, o=ho.ap()[:, oc, :], i=c.psF[:, bank, :],
                             sc_=hst.ap()[:, stripe * 8 + oc:stripe * 8 + oc + 1]:
                      s.activation(o, i, AF.Gelu, scale=sc_),
                      wait=gw, inc=[(c.s_act, 1)])
            hin_free[b] = c.s_pe.n
            P.dma("scalar",
                  out_v[:, stripe * 8:(stripe + 1) * 8, sc8 * SCW:(sc8 + 1) * SCW],
                  ho.ap(), c.s_st, wait=[(c.s_act, c.s_act.n)])
            ho_store_done = c.s_st.n

    for t in (hw, hst, hin[0], hin[1], ho):
        sb.freeit(t)
    P.barrier(f"l1ffn{tag}_done")


def emit_l1ffnB(c):
    emit_l1ffnBC(c, c.st_h1, c.h1s, c.haT, c.hbT, "B")


def emit_l1ffnC(c):
    emit_l1ffnBC(c, c.st_h2, c.h2s, c.hbT, c.haT, "C")


def emit_l1ffnD(c):
    P, sb = c.P, c.sb
    w3T = sb.alloc("w3T1", [128, 32, D], BF16)
    s3c = sb.alloc("s3c", [128, D], F32)
    hin = [sb.alloc(f"hinD{i}", [128, 32, SCW], BF16) for i in range(2)]
    x3r = sb.alloc("x3r", [128, D], F32)
    tt = sb.alloc("ttD", [128, D], F32)
    xn = [sb.alloc(f"xnD{i}", [128, D], F32) for i in range(2)]
    stats = sb.alloc("statsD", [128, 12], F32)
    sqs = sb.alloc("sqsD", [128, 512], F32)

    sd = c.s_dve
    wait_stg = ((c.s_stg_f1, c.s_stg_f1.n),)
    load_wT(c, w3T, c.st_w3, 32, wait_stg)
    P.dma("sync", s3c.ap(), c.w3s[None, :].broadcast_to([128, D]), c.s_ldw)
    ldw_done = c.s_ldw.n
    in_v = c.haT.rearrange("(a p) s -> p a s", p=128)

    hin_free = [None, None]
    for sc8 in range(NSC):
        b = sc8 % 2
        w = [(c.s_ldx, c.s_ldx.n)]
        if hin_free[b] is not None:
            w.append((c.s_pe, hin_free[b]))
        P.dma("sync", hin[b].ap(), in_v[:, :, sc8 * SCW:(sc8 + 1) * SCW],
              c.s_ldx, wait=w)
        ldx_done = c.s_ldx.n
        for st4 in range(4):
            row0 = sc8 * SCW + st4 * 128
            P.dma("sync", x3r.ap(), c.x3f[row0:row0 + 128, :], c.s_res,
                  wait=[(sd, max(0, sd.n - 1))])
            for half in range(2):
                bank = 2 + half
                for ic in range(32):
                    wt = []
                    if st4 == 0 and half == 0 and ic == 0:
                        wt = [(c.s_ldx, ldx_done), (c.s_ldw, ldw_done)]
                    if ic == 0:
                        wt.append((sd, max(0, sd.n - 9)))
                    P.add("tensor",
                          lambda t, o=c.psF[:, bank, :], l=hin[b].ap()[:, ic, st4 * 128:(st4 + 1) * 128],
                                 r=w3T.ap()[:, ic, half * 512:(half + 1) * 512],
                                 st=(ic == 0), sp=(ic == 31):
                          t.matmul(o, l, r, start=st, stop=sp),
                          wait=wt, inc=[(c.s_pe, 1)] if ic == 31 else ())
            pe_done = c.s_pe.n
            bx = st4 % 2
            t_ap = tt.ap()
            P.add("vector", lambda v, o=t_ap[:, 0:512], i=c.psF[:, 2, :],
                           s3=s3c.ap()[:, 0:512]:
                  v.tensor_mul(o, i, s3),
                  wait=[(c.s_pe, pe_done), (c.s_res, c.s_res.n),
                        (c.s_act, c.s_act.n)])
            P.add("vector", lambda v, o=t_ap[:, 512:1024], i=c.psF[:, 3, :],
                           s3=s3c.ap()[:, 512:1024]:
                  v.tensor_mul(o, i, s3), inc=[(sd, 1)])
            P.add("vector", lambda v, o=t_ap, i=x3r.ap():
                  v.tensor_add(o, o, i),
                  wait=[(sd, sd.n)], inc=[(sd, 1)])
            emit_finalize_tile(c, t_ap, sqs.ap(), stats.ap(), xn[bx].ap(),
                               row0, c.y, None)
        hin_free[b] = c.s_pe.n

    for t in (w3T, s3c, hin[0], hin[1], x3r, tt, xn[0], xn[1], stats, sqs):
        sb.freeit(t)
    P.barrier("l1ffnD_done")


# version 02
# speedup vs baseline: 1.0103x; 1.0103x over previous
import math
import numpy as np
import ml_dtypes
import sys
if '/opt/trn_rl_repo' not in sys.path:
    sys.path.insert(0, '/opt/trn_rl_repo')
import concourse.bass as bass
import concourse.mybir as mybir
from concourse.bass_utils import run_bass_kernel_spmd

F32 = mybir.dt.float32
BF16 = mybir.dt.bfloat16
AF = mybir.ActivationFunctionType
ALU = mybir.AluOpType
AX = mybir.AxisListType

S = 4096
D = 1024
HFF = 4096
NH = 16
DK = 64
NKV = 4
SCW = 512
NSC = S // SCW
NST = S // 128
EPS = 1e-5
CAP = 50.0
THETA = 1e4

ENGINES = ("tensor", "vector", "scalar", "gpsimd", "sync")

PERM1 = [0, 4, 1, 5, 2, 6, 3, 7, 8, 12, 9, 13, 10, 14, 11, 15]


class Sem:
    __slots__ = ("h", "n", "name")
    def __init__(self, h, name):
        self.h = h
        self.n = 0
        self.name = name


class Planner:
    def __init__(self, nc):
        self.nc = nc
        self.items = {e: [] for e in ENGINES}
        self.pending = {e: [] for e in ENGINES}
        self.bar_idx = {e: 0 for e in ENGINES}
        self.nsem = 0

    def sem(self, name=None):
        name = name or f"s{self.nsem}"
        self.nsem += 1
        h = self.nc.alloc_semaphore(name)
        return Sem(h, name)

    def add(self, e, fn=None, wait=(), inc=(), is_dma=False):
        waits = list(self.pending[e]) + list(wait)
        self.pending[e] = []
        for s, n in inc:
            s.n += n
        self.items[e].append({"waits": waits, "fn": fn, "incs": list(inc),
                              "is_dma": is_dma})

    def dma(self, e, out, in_, sem, wait=(), transpose=False, inc=()):
        if transpose:
            fn = lambda eng, o=out, i=in_: eng.dma_start_transpose(out=o, in_=i)
        else:
            fn = lambda eng, o=out, i=in_: eng.dma_start(out=o, in_=i)
        self.add(e, fn, wait, [(sem, 16), *inc], is_dma=True)

    def barrier(self, name="bar"):
        b = self.sem(name)
        total = 0
        for e in ENGINES:
            its = self.items[e]
            if len(its) > self.bar_idx[e]:
                last = its[-1]
                n = 16 if (last["is_dma"] and last["fn"] is not None) else 1
                if last["fn"] is not None and not last["incs"]:
                    last["incs"].append((b, n))
                else:
                    self.items[e].append({"waits": [], "fn": None,
                                          "incs": [(b, 1)], "is_dma": False})
                    n = 1
                b.n += n
                total += n
            self.bar_idx[e] = len(self.items[e])
        for e in ENGINES:
            self.pending[e].append((b, total))
        return b

    def replay(self, ename, eng):
        for it in self.items[ename]:
            for (s, v) in it["waits"]:
                eng.wait_ge(s.h, v)
            ins = it["fn"](eng) if it["fn"] is not None else None
            incs = it["incs"]
            if ins is not None and incs:
                ins.then_inc(incs[0][0].h, incs[0][1])
                for s, n in incs[1:]:
                    eng.sem_inc(s.h, n)
            else:
                for s, n in incs:
                    eng.sem_inc(s.h, n)


class _SBWrap:
    __slots__ = ("t", "_sb_range")
    def __init__(self, t, rng):
        self.t = t
        self._sb_range = rng
    def ap(self):
        return self.t.ap()


class SB:
    def __init__(self, nc, base=16640, top=229376):
        self.nc = nc
        self.free = [(base, top)]
        self.cnt = 0

    @staticmethod
    def _sz(dt):
        return {F32: 4, BF16: 2, mybir.dt.int32: 4, mybir.dt.uint8: 1}[dt]

    def alloc(self, name, shape, dtype, align=128):
        bpp = int(np.prod(shape[1:])) * self._sz(dtype)
        for i, (a, b) in enumerate(self.free):
            start = (a + align - 1) // align * align
            if start + bpp <= b:
                pre = (a, start)
                post = (start + bpp, b)
                repl = []
                if pre[1] > pre[0]:
                    repl.append(pre)
                if post[1] > post[0]:
                    repl.append(post)
                self.free[i:i + 1] = repl
                self.cnt += 1
                t = self.nc.alloc_sbuf_tensor_at(f"{name}_{self.cnt}", list(shape),
                                                 dtype, offset=start)
                return _SBWrap(t, (start, start + bpp))
        raise MemoryError(f"SBUF alloc failed for {name} {shape} ({bpp} B/part); "
                          f"free={self.free}")

    def freeit(self, t):
        a, b = t._sb_range
        self.free.append((a, b))
        self.free.sort()
        merged = []
        for seg in self.free:
            if merged and seg[0] <= merged[-1][1]:
                merged[-1] = (merged[-1][0], max(merged[-1][1], seg[1]))
            else:
                merged.append(seg)
        self.free = merged


_CONST_CACHE = {}

def host_consts():
    if _CONST_CACHE:
        return _CONST_CACHE
    inv = 1.0 / THETA ** (np.arange(0, DK, 2, dtype=np.float64) / DK)
    freqs = np.arange(S, dtype=np.float64)[:, None] * inv[None, :]
    _CONST_CACHE["cos"] = np.cos(freqs).astype(np.float32)
    _CONST_CACHE["sin"] = np.sin(freqs).astype(np.float32)
    jk = np.arange(128)[:, None]
    jq = np.arange(128)[None, :]
    _CONST_CACHE["tri"] = (jq >= jk).astype(ml_dtypes.bfloat16)
    _CONST_CACHE["eye"] = np.eye(128, dtype=ml_dtypes.bfloat16)
    return _CONST_CACHE


class Ctx:
    pass


def make_ctx(nc, taps=False):
    c = Ctx()
    c.nc = nc
    c.P = Planner(nc)
    c.sb = SB(nc)
    c.taps = taps
    T = lambda name, shape, dt, kind: nc.dram_tensor(name, shape, dt, kind=kind).ap()
    I = lambda name, shape, dt=F32: T(name, shape, dt, "ExternalInput")
    scratch_kind = "ExternalOutput" if taps else "Internal"
    N = lambda name, shape, dt: T(name, shape, dt, scratch_kind)

    c.x = I("x", [S, D])
    c.wq0 = I("wq0", [D, D]); c.wk0 = I("wk0", [D, D])
    c.wv0 = I("wv0", [D, D]); c.wo0 = I("wo0", [D, D])
    c.bo0 = I("bo0", [D])
    c.W1 = I("W1", [HFF, D]); c.b1 = I("b1", [128, HFF // 128])
    c.W2 = I("W2", [D, HFF]); c.b2 = I("b2", [D])
    c.scale0 = I("scale0", [1, 1])
    c.wq1 = I("wq1", [D, D]); c.wk1 = I("wk1", [D, 256])
    c.wv1 = I("wv1", [D, 256]); c.wo1 = I("wo1", [D, D])
    c.w1w = I("w1w", [HFF, D]); c.w1s = I("w1s", [128, 32])
    c.w2w = I("w2w", [HFF, D]); c.w2s = I("w2s", [128, 32])
    c.h1w = I("h1w", [HFF, HFF]); c.h1s = I("h1s", [128, 32])
    c.h2w = I("h2w", [HFF, HFF]); c.h2s = I("h2s", [128, 32])
    c.w3w = I("w3w", [D, HFF]); c.w3s = I("w3s", [D])
    c.cos = I("cos", [S, 32]); c.sin = I("sin", [S, 32])
    c.tri = I("tri", [128, 128], BF16)
    c.eye = I("eye", [128, 128], BF16)

    c.y = T("y", [S, D], F32, "ExternalOutput")

    ST = lambda name, shape: T(name, shape, BF16, "Internal")
    c.st_wq0 = ST("st_wq0", [D, D]); c.st_wk0 = ST("st_wk0", [D, D])
    c.st_wv0 = ST("st_wv0", [D, D]); c.st_wo0 = ST("st_wo0", [D, D])
    c.st_W1 = ST("st_W1", [HFF, D]); c.st_W2 = ST("st_W2", [D, HFF])
    c.st_w1w = ST("st_w1w", [HFF, D]); c.st_w2w = ST("st_w2w", [HFF, D])
    c.st_h1 = ST("st_h1", [HFF, HFF]); c.st_h2 = ST("st_h2", [HFF, HFF])
    c.st_w3 = ST("st_w3", [D, HFF])

    c.x0b = T("x0b", [S, D], BF16, "Internal")
    c.x1f = N("x1f", [S, D], F32); c.x1b = T("x1b", [S, D], BF16, "Internal")
    c.x2f = N("x2f", [S, D], F32); c.x2b = T("x2b", [S, D], BF16, "Internal")
    c.x3f = N("x3f", [S, D], F32); c.x3b = T("x3b", [S, D], BF16, "Internal")
    c.haT = T("haT", [HFF, S], BF16, scratch_kind)
    c.hbT = T("hbT", [HFF, S], BF16, scratch_kind)

    P = c.P
    c.s_stg_x = P.sem("stg_x")
    c.s_stg_a0 = P.sem("stg_a0")
    c.s_stg_f0 = P.sem("stg_f0")
    c.s_stg_f1 = P.sem("stg_f1")
    c.s_ldw = P.sem("ldw")
    c.s_ldx = P.sem("ldx")
    c.s_res = P.sem("res")
    c.s_st = P.sem("st")
    c.s_stg_o = P.sem("stg_o")
    c.s_pe = P.sem("pe")
    c.s_act = P.sem("act")
    c.s_dve = P.sem("dve")

    c.psF_h = nc.alloc_psum_tensor("psF", [128, 6, 512], F32)
    c.psT_h = nc.alloc_psum_tensor("psT", [128, 2, 1024], BF16)
    c.psF = c.psF_h.ap()
    c.psT = c.psT_h.ap()
    return c


def tview(ap):
    return ap.rearrange("(a p) d -> p a d", p=128)


def emit_stage(c, groups=("x", "a0", "f0", "f1")):
    P = c.P
    if "x" in groups:
        P.dma("gpsimd", c.x0b, c.x, c.s_stg_x)
    if "a0" in groups:
        for dst, src in ((c.st_wq0, c.wq0), (c.st_wk0, c.wk0),
                         (c.st_wv0, c.wv0), (c.st_wo0, c.wo0)):
            P.dma("gpsimd", dst, src, c.s_stg_a0)
    if "f0" in groups:
        P.dma("gpsimd", c.st_W1, c.W1, c.s_stg_f0)
        P.dma("gpsimd", c.st_W2, c.W2, c.s_stg_f0)
    if "f1" in groups:
        for dst, src in ((c.st_w1w, c.w1w), (c.st_w2w, c.w2w), (c.st_w3, c.w3w)):
            P.dma("gpsimd", dst, src, c.s_stg_f1)
        for dst, src in ((c.st_h1, c.h1w), (c.st_h2, c.h2w)):
            for r in range(4):
                P.dma("gpsimd", dst[r * 1024:(r + 1) * 1024, :],
                      src[r * 1024:(r + 1) * 1024, :], c.s_stg_f1)


def load_wT(c, dst_sb, st_ap, nch, wait):
    P = c.P
    for dc in range(nch):
        P.dma("sync", dst_sb.ap()[:, dc, :], st_ap[:, dc * 128:(dc + 1) * 128],
              c.s_ldw, wait=wait if dc == 0 else (), transpose=True)


def load_xT(c, dst_ap, src_bf, s0, width, wait):
    P = c.P
    for dc in range(8):
        P.dma("sync", dst_ap[:, dc, :], src_bf[s0:s0 + width, dc * 128:(dc + 1) * 128],
              c.s_ldx, wait=wait if dc == 0 else (), transpose=True)


def emit_l0kv(c):
    P, sb = c.P, c.sb
    c.kT = sb.alloc("kT", [128, 8, S], BF16)
    c.v0 = sb.alloc("v0", [128, NST, NH, 65], BF16)
    wkT = sb.alloc("wkT", [128, 8, D], BF16)
    wvT = sb.alloc("wvT", [128, 8, D], BF16)
    xt = [sb.alloc(f"xt{i}", [128, 8, SCW], BF16) for i in range(2)]

    wait_stg = ((c.s_stg_a0, c.s_stg_a0.n),)
    load_wT(c, wkT, c.st_wk0, 8, wait_stg)
    load_wT(c, wvT, c.st_wv0, 8, wait_stg)
    ldw_done = c.s_ldw.n

    P.add("vector", lambda v: v.memset(c.v0.ap()[:, :, :, 64:65], 1.0),
          inc=[(c.s_dve, 1)])

    xt_free_at = [None, None]
    bankK = [0, 0]
    bankV = [0, 0]
    for sc in range(NSC):
        b = sc % 2
        w = [(c.s_stg_x, c.s_stg_x.n)] if sc == 0 else ()
        if xt_free_at[b] is not None:
            w = ((c.s_pe, xt_free_at[b]),)
        load_xT(c, xt[b].ap(), c.x0b, sc * SCW, SCW, w)
        ldx_done = c.s_ldx.n

        first_wait = [(c.s_ldx, ldx_done), (c.s_ldw, ldw_done)]
        for ec in range(8):
            bank = ec % 2
            for dc in range(8):
                wt = list(first_wait) if (ec == 0 and dc == 0) else []
                if dc == 0:
                    wt.append((c.s_act, bankK[bank]))
                P.add("tensor",
                      lambda t, o=c.psF[:, bank, :], l=wkT.ap()[:, dc, ec * 128:(ec + 1) * 128],
                             r=xt[b].ap()[:, dc, :], st=(dc == 0), sp=(dc == 7):
                      t.matmul(o, l, r, start=st, stop=sp),
                      wait=wt, inc=[(c.s_pe, 1)] if dc == 7 else ())
            P.add("scalar",
                  lambda s, o=c.kT.ap()[:, ec, sc * SCW:(sc + 1) * SCW], i=c.psF[:, bank, :]:
                  s.activation(o, i, AF.Copy),
                  wait=[(c.s_pe, c.s_pe.n)], inc=[(c.s_act, 1)])
            bankK[bank] = c.s_act.n
        for st4 in range(4):
            for half in range(2):
                bank = 2 + (st4 * 2 + half) % 2
                for dc in range(8):
                    wt = []
                    if dc == 0:
                        wt.append((c.s_dve, bankV[bank - 2]))
                    P.add("tensor",
                          lambda t, o=c.psF[:, bank, :],
                                 l=xt[b].ap()[:, dc, st4 * 128:(st4 + 1) * 128],
                                 r=wvT.ap()[:, dc, half * 512:(half + 1) * 512],
                                 st=(dc == 0), sp=(dc == 7):
                          t.matmul(o, l, r, start=st, stop=sp),
                          wait=wt, inc=[(c.s_pe, 1)] if dc == 7 else ())
                src = c.psF[:, bank, :].rearrange("p (h d) -> p h d", d=64)
                dst = c.v0.ap()[:, sc * 4 + st4, half * 8:(half + 1) * 8, 0:64]
                P.add("vector",
                      lambda v, o=dst, i=src: v.tensor_copy(o, i),
                      wait=[(c.s_pe, c.s_pe.n)], inc=[(c.s_dve, 1)])
                bankV[bank - 2] = c.s_dve.n
        xt_free_at[b] = c.s_pe.n

    if "l0kv" in DBG:
        dbg_dump(c, "kT", c.kT.ap(), [128, 8, S], BF16)
        dbg_dump(c, "v0", c.v0.ap(), [128, NST, NH, 65], BF16)
    for t in (wkT, wvT, xt[0], xt[1]):
        sb.freeit(t)
    P.barrier("l0kv_done")

def emit_finalize_tile(c, t_ap, scratch, stats, xn, row0, outf, outb):
    P = c.P
    sumsq = stats[:, 0:1]; sumsq2 = stats[:, 1:2]; ssum = stats[:, 2:3]
    m = stats[:, 3:4]; e2 = stats[:, 4:5]; veps = stats[:, 5:6]
    rstd = stats[:, 6:7]; nmr = stats[:, 7:8]; lnv = stats[:, 8:9]
    sqw = [(c.s_dve, c.s_dve.n)]
    P.add("scalar", lambda s: s.activation(scratch, t_ap[:, 0:512], AF.Square,
                                           accum_out=sumsq),
          wait=sqw)
    P.add("scalar", lambda s: s.activation(scratch, t_ap[:, 512:1024], AF.Square,
                                           accum_out=sumsq2),
          inc=[(c.s_act, 1)])
    sd = c.s_dve
    P.add("vector", lambda v: v.reduce_sum(ssum, t_ap, axis=AX.X), inc=[(sd, 1)])
    P.add("vector", lambda v: v.tensor_scalar(m, ssum, 1.0 / D, None, ALU.mult),
          wait=[(sd, sd.n)], inc=[(sd, 1)])
    P.add("vector", lambda v: v.tensor_add(sumsq, sumsq, sumsq2),
          wait=[(c.s_act, c.s_act.n)], inc=[(sd, 1)])
    P.add("vector", lambda v: v.tensor_scalar(e2, sumsq, 1.0 / D, EPS, ALU.mult, ALU.add),
          wait=[(sd, sd.n)], inc=[(sd, 1)])
    P.add("vector", lambda v: v.scalar_tensor_tensor(veps, m, -1.0, m, ALU.mult, ALU.mult),
          wait=[(sd, sd.n)], inc=[(sd, 1)])
    P.add("vector", lambda v: v.tensor_add(veps, veps, e2),
          wait=[(sd, sd.n)], inc=[(sd, 1)])
    P.add("scalar", lambda s: s.activation(lnv, veps, AF.Ln),
          wait=[(sd, sd.n)])
    P.add("scalar", lambda s: s.activation(rstd, lnv, AF.Exp, scale=-0.5),
          inc=[(c.s_act, 1)])
    P.add("vector", lambda v: v.scalar_tensor_tensor(nmr, m, -1.0, rstd, ALU.mult, ALU.mult),
          wait=[(c.s_act, c.s_act.n)], inc=[(sd, 1)])
    aw = [(sd, sd.n), (c.s_st, max(0, c.s_st.n - 16))]
    if outb is not None:
        aw.append((c.s_stg_o, max(0, c.s_stg_o.n - 16)))
    P.add("vector", lambda v: v.tensor_scalar(xn, t_ap, rstd, nmr, ALU.mult, ALU.add),
          wait=aw, inc=[(sd, 1)])
    w = [(c.s_dve, c.s_dve.n)]
    if "fin" in DBG and getattr(c, "_fin_n", 0) == 0:
        c._fin_n = 1
        dbg_dump(c, "stats", stats, [128, 12], F32)
        dbg_dump(c, "xn", xn, [128, D], F32)
    P.dma("scalar", outf[row0:row0 + 128, :], xn, c.s_st, wait=w)
    if outb is not None:
        P.dma("gpsimd", outb[row0:row0 + 128, :], xn, c.s_stg_o, wait=w)


def emit_l0attn(c):
    P, sb = c.P, c.sb
    wqT = sb.alloc("wqT", [128, 8, D], BF16)
    woT = sb.alloc("woT", [128, 8, D], BF16)
    bocast = sb.alloc("bocast", [128, D], BF16)
    ones64 = sb.alloc("ones64", [128, 64], F32)
    qT = sb.alloc("qT", [128, 8, SCW], BF16)
    xtq = sb.alloc("xtq", [128, 8, SCW], BF16)
    Pp = sb.alloc("Pp", [128, 2, SCW], BF16)
    ynT = sb.alloc("ynT", [128, 8, SCW], BF16)
    stat = sb.alloc("stat", [128, SCW], F32)
    x0r = sb.alloc("x0r", [128, D], F32)
    tt = sb.alloc("tt", [128, D], F32)
    xn = [sb.alloc(f"xn{i}", [128, D], F32) for i in range(2)]
    stats = sb.alloc("stats", [128, 12], F32)

    wait_stg = ((c.s_stg_a0, c.s_stg_a0.n),)
    load_wT(c, wqT, c.st_wq0, 8, wait_stg)
    load_wT(c, woT, c.st_wo0, 8, wait_stg)
    P.dma("gpsimd", bocast.ap(), c.bo0[None, :].broadcast_to([128, D]), c.s_ldw)
    ldw_done = c.s_ldw.n
    P.add("vector", lambda v: v.memset(ones64.ap(), 1.0), inc=[(c.s_dve, 1)])

    xtq_free = None
    for qc in range(NSC):
        w = ((c.s_stg_x, c.s_stg_x.n),) if qc == 0 else             ((c.s_pe, xtq_free), (c.s_act, c.s_act.n))
        load_xT(c, xtq.ap(), c.x0b, qc * SCW, SCW, w)
        ldx_done = c.s_ldx.n
        for ec in range(8):
            qb = 4 + ec % 2
            for dc in range(8):
                wt = []
                if ec == 0 and dc == 0:
                    wt = [(c.s_ldx, ldx_done), (c.s_ldw, ldw_done),
                          (c.s_act, c.s_act.n)]
                elif dc == 0:
                    wt.append((c.s_act, max(0, c.s_act.n - 1)))
                P.add("tensor",
                      lambda t, o=c.psF[:, qb, :], l=wqT.ap()[:, dc, ec * 128:(ec + 1) * 128],
                             r=xtq.ap()[:, dc, :], st=(dc == 0), sp=(dc == 7):
                      t.matmul(o, l, r, start=st, stop=sp),
                      wait=wt, inc=[(c.s_pe, 1)] if dc == 7 else ())
            P.add("scalar",
                  lambda s, o=qT.ap()[:, ec, :], i=c.psF[:, qb, :]:
                  s.activation(o, i, AF.Copy),
                  wait=[(c.s_pe, c.s_pe.n)], inc=[(c.s_act, 1)])
        xtq_free = c.s_pe.n
        qT_ready = c.s_act.n

        for h in range(NH):
            p0 = (h % 2) * 64
            chunk = h // 2
            avb = 2 + h % 2
            av_args = None
            for kt in range(NST):
                sbank = kt % 2
                wt = []
                if kt == 0 and h == 0:
                    wt.append((c.s_act, qT_ready))
                    wt.append((c.s_dve, c.s_dve.n))
                v_exp2ago = c.s_act.n - 1
                if v_exp2ago > 0:
                    wt.append((c.s_act, v_exp2ago))
                P.add("tensor",
                      lambda t, o=c.psF[:, sbank, :],
                             l=c.kT.ap()[p0:p0 + 64, chunk, kt * 128:(kt + 1) * 128],
                             r=qT.ap()[p0:p0 + 64, chunk, :]:
                      t.matmul(o, l, r, start=True, stop=True),
                      wait=wt, inc=[(c.s_pe, 1)])
                P.add("scalar",
                      lambda s, o=Pp.ap()[:, kt % 2, :], i=c.psF[:, sbank, :]:
                      s.activation(o, i, AF.Exp, scale=0.125),
                      wait=[(c.s_pe, c.s_pe.n)], inc=[(c.s_act, 1)])
                if av_args is not None:
                    P.add("tensor", av_args[0], wait=av_args[1], inc=av_args[2])
                avw = [(c.s_act, c.s_act.n)]
                if kt == 0:
                    avw.append((c.s_dve, max(0, c.s_dve.n - 3)))
                av_args = (
                    (lambda t, o=c.psF[0:65, avb, :], l=c.v0.ap()[:, kt, h, :],
                            r=Pp.ap()[:, kt % 2, :], st=(kt == 0), sp=(kt == NST - 1):
                     t.matmul(o, l, r, start=st, stop=sp)),
                    avw, [(c.s_pe, 1)] if kt == NST - 1 else ())
            P.add("tensor", av_args[0], wait=av_args[1], inc=av_args[2])
            P.add("vector", lambda v, o=stat.ap()[64:65, :], i=c.psF[64:65, avb, :]:
                  v.reciprocal(o, i),
                  wait=[(c.s_pe, c.s_pe.n)], inc=[(c.s_dve, 1)])
            P.add("tensor", lambda t, o=c.psF[0:64, 5, :], l=ones64.ap()[64:65, 0:64],
                           r=stat.ap()[64:65, :]:
                  t.matmul(o, l, r, start=True, stop=True),
                  wait=[(c.s_dve, c.s_dve.n), (c.s_dve, max(0, c.s_dve.n - 2))],
                  inc=[(c.s_pe, 1)])
            P.add("vector", lambda v, o=stat.ap()[0:64, :], i=c.psF[0:64, 5, :]:
                  v.tensor_copy(o, i),
                  wait=[(c.s_pe, c.s_pe.n)], inc=[(c.s_dve, 1)])
            P.add("vector", lambda v, o=ynT.ap()[p0:p0 + 64, chunk, :],
                           a=c.psF[0:64, avb, :], b2=stat.ap()[0:64, :]:
                  v.tensor_mul(o, a, b2),
                  wait=[(c.s_dve, c.s_dve.n)], inc=[(c.s_dve, 1)])
        heads_done = c.s_dve.n

        for st4 in range(4):
            row0 = qc * SCW + st4 * 128
            P.dma("sync", x0r.ap(), c.x[row0:row0 + 128, :], c.s_res,
                  wait=[(c.s_dve, max(0, c.s_dve.n - 1))])
            for half in range(2):
                bank = half
                for hc in range(8):
                    wt = []
                    if hc == 0:
                        wt.append((c.s_dve, heads_done))
                        wt.append((c.s_dve, max(0, c.s_dve.n - 2)))
                    P.add("tensor",
                          lambda t, o=c.psF[:, bank, :],
                                 l=ynT.ap()[:, hc, st4 * 128:(st4 + 1) * 128],
                                 r=woT.ap()[:, hc, half * 512:(half + 1) * 512],
                                 st=(hc == 0), sp=(hc == 7):
                          t.matmul(o, l, r, start=st, stop=sp),
                          wait=wt, inc=[(c.s_pe, 1)] if hc == 7 else ())
            pe_done = c.s_pe.n
            b = st4 % 2
            t_ap = tt.ap()
            P.add("vector", lambda v, o=t_ap[:, 0:512], i=c.psF[:, 0, :],
                           r=x0r.ap()[:, 0:512]:
                  v.tensor_add(o, i, r),
                  wait=[(c.s_pe, pe_done), (c.s_res, c.s_res.n),
                        (c.s_act, c.s_act.n)])
            P.add("vector", lambda v, o=t_ap[:, 512:1024], i=c.psF[:, 1, :],
                           r=x0r.ap()[:, 512:1024]:
                  v.tensor_add(o, i, r), inc=[(c.s_dve, 1)])
            bankO = c.s_dve.n
            last_adds = c.s_dve.n
            P.add("vector", lambda v, o=t_ap, i=bocast.ap():
                  v.tensor_add(o, o, i),
                  wait=[(c.s_dve, c.s_dve.n)], inc=[(c.s_dve, 1)])
            if "l0attn" in DBG and qc == 0 and st4 == 0:
                dbg_dump(c, "t0", t_ap, [128, D], F32)
            emit_finalize_tile(c, t_ap, stat.ap(), stats.ap(), xn[b].ap(),
                               row0, c.x1f, c.x1b)
        if "l0attn" in DBG and qc == 0:
            dbg_dump(c, "qT", qT.ap(), [128, 8, SCW], BF16)
            dbg_dump(c, "ynT", ynT.ap(), [128, 8, SCW], BF16)

    for t in (wqT, woT, bocast, ones64, qT, xtq, Pp, ynT, stat, x0r,
              tt, xn[0], xn[1], stats, c.kT, c.v0):
        sb.freeit(t)
    P.barrier("l0attn_done")


def emit_finish(c):
    P = c.P
    P.add("scalar", wait=[(c.s_st, c.s_st.n)])
    P.add("gpsimd", wait=[(c.s_stg_o, c.s_stg_o.n)])
    P.add("sync", wait=[(c.s_st, c.s_st.n), (c.s_stg_o, c.s_stg_o.n)])


def build_nc(phases, taps=False):
    nc = bass.Bass("TRN2", target_bir_lowering=False, debug=False)
    c = make_ctx(nc, taps=taps)
    for ph in phases:
        ph(c)
    emit_finish(c)
    with nc.Block() as block:
        @block.tensor
        def _(eng):
            c.P.replay("tensor", eng)
        @block.vector
        def _(eng):
            c.P.replay("vector", eng)
        @block.scalar
        def _(eng):
            c.P.replay("scalar", eng)
        @block.gpsimd
        def _(eng):
            c.P.replay("gpsimd", eng)
        @block.sync
        def _(eng):
            c.P.replay("sync", eng)
    return nc, c


def make_inmap(inputs, core):
    hc = host_consts()
    a = lambda k: np.ascontiguousarray(inputs[k])
    rs = lambda v: np.ascontiguousarray(np.asarray(v).reshape(-1, 128).T)
    m = dict(
        x=np.ascontiguousarray(inputs["L0_attn_x"][core]),
        wq0=a("L0_attn_w_q"), wk0=a("L0_attn_w_k"), wv0=a("L0_attn_w_v"),
        wo0=a("L0_attn_w_o"), bo0=a("L0_attn_b_o"),
        W1=a("L0_ffn_W1"),
        b1=np.ascontiguousarray(np.asarray(inputs["L0_ffn_b1"]).reshape(32, 128).T),
        W2=a("L0_ffn_W2"), b2=a("L0_ffn_b2"),
        scale0=np.ascontiguousarray(inputs["L0_ffn_scale"]).reshape(1, 1),
        wq1=a("L1_attn_w_q"), wk1=a("L1_attn_w_k"), wv1=a("L1_attn_w_v"),
        wo1=a("L1_attn_w_o"),
        w1w=a("L1_ffn_w1_w"), w1s=rs(inputs["L1_ffn_w1_s"]),
        w2w=a("L1_ffn_w2_w"), w2s=rs(inputs["L1_ffn_w2_s"]),
        h1w=a("L1_ffn_h1_w"), h1s=rs(inputs["L1_ffn_h1_s"]),
        h2w=a("L1_ffn_h2_w"), h2s=rs(inputs["L1_ffn_h2_s"]),
        w3w=a("L1_ffn_w3_w"), w3s=a("L1_ffn_w3_s"),
        cos=hc["cos"], sin=hc["sin"], tri=hc["tri"], eye=hc["eye"],
    )
    return m


DBG = set()


def dbg_dump(c, name, ap, shape, dtype):
    t = c.nc.dram_tensor("dbg_" + name, list(shape), dtype, kind="ExternalOutput").ap()
    P = c.P
    P.dma("sync", t, ap, c.s_ldx,
          wait=[(c.s_pe, c.s_pe.n), (c.s_act, c.s_act.n), (c.s_dve, c.s_dve.n)])
    for e in ENGINES:
        P.pending[e].append((c.s_ldx, c.s_ldx.n))


def kernel(**inputs):
    phases = [emit_stage, emit_l0kv, emit_l0attn, emit_l0ffnw, emit_l0ffn,
              emit_l1prep, emit_l1attn, emit_l1ffnA, emit_l1ffnB, emit_l1ffnC,
              emit_l1ffnD]
    nc, c = build_nc(phases, taps=False)
    in_maps = [make_inmap(inputs, core) for core in range(8)]
    res = run_bass_kernel_spmd(nc, in_maps, core_ids=list(range(8)))
    out = np.stack([res.results[i]["y"] for i in range(8)], axis=0)
    return out.astype(np.float32)


def emit_l0ffnw(c):
    P, sb = c.P, c.sb
    c.W1T = sb.alloc("W1T", [128, 8, HFF], BF16)
    c.W2T = sb.alloc("W2T", [128, 32, D], BF16)
    c.b1t = sb.alloc("b1t", [128, 32], F32)
    c.b2cast = sb.alloc("b2cast", [128, D], F32)
    xre = sb.alloc("xre", [1, D], F32)
    xro = sb.alloc("xro", [1, D], F32)
    obf = sb.alloc("obf", [1, D], BF16)
    sev = sb.alloc("sev", [1, D], BF16)
    ev4 = sb.alloc("ev4", [1, 256], BF16)
    scs = sb.alloc("scs", [1, 1], F32)

    wait_stg = ((c.s_stg_f0, c.s_stg_f0.n),)
    load_wT(c, c.W1T, c.st_W1, 8, wait_stg)
    load_wT(c, c.W2T, c.st_W2, 32, wait_stg)
    ldw_done = c.s_ldw.n
    P.dma("sync", c.b1t.ap(), c.b1, c.s_ldw)
    P.dma("sync", c.b2cast.ap(), c.b2[None, :].broadcast_to([128, D]), c.s_ldw)
    P.dma("sync", scs.ap(), c.scale0, c.s_ldw)
    sd = c.s_dve

    def rows_prep(tok_e, tok_o):
        P.dma("sync", xre.ap(), c.x1f[tok_e:tok_e + 1, :], c.s_res,
              wait=[(sd, sd.n)])
        P.dma("sync", xro.ap(), c.x1f[tok_o:tok_o + 1, :], c.s_res)
        P.add("vector", lambda v: v.tensor_copy(obf.ap(), xro.ap()),
              wait=[(c.s_res, c.s_res.n), (c.s_ldw, ldw_done + 48),
                    (c.s_pe, c.s_pe.n)], inc=[(sd, 1)])
        P.add("vector", lambda v: v.tensor_scalar(sev.ap(), xre.ap(),
                                                  scs.ap()[0:1, 0:1], None, ALU.mult),
              inc=[(sd, 1)])

    for p in range(4):
        rows_prep(1 + 2 * p, 2 + 2 * p)
        rows_ready = sd.n
        for dc in range(8):
            for half in range(2):
                bank = (dc * 2 + half) % 2
                wt = [(sd, rows_ready)]
                wt.append((sd, max(0, sd.n - 1)))
                P.add("tensor",
                      lambda t, o=c.psF[:, bank, :], l=obf.ap()[0:1, dc * 128:(dc + 1) * 128],
                             r=sev.ap()[0:1, half * 512:(half + 1) * 512]:
                      t.matmul(o, l, r, start=True, stop=True),
                      wait=wt, inc=[(c.s_pe, 1)])
                dst = c.W1T.ap()[:, dc, p * 1024 + half * 512: p * 1024 + (half + 1) * 512]
                P.add("vector",
                      lambda v, o=dst, i=c.psF[:, bank, :]:
                      v.tensor_add(o, o, i),
                      wait=[(c.s_pe, c.s_pe.n)], inc=[(sd, 1)])
    for p in range(4):
        rows_prep(9 + 2 * p, 10 + 2 * p)
        for k in range(4):
            P.add("vector",
                  lambda v, o=ev4.ap(), i=sev.ap().rearrange("a (q f) -> a q f", f=4)[0:1, :, k]:
                  v.tensor_copy(o, i),
                  wait=[(sd, sd.n), (c.s_pe, c.s_pe.n)], inc=[(sd, 1)])
            ev_ready = sd.n
            for jc in range(8):
                bank = jc % 2
                wt = [(sd, ev_ready), (sd, max(0, sd.n - 1))]
                P.add("tensor",
                      lambda t, o=c.psF[:, bank, 0:256], l=obf.ap()[0:1, jc * 128:(jc + 1) * 128],
                             r=ev4.ap():
                      t.matmul(o, l, r, start=True, stop=True),
                      wait=wt, inc=[(c.s_pe, 1)])
                dst = c.W2T.ap()[:, 8 * k + jc, 256 * p: 256 * (p + 1)]
                P.add("vector",
                      lambda v, o=dst, i=c.psF[:, bank, 0:256]:
                      v.tensor_add(o, o, i),
                      wait=[(c.s_pe, c.s_pe.n)], inc=[(sd, 1)])

    for t in (xre, xro, obf, sev, ev4, scs):
        sb.freeit(t)
    P.barrier("l0ffnw_done")


def emit_l0ffn(c):
    P, sb = c.P, c.sb
    h_sb = sb.alloc("h0", [128, 32, SCW], BF16)
    xt = [sb.alloc(f"xtf{i}", [128, 8, SCW], BF16) for i in range(2)]
    x1r = sb.alloc("x1r", [128, D], F32)
    tt = sb.alloc("ttf", [128, D], F32)
    xn = [sb.alloc(f"xnf{i}", [128, D], F32) for i in range(2)]
    stats = sb.alloc("statsf", [128, 12], F32)
    sqs = sb.alloc("sqsf", [128, 512], F32)

    xt_free = [None, None]
    w2_done = None
    bankH = [0, 0]
    bankO2 = 0
    for sc8 in range(NSC):
        b = sc8 % 2
        w = [] if xt_free[b] is None else [(c.s_pe, xt_free[b])]
        load_xT(c, xt[b].ap(), c.x1b, sc8 * SCW, SCW, w)
        ldx_done = c.s_ldx.n
        for hc in range(32):
            bank = hc % 2
            for dc in range(8):
                wt = []
                if hc == 0 and dc == 0:
                    wt.append((c.s_ldx, ldx_done))
                if dc == 0:
                    wt.append((c.s_act, bankH[bank]))
                P.add("tensor",
                      lambda t, o=c.psF[:, bank, :], l=c.W1T.ap()[:, dc, hc * 128:(hc + 1) * 128],
                             r=xt[b].ap()[:, dc, :], st=(dc == 0), sp=(dc == 7):
                      t.matmul(o, l, r, start=st, stop=sp),
                      wait=wt, inc=[(c.s_pe, 1)] if dc == 7 else ())
            gw = [(c.s_pe, c.s_pe.n)]
            if w2_done is not None:
                gw.append((c.s_pe, w2_done))
            P.add("scalar",
                  lambda s, o=h_sb.ap()[:, hc, :], i=c.psF[:, bank, :],
                         bb=c.b1t.ap()[:, hc:hc + 1]:
                  s.activation(o, i, AF.Gelu, bias=bb),
                  wait=gw, inc=[(c.s_act, 1)])
            bankH[bank] = c.s_act.n
        h_done = c.s_act.n
        xt_free[b] = c.s_pe.n
        for st4 in range(4):
            row0 = sc8 * SCW + st4 * 128
            P.dma("sync", x1r.ap(), c.x1f[row0:row0 + 128, :], c.s_res,
                  wait=[(c.s_dve, max(0, c.s_dve.n - 1))])
            for half in range(2):
                bank = 2 + half
                for hc in range(32):
                    wt = []
                    if hc == 0:
                        wt.append((c.s_act, h_done))
                        wt.append((c.s_dve, bankO2))
                    P.add("tensor",
                          lambda t, o=c.psF[:, bank, :],
                                 l=h_sb.ap()[:, hc, st4 * 128:(st4 + 1) * 128],
                                 r=c.W2T.ap()[:, hc, half * 512:(half + 1) * 512],
                                 st=(hc == 0), sp=(hc == 31):
                          t.matmul(o, l, r, start=st, stop=sp),
                          wait=wt, inc=[(c.s_pe, 1)] if hc == 31 else ())
            pe_done = c.s_pe.n
            bx = st4 % 2
            t_ap = tt.ap()
            P.add("vector", lambda v, o=t_ap[:, 0:512], i=c.psF[:, 2, :],
                           r=x1r.ap()[:, 0:512]:
                  v.tensor_add(o, i, r),
                  wait=[(c.s_pe, pe_done), (c.s_res, c.s_res.n),
                        (c.s_act, c.s_act.n)])
            P.add("vector", lambda v, o=t_ap[:, 512:1024], i=c.psF[:, 3, :],
                           r=x1r.ap()[:, 512:1024]:
                  v.tensor_add(o, i, r), inc=[(c.s_dve, 1)])
            bankO2 = c.s_dve.n
            P.add("vector", lambda v, o=t_ap, i=c.b2cast.ap():
                  v.tensor_add(o, o, i),
                  wait=[(c.s_dve, c.s_dve.n)], inc=[(c.s_dve, 1)])
            emit_finalize_tile(c, t_ap, sqs.ap(), stats.ap(), xn[bx].ap(),
                               row0, c.x2f, c.x2b)
        w2_done = c.s_pe.n

    for t in (h_sb, xt[0], xt[1], x1r, tt, xn[0], xn[1], stats, sqs,
              c.W1T, c.W2T, c.b1t, c.b2cast):
        sb.freeit(t)
    P.barrier("l0ffn_done")


def emit_l1prep(c):
    P, sb = c.P, c.sb
    c.q1T = sb.alloc("q1T", [128, 8, S], BF16)
    c.k1T = sb.alloc("k1T", [128, 2, S], BF16)
    c.v1 = sb.alloc("v1", [128, NST, NKV, 65], BF16)
    wq1s = sb.alloc("wq1s", [128, 8, D], BF16)
    wk1s = sb.alloc("wk1s", [128, 8, 256], BF16)
    wv1s = sb.alloc("wv1s", [128, 8, 256], BF16)
    eyes = sb.alloc("eyes", [128, 128], BF16)
    xt = [sb.alloc(f"xtp{i}", [128, 8, SCW], BF16) for i in range(2)]
    qf = sb.alloc("qf", [128, NH, DK], F32)
    kf = sb.alloc("kf", [128, NKV, DK], F32)
    qsq = sb.alloc("qsq", [128, D], F32)
    qn = sb.alloc("qn", [128, NH, DK], F32)
    kn = sb.alloc("kn", [128, NKV, DK], F32)
    qrot = sb.alloc("qrot", [128, D], BF16)
    krot = sb.alloc("krot", [128, 256], BF16)
    cosb = [sb.alloc(f"cosb{i}", [128, 32], F32) for i in range(2)]
    sinb = [sb.alloc(f"sinb{i}", [128, 32], F32) for i in range(2)]
    rstat = sb.alloc("rstat", [128, NH * 4], F32)

    sd = c.s_dve
    P.dma("gpsimd", wq1s.ap(), c.wq1.rearrange("(a p) n -> p a n", p=128), c.s_ldw)
    P.dma("gpsimd", wk1s.ap(), c.wk1.rearrange("(a p) n -> p a n", p=128), c.s_ldw)
    P.dma("gpsimd", wv1s.ap(), c.wv1.rearrange("(a p) n -> p a n", p=128), c.s_ldw)
    P.dma("sync", eyes.ap(), c.eye, c.s_ldw)
    ldw_done = c.s_ldw.n
    P.add("vector", lambda v: v.memset(c.v1.ap()[:, :, :, 64:65], 1.0), inc=[(sd, 1)])

    sumq = rstat.ap()[:, 0:16]; vq = rstat.ap()[:, 16:32]
    sumk = rstat.ap()[:, 32:36]; vk = rstat.ap()[:, 36:40]
    rq = rstat.ap()[:, 40:56]; rk = rstat.ap()[:, 56:60]

    xt_free = [None, None]
    prev_tr_evac = 0
    for st in range(NST):
        sc8, off = st // 4, (st % 4) * 128
        cb = st % 2
        if st % 4 == 0:
            fa = xt_free[(st // 4) % 2]
            w = [(c.s_ldw, ldw_done)] if st == 0 else                 ([(c.s_pe, fa)] if fa is not None else [])
            load_xT(c, xt[(st // 4) % 2].ap(), c.x2b, sc8 * SCW, SCW, w)
        xb = xt[(st // 4) % 2]
        ldx_done = c.s_ldx.n
        P.dma("sync", cosb[cb].ap(), c.cos[st * 128:(st + 1) * 128, :], c.s_res,
              wait=[(sd, sd.n)])
        P.dma("sync", sinb[cb].ap(), c.sin[st * 128:(st + 1) * 128, :], c.s_res)
        res_done = c.s_res.n

        for half in range(2):
            for dc in range(8):
                wt = []
                if half == 0 and dc == 0:
                    wt = [(c.s_ldx, ldx_done), (sd, max(0, sd.n - 2))]
                P.add("tensor",
                      lambda t, o=c.psF[:, half, :], l=xb.ap()[:, dc, off:off + 128],
                             r=wq1s.ap()[:, dc, half * 512:(half + 1) * 512],
                             st_=(dc == 0), sp=(dc == 7):
                      t.matmul(o, l, r, start=st_, stop=sp),
                      wait=wt, inc=[(c.s_pe, 1)] if dc == 7 else ())
        for (bank, wsb) in ((2, wk1s), (3, wv1s)):
            for dc in range(8):
                P.add("tensor",
                      lambda t, o=c.psF[:, bank, 0:256], l=xb.ap()[:, dc, off:off + 128],
                             r=wsb.ap()[:, dc, :], st_=(dc == 0), sp=(dc == 7):
                      t.matmul(o, l, r, start=st_, stop=sp),
                      wait=(), inc=[(c.s_pe, 1)] if dc == 7 else ())
        proj_done = c.s_pe.n
        qv = qf.ap().rearrange("p (a b) d -> p a b d", b=2)
        pv0 = c.psF[:, 0, :].rearrange("p (h d) -> p h d", d=DK)
        pv1 = c.psF[:, 1, :].rearrange("p (h d) -> p h d", d=DK)
        P.add("vector", lambda v, o=qv[:, 0:4, 0, :], i=pv0[:, 0:4, :]: v.tensor_copy(o, i),
              wait=[(c.s_pe, proj_done)])
        P.add("vector", lambda v, o=qv[:, 0:4, 1, :], i=pv0[:, 4:8, :]: v.tensor_copy(o, i))
        P.add("vector", lambda v, o=qv[:, 4:8, 0, :], i=pv1[:, 0:4, :]: v.tensor_copy(o, i))
        P.add("vector", lambda v, o=qv[:, 4:8, 1, :], i=pv1[:, 4:8, :]: v.tensor_copy(o, i))
        P.add("vector", lambda v, o=kf.ap(), i=c.psF[:, 2, 0:256].rearrange("p (h d) -> p h d", d=DK):
              v.tensor_copy(o, i))
        P.add("vector", lambda v, o=c.v1.ap()[:, st, :, 0:64],
                       i=c.psF[:, 3, 0:256].rearrange("p (h d) -> p h d", d=DK):
              v.tensor_copy(o, i), inc=[(sd, 1)])
        evac_done = sd.n

        P.add("scalar", lambda s: s.activation(qsq.ap(), qf.ap().rearrange("p a b -> p (a b)"),
                                               AF.Square),
              wait=[(sd, evac_done)], inc=[(c.s_act, 1)])
        P.add("vector", lambda v: v.reduce_sum(sumq.rearrange("p (a b) -> p a b", b=1),
                                               qsq.ap().rearrange("p (a b) -> p a b", b=DK),
                                               axis=AX.X),
              wait=[(c.s_act, c.s_act.n)], inc=[(sd, 1)])
        P.add("scalar", lambda s: s.activation(qsq.ap()[:, 0:256],
                                               kf.ap().rearrange("p a b -> p (a b)"),
                                               AF.Square),
              wait=[(sd, sd.n)], inc=[(c.s_act, 1)])
        P.add("vector", lambda v: v.reduce_sum(sumk.rearrange("p (a b) -> p a b", b=1),
                                               qsq.ap()[:, 0:256].rearrange("p (a b) -> p a b", b=DK),
                                               axis=AX.X),
              wait=[(c.s_act, c.s_act.n)], inc=[(sd, 1)])
        P.add("vector", lambda v: v.tensor_scalar(vq, sumq, 1.0 / DK, EPS, ALU.mult, ALU.add),
              wait=[(sd, sd.n)])
        P.add("vector", lambda v: v.tensor_scalar(vk, sumk, 1.0 / DK, EPS, ALU.mult, ALU.add),
              inc=[(sd, 1)])
        P.add("scalar", lambda s: s.activation(vq, vq, AF.Ln), wait=[(sd, sd.n)])
        P.add("scalar", lambda s: s.activation(rq, vq, AF.Exp, scale=-0.5))
        P.add("scalar", lambda s: s.activation(vk, vk, AF.Ln))
        P.add("scalar", lambda s: s.activation(rk, vk, AF.Exp, scale=-0.5),
              inc=[(c.s_act, 1)])
        P.add("vector", lambda v: v.tensor_tensor(
              qn.ap(), qf.ap(),
              rq.rearrange("p (a b) -> p a b", b=1).broadcast_to([128, NH, DK]), ALU.mult),
              wait=[(c.s_act, c.s_act.n)])
        P.add("vector", lambda v: v.tensor_tensor(
              kn.ap(), kf.ap(),
              rk.rearrange("p (a b) -> p a b", b=1).broadcast_to([128, NKV, DK]), ALU.mult),
              inc=[(sd, 1)])
        t1q = qsq.ap()[:, 0:512].rearrange("p (a b) -> p a b", b=32)
        t2q = qsq.ap()[:, 512:1024].rearrange("p (a b) -> p a b", b=32)
        cq = cosb[cb].ap().rearrange("p (a b) -> p a b", a=1).broadcast_to([128, NH, 32])
        sq_ = sinb[cb].ap().rearrange("p (a b) -> p a b", a=1).broadcast_to([128, NH, 32])
        ck = cosb[cb].ap().rearrange("p (a b) -> p a b", a=1).broadcast_to([128, NKV, 32])
        sk = sinb[cb].ap().rearrange("p (a b) -> p a b", a=1).broadcast_to([128, NKV, 32])
        qnv = qn.ap(); knv = kn.ap()
        qrv = qrot.ap().rearrange("p (a b) -> p a b", b=DK)
        krv = krot.ap().rearrange("p (a b) -> p a b", b=DK)
        P.add("vector", lambda v, a=t1q, b=cq: v.tensor_tensor(a, qnv[:, :, 0:32], b, ALU.mult),
              wait=[(sd, sd.n), (c.s_res, res_done)])
        P.add("vector", lambda v, a=t2q, b=sq_: v.tensor_tensor(a, qnv[:, :, 32:64], b, ALU.mult),
              inc=[(sd, 1)])
        P.add("vector", lambda v, a=t1q, b=t2q: v.tensor_sub(qrv[:, :, 0:32], a, b),
              wait=[(sd, sd.n)])
        P.add("vector", lambda v, a=t1q, b=cq: v.tensor_tensor(a, qnv[:, :, 32:64], b, ALU.mult))
        P.add("vector", lambda v, a=t2q, b=sq_: v.tensor_tensor(a, qnv[:, :, 0:32], b, ALU.mult),
              inc=[(sd, 1)])
        P.add("vector", lambda v, a=t1q, b=t2q: v.tensor_add(qrv[:, :, 32:64], a, b),
              wait=[(sd, sd.n)])
        t1k = qsq.ap()[:, 0:128].rearrange("p (a b) -> p a b", b=32)
        t2k = qsq.ap()[:, 128:256].rearrange("p (a b) -> p a b", b=32)
        P.add("vector", lambda v, a=t1k, b=ck: v.tensor_tensor(a, knv[:, :, 0:32], b, ALU.mult))
        P.add("vector", lambda v, a=t2k, b=sk: v.tensor_tensor(a, knv[:, :, 32:64], b, ALU.mult),
              inc=[(sd, 1)])
        P.add("vector", lambda v, a=t1k, b=t2k: v.tensor_sub(krv[:, :, 0:32], a, b),
              wait=[(sd, sd.n)])
        P.add("vector", lambda v, a=t1k, b=ck: v.tensor_tensor(a, knv[:, :, 32:64], b, ALU.mult))
        P.add("vector", lambda v, a=t2k, b=sk: v.tensor_tensor(a, knv[:, :, 0:32], b, ALU.mult),
              inc=[(sd, 1)])
        P.add("vector", lambda v, a=t1k, b=t2k: v.tensor_add(krv[:, :, 32:64], a, b),
              wait=[(sd, sd.n)], inc=[(sd, 1)])
        rope_done = sd.n

        a_ = st % 2
        for ec in range(8):
            wt = [(sd, rope_done)] if ec == 0 else []
            if ec == 0:
                wt.append((sd, prev_tr_evac))
            P.add("tensor",
                  lambda t, o=c.psT[:, a_, ec * 128:(ec + 1) * 128],
                         i=qrot.ap()[:, ec * 128:(ec + 1) * 128], e=eyes.ap():
                  t.transpose(o, i, e),
                  wait=wt, inc=[(c.s_pe, 1)] if ec == 7 else ())
        for ec in range(2):
            P.add("tensor",
                  lambda t, o=c.psT[:, 1 - a_, ec * 128:(ec + 1) * 128],
                         i=krot.ap()[:, ec * 128:(ec + 1) * 128], e=eyes.ap():
                  t.transpose(o, i, e),
                  wait=(), inc=[(c.s_pe, 1)] if ec == 1 else ())
        P.add("vector",
              lambda v, o=c.q1T.ap()[:, :, st * 128:(st + 1) * 128],
                     i=c.psT[:, a_, :].rearrange("p (a b) -> p a b", b=128):
              v.tensor_copy(o, i),
              wait=[(c.s_pe, c.s_pe.n)])
        P.add("vector",
              lambda v, o=c.k1T.ap()[:, :, st * 128:(st + 1) * 128],
                     i=c.psT[:, 1 - a_, 0:256].rearrange("p (a b) -> p a b", b=128):
              v.tensor_copy(o, i),
              inc=[(sd, 1)])
        if "l1prepd" in DBG and st == 0:
            dbg_dump(c, "qf", qf.ap(), [128, NH, DK], F32)
            dbg_dump(c, "qn", qn.ap(), [128, NH, DK], F32)
            dbg_dump(c, "qrot", qrot.ap(), [128, D], BF16)
            dbg_dump(c, "kf", kf.ap(), [128, NKV, DK], F32)
            dbg_dump(c, "krot", krot.ap(), [128, 256], BF16)
            dbg_dump(c, "rstat", rstat.ap(), [128, NH * 4], F32)
        prev_tr_evac = sd.n
        xt_free[(st // 4) % 2] = c.s_pe.n

    if "l1prep" in DBG:
        dbg_dump(c, "q1T", c.q1T.ap(), [128, 8, S], BF16)
        dbg_dump(c, "k1T", c.k1T.ap(), [128, 2, S], BF16)
        dbg_dump(c, "v1", c.v1.ap(), [128, NST, NKV, 65], BF16)
    for t in (wq1s, wk1s, wv1s, eyes, xt[0], xt[1], qf, kf, qsq, qn, kn,
              qrot, krot, cosb[0], cosb[1], sinb[0], sinb[1], rstat):
        sb.freeit(t)
    P.barrier("l1prep_done")


def emit_l1attn(c):
    P, sb = c.P, c.sb
    wo1p = sb.alloc("wo1p", [128, 8, D], BF16)
    trim = sb.alloc("trim", [128, 128], BF16)
    Pp = sb.alloc("Pp1", [128, 2, SCW], BF16)
    ynT = sb.alloc("ynT1", [128, 8, SCW], BF16)
    stat = sb.alloc("stat1", [128, SCW], F32)
    tcap = sb.alloc("tcap", [128, SCW], F32)
    ones64 = sb.alloc("ones64b", [128, 64], F32)
    x2r = sb.alloc("x2r", [128, D], F32)
    tt = sb.alloc("tt1", [128, D], F32)
    xn = [sb.alloc(f"xn1{i}", [128, D], F32) for i in range(2)]
    stats = sb.alloc("stats1", [128, 12], F32)
    sqs = sb.alloc("sqs1", [128, 512], F32)

    sd = c.s_dve
    for p in range(NH):
        h = PERM1[p]
        b0 = (p % 2) * 64
        P.dma("gpsimd", wo1p.ap()[b0:b0 + 64, p // 2, :],
              c.wo1[h * 64:(h + 1) * 64, :], c.s_ldw)
    P.dma("sync", trim.ap(), c.tri, c.s_ldw)
    ldw_done = c.s_ldw.n
    P.add("vector", lambda v: v.memset(ones64.ap(), 1.0), inc=[(sd, 1)])

    ynmul_done = {0: 0, 1: 0}
    prev_bcs = 0
    bankS1 = [0, 0]
    bankO1 = 0
    last_adds1 = 0
    for qc in range(NSC):
        for pos in range(NH):
            h = PERM1[pos]
            p0 = (pos % 2) * 64
            chunk = pos // 2
            j = h // 4
            jchunk = j // 2
            avb = 2 + pos % 2
            kt_last = min(NST - 1, 4 * qc + 3)
            av_args = None
            for kt in range(kt_last + 1):
                m = kt - 4 * qc
                col0 = 128 * m if m > 0 else 0
                ncol = 512 - col0
                sbank = kt % 2
                wt = []
                if kt == 0 and pos == 0 and qc == 0:
                    wt.append((c.s_ldw, ldw_done))
                if kt == 0 and pos == 0:
                    wt.append((sd, last_adds1))
                wt.append((c.s_act, bankS1[sbank]))
                P.add("tensor",
                      lambda t, o=c.psF[:, sbank, col0:512],
                             l=c.k1T.ap()[p0:p0 + 64, jchunk, kt * 128:(kt + 1) * 128],
                             r=c.q1T.ap()[p0:p0 + 64, chunk, qc * 512 + col0: (qc + 1) * 512]:
                      t.matmul(o, l, r, start=True, stop=True),
                      wait=wt, inc=[(c.s_pe, 1)])
                P.add("scalar",
                      lambda s, o=tcap.ap()[:, col0:512], i=c.psF[:, sbank, col0:512]:
                      s.activation(o, i, AF.Tanh, scale=1.0 / (8.0 * CAP)),
                      wait=[(c.s_pe, c.s_pe.n)], inc=[(c.s_act, 1)])
                bankS1[sbank] = c.s_act.n
                P.add("scalar",
                      lambda s, o=Pp.ap()[:, kt % 2, col0:512], i=tcap.ap()[:, col0:512]:
                      s.activation(o, i, AF.Exp, scale=CAP),
                      inc=[(c.s_act, 1)])
                masked = 0 <= m <= 3
                if masked:
                    P.add("vector",
                          lambda v, o=Pp.ap()[:, kt % 2, col0:col0 + 128], tm=trim.ap():
                          v.tensor_mul(o, o, tm),
                          wait=[(c.s_act, c.s_act.n)], inc=[(sd, 1)])
                if av_args is not None:
                    P.add("tensor", av_args[0], wait=av_args[1], inc=av_args[2])
                avw = [(sd, sd.n)] if masked else [(c.s_act, c.s_act.n)]
                if kt == 0:
                    avw.append((sd, ynmul_done[pos % 2]))
                av_args = (
                    (lambda t, o=c.psF[0:65, avb, col0:512], l=c.v1.ap()[:, kt, j, :],
                            r=Pp.ap()[:, kt % 2, col0:512], st=(kt == 0), sp=(kt == kt_last):
                     t.matmul(o, l, r, start=st, stop=sp)),
                    avw, [(c.s_pe, 1)] if kt == kt_last else ())
            P.add("tensor", av_args[0], wait=av_args[1], inc=av_args[2])
            P.add("vector", lambda v, o=stat.ap()[64:65, :], i=c.psF[64:65, avb, :]:
                  v.reciprocal(o, i),
                  wait=[(c.s_pe, c.s_pe.n), (c.s_act, c.s_act.n)], inc=[(sd, 1)])
            P.add("tensor", lambda t, o=c.psF[0:64, 5, :], l=ones64.ap()[64:65, 0:64],
                           r=stat.ap()[64:65, :]:
                  t.matmul(o, l, r, start=True, stop=True),
                  wait=[(sd, sd.n), (sd, prev_bcs)],
                  inc=[(c.s_pe, 1)])
            P.add("vector", lambda v, o=stat.ap()[0:64, :], i=c.psF[0:64, 5, :]:
                  v.tensor_copy(o, i),
                  wait=[(c.s_pe, c.s_pe.n)], inc=[(sd, 1)])
            prev_bcs = sd.n
            P.add("vector", lambda v, o=ynT.ap()[p0:p0 + 64, chunk, :],
                           a=c.psF[0:64, avb, :], b2=stat.ap()[0:64, :]:
                  v.tensor_mul(o, a, b2),
                  wait=[(sd, sd.n)], inc=[(sd, 1)])
            ynmul_done[pos % 2] = sd.n
        heads_done = sd.n

        for st4 in range(4):
            row0 = qc * SCW + st4 * 128
            P.dma("sync", x2r.ap(), c.x2f[row0:row0 + 128, :], c.s_res,
                  wait=[(sd, max(0, sd.n - 1))])
            for half in range(2):
                bank = half
                for hc in range(8):
                    wt = []
                    if hc == 0:
                        wt.append((sd, heads_done))
                        wt.append((sd, bankO1))
                    P.add("tensor",
                          lambda t, o=c.psF[:, bank, :],
                                 l=ynT.ap()[:, hc, st4 * 128:(st4 + 1) * 128],
                                 r=wo1p.ap()[:, hc, half * 512:(half + 1) * 512],
                                 st=(hc == 0), sp=(hc == 7):
                          t.matmul(o, l, r, start=st, stop=sp),
                          wait=wt, inc=[(c.s_pe, 1)] if hc == 7 else ())
            pe_done = c.s_pe.n
            bx = st4 % 2
            t_ap = tt.ap()
            P.add("vector", lambda v, o=t_ap[:, 0:512], i=c.psF[:, 0, :],
                           r=x2r.ap()[:, 0:512]:
                  v.tensor_add(o, i, r),
                  wait=[(c.s_pe, pe_done), (c.s_res, c.s_res.n),
                        (c.s_act, c.s_act.n)])
            P.add("vector", lambda v, o=t_ap[:, 512:1024], i=c.psF[:, 1, :],
                           r=x2r.ap()[:, 512:1024]:
                  v.tensor_add(o, i, r), inc=[(sd, 1)])
            bankO1 = sd.n
            last_adds1 = sd.n
            if "l1attn" in DBG and qc == 1 and st4 == 0:
                dbg_dump(c, "t1", t_ap, [128, D], F32)
            emit_finalize_tile(c, t_ap, sqs.ap(), stats.ap(), xn[bx].ap(),
                               row0, c.x3f, c.x3b)
        if "l1attn" in DBG and qc == 1:
            dbg_dump(c, "ynT1", ynT.ap(), [128, 8, SCW], BF16)

    for t in (wo1p, trim, Pp, ynT, stat, tcap, ones64, x2r, tt, xn[0], xn[1],
              stats, sqs, c.q1T, c.k1T, c.v1):
        sb.freeit(t)
    P.barrier("l1attn_done")


def emit_l1ffnA(c):
    P, sb = c.P, c.sb
    w1T = sb.alloc("w1T1", [128, 8, HFF], BF16)
    w2T = sb.alloc("w2T1", [128, 8, HFF], BF16)
    s1t = sb.alloc("s1t", [128, 32], F32)
    s2t = sb.alloc("s2t", [128, 32], F32)
    xt = [sb.alloc(f"xta{i}", [128, 8, SCW], BF16) for i in range(2)]
    u_sb = [sb.alloc(f"usb{i}", [128, SCW], F32) for i in range(2)]
    ha = sb.alloc("ha", [128, 32, SCW], BF16)

    sd = c.s_dve
    wait_stg = ((c.s_stg_f1, c.s_stg_f1.n),)
    load_wT(c, w1T, c.st_w1w, 8, wait_stg)
    load_wT(c, w2T, c.st_w2w, 8, wait_stg)
    P.dma("sync", s1t.ap(), c.w1s, c.s_ldw)
    P.dma("sync", s2t.ap(), c.w2s, c.s_ldw)
    ldw_done = c.s_ldw.n

    xt_free = [None, None]
    ha_store_done = 0
    bankU = [0, 0]
    bankG = [0, 0]
    ustt = [0, 0]
    for sc8 in range(NSC):
        b = sc8 % 2
        w = [] if xt_free[b] is None else [(c.s_pe, xt_free[b])]
        load_xT(c, xt[b].ap(), c.x3b, sc8 * SCW, SCW, w)
        ldx_done = c.s_ldx.n
        for hc in range(32):
            ub = hc % 2
            gb = 2 + hc % 2
            for dc in range(8):
                wt = []
                if hc == 0 and dc == 0:
                    wt = [(c.s_ldx, ldx_done), (c.s_ldw, ldw_done)]
                if dc == 0:
                    wt.append((c.s_act, bankU[ub]))
                P.add("tensor",
                      lambda t, o=c.psF[:, ub, :], l=w1T.ap()[:, dc, hc * 128:(hc + 1) * 128],
                             r=xt[b].ap()[:, dc, :], st=(dc == 0), sp=(dc == 7):
                      t.matmul(o, l, r, start=st, stop=sp),
                      wait=wt, inc=[(c.s_pe, 1)] if dc == 7 else ())
            for dc in range(8):
                wt = []
                if dc == 0:
                    wt.append((sd, bankG[gb - 2]))
                P.add("tensor",
                      lambda t, o=c.psF[:, gb, :], l=w2T.ap()[:, dc, hc * 128:(hc + 1) * 128],
                             r=xt[b].ap()[:, dc, :], st=(dc == 0), sp=(dc == 7):
                      t.matmul(o, l, r, start=st, stop=sp),
                      wait=wt, inc=[(c.s_pe, 1)] if dc == 7 else ())
            P.add("scalar",
                  lambda s, o=u_sb[hc % 2].ap(), i=c.psF[:, ub, :],
                         sc_=s1t.ap()[:, hc:hc + 1]:
                  s.activation(o, i, AF.Silu, scale=sc_),
                  wait=[(c.s_pe, c.s_pe.n), (sd, ustt[hc % 2])],
                  inc=[(c.s_act, 1)])
            bankU[ub] = c.s_act.n
            stw = [(c.s_act, c.s_act.n)]
            if hc == 0:
                stw.append((c.s_st, ha_store_done))
            P.add("vector",
                  lambda v, o=ha.ap()[:, hc, :], i=c.psF[:, gb, :],
                         sc_=s2t.ap()[:, hc:hc + 1], u=u_sb[hc % 2].ap():
                  v.scalar_tensor_tensor(o, i, sc_, u, ALU.mult, ALU.mult),
                  wait=stw, inc=[(sd, 1)])
            bankG[gb - 2] = sd.n
            ustt[hc % 2] = sd.n
        xt_free[b] = c.s_pe.n
        if "l1ffnA" in DBG and sc8 == 0:
            dbg_dump(c, "xta", xt[b].ap(), [128, 8, SCW], BF16)
            dbg_dump(c, "w1Tc", w1T.ap()[:, :, 0:512], [128, 8, 512], BF16)
            dbg_dump(c, "hasc", ha.ap(), [128, 32, SCW], BF16)
            dbg_dump(c, "s1td", s1t.ap(), [128, 32], F32)
        P.dma("scalar",
              c.haT.rearrange("(a p) s -> p a s", p=128)[:, :, sc8 * SCW:(sc8 + 1) * SCW],
              ha.ap(), c.s_st, wait=[(sd, sd.n)])
        ha_store_done = c.s_st.n

    for t in (w1T, w2T, s1t, s2t, xt[0], xt[1], u_sb[0], u_sb[1], ha):
        sb.freeit(t)
    P.barrier("l1ffnA_done")


def emit_l1ffnBC(c, st_w, scales_in, in_d, out_d, tag):
    P, sb = c.P, c.sb
    hw = [sb.alloc(f"hw{tag}{i}", [128, 32, 1024], BF16) for i in range(2)]
    hst = sb.alloc(f"hst{tag}", [128, 32], F32)
    hin = [sb.alloc(f"hin{tag}{i}", [128, 32, SCW], BF16) for i in range(2)]
    ho = sb.alloc(f"ho{tag}", [128, 8, SCW], BF16)

    P.dma("sync", hst.ap(), scales_in, c.s_ldw)
    in_v = in_d.rearrange("(a p) s -> p a s", p=128)
    out_v = out_d.rearrange("(a p) s -> p a s", p=128)

    ldw_after = {}
    hw_free = [None, None]
    s_bc = [P.sem(f"bc{tag}0"), P.sem(f"bc{tag}1")]

    def load_stripe(s, extra_wait):
        for ic in range(32):
            P.dma("scalar", hw[s % 2].ap()[:, ic, :],
                  st_w[s * 1024:(s + 1) * 1024, ic * 128:(ic + 1) * 128],
                  s_bc[s % 2], wait=extra_wait if ic == 0 else (), transpose=True)
        ldw_after[s] = (s_bc[s % 2], s_bc[s % 2].n)

    load_stripe(0, [(c.s_stg_f1, c.s_stg_f1.n)])
    hin_free = [None, None]
    ho_store_done = 0
    bankB = [0, 0]
    for stripe in range(4):
        if stripe + 1 < 4:
            w = [(c.s_stg_f1, c.s_stg_f1.n)]
            if hw_free[(stripe + 1) % 2] is not None:
                w.append((c.s_pe, hw_free[(stripe + 1) % 2]))
            load_stripe(stripe + 1, w)
        for sc8 in range(NSC):
            b = sc8 % 2
            w = []
            if hin_free[b] is not None:
                w.append((c.s_pe, hin_free[b]))
            P.dma("sync", hin[b].ap(), in_v[:, :, sc8 * SCW:(sc8 + 1) * SCW],
                  c.s_ldx, wait=w)
            ldx_done = c.s_ldx.n
            for oc in range(8):
                bank = oc % 2
                for ic in range(32):
                    wt = []
                    if oc == 0 and ic == 0:
                        wt = [(c.s_ldx, ldx_done), ldw_after[stripe]]
                    if ic == 0:
                        wt.append((c.s_act, bankB[bank]))
                    P.add("tensor",
                          lambda t, o=c.psF[:, bank, :],
                                 l=hw[stripe % 2].ap()[:, ic, oc * 128:(oc + 1) * 128],
                                 r=hin[b].ap()[:, ic, :], st=(ic == 0), sp=(ic == 31):
                          t.matmul(o, l, r, start=st, stop=sp),
                          wait=wt, inc=[(c.s_pe, 1)] if ic == 31 else ())
                gw = [(c.s_pe, c.s_pe.n)]
                if oc == 0:
                    gw.append((c.s_st, ho_store_done))
                P.add("scalar",
                      lambda s, o=ho.ap()[:, oc, :], i=c.psF[:, bank, :],
                             sc_=hst.ap()[:, stripe * 8 + oc:stripe * 8 + oc + 1]:
                      s.activation(o, i, AF.Gelu, scale=sc_),
                      wait=gw, inc=[(c.s_act, 1)])
                bankB[bank] = c.s_act.n
            hin_free[b] = c.s_pe.n
            P.dma("scalar",
                  out_v[:, stripe * 8:(stripe + 1) * 8, sc8 * SCW:(sc8 + 1) * SCW],
                  ho.ap(), c.s_st, wait=[(c.s_act, c.s_act.n)])
            ho_store_done = c.s_st.n
        hw_free[stripe % 2] = c.s_pe.n

    for t in (hw[0], hw[1], hst, hin[0], hin[1], ho):
        sb.freeit(t)
    P.barrier(f"l1ffn{tag}_done")


def emit_l1ffnB(c):
    emit_l1ffnBC(c, c.st_h1, c.h1s, c.haT, c.hbT, "B")


def emit_l1ffnC(c):
    emit_l1ffnBC(c, c.st_h2, c.h2s, c.hbT, c.haT, "C")


def emit_l1ffnD(c):
    P, sb = c.P, c.sb
    w3T = sb.alloc("w3T1", [128, 32, D], BF16)
    s3c = sb.alloc("s3c", [128, D], F32)
    hin = [sb.alloc(f"hinD{i}", [128, 32, SCW], BF16) for i in range(2)]
    x3r = sb.alloc("x3r", [128, D], F32)
    tt = sb.alloc("ttD", [128, D], F32)
    xn = [sb.alloc(f"xnD{i}", [128, D], F32) for i in range(2)]
    stats = sb.alloc("statsD", [128, 12], F32)
    sqs = sb.alloc("sqsD", [128, 512], F32)

    sd = c.s_dve
    wait_stg = ((c.s_stg_f1, c.s_stg_f1.n),)
    load_wT(c, w3T, c.st_w3, 32, wait_stg)
    P.dma("sync", s3c.ap(), c.w3s[None, :].broadcast_to([128, D]), c.s_ldw)
    ldw_done = c.s_ldw.n
    in_v = c.haT.rearrange("(a p) s -> p a s", p=128)

    hin_free = [None, None]
    bankD = 0
    for sc8 in range(NSC):
        b = sc8 % 2
        w = []
        if hin_free[b] is not None:
            w.append((c.s_pe, hin_free[b]))
        P.dma("sync", hin[b].ap(), in_v[:, :, sc8 * SCW:(sc8 + 1) * SCW],
              c.s_ldx, wait=w)
        ldx_done = c.s_ldx.n
        for st4 in range(4):
            row0 = sc8 * SCW + st4 * 128
            P.dma("sync", x3r.ap(), c.x3f[row0:row0 + 128, :], c.s_res,
                  wait=[(sd, max(0, sd.n - 1))])
            for half in range(2):
                bank = 2 + half
                for ic in range(32):
                    wt = []
                    if st4 == 0 and half == 0 and ic == 0:
                        wt = [(c.s_ldx, ldx_done), (c.s_ldw, ldw_done)]
                    if ic == 0:
                        wt.append((sd, bankD))
                    P.add("tensor",
                          lambda t, o=c.psF[:, bank, :], l=hin[b].ap()[:, ic, st4 * 128:(st4 + 1) * 128],
                                 r=w3T.ap()[:, ic, half * 512:(half + 1) * 512],
                                 st=(ic == 0), sp=(ic == 31):
                          t.matmul(o, l, r, start=st, stop=sp),
                          wait=wt, inc=[(c.s_pe, 1)] if ic == 31 else ())
            pe_done = c.s_pe.n
            bx = st4 % 2
            t_ap = tt.ap()
            P.add("vector", lambda v, o=t_ap[:, 0:512], i=c.psF[:, 2, :],
                           s3=s3c.ap()[:, 0:512]:
                  v.tensor_mul(o, i, s3),
                  wait=[(c.s_pe, pe_done), (c.s_res, c.s_res.n),
                        (c.s_act, c.s_act.n)])
            P.add("vector", lambda v, o=t_ap[:, 512:1024], i=c.psF[:, 3, :],
                           s3=s3c.ap()[:, 512:1024]:
                  v.tensor_mul(o, i, s3), inc=[(sd, 1)])
            bankD = sd.n
            P.add("vector", lambda v, o=t_ap, i=x3r.ap():
                  v.tensor_add(o, o, i),
                  wait=[(sd, sd.n)], inc=[(sd, 1)])
            emit_finalize_tile(c, t_ap, sqs.ap(), stats.ap(), xn[bx].ap(),
                               row0, c.y, None)
        hin_free[b] = c.s_pe.n

    for t in (w3T, s3c, hin[0], hin[1], x3r, tt, xn[0], xn[1], stats, sqs):
        sb.freeit(t)
    P.barrier("l1ffnD_done")


# version 03
# speedup vs baseline: 1.0304x; 1.0199x over previous
import math
import numpy as np
import ml_dtypes
import sys
if '/opt/trn_rl_repo' not in sys.path:
    sys.path.insert(0, '/opt/trn_rl_repo')
import concourse.bass as bass
import concourse.mybir as mybir
from concourse.bass_utils import run_bass_kernel_spmd

F32 = mybir.dt.float32
BF16 = mybir.dt.bfloat16
AF = mybir.ActivationFunctionType
ALU = mybir.AluOpType
AX = mybir.AxisListType

S = 4096
D = 1024
HFF = 4096
NH = 16
DK = 64
NKV = 4
SCW = 512
NSC = S // SCW
NST = S // 128
EPS = 1e-5
CAP = 50.0
THETA = 1e4

ENGINES = ("tensor", "vector", "scalar", "gpsimd", "sync")

PERM1 = [0, 4, 1, 5, 2, 6, 3, 7, 8, 12, 9, 13, 10, 14, 11, 15]


class Sem:
    __slots__ = ("h", "n", "name")
    def __init__(self, h, name):
        self.h = h
        self.n = 0
        self.name = name


class Planner:
    def __init__(self, nc):
        self.nc = nc
        self.items = {e: [] for e in ENGINES}
        self.pending = {e: [] for e in ENGINES}
        self.bar_idx = {e: 0 for e in ENGINES}
        self.nsem = 0

    def sem(self, name=None):
        name = name or f"s{self.nsem}"
        self.nsem += 1
        h = self.nc.alloc_semaphore(name)
        return Sem(h, name)

    def add(self, e, fn=None, wait=(), inc=(), is_dma=False):
        waits = list(self.pending[e]) + list(wait)
        self.pending[e] = []
        for s, n in inc:
            s.n += n
        self.items[e].append({"waits": waits, "fn": fn, "incs": list(inc),
                              "is_dma": is_dma})

    def dma(self, e, out, in_, sem, wait=(), transpose=False, inc=()):
        if transpose:
            fn = lambda eng, o=out, i=in_: eng.dma_start_transpose(out=o, in_=i)
        else:
            fn = lambda eng, o=out, i=in_: eng.dma_start(out=o, in_=i)
        self.add(e, fn, wait, [(sem, 16), *inc], is_dma=True)

    def barrier(self, name="bar"):
        b = self.sem(name)
        total = 0
        for e in ENGINES:
            its = self.items[e]
            if len(its) > self.bar_idx[e]:
                last = its[-1]
                n = 16 if (last["is_dma"] and last["fn"] is not None) else 1
                if last["fn"] is not None and not last["incs"]:
                    last["incs"].append((b, n))
                else:
                    self.items[e].append({"waits": [], "fn": None,
                                          "incs": [(b, 1)], "is_dma": False})
                    n = 1
                b.n += n
                total += n
            self.bar_idx[e] = len(self.items[e])
        for e in ENGINES:
            self.pending[e].append((b, total))
        return b

    def replay(self, ename, eng):
        for it in self.items[ename]:
            for (s, v) in it["waits"]:
                eng.wait_ge(s.h, v)
            ins = it["fn"](eng) if it["fn"] is not None else None
            incs = it["incs"]
            if ins is not None and incs:
                ins.then_inc(incs[0][0].h, incs[0][1])
                for s, n in incs[1:]:
                    eng.sem_inc(s.h, n)
            else:
                for s, n in incs:
                    eng.sem_inc(s.h, n)


class _SBWrap:
    __slots__ = ("t", "_sb_range")
    def __init__(self, t, rng):
        self.t = t
        self._sb_range = rng
    def ap(self):
        return self.t.ap()


class SB:
    def __init__(self, nc, base=16640, top=229376):
        self.nc = nc
        self.free = [(base, top)]
        self.cnt = 0

    @staticmethod
    def _sz(dt):
        return {F32: 4, BF16: 2, mybir.dt.int32: 4, mybir.dt.uint8: 1}[dt]

    def alloc(self, name, shape, dtype, align=128):
        bpp = int(np.prod(shape[1:])) * self._sz(dtype)
        for i, (a, b) in enumerate(self.free):
            start = (a + align - 1) // align * align
            if start + bpp <= b:
                pre = (a, start)
                post = (start + bpp, b)
                repl = []
                if pre[1] > pre[0]:
                    repl.append(pre)
                if post[1] > post[0]:
                    repl.append(post)
                self.free[i:i + 1] = repl
                self.cnt += 1
                t = self.nc.alloc_sbuf_tensor_at(f"{name}_{self.cnt}", list(shape),
                                                 dtype, offset=start)
                return _SBWrap(t, (start, start + bpp))
        raise MemoryError(f"SBUF alloc failed for {name} {shape} ({bpp} B/part); "
                          f"free={self.free}")

    def freeit(self, t):
        a, b = t._sb_range
        self.free.append((a, b))
        self.free.sort()
        merged = []
        for seg in self.free:
            if merged and seg[0] <= merged[-1][1]:
                merged[-1] = (merged[-1][0], max(merged[-1][1], seg[1]))
            else:
                merged.append(seg)
        self.free = merged


_CONST_CACHE = {}

def host_consts():
    if _CONST_CACHE:
        return _CONST_CACHE
    inv = 1.0 / THETA ** (np.arange(0, DK, 2, dtype=np.float64) / DK)
    freqs = np.arange(S, dtype=np.float64)[:, None] * inv[None, :]
    _CONST_CACHE["cos"] = np.cos(freqs).astype(np.float32)
    _CONST_CACHE["sin"] = np.sin(freqs).astype(np.float32)
    jk = np.arange(128)[:, None]
    jq = np.arange(128)[None, :]
    _CONST_CACHE["tri"] = (jq >= jk).astype(ml_dtypes.bfloat16)
    _CONST_CACHE["eye"] = np.eye(128, dtype=ml_dtypes.bfloat16)
    return _CONST_CACHE


class Ctx:
    pass


def make_ctx(nc, taps=False):
    c = Ctx()
    c.nc = nc
    c.P = Planner(nc)
    c.sb = SB(nc)
    c.taps = taps
    T = lambda name, shape, dt, kind: nc.dram_tensor(name, shape, dt, kind=kind).ap()
    I = lambda name, shape, dt=F32: T(name, shape, dt, "ExternalInput")
    scratch_kind = "ExternalOutput" if taps else "Internal"
    N = lambda name, shape, dt: T(name, shape, dt, scratch_kind)

    c.x = I("x", [S, D])
    c.wq0 = I("wq0", [D, D]); c.wk0 = I("wk0", [D, D])
    c.wv0 = I("wv0", [D, D]); c.wo0 = I("wo0", [D, D])
    c.bo0 = I("bo0", [D])
    c.W1 = I("W1", [HFF, D]); c.b1 = I("b1", [128, HFF // 128])
    c.W2 = I("W2", [D, HFF]); c.b2 = I("b2", [D])
    c.scale0 = I("scale0", [1, 1])
    c.wq1 = I("wq1", [D, D]); c.wk1 = I("wk1", [D, 256])
    c.wv1 = I("wv1", [D, 256]); c.wo1 = I("wo1", [D, D])
    c.w1w = I("w1w", [HFF, D]); c.w1s = I("w1s", [128, 32])
    c.w2w = I("w2w", [HFF, D]); c.w2s = I("w2s", [128, 32])
    c.h1w = I("h1w", [HFF, HFF]); c.h1s = I("h1s", [128, 32])
    c.h2w = I("h2w", [HFF, HFF]); c.h2s = I("h2s", [128, 32])
    c.w3w = I("w3w", [D, HFF]); c.w3s = I("w3s", [D])
    c.cos = I("cos", [S, 32]); c.sin = I("sin", [S, 32])
    c.tri = I("tri", [128, 128], BF16)
    c.eye = I("eye", [128, 128], BF16)

    c.y = T("y", [S, D], F32, "ExternalOutput")

    ST = lambda name, shape: T(name, shape, BF16, "Internal")
    c.st_wq0 = ST("st_wq0", [D, D]); c.st_wk0 = ST("st_wk0", [D, D])
    c.st_wv0 = ST("st_wv0", [D, D]); c.st_wo0 = ST("st_wo0", [D, D])
    c.st_W1 = ST("st_W1", [HFF, D]); c.st_W2 = ST("st_W2", [D, HFF])
    c.st_w1w = ST("st_w1w", [HFF, D]); c.st_w2w = ST("st_w2w", [HFF, D])
    c.st_h1 = ST("st_h1", [HFF, HFF]); c.st_h2 = ST("st_h2", [HFF, HFF])
    c.st_w3 = ST("st_w3", [D, HFF])

    c.x0b = T("x0b", [S, D], BF16, "Internal")
    c.x1f = N("x1f", [S, D], F32); c.x1b = T("x1b", [S, D], BF16, "Internal")
    c.x2f = N("x2f", [S, D], F32); c.x2b = T("x2b", [S, D], BF16, "Internal")
    c.x3f = N("x3f", [S, D], F32); c.x3b = T("x3b", [S, D], BF16, "Internal")
    c.haT = T("haT", [HFF, S], BF16, scratch_kind)
    c.hbT = T("hbT", [HFF, S], BF16, scratch_kind)

    P = c.P
    c.s_stg_x = P.sem("stg_x")
    c.s_stg_a0 = P.sem("stg_a0")
    c.s_stg_f0 = P.sem("stg_f0")
    c.s_stg_f1 = P.sem("stg_f1")
    c.s_ldw = P.sem("ldw")
    c.s_ldx = P.sem("ldx")
    c.s_res = P.sem("res")
    c.s_st = P.sem("st")
    c.s_stg_o = P.sem("stg_o")
    c.s_pe = P.sem("pe")
    c.s_act = P.sem("act")
    c.s_dve = P.sem("dve")

    c.psF_h = nc.alloc_psum_tensor("psF", [128, 6, 512], F32)
    c.psT_h = nc.alloc_psum_tensor("psT", [128, 2, 1024], BF16)
    c.psF = c.psF_h.ap()
    c.psT = c.psT_h.ap()
    return c


def tview(ap):
    return ap.rearrange("(a p) d -> p a d", p=128)


def emit_stage(c, groups=("x", "a0", "f0", "f1")):
    P = c.P
    if "x" in groups:
        P.dma("gpsimd", c.x0b, c.x, c.s_stg_x)
    if "a0" in groups:
        for dst, src in ((c.st_wq0, c.wq0), (c.st_wk0, c.wk0),
                         (c.st_wv0, c.wv0), (c.st_wo0, c.wo0)):
            P.dma("gpsimd", dst, src, c.s_stg_a0)
    if "f0" in groups:
        P.dma("gpsimd", c.st_W1, c.W1, c.s_stg_f0)
        P.dma("gpsimd", c.st_W2, c.W2, c.s_stg_f0)
    if "f1" in groups:
        for dst, src in ((c.st_w1w, c.w1w), (c.st_w2w, c.w2w), (c.st_w3, c.w3w)):
            P.dma("gpsimd", dst, src, c.s_stg_f1)
        for dst, src in ((c.st_h1, c.h1w), (c.st_h2, c.h2w)):
            for r in range(4):
                P.dma("gpsimd", dst[r * 1024:(r + 1) * 1024, :],
                      src[r * 1024:(r + 1) * 1024, :], c.s_stg_f1)


def load_wT(c, dst_sb, st_ap, nch, wait):
    P = c.P
    for dc in range(nch):
        P.dma("sync", dst_sb.ap()[:, dc, :], st_ap[:, dc * 128:(dc + 1) * 128],
              c.s_ldw, wait=wait if dc == 0 else (), transpose=True)


def load_xT(c, dst_ap, src_bf, s0, width, wait):
    P = c.P
    for dc in range(8):
        P.dma("sync", dst_ap[:, dc, :], src_bf[s0:s0 + width, dc * 128:(dc + 1) * 128],
              c.s_ldx, wait=wait if dc == 0 else (), transpose=True)


def emit_l0kv(c):
    P, sb = c.P, c.sb
    c.kT = sb.alloc("kT", [128, 8, S], BF16)
    c.v0 = sb.alloc("v0", [128, NST, NH, 65], BF16)
    wkT = sb.alloc("wkT", [128, 8, D], BF16)
    wvT = sb.alloc("wvT", [128, 8, D], BF16)
    xt = [sb.alloc(f"xt{i}", [128, 8, SCW], BF16) for i in range(2)]

    wait_stg = ((c.s_stg_a0, c.s_stg_a0.n),)
    load_wT(c, wkT, c.st_wk0, 8, wait_stg)
    load_wT(c, wvT, c.st_wv0, 8, wait_stg)
    ldw_done = c.s_ldw.n

    P.add("vector", lambda v: v.memset(c.v0.ap()[:, :, :, 64:65], 1.0),
          inc=[(c.s_dve, 1)])

    xt_free_at = [None, None]
    bankK = [0, 0]
    bankV = [0, 0]
    for sc in range(NSC):
        b = sc % 2
        w = [(c.s_stg_x, c.s_stg_x.n)] if sc == 0 else ()
        if xt_free_at[b] is not None:
            w = ((c.s_pe, xt_free_at[b]),)
        load_xT(c, xt[b].ap(), c.x0b, sc * SCW, SCW, w)
        ldx_done = c.s_ldx.n

        first_wait = [(c.s_ldx, ldx_done), (c.s_ldw, ldw_done)]
        for ec in range(8):
            bank = ec % 2
            for dc in range(8):
                wt = list(first_wait) if (ec == 0 and dc == 0) else []
                if dc == 0:
                    wt.append((c.s_act, bankK[bank]))
                P.add("tensor",
                      lambda t, o=c.psF[:, bank, :], l=wkT.ap()[:, dc, ec * 128:(ec + 1) * 128],
                             r=xt[b].ap()[:, dc, :], st=(dc == 0), sp=(dc == 7):
                      t.matmul(o, l, r, start=st, stop=sp),
                      wait=wt, inc=[(c.s_pe, 1)] if dc == 7 else ())
            P.add("scalar",
                  lambda s, o=c.kT.ap()[:, ec, sc * SCW:(sc + 1) * SCW], i=c.psF[:, bank, :]:
                  s.activation(o, i, AF.Copy),
                  wait=[(c.s_pe, c.s_pe.n)], inc=[(c.s_act, 1)])
            bankK[bank] = c.s_act.n
        for st4 in range(4):
            for half in range(2):
                bank = 2 + (st4 * 2 + half) % 2
                for dc in range(8):
                    wt = []
                    if dc == 0:
                        wt.append((c.s_dve, bankV[bank - 2]))
                    P.add("tensor",
                          lambda t, o=c.psF[:, bank, :],
                                 l=xt[b].ap()[:, dc, st4 * 128:(st4 + 1) * 128],
                                 r=wvT.ap()[:, dc, half * 512:(half + 1) * 512],
                                 st=(dc == 0), sp=(dc == 7):
                          t.matmul(o, l, r, start=st, stop=sp),
                          wait=wt, inc=[(c.s_pe, 1)] if dc == 7 else ())
                src = c.psF[:, bank, :].rearrange("p (h d) -> p h d", d=64)
                dst = c.v0.ap()[:, sc * 4 + st4, half * 8:(half + 1) * 8, 0:64]
                P.add("vector",
                      lambda v, o=dst, i=src: v.tensor_copy(o, i),
                      wait=[(c.s_pe, c.s_pe.n)], inc=[(c.s_dve, 1)])
                bankV[bank - 2] = c.s_dve.n
        xt_free_at[b] = c.s_pe.n

    if "l0kv" in DBG:
        dbg_dump(c, "kT", c.kT.ap(), [128, 8, S], BF16)
        dbg_dump(c, "v0", c.v0.ap(), [128, NST, NH, 65], BF16)
    for t in (wkT, wvT, xt[0], xt[1]):
        sb.freeit(t)
    P.barrier("l0kv_done")

def emit_finalize_tile(c, t_ap, scratch, stats, xn, row0, outf, outb,
                       xn_single=False):
    P = c.P
    sumsq = stats[:, 0:1]; sumsq2 = stats[:, 1:2]; ssum = stats[:, 2:3]
    m = stats[:, 3:4]; e2 = stats[:, 4:5]; veps = stats[:, 5:6]
    rstd = stats[:, 6:7]; nmr = stats[:, 7:8]; lnv = stats[:, 8:9]
    sqw = [(c.s_dve, c.s_dve.n)]
    P.add("scalar", lambda s: s.activation(scratch, t_ap[:, 0:512], AF.Square,
                                           accum_out=sumsq),
          wait=sqw)
    P.add("scalar", lambda s: s.activation(scratch, t_ap[:, 512:1024], AF.Square,
                                           accum_out=sumsq2),
          inc=[(c.s_act, 1)])
    sd = c.s_dve
    P.add("vector", lambda v: v.reduce_sum(ssum, t_ap, axis=AX.X), inc=[(sd, 1)])
    P.add("vector", lambda v: v.tensor_scalar(m, ssum, 1.0 / D, None, ALU.mult),
          wait=[(sd, sd.n)], inc=[(sd, 1)])
    P.add("vector", lambda v: v.tensor_add(sumsq, sumsq, sumsq2),
          wait=[(c.s_act, c.s_act.n)], inc=[(sd, 1)])
    P.add("vector", lambda v: v.tensor_scalar(e2, sumsq, 1.0 / D, EPS, ALU.mult, ALU.add),
          wait=[(sd, sd.n)], inc=[(sd, 1)])
    P.add("vector", lambda v: v.scalar_tensor_tensor(veps, m, -1.0, m, ALU.mult, ALU.mult),
          wait=[(sd, sd.n)], inc=[(sd, 1)])
    P.add("vector", lambda v: v.tensor_add(veps, veps, e2),
          wait=[(sd, sd.n)], inc=[(sd, 1)])
    P.add("scalar", lambda s: s.activation(lnv, veps, AF.Ln),
          wait=[(sd, sd.n)])
    P.add("scalar", lambda s: s.activation(rstd, lnv, AF.Exp, scale=-0.5),
          inc=[(c.s_act, 1)])
    P.add("vector", lambda v: v.scalar_tensor_tensor(nmr, m, -1.0, rstd, ALU.mult, ALU.mult),
          wait=[(c.s_act, c.s_act.n)], inc=[(sd, 1)])
    back = 0 if xn_single else 16
    aw = [(sd, sd.n), (c.s_st, max(0, c.s_st.n - back))]
    if outb is not None:
        aw.append((c.s_stg_o, max(0, c.s_stg_o.n - back)))
    P.add("vector", lambda v: v.tensor_scalar(xn, t_ap, rstd, nmr, ALU.mult, ALU.add),
          wait=aw, inc=[(sd, 1)])
    w = [(c.s_dve, c.s_dve.n)]
    if "fin" in DBG and getattr(c, "_fin_n", 0) == 0:
        c._fin_n = 1
        dbg_dump(c, "stats", stats, [128, 12], F32)
        dbg_dump(c, "xn", xn, [128, D], F32)
    P.dma("scalar", outf[row0:row0 + 128, :], xn, c.s_st, wait=w)
    if outb is not None:
        P.dma("gpsimd", outb[row0:row0 + 128, :], xn, c.s_stg_o, wait=w)


def emit_l0attn(c):
    P, sb = c.P, c.sb
    wqT = sb.alloc("wqT", [128, 8, D], BF16)
    woT = sb.alloc("woT", [128, 8, D], BF16)
    bocast = sb.alloc("bocast", [128, D], BF16)
    ones64 = sb.alloc("ones64", [128, 64], F32)
    qT = sb.alloc("qT", [128, 8, SCW], BF16)
    xtq = sb.alloc("xtq", [128, 8, SCW], BF16)
    Pp = sb.alloc("Pp", [128, 4, SCW], BF16)
    ynT = sb.alloc("ynT", [128, 8, SCW], BF16)
    stat = sb.alloc("stat", [128, SCW], F32)
    x0r = sb.alloc("x0r", [128, D], F32)
    tt = sb.alloc("tt", [128, D], F32)
    xn1b = sb.alloc("xn", [128, D], F32)
    xn = [xn1b, xn1b]
    stats = sb.alloc("stats", [128, 12], F32)

    wait_stg = ((c.s_stg_a0, c.s_stg_a0.n),)
    load_wT(c, wqT, c.st_wq0, 8, wait_stg)
    load_wT(c, woT, c.st_wo0, 8, wait_stg)
    P.dma("gpsimd", bocast.ap(), c.bo0[None, :].broadcast_to([128, D]), c.s_ldw)
    ldw_done = c.s_ldw.n
    P.add("vector", lambda v: v.memset(ones64.ap(), 1.0), inc=[(c.s_dve, 1)])

    xtq_free = None
    bankQ = [0, 0]
    bankS = [0]
    last_adds = 0
    for qc in range(NSC):
        w = ((c.s_stg_x, c.s_stg_x.n),) if qc == 0 else ((c.s_pe, xtq_free),)
        load_xT(c, xtq.ap(), c.x0b, qc * SCW, SCW, w)
        ldx_done = c.s_ldx.n
        for ec in range(8):
            qb = 4 + ec % 2
            for dc in range(8):
                wt = []
                if ec == 0 and dc == 0:
                    wt = [(c.s_ldx, ldx_done), (c.s_ldw, ldw_done)]
                if dc == 0:
                    wt.append((c.s_act, bankQ[ec % 2]))
                P.add("tensor",
                      lambda t, o=c.psF[:, qb, :], l=wqT.ap()[:, dc, ec * 128:(ec + 1) * 128],
                             r=xtq.ap()[:, dc, :], st=(dc == 0), sp=(dc == 7):
                      t.matmul(o, l, r, start=st, stop=sp),
                      wait=wt, inc=[(c.s_pe, 1)] if dc == 7 else ())
            P.add("scalar",
                  lambda s, o=qT.ap()[:, ec, :], i=c.psF[:, qb, :]:
                  s.activation(o, i, AF.Copy),
                  wait=[(c.s_pe, c.s_pe.n)], inc=[(c.s_act, 1)])
            bankQ[ec % 2] = c.s_act.n
        xtq_free = c.s_pe.n
        qT_ready = c.s_act.n

        ynd = getattr(c, "_yn0", None)
        if ynd is None:
            ynd = {0: 0, 1: 0}
            c._yn0 = ynd
            c._pb0 = 0

        def emit_norm0(avb_, p0_, chunk_, av_stop, act_after):
            P.add("vector", lambda v, o=stat.ap()[64:65, :], i=c.psF[64:65, avb_, :]:
                  v.reciprocal(o, i),
                  wait=[(c.s_pe, av_stop), (c.s_act, act_after)], inc=[(c.s_dve, 1)])
            P.add("tensor", lambda t, o=c.psF[0:64, 5, :], l=ones64.ap()[64:65, 0:64],
                           r=stat.ap()[64:65, :]:
                  t.matmul(o, l, r, start=True, stop=True),
                  wait=[(c.s_dve, c.s_dve.n), (c.s_dve, c._pb0)],
                  inc=[(c.s_pe, 1)])
            P.add("vector", lambda v, o=stat.ap()[0:64, :], i=c.psF[0:64, 5, :]:
                  v.tensor_copy(o, i),
                  wait=[(c.s_pe, c.s_pe.n)], inc=[(c.s_dve, 1)])
            c._pb0 = c.s_dve.n
            P.add("vector", lambda v, o=ynT.ap()[p0_:p0_ + 64, chunk_, :],
                           a=c.psF[0:64, avb_, :], b2=stat.ap()[0:64, :]:
                  v.tensor_mul(o, a, b2),
                  wait=[(c.s_dve, c.s_dve.n)], inc=[(c.s_dve, 1)])
            ynd[avb_ - 2] = c.s_dve.n

        pending_norm = None
        for h in range(NH):
            p0 = (h % 2) * 64
            chunk = h // 2
            avb = 2 + h % 2
            av_q = []
            for pr in range(NST // 2):
                for sbank in range(2):
                    kt = 2 * pr + sbank
                    wt = []
                    if kt == 0 and h == 0:
                        wt.append((c.s_act, qT_ready))
                        wt.append((c.s_dve, last_adds))
                    if sbank == 0:
                        wt.append((c.s_act, bankS[0]))
                    P.add("tensor",
                          lambda t, o=c.psF[:, sbank, :],
                                 l=c.kT.ap()[p0:p0 + 64, chunk, kt * 128:(kt + 1) * 128],
                                 r=qT.ap()[p0:p0 + 64, chunk, :]:
                          t.matmul(o, l, r, start=True, stop=True),
                          wait=wt, inc=[(c.s_pe, 1)])
                P.add("scalar",
                      lambda s, o=Pp.ap()[:, 2 * (pr % 2):2 * (pr % 2) + 2, :].rearrange("p a b -> p (a b)"),
                             i=c.psF[:, 0:2, :].rearrange("p a b -> p (a b)"):
                      s.activation(o, i, AF.Exp, scale=0.125),
                      wait=[(c.s_pe, c.s_pe.n)], inc=[(c.s_act, 1)])
                bankS[0] = c.s_act.n
                exp_done = c.s_act.n
                if pr == 1 and pending_norm is not None:
                    emit_norm0(*pending_norm)
                    pending_norm = None
                for kt in (2 * pr, 2 * pr + 1):
                    avw = [(c.s_act, exp_done)]
                    if kt == 0:
                        avw.append((c.s_dve, ynd[avb - 2]))
                    av_q.append((
                        (lambda t, o=c.psF[0:65, avb, :], l=c.v0.ap()[:, kt, h, :],
                                r=Pp.ap()[:, 2 * (pr % 2) + kt % 2, :],
                                st=(kt == 0), sp=(kt == NST - 1):
                         t.matmul(o, l, r, start=st, stop=sp)),
                        avw, [(c.s_pe, 1)] if kt == NST - 1 else ()))
                while len(av_q) > 2:
                    a = av_q.pop(0)
                    P.add("tensor", a[0], wait=a[1], inc=a[2])
            for a in av_q:
                P.add("tensor", a[0], wait=a[1], inc=a[2])
            pending_norm = (avb, p0, chunk, c.s_pe.n, c.s_act.n)
        emit_norm0(*pending_norm)
        pending_norm = None
        heads_done = c.s_dve.n

        for st4 in range(4):
            row0 = qc * SCW + st4 * 128
            P.dma("sync", x0r.ap(), c.x[row0:row0 + 128, :], c.s_res,
                  wait=[(c.s_dve, max(0, c.s_dve.n - 1))])
            for half in range(2):
                bank = half
                for hc in range(8):
                    wt = []
                    if hc == 0:
                        wt.append((c.s_dve, heads_done))
                        wt.append((c.s_dve, max(0, c.s_dve.n - 2)))
                    P.add("tensor",
                          lambda t, o=c.psF[:, bank, :],
                                 l=ynT.ap()[:, hc, st4 * 128:(st4 + 1) * 128],
                                 r=woT.ap()[:, hc, half * 512:(half + 1) * 512],
                                 st=(hc == 0), sp=(hc == 7):
                          t.matmul(o, l, r, start=st, stop=sp),
                          wait=wt, inc=[(c.s_pe, 1)] if hc == 7 else ())
            pe_done = c.s_pe.n
            b = st4 % 2
            t_ap = tt.ap()
            P.add("vector", lambda v, o=t_ap[:, 0:512], i=c.psF[:, 0, :],
                           r=x0r.ap()[:, 0:512]:
                  v.tensor_add(o, i, r),
                  wait=[(c.s_pe, pe_done), (c.s_res, c.s_res.n),
                        (c.s_act, c.s_act.n)])
            P.add("vector", lambda v, o=t_ap[:, 512:1024], i=c.psF[:, 1, :],
                           r=x0r.ap()[:, 512:1024]:
                  v.tensor_add(o, i, r), inc=[(c.s_dve, 1)])
            bankO = c.s_dve.n
            last_adds = c.s_dve.n
            P.add("vector", lambda v, o=t_ap, i=bocast.ap():
                  v.tensor_add(o, o, i),
                  wait=[(c.s_dve, c.s_dve.n)], inc=[(c.s_dve, 1)])
            if "l0attn" in DBG and qc == 0 and st4 == 0:
                dbg_dump(c, "t0", t_ap, [128, D], F32)
            emit_finalize_tile(c, t_ap, stat.ap(), stats.ap(), xn[b].ap(),
                               row0, c.x1f, c.x1b, xn_single=True)
        if "l0attn" in DBG and qc == 0:
            dbg_dump(c, "qT", qT.ap(), [128, 8, SCW], BF16)
            dbg_dump(c, "ynT", ynT.ap(), [128, 8, SCW], BF16)

    for t in (wqT, woT, bocast, ones64, qT, xtq, Pp, ynT, stat, x0r,
              tt, xn1b, stats, c.kT, c.v0):
        sb.freeit(t)
    P.barrier("l0attn_done")


def emit_finish(c):
    P = c.P
    P.add("scalar", wait=[(c.s_st, c.s_st.n)])
    P.add("gpsimd", wait=[(c.s_stg_o, c.s_stg_o.n)])
    P.add("sync", wait=[(c.s_st, c.s_st.n), (c.s_stg_o, c.s_stg_o.n)])


def build_nc(phases, taps=False):
    nc = bass.Bass("TRN2", target_bir_lowering=False, debug=False)
    c = make_ctx(nc, taps=taps)
    for ph in phases:
        ph(c)
    emit_finish(c)
    with nc.Block() as block:
        @block.tensor
        def _(eng):
            c.P.replay("tensor", eng)
        @block.vector
        def _(eng):
            c.P.replay("vector", eng)
        @block.scalar
        def _(eng):
            c.P.replay("scalar", eng)
        @block.gpsimd
        def _(eng):
            c.P.replay("gpsimd", eng)
        @block.sync
        def _(eng):
            c.P.replay("sync", eng)
    return nc, c


def make_inmap(inputs, core):
    hc = host_consts()
    a = lambda k: np.ascontiguousarray(inputs[k])
    rs = lambda v: np.ascontiguousarray(np.asarray(v).reshape(-1, 128).T)
    m = dict(
        x=np.ascontiguousarray(inputs["L0_attn_x"][core]),
        wq0=a("L0_attn_w_q"), wk0=a("L0_attn_w_k"), wv0=a("L0_attn_w_v"),
        wo0=a("L0_attn_w_o"), bo0=a("L0_attn_b_o"),
        W1=a("L0_ffn_W1"),
        b1=np.ascontiguousarray(np.asarray(inputs["L0_ffn_b1"]).reshape(32, 128).T),
        W2=a("L0_ffn_W2"), b2=a("L0_ffn_b2"),
        scale0=np.ascontiguousarray(inputs["L0_ffn_scale"]).reshape(1, 1),
        wq1=a("L1_attn_w_q"), wk1=a("L1_attn_w_k"), wv1=a("L1_attn_w_v"),
        wo1=a("L1_attn_w_o"),
        w1w=a("L1_ffn_w1_w"), w1s=rs(inputs["L1_ffn_w1_s"]),
        w2w=a("L1_ffn_w2_w"), w2s=rs(inputs["L1_ffn_w2_s"]),
        h1w=a("L1_ffn_h1_w"), h1s=rs(inputs["L1_ffn_h1_s"]),
        h2w=a("L1_ffn_h2_w"), h2s=rs(inputs["L1_ffn_h2_s"]),
        w3w=a("L1_ffn_w3_w"), w3s=a("L1_ffn_w3_s"),
        cos=hc["cos"], sin=hc["sin"], tri=hc["tri"], eye=hc["eye"],
    )
    return m


DBG = set()


def dbg_dump(c, name, ap, shape, dtype):
    t = c.nc.dram_tensor("dbg_" + name, list(shape), dtype, kind="ExternalOutput").ap()
    P = c.P
    P.dma("sync", t, ap, c.s_ldx,
          wait=[(c.s_pe, c.s_pe.n), (c.s_act, c.s_act.n), (c.s_dve, c.s_dve.n)])
    for e in ENGINES:
        P.pending[e].append((c.s_ldx, c.s_ldx.n))


def kernel(**inputs):
    phases = [emit_stage, emit_l0kv, emit_l0attn, emit_l0ffnw, emit_l0ffn,
              emit_l1prep, emit_l1attn, emit_l1ffnA, emit_l1ffnB, emit_l1ffnC,
              emit_l1ffnD]
    nc, c = build_nc(phases, taps=False)
    in_maps = [make_inmap(inputs, core) for core in range(8)]
    res = run_bass_kernel_spmd(nc, in_maps, core_ids=list(range(8)))
    out = np.stack([res.results[i]["y"] for i in range(8)], axis=0)
    return out.astype(np.float32)


def emit_l0ffnw(c):
    P, sb = c.P, c.sb
    c.W1T = sb.alloc("W1T", [128, 8, HFF], BF16)
    c.W2T = sb.alloc("W2T", [128, 32, D], BF16)
    c.b1t = sb.alloc("b1t", [128, 32], F32)
    c.b2cast = sb.alloc("b2cast", [128, D], F32)
    xre = sb.alloc("xre", [1, D], F32)
    xro = sb.alloc("xro", [1, D], F32)
    obf = sb.alloc("obf", [1, D], BF16)
    sev = sb.alloc("sev", [1, D], BF16)
    ev4 = sb.alloc("ev4", [1, 256], BF16)
    scs = sb.alloc("scs", [1, 1], F32)

    wait_stg = ((c.s_stg_f0, c.s_stg_f0.n),)
    load_wT(c, c.W1T, c.st_W1, 8, wait_stg)
    load_wT(c, c.W2T, c.st_W2, 32, wait_stg)
    ldw_done = c.s_ldw.n
    P.dma("sync", c.b1t.ap(), c.b1, c.s_ldw)
    P.dma("sync", c.b2cast.ap(), c.b2[None, :].broadcast_to([128, D]), c.s_ldw)
    P.dma("sync", scs.ap(), c.scale0, c.s_ldw)
    sd = c.s_dve

    def rows_prep(tok_e, tok_o):
        P.dma("sync", xre.ap(), c.x1f[tok_e:tok_e + 1, :], c.s_res,
              wait=[(sd, sd.n)])
        P.dma("sync", xro.ap(), c.x1f[tok_o:tok_o + 1, :], c.s_res)
        P.add("vector", lambda v: v.tensor_copy(obf.ap(), xro.ap()),
              wait=[(c.s_res, c.s_res.n), (c.s_ldw, ldw_done + 48),
                    (c.s_pe, c.s_pe.n)], inc=[(sd, 1)])
        P.add("vector", lambda v: v.tensor_scalar(sev.ap(), xre.ap(),
                                                  scs.ap()[0:1, 0:1], None, ALU.mult),
              inc=[(sd, 1)])

    for p in range(4):
        rows_prep(1 + 2 * p, 2 + 2 * p)
        rows_ready = sd.n
        for dc in range(8):
            for half in range(2):
                bank = (dc * 2 + half) % 2
                wt = [(sd, rows_ready)]
                wt.append((sd, max(0, sd.n - 1)))
                P.add("tensor",
                      lambda t, o=c.psF[:, bank, :], l=obf.ap()[0:1, dc * 128:(dc + 1) * 128],
                             r=sev.ap()[0:1, half * 512:(half + 1) * 512]:
                      t.matmul(o, l, r, start=True, stop=True),
                      wait=wt, inc=[(c.s_pe, 1)])
                dst = c.W1T.ap()[:, dc, p * 1024 + half * 512: p * 1024 + (half + 1) * 512]
                P.add("vector",
                      lambda v, o=dst, i=c.psF[:, bank, :]:
                      v.tensor_add(o, o, i),
                      wait=[(c.s_pe, c.s_pe.n)], inc=[(sd, 1)])
    for p in range(4):
        rows_prep(9 + 2 * p, 10 + 2 * p)
        for k in range(4):
            P.add("vector",
                  lambda v, o=ev4.ap(), i=sev.ap().rearrange("a (q f) -> a q f", f=4)[0:1, :, k]:
                  v.tensor_copy(o, i),
                  wait=[(sd, sd.n), (c.s_pe, c.s_pe.n)], inc=[(sd, 1)])
            ev_ready = sd.n
            for jc in range(8):
                bank = jc % 2
                wt = [(sd, ev_ready), (sd, max(0, sd.n - 1))]
                P.add("tensor",
                      lambda t, o=c.psF[:, bank, 0:256], l=obf.ap()[0:1, jc * 128:(jc + 1) * 128],
                             r=ev4.ap():
                      t.matmul(o, l, r, start=True, stop=True),
                      wait=wt, inc=[(c.s_pe, 1)])
                dst = c.W2T.ap()[:, 8 * k + jc, 256 * p: 256 * (p + 1)]
                P.add("vector",
                      lambda v, o=dst, i=c.psF[:, bank, 0:256]:
                      v.tensor_add(o, o, i),
                      wait=[(c.s_pe, c.s_pe.n)], inc=[(sd, 1)])

    for t in (xre, xro, obf, sev, ev4, scs):
        sb.freeit(t)
    P.barrier("l0ffnw_done")


def emit_l0ffn(c):
    P, sb = c.P, c.sb
    h_sb = sb.alloc("h0", [128, 32, SCW], BF16)
    xt = [sb.alloc(f"xtf{i}", [128, 8, SCW], BF16) for i in range(2)]
    x1r = sb.alloc("x1r", [128, D], F32)
    tt = sb.alloc("ttf", [128, D], F32)
    xn = [sb.alloc(f"xnf{i}", [128, D], F32) for i in range(2)]
    stats = sb.alloc("statsf", [128, 12], F32)
    sqs = sb.alloc("sqsf", [128, 512], F32)

    xt_free = [None, None]
    w2_done = None
    bankH = [0, 0]
    bankO2 = 0
    for sc8 in range(NSC):
        b = sc8 % 2
        w = [] if xt_free[b] is None else [(c.s_pe, xt_free[b])]
        load_xT(c, xt[b].ap(), c.x1b, sc8 * SCW, SCW, w)
        ldx_done = c.s_ldx.n
        for hc in range(32):
            bank = hc % 2
            for dc in range(8):
                wt = []
                if hc == 0 and dc == 0:
                    wt.append((c.s_ldx, ldx_done))
                if dc == 0:
                    wt.append((c.s_act, bankH[bank]))
                P.add("tensor",
                      lambda t, o=c.psF[:, bank, :], l=c.W1T.ap()[:, dc, hc * 128:(hc + 1) * 128],
                             r=xt[b].ap()[:, dc, :], st=(dc == 0), sp=(dc == 7):
                      t.matmul(o, l, r, start=st, stop=sp),
                      wait=wt, inc=[(c.s_pe, 1)] if dc == 7 else ())
            gw = [(c.s_pe, c.s_pe.n)]
            if w2_done is not None:
                gw.append((c.s_pe, w2_done))
            P.add("scalar",
                  lambda s, o=h_sb.ap()[:, hc, :], i=c.psF[:, bank, :],
                         bb=c.b1t.ap()[:, hc:hc + 1]:
                  s.activation(o, i, AF.Gelu, bias=bb),
                  wait=gw, inc=[(c.s_act, 1)])
            bankH[bank] = c.s_act.n
        h_done = c.s_act.n
        xt_free[b] = c.s_pe.n
        for st4 in range(4):
            row0 = sc8 * SCW + st4 * 128
            P.dma("sync", x1r.ap(), c.x1f[row0:row0 + 128, :], c.s_res,
                  wait=[(c.s_dve, max(0, c.s_dve.n - 1))])
            for half in range(2):
                bank = 2 + half
                for hc in range(32):
                    wt = []
                    if hc == 0:
                        wt.append((c.s_act, h_done))
                        wt.append((c.s_dve, bankO2))
                    P.add("tensor",
                          lambda t, o=c.psF[:, bank, :],
                                 l=h_sb.ap()[:, hc, st4 * 128:(st4 + 1) * 128],
                                 r=c.W2T.ap()[:, hc, half * 512:(half + 1) * 512],
                                 st=(hc == 0), sp=(hc == 31):
                          t.matmul(o, l, r, start=st, stop=sp),
                          wait=wt, inc=[(c.s_pe, 1)] if hc == 31 else ())
            pe_done = c.s_pe.n
            bx = st4 % 2
            t_ap = tt.ap()
            P.add("vector", lambda v, o=t_ap[:, 0:512], i=c.psF[:, 2, :],
                           r=x1r.ap()[:, 0:512]:
                  v.tensor_add(o, i, r),
                  wait=[(c.s_pe, pe_done), (c.s_res, c.s_res.n),
                        (c.s_act, c.s_act.n)])
            P.add("vector", lambda v, o=t_ap[:, 512:1024], i=c.psF[:, 3, :],
                           r=x1r.ap()[:, 512:1024]:
                  v.tensor_add(o, i, r), inc=[(c.s_dve, 1)])
            bankO2 = c.s_dve.n
            P.add("vector", lambda v, o=t_ap, i=c.b2cast.ap():
                  v.tensor_add(o, o, i),
                  wait=[(c.s_dve, c.s_dve.n)], inc=[(c.s_dve, 1)])
            emit_finalize_tile(c, t_ap, sqs.ap(), stats.ap(), xn[bx].ap(),
                               row0, c.x2f, c.x2b)
        w2_done = c.s_pe.n

    for t in (h_sb, xt[0], xt[1], x1r, tt, xn[0], xn[1], stats, sqs,
              c.W1T, c.W2T, c.b1t, c.b2cast):
        sb.freeit(t)
    P.barrier("l0ffn_done")


def emit_l1prep(c):
    P, sb = c.P, c.sb
    c.q1T = sb.alloc("q1T", [128, 8, S], BF16)
    c.k1T = sb.alloc("k1T", [128, 2, S], BF16)
    c.v1 = sb.alloc("v1", [128, NST, NKV, 65], BF16)
    wq1s = sb.alloc("wq1s", [128, 8, D], BF16)
    wk1s = sb.alloc("wk1s", [128, 8, 256], BF16)
    wv1s = sb.alloc("wv1s", [128, 8, 256], BF16)
    eyes = sb.alloc("eyes", [128, 128], BF16)
    xt = [sb.alloc(f"xtp{i}", [128, 8, SCW], BF16) for i in range(2)]
    qf = sb.alloc("qf", [128, NH, DK], F32)
    kf = sb.alloc("kf", [128, NKV, DK], F32)
    qsq = sb.alloc("qsq", [128, D], F32)
    qn = sb.alloc("qn", [128, NH, DK], F32)
    kn = sb.alloc("kn", [128, NKV, DK], F32)
    qrot = sb.alloc("qrot", [128, D], BF16)
    krot = sb.alloc("krot", [128, 256], BF16)
    cosb = [sb.alloc(f"cosb{i}", [128, 32], F32) for i in range(2)]
    sinb = [sb.alloc(f"sinb{i}", [128, 32], F32) for i in range(2)]
    rstat = sb.alloc("rstat", [128, NH * 4], F32)

    sd = c.s_dve
    P.dma("gpsimd", wq1s.ap(), c.wq1.rearrange("(a p) n -> p a n", p=128), c.s_ldw)
    P.dma("gpsimd", wk1s.ap(), c.wk1.rearrange("(a p) n -> p a n", p=128), c.s_ldw)
    P.dma("gpsimd", wv1s.ap(), c.wv1.rearrange("(a p) n -> p a n", p=128), c.s_ldw)
    P.dma("sync", eyes.ap(), c.eye, c.s_ldw)
    ldw_done = c.s_ldw.n
    P.add("vector", lambda v: v.memset(c.v1.ap()[:, :, :, 64:65], 1.0), inc=[(sd, 1)])

    sumq = rstat.ap()[:, 0:16]; vq = rstat.ap()[:, 16:32]
    sumk = rstat.ap()[:, 32:36]; vk = rstat.ap()[:, 36:40]
    rq = rstat.ap()[:, 40:56]; rk = rstat.ap()[:, 56:60]

    xt_free = [None, None]
    prev_tr_evac = 0
    for st in range(NST):
        sc8, off = st // 4, (st % 4) * 128
        cb = st % 2
        if st % 4 == 0:
            fa = xt_free[(st // 4) % 2]
            w = [(c.s_ldw, ldw_done)] if st == 0 else                 ([(c.s_pe, fa)] if fa is not None else [])
            load_xT(c, xt[(st // 4) % 2].ap(), c.x2b, sc8 * SCW, SCW, w)
        xb = xt[(st // 4) % 2]
        ldx_done = c.s_ldx.n
        P.dma("sync", cosb[cb].ap(), c.cos[st * 128:(st + 1) * 128, :], c.s_res,
              wait=[(sd, sd.n)])
        P.dma("sync", sinb[cb].ap(), c.sin[st * 128:(st + 1) * 128, :], c.s_res)
        res_done = c.s_res.n

        for half in range(2):
            for dc in range(8):
                wt = []
                if half == 0 and dc == 0:
                    wt = [(c.s_ldx, ldx_done), (sd, max(0, sd.n - 2))]
                P.add("tensor",
                      lambda t, o=c.psF[:, half, :], l=xb.ap()[:, dc, off:off + 128],
                             r=wq1s.ap()[:, dc, half * 512:(half + 1) * 512],
                             st_=(dc == 0), sp=(dc == 7):
                      t.matmul(o, l, r, start=st_, stop=sp),
                      wait=wt, inc=[(c.s_pe, 1)] if dc == 7 else ())
        for (bank, wsb) in ((2, wk1s), (3, wv1s)):
            for dc in range(8):
                P.add("tensor",
                      lambda t, o=c.psF[:, bank, 0:256], l=xb.ap()[:, dc, off:off + 128],
                             r=wsb.ap()[:, dc, :], st_=(dc == 0), sp=(dc == 7):
                      t.matmul(o, l, r, start=st_, stop=sp),
                      wait=(), inc=[(c.s_pe, 1)] if dc == 7 else ())
        proj_done = c.s_pe.n
        qv = qf.ap().rearrange("p (a b) d -> p a b d", b=2)
        pv0 = c.psF[:, 0, :].rearrange("p (h d) -> p h d", d=DK)
        pv1 = c.psF[:, 1, :].rearrange("p (h d) -> p h d", d=DK)
        P.add("vector", lambda v, o=qv[:, 0:4, 0, :], i=pv0[:, 0:4, :]: v.tensor_copy(o, i),
              wait=[(c.s_pe, proj_done)])
        P.add("vector", lambda v, o=qv[:, 0:4, 1, :], i=pv0[:, 4:8, :]: v.tensor_copy(o, i))
        P.add("vector", lambda v, o=qv[:, 4:8, 0, :], i=pv1[:, 0:4, :]: v.tensor_copy(o, i))
        P.add("vector", lambda v, o=qv[:, 4:8, 1, :], i=pv1[:, 4:8, :]: v.tensor_copy(o, i))
        P.add("vector", lambda v, o=kf.ap(), i=c.psF[:, 2, 0:256].rearrange("p (h d) -> p h d", d=DK):
              v.tensor_copy(o, i))
        P.add("vector", lambda v, o=c.v1.ap()[:, st, :, 0:64],
                       i=c.psF[:, 3, 0:256].rearrange("p (h d) -> p h d", d=DK):
              v.tensor_copy(o, i), inc=[(sd, 1)])
        evac_done = sd.n

        P.add("scalar", lambda s: s.activation(qsq.ap(), qf.ap().rearrange("p a b -> p (a b)"),
                                               AF.Square),
              wait=[(sd, evac_done)], inc=[(c.s_act, 1)])
        P.add("vector", lambda v: v.reduce_sum(sumq.rearrange("p (a b) -> p a b", b=1),
                                               qsq.ap().rearrange("p (a b) -> p a b", b=DK),
                                               axis=AX.X),
              wait=[(c.s_act, c.s_act.n)], inc=[(sd, 1)])
        P.add("scalar", lambda s: s.activation(qsq.ap()[:, 0:256],
                                               kf.ap().rearrange("p a b -> p (a b)"),
                                               AF.Square),
              wait=[(sd, sd.n)], inc=[(c.s_act, 1)])
        P.add("vector", lambda v: v.reduce_sum(sumk.rearrange("p (a b) -> p a b", b=1),
                                               qsq.ap()[:, 0:256].rearrange("p (a b) -> p a b", b=DK),
                                               axis=AX.X),
              wait=[(c.s_act, c.s_act.n)], inc=[(sd, 1)])
        P.add("vector", lambda v: v.tensor_scalar(vq, sumq, 1.0 / DK, EPS, ALU.mult, ALU.add),
              wait=[(sd, sd.n)])
        P.add("vector", lambda v: v.tensor_scalar(vk, sumk, 1.0 / DK, EPS, ALU.mult, ALU.add),
              inc=[(sd, 1)])
        P.add("scalar", lambda s: s.activation(vq, vq, AF.Ln), wait=[(sd, sd.n)])
        P.add("scalar", lambda s: s.activation(rq, vq, AF.Exp, scale=-0.5))
        P.add("scalar", lambda s: s.activation(vk, vk, AF.Ln))
        P.add("scalar", lambda s: s.activation(rk, vk, AF.Exp, scale=-0.5),
              inc=[(c.s_act, 1)])
        P.add("vector", lambda v: v.tensor_tensor(
              qn.ap(), qf.ap(),
              rq.rearrange("p (a b) -> p a b", b=1).broadcast_to([128, NH, DK]), ALU.mult),
              wait=[(c.s_act, c.s_act.n)])
        P.add("vector", lambda v: v.tensor_tensor(
              kn.ap(), kf.ap(),
              rk.rearrange("p (a b) -> p a b", b=1).broadcast_to([128, NKV, DK]), ALU.mult),
              inc=[(sd, 1)])
        t1q = qsq.ap()[:, 0:512].rearrange("p (a b) -> p a b", b=32)
        t2q = qsq.ap()[:, 512:1024].rearrange("p (a b) -> p a b", b=32)
        cq = cosb[cb].ap().rearrange("p (a b) -> p a b", a=1).broadcast_to([128, NH, 32])
        sq_ = sinb[cb].ap().rearrange("p (a b) -> p a b", a=1).broadcast_to([128, NH, 32])
        ck = cosb[cb].ap().rearrange("p (a b) -> p a b", a=1).broadcast_to([128, NKV, 32])
        sk = sinb[cb].ap().rearrange("p (a b) -> p a b", a=1).broadcast_to([128, NKV, 32])
        qnv = qn.ap(); knv = kn.ap()
        qrv = qrot.ap().rearrange("p (a b) -> p a b", b=DK)
        krv = krot.ap().rearrange("p (a b) -> p a b", b=DK)
        P.add("vector", lambda v, a=t1q, b=cq: v.tensor_tensor(a, qnv[:, :, 0:32], b, ALU.mult),
              wait=[(sd, sd.n), (c.s_res, res_done)])
        P.add("vector", lambda v, a=t2q, b=sq_: v.tensor_tensor(a, qnv[:, :, 32:64], b, ALU.mult),
              inc=[(sd, 1)])
        P.add("vector", lambda v, a=t1q, b=t2q: v.tensor_sub(qrv[:, :, 0:32], a, b),
              wait=[(sd, sd.n)])
        P.add("vector", lambda v, a=t1q, b=cq: v.tensor_tensor(a, qnv[:, :, 32:64], b, ALU.mult))
        P.add("vector", lambda v, a=t2q, b=sq_: v.tensor_tensor(a, qnv[:, :, 0:32], b, ALU.mult),
              inc=[(sd, 1)])
        P.add("vector", lambda v, a=t1q, b=t2q: v.tensor_add(qrv[:, :, 32:64], a, b),
              wait=[(sd, sd.n)])
        t1k = qsq.ap()[:, 0:128].rearrange("p (a b) -> p a b", b=32)
        t2k = qsq.ap()[:, 128:256].rearrange("p (a b) -> p a b", b=32)
        P.add("vector", lambda v, a=t1k, b=ck: v.tensor_tensor(a, knv[:, :, 0:32], b, ALU.mult))
        P.add("vector", lambda v, a=t2k, b=sk: v.tensor_tensor(a, knv[:, :, 32:64], b, ALU.mult),
              inc=[(sd, 1)])
        P.add("vector", lambda v, a=t1k, b=t2k: v.tensor_sub(krv[:, :, 0:32], a, b),
              wait=[(sd, sd.n)])
        P.add("vector", lambda v, a=t1k, b=ck: v.tensor_tensor(a, knv[:, :, 32:64], b, ALU.mult))
        P.add("vector", lambda v, a=t2k, b=sk: v.tensor_tensor(a, knv[:, :, 0:32], b, ALU.mult),
              inc=[(sd, 1)])
        P.add("vector", lambda v, a=t1k, b=t2k: v.tensor_add(krv[:, :, 32:64], a, b),
              wait=[(sd, sd.n)], inc=[(sd, 1)])
        rope_done = sd.n

        a_ = st % 2
        for ec in range(8):
            wt = [(sd, rope_done)] if ec == 0 else []
            if ec == 0:
                wt.append((sd, prev_tr_evac))
            P.add("tensor",
                  lambda t, o=c.psT[:, a_, ec * 128:(ec + 1) * 128],
                         i=qrot.ap()[:, ec * 128:(ec + 1) * 128], e=eyes.ap():
                  t.transpose(o, i, e),
                  wait=wt, inc=[(c.s_pe, 1)] if ec == 7 else ())
        for ec in range(2):
            P.add("tensor",
                  lambda t, o=c.psT[:, 1 - a_, ec * 128:(ec + 1) * 128],
                         i=krot.ap()[:, ec * 128:(ec + 1) * 128], e=eyes.ap():
                  t.transpose(o, i, e),
                  wait=(), inc=[(c.s_pe, 1)] if ec == 1 else ())
        P.add("vector",
              lambda v, o=c.q1T.ap()[:, :, st * 128:(st + 1) * 128],
                     i=c.psT[:, a_, :].rearrange("p (a b) -> p a b", b=128):
              v.tensor_copy(o, i),
              wait=[(c.s_pe, c.s_pe.n)])
        P.add("vector",
              lambda v, o=c.k1T.ap()[:, :, st * 128:(st + 1) * 128],
                     i=c.psT[:, 1 - a_, 0:256].rearrange("p (a b) -> p a b", b=128):
              v.tensor_copy(o, i),
              inc=[(sd, 1)])
        if "l1prepd" in DBG and st == 0:
            dbg_dump(c, "qf", qf.ap(), [128, NH, DK], F32)
            dbg_dump(c, "qn", qn.ap(), [128, NH, DK], F32)
            dbg_dump(c, "qrot", qrot.ap(), [128, D], BF16)
            dbg_dump(c, "kf", kf.ap(), [128, NKV, DK], F32)
            dbg_dump(c, "krot", krot.ap(), [128, 256], BF16)
            dbg_dump(c, "rstat", rstat.ap(), [128, NH * 4], F32)
        prev_tr_evac = sd.n
        xt_free[(st // 4) % 2] = c.s_pe.n

    if "l1prep" in DBG:
        dbg_dump(c, "q1T", c.q1T.ap(), [128, 8, S], BF16)
        dbg_dump(c, "k1T", c.k1T.ap(), [128, 2, S], BF16)
        dbg_dump(c, "v1", c.v1.ap(), [128, NST, NKV, 65], BF16)
    for t in (wq1s, wk1s, wv1s, eyes, xt[0], xt[1], qf, kf, qsq, qn, kn,
              qrot, krot, cosb[0], cosb[1], sinb[0], sinb[1], rstat):
        sb.freeit(t)
    P.barrier("l1prep_done")


def emit_l1attn(c):
    P, sb = c.P, c.sb
    wo1p = sb.alloc("wo1p", [128, 8, D], BF16)
    trim = sb.alloc("trim", [128, 128], BF16)
    Pp = sb.alloc("Pp1", [128, 4, SCW], BF16)
    ynT = sb.alloc("ynT1", [128, 8, SCW], BF16)
    stat = sb.alloc("stat1", [128, SCW], F32)
    tcap = sb.alloc("tcap", [128, 2 * SCW], F32)
    ones64 = sb.alloc("ones64b", [128, 64], F32)
    x2r = sb.alloc("x2r", [128, D], F32)
    tt = sb.alloc("tt1", [128, D], F32)
    xn = [sb.alloc(f"xn1{i}", [128, D], F32) for i in range(2)]
    stats = sb.alloc("stats1", [128, 12], F32)
    sqs = sb.alloc("sqs1", [128, 512], F32)

    sd = c.s_dve
    for p in range(NH):
        h = PERM1[p]
        b0 = (p % 2) * 64
        P.dma("gpsimd", wo1p.ap()[b0:b0 + 64, p // 2, :],
              c.wo1[h * 64:(h + 1) * 64, :], c.s_ldw)
    P.dma("sync", trim.ap(), c.tri, c.s_ldw)
    ldw_done = c.s_ldw.n
    P.add("vector", lambda v: v.memset(ones64.ap(), 1.0), inc=[(sd, 1)])

    ynmul_done = {0: 0, 1: 0}
    c._pb1 = 0
    bankS1 = [0]
    bankO1 = 0
    last_adds1 = 0

    def emit_norm1(avb_, p0_, chunk_, av_stop, act_after):
        P.add("vector", lambda v, o=stat.ap()[64:65, :], i=c.psF[64:65, avb_, :]:
              v.reciprocal(o, i),
              wait=[(c.s_pe, av_stop), (c.s_act, act_after)], inc=[(sd, 1)])
        P.add("tensor", lambda t, o=c.psF[0:64, 5, :], l=ones64.ap()[64:65, 0:64],
                       r=stat.ap()[64:65, :]:
              t.matmul(o, l, r, start=True, stop=True),
              wait=[(sd, sd.n), (sd, c._pb1)],
              inc=[(c.s_pe, 1)])
        P.add("vector", lambda v, o=stat.ap()[0:64, :], i=c.psF[0:64, 5, :]:
              v.tensor_copy(o, i),
              wait=[(c.s_pe, c.s_pe.n)], inc=[(sd, 1)])
        c._pb1 = sd.n
        P.add("vector", lambda v, o=ynT.ap()[p0_:p0_ + 64, chunk_, :],
                       a=c.psF[0:64, avb_, :], b2=stat.ap()[0:64, :]:
              v.tensor_mul(o, a, b2),
              wait=[(sd, sd.n)], inc=[(sd, 1)])
        ynmul_done[avb_ - 2] = sd.n

    for qc in range(NSC):
        pending_norm = None
        for pos in range(NH):
            h = PERM1[pos]
            p0 = (pos % 2) * 64
            chunk = pos // 2
            j = h // 4
            jchunk = j // 2
            avb = 2 + pos % 2
            kt_last = min(NST - 1, 4 * qc + 3)
            av_q = []
            for pr in range((kt_last + 1) // 2):
                for sbank in range(2):
                    kt = 2 * pr + sbank
                    m = kt - 4 * qc
                    col0 = 128 * m if m > 0 else 0
                    wt = []
                    if kt == 0 and pos == 0 and qc == 0:
                        wt.append((c.s_ldw, ldw_done))
                    if kt == 0 and pos == 0:
                        wt.append((sd, last_adds1))
                    if sbank == 0:
                        wt.append((c.s_act, bankS1[0]))
                    P.add("tensor",
                          lambda t, o=c.psF[:, sbank, col0:512],
                                 l=c.k1T.ap()[p0:p0 + 64, jchunk, kt * 128:(kt + 1) * 128],
                                 r=c.q1T.ap()[p0:p0 + 64, chunk, qc * 512 + col0: (qc + 1) * 512]:
                          t.matmul(o, l, r, start=True, stop=True),
                          wait=wt, inc=[(c.s_pe, 1)])
                sl = 2 * (pr % 2)
                P.add("scalar",
                      lambda s, o=tcap.ap(), i=c.psF[:, 0:2, :].rearrange("p a b -> p (a b)"):
                      s.activation(o, i, AF.Tanh, scale=1.0 / (8.0 * CAP)),
                      wait=[(c.s_pe, c.s_pe.n)], inc=[(c.s_act, 1)])
                bankS1[0] = c.s_act.n
                P.add("scalar",
                      lambda s, o=Pp.ap()[:, sl:sl + 2, :].rearrange("p a b -> p (a b)"),
                             i=tcap.ap():
                      s.activation(o, i, AF.Exp, scale=CAP),
                      inc=[(c.s_act, 1)])
                exp_done = c.s_act.n
                if pr == 1 and pending_norm is not None:
                    emit_norm1(*pending_norm)
                    pending_norm = None
                for kt in (2 * pr, 2 * pr + 1):
                    m = kt - 4 * qc
                    col0 = 128 * m if m > 0 else 0
                    masked = 0 <= m <= 3
                    slot = sl + kt % 2
                    if masked:
                        P.add("vector",
                              lambda v, o=Pp.ap()[:, slot, col0:col0 + 128], tm=trim.ap():
                              v.tensor_mul(o, o, tm),
                              wait=[(c.s_act, exp_done)], inc=[(sd, 1)])
                    avw = [(sd, sd.n)] if masked else [(c.s_act, exp_done)]
                    if kt == 0:
                        avw.append((sd, ynmul_done[avb - 2]))
                    av_q.append((
                        (lambda t, o=c.psF[0:65, avb, col0:512], l=c.v1.ap()[:, kt, j, :],
                                r=Pp.ap()[:, slot, col0:512], st=(kt == 0), sp=(kt == kt_last):
                         t.matmul(o, l, r, start=st, stop=sp)),
                        avw, [(c.s_pe, 1)] if kt == kt_last else ()))
                while len(av_q) > 2:
                    a = av_q.pop(0)
                    P.add("tensor", a[0], wait=a[1], inc=a[2])
            for a in av_q:
                P.add("tensor", a[0], wait=a[1], inc=a[2])
            pending_norm = (avb, p0, chunk, c.s_pe.n, c.s_act.n)
        emit_norm1(*pending_norm)
        pending_norm = None
        heads_done = sd.n

        for st4 in range(4):
            row0 = qc * SCW + st4 * 128
            P.dma("sync", x2r.ap(), c.x2f[row0:row0 + 128, :], c.s_res,
                  wait=[(sd, max(0, sd.n - 1))])
            for half in range(2):
                bank = half
                for hc in range(8):
                    wt = []
                    if hc == 0:
                        wt.append((sd, heads_done))
                        wt.append((sd, bankO1))
                    P.add("tensor",
                          lambda t, o=c.psF[:, bank, :],
                                 l=ynT.ap()[:, hc, st4 * 128:(st4 + 1) * 128],
                                 r=wo1p.ap()[:, hc, half * 512:(half + 1) * 512],
                                 st=(hc == 0), sp=(hc == 7):
                          t.matmul(o, l, r, start=st, stop=sp),
                          wait=wt, inc=[(c.s_pe, 1)] if hc == 7 else ())
            pe_done = c.s_pe.n
            bx = st4 % 2
            t_ap = tt.ap()
            P.add("vector", lambda v, o=t_ap[:, 0:512], i=c.psF[:, 0, :],
                           r=x2r.ap()[:, 0:512]:
                  v.tensor_add(o, i, r),
                  wait=[(c.s_pe, pe_done), (c.s_res, c.s_res.n),
                        (c.s_act, c.s_act.n)])
            P.add("vector", lambda v, o=t_ap[:, 512:1024], i=c.psF[:, 1, :],
                           r=x2r.ap()[:, 512:1024]:
                  v.tensor_add(o, i, r), inc=[(sd, 1)])
            bankO1 = sd.n
            last_adds1 = sd.n
            if "l1attn" in DBG and qc == 1 and st4 == 0:
                dbg_dump(c, "t1", t_ap, [128, D], F32)
            emit_finalize_tile(c, t_ap, sqs.ap(), stats.ap(), xn[bx].ap(),
                               row0, c.x3f, c.x3b)
        if "l1attn" in DBG and qc == 1:
            dbg_dump(c, "ynT1", ynT.ap(), [128, 8, SCW], BF16)

    for t in (wo1p, trim, Pp, ynT, stat, tcap, ones64, x2r, tt, xn[0], xn[1],
              stats, sqs, c.q1T, c.k1T, c.v1):
        sb.freeit(t)
    P.barrier("l1attn_done")


def emit_l1ffnA(c):
    P, sb = c.P, c.sb
    w1T = sb.alloc("w1T1", [128, 8, HFF], BF16)
    w2T = sb.alloc("w2T1", [128, 8, HFF], BF16)
    s1t = sb.alloc("s1t", [128, 32], F32)
    s2t = sb.alloc("s2t", [128, 32], F32)
    xt = [sb.alloc(f"xta{i}", [128, 8, SCW], BF16) for i in range(2)]
    u_sb = [sb.alloc(f"usb{i}", [128, SCW], F32) for i in range(2)]
    ha = sb.alloc("ha", [128, 32, SCW], BF16)

    sd = c.s_dve
    wait_stg = ((c.s_stg_f1, c.s_stg_f1.n),)
    load_wT(c, w1T, c.st_w1w, 8, wait_stg)
    load_wT(c, w2T, c.st_w2w, 8, wait_stg)
    P.dma("sync", s1t.ap(), c.w1s, c.s_ldw)
    P.dma("sync", s2t.ap(), c.w2s, c.s_ldw)
    ldw_done = c.s_ldw.n

    xt_free = [None, None]
    ha_store_done = 0
    bankU = [0, 0]
    bankG = [0, 0]
    ustt = [0, 0]
    for sc8 in range(NSC):
        b = sc8 % 2
        w = [] if xt_free[b] is None else [(c.s_pe, xt_free[b])]
        load_xT(c, xt[b].ap(), c.x3b, sc8 * SCW, SCW, w)
        ldx_done = c.s_ldx.n
        for hc in range(32):
            ub = hc % 2
            gb = 2 + hc % 2
            for dc in range(8):
                wt = []
                if hc == 0 and dc == 0:
                    wt = [(c.s_ldx, ldx_done), (c.s_ldw, ldw_done)]
                if dc == 0:
                    wt.append((c.s_act, bankU[ub]))
                P.add("tensor",
                      lambda t, o=c.psF[:, ub, :], l=w1T.ap()[:, dc, hc * 128:(hc + 1) * 128],
                             r=xt[b].ap()[:, dc, :], st=(dc == 0), sp=(dc == 7):
                      t.matmul(o, l, r, start=st, stop=sp),
                      wait=wt, inc=[(c.s_pe, 1)] if dc == 7 else ())
            for dc in range(8):
                wt = []
                if dc == 0:
                    wt.append((sd, bankG[gb - 2]))
                P.add("tensor",
                      lambda t, o=c.psF[:, gb, :], l=w2T.ap()[:, dc, hc * 128:(hc + 1) * 128],
                             r=xt[b].ap()[:, dc, :], st=(dc == 0), sp=(dc == 7):
                      t.matmul(o, l, r, start=st, stop=sp),
                      wait=wt, inc=[(c.s_pe, 1)] if dc == 7 else ())
            P.add("scalar",
                  lambda s, o=u_sb[hc % 2].ap(), i=c.psF[:, ub, :],
                         sc_=s1t.ap()[:, hc:hc + 1]:
                  s.activation(o, i, AF.Silu, scale=sc_),
                  wait=[(c.s_pe, c.s_pe.n), (sd, ustt[hc % 2])],
                  inc=[(c.s_act, 1)])
            bankU[ub] = c.s_act.n
            stw = [(c.s_act, c.s_act.n)]
            if hc == 0:
                stw.append((c.s_st, ha_store_done))
            P.add("vector",
                  lambda v, o=ha.ap()[:, hc, :], i=c.psF[:, gb, :],
                         sc_=s2t.ap()[:, hc:hc + 1], u=u_sb[hc % 2].ap():
                  v.scalar_tensor_tensor(o, i, sc_, u, ALU.mult, ALU.mult),
                  wait=stw, inc=[(sd, 1)])
            bankG[gb - 2] = sd.n
            ustt[hc % 2] = sd.n
        xt_free[b] = c.s_pe.n
        if "l1ffnA" in DBG and sc8 == 0:
            dbg_dump(c, "xta", xt[b].ap(), [128, 8, SCW], BF16)
            dbg_dump(c, "w1Tc", w1T.ap()[:, :, 0:512], [128, 8, 512], BF16)
            dbg_dump(c, "hasc", ha.ap(), [128, 32, SCW], BF16)
            dbg_dump(c, "s1td", s1t.ap(), [128, 32], F32)
        P.dma("scalar",
              c.haT.rearrange("(a p) s -> p a s", p=128)[:, :, sc8 * SCW:(sc8 + 1) * SCW],
              ha.ap(), c.s_st, wait=[(sd, sd.n)])
        ha_store_done = c.s_st.n

    for t in (w1T, w2T, s1t, s2t, xt[0], xt[1], u_sb[0], u_sb[1], ha):
        sb.freeit(t)
    P.barrier("l1ffnA_done")


def emit_l1ffnBC(c, st_w, scales_in, in_d, out_d, tag):
    P, sb = c.P, c.sb
    hw = [sb.alloc(f"hw{tag}{i}", [128, 32, 1024], BF16) for i in range(2)]
    hst = sb.alloc(f"hst{tag}", [128, 32], F32)
    hin = [sb.alloc(f"hin{tag}{i}", [128, 32, SCW], BF16) for i in range(2)]
    ho = sb.alloc(f"ho{tag}", [128, 8, SCW], BF16)

    P.dma("sync", hst.ap(), scales_in, c.s_ldw)
    in_v = in_d.rearrange("(a p) s -> p a s", p=128)
    out_v = out_d.rearrange("(a p) s -> p a s", p=128)

    ldw_after = {}
    hw_free = [None, None]
    s_bc = [P.sem(f"bc{tag}0"), P.sem(f"bc{tag}1")]

    def load_stripe(s, extra_wait):
        for ic in range(32):
            P.dma("scalar", hw[s % 2].ap()[:, ic, :],
                  st_w[s * 1024:(s + 1) * 1024, ic * 128:(ic + 1) * 128],
                  s_bc[s % 2], wait=extra_wait if ic == 0 else (), transpose=True)
        ldw_after[s] = (s_bc[s % 2], s_bc[s % 2].n)

    load_stripe(0, [(c.s_stg_f1, c.s_stg_f1.n)])
    hin_free = [None, None]
    ho_store_done = 0
    bankB = [0, 0]
    for stripe in range(4):
        if stripe + 1 < 4:
            w = [(c.s_stg_f1, c.s_stg_f1.n)]
            if hw_free[(stripe + 1) % 2] is not None:
                w.append((c.s_pe, hw_free[(stripe + 1) % 2]))
            load_stripe(stripe + 1, w)
        for sc8 in range(NSC):
            b = sc8 % 2
            w = []
            if hin_free[b] is not None:
                w.append((c.s_pe, hin_free[b]))
            P.dma("sync", hin[b].ap(), in_v[:, :, sc8 * SCW:(sc8 + 1) * SCW],
                  c.s_ldx, wait=w)
            ldx_done = c.s_ldx.n
            for oc in range(8):
                bank = oc % 2
                for ic in range(32):
                    wt = []
                    if oc == 0 and ic == 0:
                        wt = [(c.s_ldx, ldx_done), ldw_after[stripe]]
                    if ic == 0:
                        wt.append((c.s_act, bankB[bank]))
                    P.add("tensor",
                          lambda t, o=c.psF[:, bank, :],
                                 l=hw[stripe % 2].ap()[:, ic, oc * 128:(oc + 1) * 128],
                                 r=hin[b].ap()[:, ic, :], st=(ic == 0), sp=(ic == 31):
                          t.matmul(o, l, r, start=st, stop=sp),
                          wait=wt, inc=[(c.s_pe, 1)] if ic == 31 else ())
                gw = [(c.s_pe, c.s_pe.n)]
                if oc == 0:
                    gw.append((c.s_st, ho_store_done))
                P.add("scalar",
                      lambda s, o=ho.ap()[:, oc, :], i=c.psF[:, bank, :],
                             sc_=hst.ap()[:, stripe * 8 + oc:stripe * 8 + oc + 1]:
                      s.activation(o, i, AF.Gelu, scale=sc_),
                      wait=gw, inc=[(c.s_act, 1)])
                bankB[bank] = c.s_act.n
            hin_free[b] = c.s_pe.n
            P.dma("scalar",
                  out_v[:, stripe * 8:(stripe + 1) * 8, sc8 * SCW:(sc8 + 1) * SCW],
                  ho.ap(), c.s_st, wait=[(c.s_act, c.s_act.n)])
            ho_store_done = c.s_st.n
        hw_free[stripe % 2] = c.s_pe.n

    for t in (hw[0], hw[1], hst, hin[0], hin[1], ho):
        sb.freeit(t)
    P.barrier(f"l1ffn{tag}_done")


def emit_l1ffnB(c):
    emit_l1ffnBC(c, c.st_h1, c.h1s, c.haT, c.hbT, "B")


def emit_l1ffnC(c):
    emit_l1ffnBC(c, c.st_h2, c.h2s, c.hbT, c.haT, "C")


def emit_l1ffnD(c):
    P, sb = c.P, c.sb
    w3T = sb.alloc("w3T1", [128, 32, D], BF16)
    s3c = sb.alloc("s3c", [128, D], F32)
    hin = [sb.alloc(f"hinD{i}", [128, 32, SCW], BF16) for i in range(2)]
    x3r = sb.alloc("x3r", [128, D], F32)
    tt = sb.alloc("ttD", [128, D], F32)
    xn = [sb.alloc(f"xnD{i}", [128, D], F32) for i in range(2)]
    stats = sb.alloc("statsD", [128, 12], F32)
    sqs = sb.alloc("sqsD", [128, 512], F32)

    sd = c.s_dve
    wait_stg = ((c.s_stg_f1, c.s_stg_f1.n),)
    load_wT(c, w3T, c.st_w3, 32, wait_stg)
    P.dma("sync", s3c.ap(), c.w3s[None, :].broadcast_to([128, D]), c.s_ldw)
    ldw_done = c.s_ldw.n
    in_v = c.haT.rearrange("(a p) s -> p a s", p=128)

    hin_free = [None, None]
    bankD = 0
    for sc8 in range(NSC):
        b = sc8 % 2
        w = []
        if hin_free[b] is not None:
            w.append((c.s_pe, hin_free[b]))
        P.dma("sync", hin[b].ap(), in_v[:, :, sc8 * SCW:(sc8 + 1) * SCW],
              c.s_ldx, wait=w)
        ldx_done = c.s_ldx.n
        for st4 in range(4):
            row0 = sc8 * SCW + st4 * 128
            P.dma("sync", x3r.ap(), c.x3f[row0:row0 + 128, :], c.s_res,
                  wait=[(sd, max(0, sd.n - 1))])
            for half in range(2):
                bank = 2 + half
                for ic in range(32):
                    wt = []
                    if st4 == 0 and half == 0 and ic == 0:
                        wt = [(c.s_ldx, ldx_done), (c.s_ldw, ldw_done)]
                    if ic == 0:
                        wt.append((sd, bankD))
                    P.add("tensor",
                          lambda t, o=c.psF[:, bank, :], l=hin[b].ap()[:, ic, st4 * 128:(st4 + 1) * 128],
                                 r=w3T.ap()[:, ic, half * 512:(half + 1) * 512],
                                 st=(ic == 0), sp=(ic == 31):
                          t.matmul(o, l, r, start=st, stop=sp),
                          wait=wt, inc=[(c.s_pe, 1)] if ic == 31 else ())
            pe_done = c.s_pe.n
            bx = st4 % 2
            t_ap = tt.ap()
            P.add("vector", lambda v, o=t_ap[:, 0:512], i=c.psF[:, 2, :],
                           s3=s3c.ap()[:, 0:512]:
                  v.tensor_mul(o, i, s3),
                  wait=[(c.s_pe, pe_done), (c.s_res, c.s_res.n),
                        (c.s_act, c.s_act.n)])
            P.add("vector", lambda v, o=t_ap[:, 512:1024], i=c.psF[:, 3, :],
                           s3=s3c.ap()[:, 512:1024]:
                  v.tensor_mul(o, i, s3), inc=[(sd, 1)])
            bankD = sd.n
            P.add("vector", lambda v, o=t_ap, i=x3r.ap():
                  v.tensor_add(o, o, i),
                  wait=[(sd, sd.n)], inc=[(sd, 1)])
            emit_finalize_tile(c, t_ap, sqs.ap(), stats.ap(), xn[bx].ap(),
                               row0, c.y, None)
        hin_free[b] = c.s_pe.n

    for t in (w3T, s3c, hin[0], hin[1], x3r, tt, xn[0], xn[1], stats, sqs):
        sb.freeit(t)
    P.barrier("l1ffnD_done")


# version 06 (final)
# speedup vs baseline: 1.1280x; 1.0947x over previous
import math
import numpy as np
import ml_dtypes
import sys
if '/opt/trn_rl_repo' not in sys.path:
    sys.path.insert(0, '/opt/trn_rl_repo')
import concourse.bass as bass
import concourse.mybir as mybir
from concourse.bass_utils import run_bass_kernel_spmd

F32 = mybir.dt.float32
BF16 = mybir.dt.bfloat16
AF = mybir.ActivationFunctionType
ALU = mybir.AluOpType
AX = mybir.AxisListType

S = 4096
D = 1024
HFF = 4096
NH = 16
DK = 64
NKV = 4
SCW = 512
NSC = S // SCW
NST = S // 128
EPS = 1e-5
CAP = 50.0
THETA = 1e4

ENGINES = ("tensor", "vector", "scalar", "gpsimd", "sync")

PERM1 = [0, 4, 1, 5, 2, 6, 3, 7, 8, 12, 9, 13, 10, 14, 11, 15]


class Sem:
    __slots__ = ("h", "n", "name")
    def __init__(self, h, name):
        self.h = h
        self.n = 0
        self.name = name


class Planner:
    def __init__(self, nc):
        self.nc = nc
        self.items = {e: [] for e in ENGINES}
        self.pending = {e: [] for e in ENGINES}
        self.bar_idx = {e: 0 for e in ENGINES}
        self.nsem = 0

    def sem(self, name=None):
        name = name or f"s{self.nsem}"
        self.nsem += 1
        h = self.nc.alloc_semaphore(name)
        return Sem(h, name)

    def add(self, e, fn=None, wait=(), inc=(), is_dma=False):
        waits = list(self.pending[e]) + list(wait)
        self.pending[e] = []
        for s, n in inc:
            s.n += n
        self.items[e].append({"waits": waits, "fn": fn, "incs": list(inc),
                              "is_dma": is_dma})

    def dma(self, e, out, in_, sem, wait=(), transpose=False, inc=()):
        if transpose:
            fn = lambda eng, o=out, i=in_: eng.dma_start_transpose(out=o, in_=i)
        else:
            fn = lambda eng, o=out, i=in_: eng.dma_start(out=o, in_=i)
        self.add(e, fn, wait, [(sem, 16), *inc], is_dma=True)

    def barrier(self, name="bar"):
        b = self.sem(name)
        total = 0
        for e in ENGINES:
            its = self.items[e]
            if len(its) > self.bar_idx[e]:
                last = its[-1]
                n = 16 if (last["is_dma"] and last["fn"] is not None) else 1
                if last["fn"] is not None and not last["incs"]:
                    last["incs"].append((b, n))
                else:
                    self.items[e].append({"waits": [], "fn": None,
                                          "incs": [(b, 1)], "is_dma": False})
                    n = 1
                b.n += n
                total += n
            self.bar_idx[e] = len(self.items[e])
        for e in ENGINES:
            self.pending[e].append((b, total))
        return b

    def replay(self, ename, eng):
        for it in self.items[ename]:
            for (s, v) in it["waits"]:
                eng.wait_ge(s.h, v)
            ins = it["fn"](eng) if it["fn"] is not None else None
            incs = it["incs"]
            if ins is not None and incs:
                ins.then_inc(incs[0][0].h, incs[0][1])
                for s, n in incs[1:]:
                    eng.sem_inc(s.h, n)
            else:
                for s, n in incs:
                    eng.sem_inc(s.h, n)


class _SBWrap:
    __slots__ = ("t", "_sb_range")
    def __init__(self, t, rng):
        self.t = t
        self._sb_range = rng
    def ap(self):
        return self.t.ap()


class SB:
    def __init__(self, nc, base=16640, top=229376):
        self.nc = nc
        self.free = [(base, top)]
        self.cnt = 0

    @staticmethod
    def _sz(dt):
        return {F32: 4, BF16: 2, mybir.dt.int32: 4, mybir.dt.uint8: 1}[dt]

    def alloc(self, name, shape, dtype, align=128):
        bpp = int(np.prod(shape[1:])) * self._sz(dtype)
        for i, (a, b) in enumerate(self.free):
            start = (a + align - 1) // align * align
            if start + bpp <= b:
                pre = (a, start)
                post = (start + bpp, b)
                repl = []
                if pre[1] > pre[0]:
                    repl.append(pre)
                if post[1] > post[0]:
                    repl.append(post)
                self.free[i:i + 1] = repl
                self.cnt += 1
                t = self.nc.alloc_sbuf_tensor_at(f"{name}_{self.cnt}", list(shape),
                                                 dtype, offset=start)
                return _SBWrap(t, (start, start + bpp))
        raise MemoryError(f"SBUF alloc failed for {name} {shape} ({bpp} B/part); "
                          f"free={self.free}")

    def freeit(self, t):
        a, b = t._sb_range
        self.free.append((a, b))
        self.free.sort()
        merged = []
        for seg in self.free:
            if merged and seg[0] <= merged[-1][1]:
                merged[-1] = (merged[-1][0], max(merged[-1][1], seg[1]))
            else:
                merged.append(seg)
        self.free = merged


_CONST_CACHE = {}

def host_consts():
    if _CONST_CACHE:
        return _CONST_CACHE
    inv = 1.0 / THETA ** (np.arange(0, DK, 2, dtype=np.float64) / DK)
    freqs = np.arange(S, dtype=np.float64)[:, None] * inv[None, :]
    _CONST_CACHE["cos"] = np.cos(freqs).astype(np.float32)
    _CONST_CACHE["sin"] = np.sin(freqs).astype(np.float32)
    jk = np.arange(128)[:, None]
    jq = np.arange(128)[None, :]
    _CONST_CACHE["tri"] = (jq >= jk).astype(ml_dtypes.bfloat16)
    _CONST_CACHE["eye"] = np.eye(128, dtype=ml_dtypes.bfloat16)
    return _CONST_CACHE


class Ctx:
    pass


def make_ctx(nc, taps=False):
    c = Ctx()
    c.nc = nc
    c.P = Planner(nc)
    c.sb = SB(nc)
    c.taps = taps
    T = lambda name, shape, dt, kind: nc.dram_tensor(name, shape, dt, kind=kind).ap()
    I = lambda name, shape, dt=F32: T(name, shape, dt, "ExternalInput")
    scratch_kind = "ExternalOutput" if taps else "Internal"
    N = lambda name, shape, dt: T(name, shape, dt, scratch_kind)

    c.x = I("x", [S, D])
    c.wq0 = I("wq0", [D, D]); c.wk0 = I("wk0", [D, D])
    c.wv0 = I("wv0", [D, D]); c.wo0 = I("wo0", [D, D])
    c.bo0 = I("bo0", [D])
    c.W1 = I("W1", [HFF, D]); c.b1 = I("b1", [128, HFF // 128])
    c.W2 = I("W2", [D, HFF]); c.b2 = I("b2", [D])
    c.scale0 = I("scale0", [1, 1])
    c.wq1 = I("wq1", [D, D]); c.wk1 = I("wk1", [D, 256])
    c.wv1 = I("wv1", [D, 256]); c.wo1 = I("wo1", [D, D])
    c.w1w = I("w1w", [HFF, D]); c.w1s = I("w1s", [128, 32])
    c.w2w = I("w2w", [HFF, D]); c.w2s = I("w2s", [128, 32])
    c.h1w = I("h1w", [HFF, HFF]); c.h1s = I("h1s", [128, 32])
    c.h2w = I("h2w", [HFF, HFF]); c.h2s = I("h2s", [128, 32])
    c.w3w = I("w3w", [D, HFF]); c.w3s = I("w3s", [D])
    c.cos = I("cos", [S, 32]); c.sin = I("sin", [S, 32])
    c.tri = I("tri", [128, 128], BF16)
    c.eye = I("eye", [128, 128], BF16)

    c.y = T("y", [S, D], F32, "ExternalOutput")

    ST = lambda name, shape: T(name, shape, BF16, "Internal")
    c.st_wq0 = ST("st_wq0", [D, D]); c.st_wk0 = ST("st_wk0", [D, D])
    c.st_wv0 = ST("st_wv0", [D, D]); c.st_wo0 = ST("st_wo0", [D, D])
    c.st_W1 = ST("st_W1", [HFF, D]); c.st_W2 = ST("st_W2", [D, HFF])
    c.st_w1w = ST("st_w1w", [HFF, D]); c.st_w2w = ST("st_w2w", [HFF, D])
    c.st_h1 = ST("st_h1", [HFF, HFF]); c.st_h2 = ST("st_h2", [HFF, HFF])
    c.st_w3 = ST("st_w3", [D, HFF])

    c.x0b = T("x0b", [S, D], BF16, "Internal")
    c.x1f = N("x1f", [S, D], F32); c.x1b = T("x1b", [S, D], BF16, "Internal")
    c.x2f = N("x2f", [S, D], F32); c.x2b = T("x2b", [S, D], BF16, "Internal")
    c.x3f = N("x3f", [S, D], F32); c.x3b = T("x3b", [S, D], BF16, "Internal")
    c.haT = T("haT", [HFF, S], BF16, scratch_kind)
    c.hbT = T("hbT", [HFF, S], BF16, scratch_kind)

    P = c.P
    c.s_stg_x = P.sem("stg_x")
    c.s_stg_a0 = P.sem("stg_a0")
    c.s_stg_f0 = P.sem("stg_f0")
    c.s_stg_f1 = P.sem("stg_f1")
    c.s_ldw = P.sem("ldw")
    c.s_ldx = P.sem("ldx")
    c.s_res = P.sem("res")
    c.s_st = P.sem("st")
    c.s_stg_o = P.sem("stg_o")
    c.s_pe = P.sem("pe")
    c.s_act = P.sem("act")
    c.s_dve = P.sem("dve")

    c.psF_h = nc.alloc_psum_tensor("psF", [128, 7, 512], F32)
    c.psT_h = nc.alloc_psum_tensor("psT", [128, 1, 1024], BF16)
    c.psF = c.psF_h.ap()
    c.psT = c.psT_h.ap()
    return c


def tview(ap):
    return ap.rearrange("(a p) d -> p a d", p=128)


def emit_stage(c, groups=("x", "a0", "f0", "f1")):
    P = c.P
    if "x" in groups:
        P.dma("gpsimd", c.x0b, c.x, c.s_stg_x)
    if "a0" in groups:
        for dst, src in ((c.st_wq0, c.wq0), (c.st_wk0, c.wk0),
                         (c.st_wv0, c.wv0), (c.st_wo0, c.wo0)):
            P.dma("gpsimd", dst, src, c.s_stg_a0)
    if "f0" in groups:
        P.dma("gpsimd", c.st_W1, c.W1, c.s_stg_f0)
        P.dma("gpsimd", c.st_W2, c.W2, c.s_stg_f0)
    if "f1" in groups:
        for dst, src in ((c.st_w1w, c.w1w), (c.st_w2w, c.w2w), (c.st_w3, c.w3w)):
            P.dma("gpsimd", dst, src, c.s_stg_f1)
        for dst, src in ((c.st_h1, c.h1w), (c.st_h2, c.h2w)):
            for r in range(4):
                P.dma("gpsimd", dst[r * 1024:(r + 1) * 1024, :],
                      src[r * 1024:(r + 1) * 1024, :], c.s_stg_f1)


def load_wT(c, dst_sb, st_ap, nch, wait):
    P = c.P
    for dc in range(nch):
        P.dma("sync", dst_sb.ap()[:, dc, :], st_ap[:, dc * 128:(dc + 1) * 128],
              c.s_ldw, wait=wait if dc == 0 else (), transpose=True)


def load_xT(c, dst_ap, src_bf, s0, width, wait):
    P = c.P
    for dc in range(8):
        P.dma("sync", dst_ap[:, dc, :], src_bf[s0:s0 + width, dc * 128:(dc + 1) * 128],
              c.s_ldx, wait=wait if dc == 0 else (), transpose=True)


def emit_l0kv(c):
    P, sb = c.P, c.sb
    c.kT = sb.alloc("kT", [128, 8, S], BF16)
    c.v0 = sb.alloc("v0", [128, NST, NH, 65], BF16)
    wkT = sb.alloc("wkT", [128, 8, D], BF16)
    wvT = sb.alloc("wvT", [128, 8, D], BF16)
    xt = [sb.alloc(f"xt{i}", [128, 8, SCW], BF16) for i in range(2)]

    wait_stg = ((c.s_stg_a0, c.s_stg_a0.n),)
    load_wT(c, wkT, c.st_wk0, 8, wait_stg)
    load_wT(c, wvT, c.st_wv0, 8, wait_stg)
    ldw_done = c.s_ldw.n

    P.add("vector", lambda v: v.memset(c.v0.ap()[:, :, :, 64:65], 1.0),
          inc=[(c.s_dve, 1)])

    xt_free_at = [None, None]
    bankK = [0, 0]
    bankV = [0, 0]
    for sc in range(NSC):
        b = sc % 2
        w = [(c.s_stg_x, c.s_stg_x.n)] if sc == 0 else ()
        if xt_free_at[b] is not None:
            w = ((c.s_pe, xt_free_at[b]),)
        load_xT(c, xt[b].ap(), c.x0b, sc * SCW, SCW, w)
        ldx_done = c.s_ldx.n

        first_wait = [(c.s_ldx, ldx_done), (c.s_ldw, ldw_done)]
        for ec in range(8):
            bank = ec % 2
            for dc in range(8):
                wt = list(first_wait) if (ec == 0 and dc == 0) else []
                if dc == 0:
                    wt.append((c.s_act, bankK[bank]))
                P.add("tensor",
                      lambda t, o=c.psF[:, bank, :], l=wkT.ap()[:, dc, ec * 128:(ec + 1) * 128],
                             r=xt[b].ap()[:, dc, :], st=(dc == 0), sp=(dc == 7):
                      t.matmul(o, l, r, start=st, stop=sp),
                      wait=wt, inc=[(c.s_pe, 1)] if dc == 7 else ())
            P.add("scalar",
                  lambda s, o=c.kT.ap()[:, ec, sc * SCW:(sc + 1) * SCW], i=c.psF[:, bank, :]:
                  s.activation(o, i, AF.Copy),
                  wait=[(c.s_pe, c.s_pe.n)], inc=[(c.s_act, 1)])
            bankK[bank] = c.s_act.n
        for st4 in range(4):
            for half in range(2):
                bank = 2 + (st4 * 2 + half) % 2
                for dc in range(8):
                    wt = []
                    if dc == 0:
                        wt.append((c.s_dve, bankV[bank - 2]))
                    P.add("tensor",
                          lambda t, o=c.psF[:, bank, :],
                                 l=xt[b].ap()[:, dc, st4 * 128:(st4 + 1) * 128],
                                 r=wvT.ap()[:, dc, half * 512:(half + 1) * 512],
                                 st=(dc == 0), sp=(dc == 7):
                          t.matmul(o, l, r, start=st, stop=sp),
                          wait=wt, inc=[(c.s_pe, 1)] if dc == 7 else ())
                src = c.psF[:, bank, :].rearrange("p (h d) -> p h d", d=64)
                dst = c.v0.ap()[:, sc * 4 + st4, half * 8:(half + 1) * 8, 0:64]
                P.add("vector",
                      lambda v, o=dst, i=src: v.tensor_copy(o, i),
                      wait=[(c.s_pe, c.s_pe.n)], inc=[(c.s_dve, 1)])
                bankV[bank - 2] = c.s_dve.n
        xt_free_at[b] = c.s_pe.n

    if "l0kv" in DBG:
        dbg_dump(c, "kT", c.kT.ap(), [128, 8, S], BF16)
        dbg_dump(c, "v0", c.v0.ap(), [128, NST, NH, 65], BF16)
    for t in (wkT, wvT, xt[0], xt[1]):
        sb.freeit(t)
    P.barrier("l0kv_done")

def emit_finalize_tile(c, t_ap, scratch, stats, xn, row0, outf, outb,
                       xn_single=False):
    P = c.P
    sumsq = stats[:, 0:1]; sumsq2 = stats[:, 1:2]; ssum = stats[:, 2:3]
    m = stats[:, 3:4]; e2 = stats[:, 4:5]; veps = stats[:, 5:6]
    rstd = stats[:, 6:7]; nmr = stats[:, 7:8]; lnv = stats[:, 8:9]
    sqw = [(c.s_dve, c.s_dve.n)]
    P.add("scalar", lambda s: s.activation(scratch, t_ap[:, 0:512], AF.Square,
                                           accum_out=sumsq),
          wait=sqw)
    P.add("scalar", lambda s: s.activation(scratch, t_ap[:, 512:1024], AF.Square,
                                           accum_out=sumsq2),
          inc=[(c.s_act, 1)])
    sd = c.s_dve
    P.add("vector", lambda v: v.reduce_sum(ssum, t_ap, axis=AX.X), inc=[(sd, 1)])
    P.add("vector", lambda v: v.tensor_scalar(m, ssum, 1.0 / D, None, ALU.mult),
          wait=[(sd, sd.n)], inc=[(sd, 1)])
    P.add("vector", lambda v: v.tensor_add(sumsq, sumsq, sumsq2),
          wait=[(c.s_act, c.s_act.n)], inc=[(sd, 1)])
    P.add("vector", lambda v: v.tensor_scalar(e2, sumsq, 1.0 / D, EPS, ALU.mult, ALU.add),
          wait=[(sd, sd.n)], inc=[(sd, 1)])
    P.add("vector", lambda v: v.scalar_tensor_tensor(veps, m, -1.0, m, ALU.mult, ALU.mult),
          wait=[(sd, sd.n)], inc=[(sd, 1)])
    P.add("vector", lambda v: v.tensor_add(veps, veps, e2),
          wait=[(sd, sd.n)], inc=[(sd, 1)])
    P.add("scalar", lambda s: s.activation(lnv, veps, AF.Ln),
          wait=[(sd, sd.n)])
    P.add("scalar", lambda s: s.activation(rstd, lnv, AF.Exp, scale=-0.5),
          inc=[(c.s_act, 1)])
    P.add("vector", lambda v: v.scalar_tensor_tensor(nmr, m, -1.0, rstd, ALU.mult, ALU.mult),
          wait=[(c.s_act, c.s_act.n)], inc=[(sd, 1)])
    back = 0 if xn_single else 16
    aw = [(sd, sd.n), (c.s_st, max(0, c.s_st.n - back))]
    if outb is not None:
        aw.append((c.s_stg_o, max(0, c.s_stg_o.n - back)))
    P.add("vector", lambda v: v.tensor_scalar(xn, t_ap, rstd, nmr, ALU.mult, ALU.add),
          wait=aw, inc=[(sd, 1)])
    w = [(c.s_dve, c.s_dve.n)]
    if "fin" in DBG and getattr(c, "_fin_n", 0) == 0:
        c._fin_n = 1
        dbg_dump(c, "stats", stats, [128, 12], F32)
        dbg_dump(c, "xn", xn, [128, D], F32)
    P.dma("scalar", outf[row0:row0 + 128, :], xn, c.s_st, wait=w)
    if outb is not None:
        P.dma("gpsimd", outb[row0:row0 + 128, :], xn, c.s_stg_o, wait=w)


def emit_l0attn(c):
    P, sb = c.P, c.sb
    wqT = sb.alloc("wqT", [128, 8, D], BF16)
    woT = sb.alloc("woT", [128, 8, D], BF16)
    bocast = sb.alloc("bocast", [128, D], BF16)
    ones64 = sb.alloc("ones64", [128, 64], F32)
    qT = sb.alloc("qT", [128, 8, SCW], BF16)
    xtq = sb.alloc("xtq", [128, 8, SCW], BF16)
    Pp = sb.alloc("Pp", [128, 4, SCW], BF16)
    ynT = sb.alloc("ynT", [128, 8, SCW], BF16)
    stat = sb.alloc("stat", [128, SCW], F32)
    x0r = sb.alloc("x0r", [128, D], F32)
    tt = sb.alloc("tt", [128, D], F32)
    xn1b = sb.alloc("xn", [128, D], F32)
    xn = [xn1b, xn1b]
    stats = sb.alloc("stats", [128, 12], F32)

    wait_stg = ((c.s_stg_a0, c.s_stg_a0.n),)
    load_wT(c, wqT, c.st_wq0, 8, wait_stg)
    load_wT(c, woT, c.st_wo0, 8, wait_stg)
    P.dma("gpsimd", bocast.ap(), c.bo0[None, :].broadcast_to([128, D]), c.s_ldw)
    ldw_done = c.s_ldw.n
    P.add("vector", lambda v: v.memset(ones64.ap(), 1.0), inc=[(c.s_dve, 1)])

    xtq_free = None
    bankQ = [0, 0]
    bankS = [0, 0]
    last_adds = 0
    for qc in range(NSC):
        w = ((c.s_stg_x, c.s_stg_x.n),) if qc == 0 else ((c.s_pe, xtq_free),)
        load_xT(c, xtq.ap(), c.x0b, qc * SCW, SCW, w)
        ldx_done = c.s_ldx.n
        for ec in range(8):
            qb = 4 + ec % 2
            for dc in range(8):
                wt = []
                if ec == 0 and dc == 0:
                    wt = [(c.s_ldx, ldx_done), (c.s_ldw, ldw_done),
                          (c.s_act, getattr(c, "_exps0", 0))]
                if dc == 0:
                    wt.append((c.s_act, bankQ[ec % 2]))
                P.add("tensor",
                      lambda t, o=c.psF[:, qb, :], l=wqT.ap()[:, dc, ec * 128:(ec + 1) * 128],
                             r=xtq.ap()[:, dc, :], st=(dc == 0), sp=(dc == 7):
                      t.matmul(o, l, r, start=st, stop=sp),
                      wait=wt, inc=[(c.s_pe, 1)] if dc == 7 else ())
            P.add("scalar",
                  lambda s, o=qT.ap()[:, ec, :], i=c.psF[:, qb, :]:
                  s.activation(o, i, AF.Copy),
                  wait=[(c.s_pe, c.s_pe.n)], inc=[(c.s_act, 1)])
            bankQ[ec % 2] = c.s_act.n
        xtq_free = c.s_pe.n
        qT_ready = c.s_act.n

        ynd = getattr(c, "_yn0", None)
        if ynd is None:
            ynd = {0: 0, 1: 0}
            c._yn0 = ynd
            c._pb0 = 0

        def emit_norm0(avb_, p0_, chunk_, av_stop, act_after):
            P.add("vector", lambda v, o=stat.ap()[64:65, :], i=c.psF[64:65, avb_, :]:
                  v.reciprocal(o, i),
                  wait=[(c.s_pe, av_stop), (c.s_act, act_after)], inc=[(c.s_dve, 1)])
            P.add("tensor", lambda t, o=c.psF[0:64, 6, :], l=ones64.ap()[64:65, 0:64],
                           r=stat.ap()[64:65, :]:
                  t.matmul(o, l, r, start=True, stop=True),
                  wait=[(c.s_dve, c.s_dve.n), (c.s_dve, c._pb0)],
                  inc=[(c.s_pe, 1)])
            P.add("vector", lambda v, o=stat.ap()[0:64, :], i=c.psF[0:64, 6, :]:
                  v.tensor_copy(o, i),
                  wait=[(c.s_pe, c.s_pe.n)], inc=[(c.s_dve, 1)])
            c._pb0 = c.s_dve.n
            P.add("vector", lambda v, o=ynT.ap()[p0_:p0_ + 64, chunk_, :],
                           a=c.psF[0:64, avb_, :], b2=stat.ap()[0:64, :]:
                  v.tensor_mul(o, a, b2),
                  wait=[(c.s_dve, c.s_dve.n)], inc=[(c.s_dve, 1)])
            ynd[avb_ - 2] = c.s_dve.n

        SBPAIR = ((0, 1), (4, 5))
        pending_norms = []
        exps_done = 0
        for hp in range(8):
            av_q = []
            for kt in range(NST):
                pair = SBPAIR[kt % 2]
                for hpar in range(2):
                    p0 = hpar * 64
                    sbank = pair[hpar]
                    wt = []
                    if kt == 0 and hp == 0 and hpar == 0:
                        wt.append((c.s_act, qT_ready))
                        wt.append((c.s_dve, last_adds))
                    if hpar == 0:
                        wt.append((c.s_act, bankS[kt % 2]))
                    P.add("tensor",
                          lambda t, o=c.psF[:, sbank, :],
                                 l=c.kT.ap()[p0:p0 + 64, hp, kt * 128:(kt + 1) * 128],
                                 r=qT.ap()[p0:p0 + 64, hp, :]:
                          t.matmul(o, l, r, start=True, stop=True),
                          wait=wt, inc=[(c.s_pe, 1)])
                P.add("scalar",
                      lambda s, o=Pp.ap()[:, 2 * (kt % 2):2 * (kt % 2) + 2, :].rearrange("p a b -> p (a b)"),
                             i=c.psF[:, pair[0]:pair[0] + 2, :].rearrange("p a b -> p (a b)"):
                      s.activation(o, i, AF.Exp, scale=0.125),
                      wait=[(c.s_pe, c.s_pe.n)], inc=[(c.s_act, 1)])
                bankS[kt % 2] = c.s_act.n
                exp_done = c.s_act.n
                if kt == 1 and pending_norms:
                    for pn in pending_norms:
                        emit_norm0(*pn)
                    pending_norms = []
                for hpar in range(2):
                    avb = 2 + hpar
                    avw = [(c.s_act, exp_done)]
                    av_q.append((
                        (lambda t, o=c.psF[0:65, avb, :],
                                l=c.v0.ap()[:, kt, 2 * hp + hpar, :],
                                r=Pp.ap()[:, 2 * (kt % 2) + hpar, :],
                                st=(kt == 0), sp=(kt == NST - 1):
                         t.matmul(o, l, r, start=st, stop=sp)),
                        avw, [(c.s_pe, 1)] if kt == NST - 1 else (),
                        hpar if kt == 0 else None))
                while len(av_q) > 2:
                    a = av_q.pop(0)
                    w = list(a[1])
                    if a[3] is not None:
                        w.append((c.s_dve, ynd[a[3]]))
                    P.add("tensor", a[0], wait=w, inc=a[2])
            for a in av_q:
                w = list(a[1])
                if a[3] is not None:
                    w.append((c.s_dve, ynd[a[3]]))
                P.add("tensor", a[0], wait=w, inc=a[2])
            exps_done = c.s_act.n
            pending_norms = [(2, 0, hp, c.s_pe.n, c.s_act.n),
                             (3, 64, hp, c.s_pe.n, c.s_act.n)]
        for pn in pending_norms:
            emit_norm0(*pn)
        pending_norms = []
        c._exps0 = exps_done
        heads_done = c.s_dve.n

        for st4 in range(4):
            row0 = qc * SCW + st4 * 128
            P.dma("sync", x0r.ap(), c.x[row0:row0 + 128, :], c.s_res,
                  wait=[(c.s_dve, max(0, c.s_dve.n - 1))])
            for half in range(2):
                bank = half
                for hc in range(8):
                    wt = []
                    if hc == 0:
                        wt.append((c.s_dve, heads_done))
                        wt.append((c.s_dve, max(0, c.s_dve.n - 2)))
                    P.add("tensor",
                          lambda t, o=c.psF[:, bank, :],
                                 l=ynT.ap()[:, hc, st4 * 128:(st4 + 1) * 128],
                                 r=woT.ap()[:, hc, half * 512:(half + 1) * 512],
                                 st=(hc == 0), sp=(hc == 7):
                          t.matmul(o, l, r, start=st, stop=sp),
                          wait=wt, inc=[(c.s_pe, 1)] if hc == 7 else ())
            pe_done = c.s_pe.n
            b = st4 % 2
            t_ap = tt.ap()
            P.add("vector", lambda v, o=t_ap[:, 0:512], i=c.psF[:, 0, :],
                           r=x0r.ap()[:, 0:512]:
                  v.tensor_add(o, i, r),
                  wait=[(c.s_pe, pe_done), (c.s_res, c.s_res.n),
                        (c.s_act, c.s_act.n)])
            P.add("vector", lambda v, o=t_ap[:, 512:1024], i=c.psF[:, 1, :],
                           r=x0r.ap()[:, 512:1024]:
                  v.tensor_add(o, i, r), inc=[(c.s_dve, 1)])
            bankO = c.s_dve.n
            last_adds = c.s_dve.n
            P.add("vector", lambda v, o=t_ap, i=bocast.ap():
                  v.tensor_add(o, o, i),
                  wait=[(c.s_dve, c.s_dve.n)], inc=[(c.s_dve, 1)])
            if "l0attn" in DBG and qc == 0 and st4 == 0:
                dbg_dump(c, "t0", t_ap, [128, D], F32)
            emit_finalize_tile(c, t_ap, stat.ap(), stats.ap(), xn[b].ap(),
                               row0, c.x1f, c.x1b, xn_single=True)
        if "l0attn" in DBG and qc == 0:
            dbg_dump(c, "qT", qT.ap(), [128, 8, SCW], BF16)
            dbg_dump(c, "ynT", ynT.ap(), [128, 8, SCW], BF16)

    for t in (wqT, woT, bocast, ones64, qT, xtq, Pp, ynT, stat, x0r,
              tt, xn1b, stats, c.kT, c.v0):
        sb.freeit(t)
    P.barrier("l0attn_done")


def emit_finish(c):
    P = c.P
    P.add("scalar", wait=[(c.s_st, c.s_st.n)])
    P.add("gpsimd", wait=[(c.s_stg_o, c.s_stg_o.n)])
    P.add("sync", wait=[(c.s_st, c.s_st.n), (c.s_stg_o, c.s_stg_o.n)])


def build_nc(phases, taps=False):
    nc = bass.Bass("TRN2", target_bir_lowering=False, debug=False)
    c = make_ctx(nc, taps=taps)
    for ph in phases:
        ph(c)
    emit_finish(c)
    with nc.Block() as block:
        @block.tensor
        def _(eng):
            c.P.replay("tensor", eng)
        @block.vector
        def _(eng):
            c.P.replay("vector", eng)
        @block.scalar
        def _(eng):
            c.P.replay("scalar", eng)
        @block.gpsimd
        def _(eng):
            c.P.replay("gpsimd", eng)
        @block.sync
        def _(eng):
            c.P.replay("sync", eng)
    return nc, c


def make_inmap(inputs, core):
    hc = host_consts()
    a = lambda k: np.ascontiguousarray(inputs[k])
    rs = lambda v: np.ascontiguousarray(np.asarray(v).reshape(-1, 128).T)
    m = dict(
        x=np.ascontiguousarray(inputs["L0_attn_x"][core]),
        wq0=a("L0_attn_w_q"), wk0=a("L0_attn_w_k"), wv0=a("L0_attn_w_v"),
        wo0=a("L0_attn_w_o"), bo0=a("L0_attn_b_o"),
        W1=a("L0_ffn_W1"),
        b1=np.ascontiguousarray(np.asarray(inputs["L0_ffn_b1"]).reshape(32, 128).T),
        W2=a("L0_ffn_W2"), b2=a("L0_ffn_b2"),
        scale0=np.ascontiguousarray(inputs["L0_ffn_scale"]).reshape(1, 1),
        wq1=a("L1_attn_w_q"), wk1=a("L1_attn_w_k"), wv1=a("L1_attn_w_v"),
        wo1=a("L1_attn_w_o"),
        w1w=a("L1_ffn_w1_w"), w1s=rs(inputs["L1_ffn_w1_s"]),
        w2w=a("L1_ffn_w2_w"), w2s=rs(inputs["L1_ffn_w2_s"]),
        h1w=a("L1_ffn_h1_w"), h1s=rs(inputs["L1_ffn_h1_s"]),
        h2w=a("L1_ffn_h2_w"), h2s=rs(inputs["L1_ffn_h2_s"]),
        w3w=a("L1_ffn_w3_w"), w3s=a("L1_ffn_w3_s"),
        cos=hc["cos"], sin=hc["sin"], tri=hc["tri"], eye=hc["eye"],
    )
    return m


DBG = set()


def dbg_dump(c, name, ap, shape, dtype):
    t = c.nc.dram_tensor("dbg_" + name, list(shape), dtype, kind="ExternalOutput").ap()
    P = c.P
    P.dma("sync", t, ap, c.s_ldx,
          wait=[(c.s_pe, c.s_pe.n), (c.s_act, c.s_act.n), (c.s_dve, c.s_dve.n)])
    for e in ENGINES:
        P.pending[e].append((c.s_ldx, c.s_ldx.n))


def kernel(**inputs):
    phases = [emit_stage, emit_l0kv, emit_l0attn, emit_l0ffnw, emit_l0ffn,
              emit_l1prep, emit_l1attn, emit_l1ffnA, emit_l1ffnB, emit_l1ffnC,
              emit_l1ffnD]
    nc, c = build_nc(phases, taps=False)
    in_maps = [make_inmap(inputs, core) for core in range(8)]
    res = run_bass_kernel_spmd(nc, in_maps, core_ids=list(range(8)))
    out = np.stack([res.results[i]["y"] for i in range(8)], axis=0)
    return out.astype(np.float32)


def emit_l0ffnw(c):
    P, sb = c.P, c.sb
    c.W1T = sb.alloc("W1T", [128, 8, HFF], BF16)
    c.W2T = sb.alloc("W2T", [128, 32, D], BF16)
    c.b1t = sb.alloc("b1t", [128, 32], F32)
    c.b2cast = sb.alloc("b2cast", [128, D], F32)
    xre = sb.alloc("xre", [1, D], F32)
    xro = sb.alloc("xro", [1, D], F32)
    obf = sb.alloc("obf", [1, D], BF16)
    sev = sb.alloc("sev", [1, D], BF16)
    ev4 = sb.alloc("ev4", [1, 256], BF16)
    scs = sb.alloc("scs", [1, 1], F32)

    wait_stg = ((c.s_stg_f0, c.s_stg_f0.n),)
    load_wT(c, c.W1T, c.st_W1, 8, wait_stg)
    load_wT(c, c.W2T, c.st_W2, 32, wait_stg)
    ldw_done = c.s_ldw.n
    P.dma("sync", c.b1t.ap(), c.b1, c.s_ldw)
    P.dma("sync", c.b2cast.ap(), c.b2[None, :].broadcast_to([128, D]), c.s_ldw)
    P.dma("sync", scs.ap(), c.scale0, c.s_ldw)
    sd = c.s_dve

    def rows_prep(tok_e, tok_o):
        P.dma("sync", xre.ap(), c.x1f[tok_e:tok_e + 1, :], c.s_res,
              wait=[(sd, sd.n)])
        P.dma("sync", xro.ap(), c.x1f[tok_o:tok_o + 1, :], c.s_res)
        P.add("vector", lambda v: v.tensor_copy(obf.ap(), xro.ap()),
              wait=[(c.s_res, c.s_res.n), (c.s_ldw, ldw_done + 48),
                    (c.s_pe, c.s_pe.n)], inc=[(sd, 1)])
        P.add("vector", lambda v: v.tensor_scalar(sev.ap(), xre.ap(),
                                                  scs.ap()[0:1, 0:1], None, ALU.mult),
              inc=[(sd, 1)])

    for p in range(4):
        rows_prep(1 + 2 * p, 2 + 2 * p)
        rows_ready = sd.n
        for dc in range(8):
            for half in range(2):
                bank = (dc * 2 + half) % 2
                wt = [(sd, rows_ready)]
                wt.append((sd, max(0, sd.n - 1)))
                P.add("tensor",
                      lambda t, o=c.psF[:, bank, :], l=obf.ap()[0:1, dc * 128:(dc + 1) * 128],
                             r=sev.ap()[0:1, half * 512:(half + 1) * 512]:
                      t.matmul(o, l, r, start=True, stop=True),
                      wait=wt, inc=[(c.s_pe, 1)])
                dst = c.W1T.ap()[:, dc, p * 1024 + half * 512: p * 1024 + (half + 1) * 512]
                P.add("vector",
                      lambda v, o=dst, i=c.psF[:, bank, :]:
                      v.tensor_add(o, o, i),
                      wait=[(c.s_pe, c.s_pe.n)], inc=[(sd, 1)])
    for p in range(4):
        rows_prep(9 + 2 * p, 10 + 2 * p)
        for k in range(4):
            P.add("vector",
                  lambda v, o=ev4.ap(), i=sev.ap().rearrange("a (q f) -> a q f", f=4)[0:1, :, k]:
                  v.tensor_copy(o, i),
                  wait=[(sd, sd.n), (c.s_pe, c.s_pe.n)], inc=[(sd, 1)])
            ev_ready = sd.n
            for jc in range(8):
                bank = jc % 2
                wt = [(sd, ev_ready), (sd, max(0, sd.n - 1))]
                P.add("tensor",
                      lambda t, o=c.psF[:, bank, 0:256], l=obf.ap()[0:1, jc * 128:(jc + 1) * 128],
                             r=ev4.ap():
                      t.matmul(o, l, r, start=True, stop=True),
                      wait=wt, inc=[(c.s_pe, 1)])
                dst = c.W2T.ap()[:, 8 * k + jc, 256 * p: 256 * (p + 1)]
                P.add("vector",
                      lambda v, o=dst, i=c.psF[:, bank, 0:256]:
                      v.tensor_add(o, o, i),
                      wait=[(c.s_pe, c.s_pe.n)], inc=[(sd, 1)])

    for t in (xre, xro, obf, sev, ev4, scs):
        sb.freeit(t)
    P.barrier("l0ffnw_done")


def emit_l0ffn(c):
    P, sb = c.P, c.sb
    h_sb = sb.alloc("h0", [128, 32, SCW], BF16)
    xt = [sb.alloc(f"xtf{i}", [128, 8, SCW], BF16) for i in range(2)]
    x1r = sb.alloc("x1r", [128, D], F32)
    tt = sb.alloc("ttf", [128, D], F32)
    xn = [sb.alloc(f"xnf{i}", [128, D], F32) for i in range(2)]
    stats = sb.alloc("statsf", [128, 12], F32)
    sqs = sb.alloc("sqsf", [128, 512], F32)

    xt_free = [None, None]
    w2_done = None
    bankH = [0, 0]
    bankO2 = 0
    for sc8 in range(NSC):
        b = sc8 % 2
        w = [] if xt_free[b] is None else [(c.s_pe, xt_free[b])]
        load_xT(c, xt[b].ap(), c.x1b, sc8 * SCW, SCW, w)
        ldx_done = c.s_ldx.n
        for hc in range(32):
            bank = hc % 2
            for dc in range(8):
                wt = []
                if hc == 0 and dc == 0:
                    wt.append((c.s_ldx, ldx_done))
                if dc == 0:
                    wt.append((c.s_act, bankH[bank]))
                P.add("tensor",
                      lambda t, o=c.psF[:, bank, :], l=c.W1T.ap()[:, dc, hc * 128:(hc + 1) * 128],
                             r=xt[b].ap()[:, dc, :], st=(dc == 0), sp=(dc == 7):
                      t.matmul(o, l, r, start=st, stop=sp),
                      wait=wt, inc=[(c.s_pe, 1)] if dc == 7 else ())
            gw = [(c.s_pe, c.s_pe.n)]
            if w2_done is not None:
                gw.append((c.s_pe, w2_done))
            P.add("scalar",
                  lambda s, o=h_sb.ap()[:, hc, :], i=c.psF[:, bank, :],
                         bb=c.b1t.ap()[:, hc:hc + 1]:
                  s.activation(o, i, AF.Gelu, bias=bb),
                  wait=gw, inc=[(c.s_act, 1)])
            bankH[bank] = c.s_act.n
        h_done = c.s_act.n
        xt_free[b] = c.s_pe.n
        for st4 in range(4):
            row0 = sc8 * SCW + st4 * 128
            P.dma("sync", x1r.ap(), c.x1f[row0:row0 + 128, :], c.s_res,
                  wait=[(c.s_dve, max(0, c.s_dve.n - 1))])
            for half in range(2):
                bank = 2 + half
                for hc in range(32):
                    wt = []
                    if hc == 0:
                        wt.append((c.s_act, h_done))
                        wt.append((c.s_dve, bankO2))
                    P.add("tensor",
                          lambda t, o=c.psF[:, bank, :],
                                 l=h_sb.ap()[:, hc, st4 * 128:(st4 + 1) * 128],
                                 r=c.W2T.ap()[:, hc, half * 512:(half + 1) * 512],
                                 st=(hc == 0), sp=(hc == 31):
                          t.matmul(o, l, r, start=st, stop=sp),
                          wait=wt, inc=[(c.s_pe, 1)] if hc == 31 else ())
            pe_done = c.s_pe.n
            bx = st4 % 2
            t_ap = tt.ap()
            P.add("vector", lambda v, o=t_ap[:, 0:512], i=c.psF[:, 2, :],
                           r=x1r.ap()[:, 0:512]:
                  v.tensor_add(o, i, r),
                  wait=[(c.s_pe, pe_done), (c.s_res, c.s_res.n),
                        (c.s_act, c.s_act.n)])
            P.add("vector", lambda v, o=t_ap[:, 512:1024], i=c.psF[:, 3, :],
                           r=x1r.ap()[:, 512:1024]:
                  v.tensor_add(o, i, r), inc=[(c.s_dve, 1)])
            bankO2 = c.s_dve.n
            P.add("vector", lambda v, o=t_ap, i=c.b2cast.ap():
                  v.tensor_add(o, o, i),
                  wait=[(c.s_dve, c.s_dve.n)], inc=[(c.s_dve, 1)])
            emit_finalize_tile(c, t_ap, sqs.ap(), stats.ap(), xn[bx].ap(),
                               row0, c.x2f, c.x2b)
        w2_done = c.s_pe.n

    for t in (h_sb, xt[0], xt[1], x1r, tt, xn[0], xn[1], stats, sqs,
              c.W1T, c.W2T, c.b1t, c.b2cast):
        sb.freeit(t)
    P.barrier("l0ffn_done")


def emit_l1prep(c):
    P, sb = c.P, c.sb
    c.q1T = sb.alloc("q1T", [128, 8, S], BF16)
    c.k1T = sb.alloc("k1T", [128, 2, S], BF16)
    c.v1 = sb.alloc("v1", [128, NST, NKV, 65], BF16)
    wq1s = sb.alloc("wq1s", [128, 8, D], BF16)
    wk1s = sb.alloc("wk1s", [128, 8, 256], BF16)
    wv1s = sb.alloc("wv1s", [128, 8, 256], BF16)
    eyes = sb.alloc("eyes", [128, 128], BF16)
    xt = [sb.alloc(f"xtp{i}", [128, 8, SCW], BF16) for i in range(2)]
    qf = sb.alloc("qf", [128, NH, DK], F32)
    kf = sb.alloc("kf", [128, NKV, DK], F32)
    qsq = sb.alloc("qsq", [128, D], F32)
    qn = sb.alloc("qn", [128, NH, DK], F32)
    kn = sb.alloc("kn", [128, NKV, DK], F32)
    qrot = sb.alloc("qrot", [128, D], BF16)
    krot = sb.alloc("krot", [128, 256], BF16)
    cosb = [sb.alloc(f"cosb{i}", [128, 32], F32) for i in range(2)]
    sinb = [sb.alloc(f"sinb{i}", [128, 32], F32) for i in range(2)]
    rstat = sb.alloc("rstat", [128, NH * 4], F32)

    sd = c.s_dve
    P.dma("gpsimd", wq1s.ap(), c.wq1.rearrange("(a p) n -> p a n", p=128), c.s_ldw)
    P.dma("gpsimd", wk1s.ap(), c.wk1.rearrange("(a p) n -> p a n", p=128), c.s_ldw)
    P.dma("gpsimd", wv1s.ap(), c.wv1.rearrange("(a p) n -> p a n", p=128), c.s_ldw)
    P.dma("sync", eyes.ap(), c.eye, c.s_ldw)
    ldw_done = c.s_ldw.n
    P.add("vector", lambda v: v.memset(c.v1.ap()[:, :, :, 64:65], 1.0), inc=[(sd, 1)])

    sumq = rstat.ap()[:, 0:16]; vq = rstat.ap()[:, 16:32]
    sumk = rstat.ap()[:, 32:36]; vk = rstat.ap()[:, 36:40]
    rq = rstat.ap()[:, 40:56]; rk = rstat.ap()[:, 56:60]

    xt_free = [None, None]
    prev_tr_evac = 0
    for st in range(NST):
        sc8, off = st // 4, (st % 4) * 128
        cb = st % 2
        if st % 4 == 0:
            fa = xt_free[(st // 4) % 2]
            w = [(c.s_ldw, ldw_done)] if st == 0 else                 ([(c.s_pe, fa)] if fa is not None else [])
            load_xT(c, xt[(st // 4) % 2].ap(), c.x2b, sc8 * SCW, SCW, w)
        xb = xt[(st // 4) % 2]
        ldx_done = c.s_ldx.n
        P.dma("sync", cosb[cb].ap(), c.cos[st * 128:(st + 1) * 128, :], c.s_res,
              wait=[(sd, sd.n)])
        P.dma("sync", sinb[cb].ap(), c.sin[st * 128:(st + 1) * 128, :], c.s_res)
        res_done = c.s_res.n

        for half in range(2):
            for dc in range(8):
                wt = []
                if half == 0 and dc == 0:
                    wt = [(c.s_ldx, ldx_done), (sd, max(0, sd.n - 2))]
                P.add("tensor",
                      lambda t, o=c.psF[:, half, :], l=xb.ap()[:, dc, off:off + 128],
                             r=wq1s.ap()[:, dc, half * 512:(half + 1) * 512],
                             st_=(dc == 0), sp=(dc == 7):
                      t.matmul(o, l, r, start=st_, stop=sp),
                      wait=wt, inc=[(c.s_pe, 1)] if dc == 7 else ())
        for (bank, wsb) in ((2, wk1s), (3, wv1s)):
            for dc in range(8):
                P.add("tensor",
                      lambda t, o=c.psF[:, bank, 0:256], l=xb.ap()[:, dc, off:off + 128],
                             r=wsb.ap()[:, dc, :], st_=(dc == 0), sp=(dc == 7):
                      t.matmul(o, l, r, start=st_, stop=sp),
                      wait=(), inc=[(c.s_pe, 1)] if dc == 7 else ())
        proj_done = c.s_pe.n
        qv = qf.ap().rearrange("p (a b) d -> p a b d", b=2)
        pv0 = c.psF[:, 0, :].rearrange("p (h d) -> p h d", d=DK)
        pv1 = c.psF[:, 1, :].rearrange("p (h d) -> p h d", d=DK)
        P.add("vector", lambda v, o=qv[:, 0:4, 0, :], i=pv0[:, 0:4, :]: v.tensor_copy(o, i),
              wait=[(c.s_pe, proj_done)])
        P.add("vector", lambda v, o=qv[:, 0:4, 1, :], i=pv0[:, 4:8, :]: v.tensor_copy(o, i))
        P.add("vector", lambda v, o=qv[:, 4:8, 0, :], i=pv1[:, 0:4, :]: v.tensor_copy(o, i))
        P.add("vector", lambda v, o=qv[:, 4:8, 1, :], i=pv1[:, 4:8, :]: v.tensor_copy(o, i))
        P.add("vector", lambda v, o=kf.ap(), i=c.psF[:, 2, 0:256].rearrange("p (h d) -> p h d", d=DK):
              v.tensor_copy(o, i))
        P.add("vector", lambda v, o=c.v1.ap()[:, st, :, 0:64],
                       i=c.psF[:, 3, 0:256].rearrange("p (h d) -> p h d", d=DK):
              v.tensor_copy(o, i), inc=[(sd, 1)])
        evac_done = sd.n

        P.add("scalar", lambda s: s.activation(qsq.ap(), qf.ap().rearrange("p a b -> p (a b)"),
                                               AF.Square),
              wait=[(sd, evac_done)], inc=[(c.s_act, 1)])
        P.add("vector", lambda v: v.reduce_sum(sumq.rearrange("p (a b) -> p a b", b=1),
                                               qsq.ap().rearrange("p (a b) -> p a b", b=DK),
                                               axis=AX.X),
              wait=[(c.s_act, c.s_act.n)], inc=[(sd, 1)])
        P.add("scalar", lambda s: s.activation(qsq.ap()[:, 0:256],
                                               kf.ap().rearrange("p a b -> p (a b)"),
                                               AF.Square),
              wait=[(sd, sd.n)], inc=[(c.s_act, 1)])
        P.add("vector", lambda v: v.reduce_sum(sumk.rearrange("p (a b) -> p a b", b=1),
                                               qsq.ap()[:, 0:256].rearrange("p (a b) -> p a b", b=DK),
                                               axis=AX.X),
              wait=[(c.s_act, c.s_act.n)], inc=[(sd, 1)])
        P.add("vector", lambda v: v.tensor_scalar(vq, sumq, 1.0 / DK, EPS, ALU.mult, ALU.add),
              wait=[(sd, sd.n)])
        P.add("vector", lambda v: v.tensor_scalar(vk, sumk, 1.0 / DK, EPS, ALU.mult, ALU.add),
              inc=[(sd, 1)])
        P.add("scalar", lambda s: s.activation(vq, vq, AF.Ln), wait=[(sd, sd.n)])
        P.add("scalar", lambda s: s.activation(rq, vq, AF.Exp, scale=-0.5))
        P.add("scalar", lambda s: s.activation(vk, vk, AF.Ln))
        P.add("scalar", lambda s: s.activation(rk, vk, AF.Exp, scale=-0.5),
              inc=[(c.s_act, 1)])
        P.add("vector", lambda v: v.tensor_tensor(
              qn.ap(), qf.ap(),
              rq.rearrange("p (a b) -> p a b", b=1).broadcast_to([128, NH, DK]), ALU.mult),
              wait=[(c.s_act, c.s_act.n)])
        P.add("vector", lambda v: v.tensor_tensor(
              kn.ap(), kf.ap(),
              rk.rearrange("p (a b) -> p a b", b=1).broadcast_to([128, NKV, DK]), ALU.mult),
              inc=[(sd, 1)])
        t1q = qsq.ap()[:, 0:512].rearrange("p (a b) -> p a b", b=32)
        t2q = qsq.ap()[:, 512:1024].rearrange("p (a b) -> p a b", b=32)
        cq = cosb[cb].ap().rearrange("p (a b) -> p a b", a=1).broadcast_to([128, NH, 32])
        sq_ = sinb[cb].ap().rearrange("p (a b) -> p a b", a=1).broadcast_to([128, NH, 32])
        ck = cosb[cb].ap().rearrange("p (a b) -> p a b", a=1).broadcast_to([128, NKV, 32])
        sk = sinb[cb].ap().rearrange("p (a b) -> p a b", a=1).broadcast_to([128, NKV, 32])
        qnv = qn.ap(); knv = kn.ap()
        qrv = qrot.ap().rearrange("p (a b) -> p a b", b=DK)
        krv = krot.ap().rearrange("p (a b) -> p a b", b=DK)
        P.add("vector", lambda v, a=t1q, b=cq: v.tensor_tensor(a, qnv[:, :, 0:32], b, ALU.mult),
              wait=[(sd, sd.n), (c.s_res, res_done)])
        P.add("vector", lambda v, a=t2q, b=sq_: v.tensor_tensor(a, qnv[:, :, 32:64], b, ALU.mult),
              inc=[(sd, 1)])
        P.add("vector", lambda v, a=t1q, b=t2q: v.tensor_sub(qrv[:, :, 0:32], a, b),
              wait=[(sd, sd.n)])
        P.add("vector", lambda v, a=t1q, b=cq: v.tensor_tensor(a, qnv[:, :, 32:64], b, ALU.mult))
        P.add("vector", lambda v, a=t2q, b=sq_: v.tensor_tensor(a, qnv[:, :, 0:32], b, ALU.mult),
              inc=[(sd, 1)])
        P.add("vector", lambda v, a=t1q, b=t2q: v.tensor_add(qrv[:, :, 32:64], a, b),
              wait=[(sd, sd.n)])
        t1k = qsq.ap()[:, 0:128].rearrange("p (a b) -> p a b", b=32)
        t2k = qsq.ap()[:, 128:256].rearrange("p (a b) -> p a b", b=32)
        P.add("vector", lambda v, a=t1k, b=ck: v.tensor_tensor(a, knv[:, :, 0:32], b, ALU.mult))
        P.add("vector", lambda v, a=t2k, b=sk: v.tensor_tensor(a, knv[:, :, 32:64], b, ALU.mult),
              inc=[(sd, 1)])
        P.add("vector", lambda v, a=t1k, b=t2k: v.tensor_sub(krv[:, :, 0:32], a, b),
              wait=[(sd, sd.n)])
        P.add("vector", lambda v, a=t1k, b=ck: v.tensor_tensor(a, knv[:, :, 32:64], b, ALU.mult))
        P.add("vector", lambda v, a=t2k, b=sk: v.tensor_tensor(a, knv[:, :, 0:32], b, ALU.mult),
              inc=[(sd, 1)])
        P.add("vector", lambda v, a=t1k, b=t2k: v.tensor_add(krv[:, :, 32:64], a, b),
              wait=[(sd, sd.n)], inc=[(sd, 1)])
        rope_done = sd.n

        for ec in range(8):
            wt = [(sd, rope_done)] if ec == 0 else []
            if ec == 0:
                wt.append((sd, prev_tr_evac))
            P.add("tensor",
                  lambda t, o=c.psT[:, 0, ec * 128:(ec + 1) * 128],
                         i=qrot.ap()[:, ec * 128:(ec + 1) * 128], e=eyes.ap():
                  t.transpose(o, i, e),
                  wait=wt, inc=[(c.s_pe, 1)] if ec == 7 else ())
        P.add("vector",
              lambda v, o=c.q1T.ap()[:, :, st * 128:(st + 1) * 128],
                     i=c.psT[:, 0, :].rearrange("p (a b) -> p a b", b=128):
              v.tensor_copy(o, i),
              wait=[(c.s_pe, c.s_pe.n)], inc=[(sd, 1)])
        qev = sd.n
        for ec in range(2):
            wt = [(sd, qev)] if ec == 0 else []
            P.add("tensor",
                  lambda t, o=c.psT[:, 0, ec * 128:(ec + 1) * 128],
                         i=krot.ap()[:, ec * 128:(ec + 1) * 128], e=eyes.ap():
                  t.transpose(o, i, e),
                  wait=wt, inc=[(c.s_pe, 1)] if ec == 1 else ())
        P.add("vector",
              lambda v, o=c.k1T.ap()[:, :, st * 128:(st + 1) * 128],
                     i=c.psT[:, 0, 0:256].rearrange("p (a b) -> p a b", b=128):
              v.tensor_copy(o, i),
              wait=[(c.s_pe, c.s_pe.n)], inc=[(sd, 1)])
        if "l1prepd" in DBG and st == 0:
            dbg_dump(c, "qf", qf.ap(), [128, NH, DK], F32)
            dbg_dump(c, "qn", qn.ap(), [128, NH, DK], F32)
            dbg_dump(c, "qrot", qrot.ap(), [128, D], BF16)
            dbg_dump(c, "kf", kf.ap(), [128, NKV, DK], F32)
            dbg_dump(c, "krot", krot.ap(), [128, 256], BF16)
            dbg_dump(c, "rstat", rstat.ap(), [128, NH * 4], F32)
        prev_tr_evac = sd.n
        xt_free[(st // 4) % 2] = c.s_pe.n

    if "l1prep" in DBG:
        dbg_dump(c, "q1T", c.q1T.ap(), [128, 8, S], BF16)
        dbg_dump(c, "k1T", c.k1T.ap(), [128, 2, S], BF16)
        dbg_dump(c, "v1", c.v1.ap(), [128, NST, NKV, 65], BF16)
    for t in (wq1s, wk1s, wv1s, eyes, xt[0], xt[1], qf, kf, qsq, qn, kn,
              qrot, krot, cosb[0], cosb[1], sinb[0], sinb[1], rstat):
        sb.freeit(t)
    P.barrier("l1prep_done")


def emit_l1attn(c):
    P, sb = c.P, c.sb
    wo1p = sb.alloc("wo1p", [128, 8, D], BF16)
    trim = sb.alloc("trim", [128, 128], BF16)
    Pp = sb.alloc("Pp1", [128, 4, SCW], BF16)
    ynT = sb.alloc("ynT1", [128, 8, SCW], BF16)
    stat = sb.alloc("stat1", [128, SCW], F32)
    tcap = sb.alloc("tcap", [128, 2 * SCW], F32)
    ones64 = sb.alloc("ones64b", [128, 64], F32)
    x2r = sb.alloc("x2r", [128, D], F32)
    tt = sb.alloc("tt1", [128, D], F32)
    xn = [sb.alloc(f"xn1{i}", [128, D], F32) for i in range(2)]
    stats = sb.alloc("stats1", [128, 12], F32)
    sqs = sb.alloc("sqs1", [128, 512], F32)

    sd = c.s_dve
    for p in range(NH):
        h = PERM1[p]
        b0 = (p % 2) * 64
        P.dma("gpsimd", wo1p.ap()[b0:b0 + 64, p // 2, :],
              c.wo1[h * 64:(h + 1) * 64, :], c.s_ldw)
    P.dma("sync", trim.ap(), c.tri, c.s_ldw)
    ldw_done = c.s_ldw.n
    P.add("vector", lambda v: v.memset(ones64.ap(), 1.0), inc=[(sd, 1)])

    ynmul_done = {0: 0, 1: 0}
    c._pb1 = 0
    bankS1 = [0]
    bankO1 = 0
    last_adds1 = 0

    def emit_norm1(avb_, p0_, chunk_, av_stop, act_after):
        P.add("vector", lambda v, o=stat.ap()[64:65, :], i=c.psF[64:65, avb_, :]:
              v.reciprocal(o, i),
              wait=[(c.s_pe, av_stop), (c.s_act, act_after)], inc=[(sd, 1)])
        P.add("tensor", lambda t, o=c.psF[0:64, 5, :], l=ones64.ap()[64:65, 0:64],
                       r=stat.ap()[64:65, :]:
              t.matmul(o, l, r, start=True, stop=True),
              wait=[(sd, sd.n), (sd, c._pb1)],
              inc=[(c.s_pe, 1)])
        P.add("vector", lambda v, o=stat.ap()[0:64, :], i=c.psF[0:64, 5, :]:
              v.tensor_copy(o, i),
              wait=[(c.s_pe, c.s_pe.n)], inc=[(sd, 1)])
        c._pb1 = sd.n
        P.add("vector", lambda v, o=ynT.ap()[p0_:p0_ + 64, chunk_, :],
                       a=c.psF[0:64, avb_, :], b2=stat.ap()[0:64, :]:
              v.tensor_mul(o, a, b2),
              wait=[(sd, sd.n)], inc=[(sd, 1)])
        ynmul_done[avb_ - 2] = sd.n

    for qc in range(NSC):
        pending_norm = None
        for pos in range(NH):
            h = PERM1[pos]
            p0 = (pos % 2) * 64
            chunk = pos // 2
            j = h // 4
            jchunk = j // 2
            avb = 2 + pos % 2
            kt_last = min(NST - 1, 4 * qc + 3)
            av_q = []
            for pr in range((kt_last + 1) // 2):
                for sbank in range(2):
                    kt = 2 * pr + sbank
                    m = kt - 4 * qc
                    col0 = 128 * m if m > 0 else 0
                    wt = []
                    if kt == 0 and pos == 0 and qc == 0:
                        wt.append((c.s_ldw, ldw_done))
                    if kt == 0 and pos == 0:
                        wt.append((sd, last_adds1))
                    if sbank == 0:
                        wt.append((c.s_act, bankS1[0]))
                    P.add("tensor",
                          lambda t, o=c.psF[:, sbank, col0:512],
                                 l=c.k1T.ap()[p0:p0 + 64, jchunk, kt * 128:(kt + 1) * 128],
                                 r=c.q1T.ap()[p0:p0 + 64, chunk, qc * 512 + col0: (qc + 1) * 512]:
                          t.matmul(o, l, r, start=True, stop=True),
                          wait=wt, inc=[(c.s_pe, 1)])
                sl = 2 * (pr % 2)
                P.add("scalar",
                      lambda s, o=tcap.ap(), i=c.psF[:, 0:2, :].rearrange("p a b -> p (a b)"):
                      s.activation(o, i, AF.Tanh, scale=1.0 / (8.0 * CAP)),
                      wait=[(c.s_pe, c.s_pe.n)], inc=[(c.s_act, 1)])
                bankS1[0] = c.s_act.n
                P.add("scalar",
                      lambda s, o=Pp.ap()[:, sl:sl + 2, :].rearrange("p a b -> p (a b)"),
                             i=tcap.ap():
                      s.activation(o, i, AF.Exp, scale=CAP),
                      inc=[(c.s_act, 1)])
                exp_done = c.s_act.n
                if pr == 1 and pending_norm is not None:
                    emit_norm1(*pending_norm)
                    pending_norm = None
                for kt in (2 * pr, 2 * pr + 1):
                    m = kt - 4 * qc
                    col0 = 128 * m if m > 0 else 0
                    masked = 0 <= m <= 3
                    slot = sl + kt % 2
                    if masked:
                        P.add("vector",
                              lambda v, o=Pp.ap()[:, slot, col0:col0 + 128], tm=trim.ap():
                              v.tensor_mul(o, o, tm),
                              wait=[(c.s_act, exp_done)], inc=[(sd, 1)])
                    avw = [(sd, sd.n)] if masked else [(c.s_act, exp_done)]
                    if kt == 0:
                        avw.append((sd, ynmul_done[avb - 2]))
                    av_q.append((
                        (lambda t, o=c.psF[0:65, avb, col0:512], l=c.v1.ap()[:, kt, j, :],
                                r=Pp.ap()[:, slot, col0:512], st=(kt == 0), sp=(kt == kt_last):
                         t.matmul(o, l, r, start=st, stop=sp)),
                        avw, [(c.s_pe, 1)] if kt == kt_last else ()))
                while len(av_q) > 2:
                    a = av_q.pop(0)
                    P.add("tensor", a[0], wait=a[1], inc=a[2])
            for a in av_q:
                P.add("tensor", a[0], wait=a[1], inc=a[2])
            pending_norm = (avb, p0, chunk, c.s_pe.n, c.s_act.n)
        emit_norm1(*pending_norm)
        pending_norm = None
        heads_done = sd.n

        for st4 in range(4):
            row0 = qc * SCW + st4 * 128
            P.dma("sync", x2r.ap(), c.x2f[row0:row0 + 128, :], c.s_res,
                  wait=[(sd, max(0, sd.n - 1))])
            for half in range(2):
                bank = half
                for hc in range(8):
                    wt = []
                    if hc == 0:
                        wt.append((sd, heads_done))
                        wt.append((sd, bankO1))
                    P.add("tensor",
                          lambda t, o=c.psF[:, bank, :],
                                 l=ynT.ap()[:, hc, st4 * 128:(st4 + 1) * 128],
                                 r=wo1p.ap()[:, hc, half * 512:(half + 1) * 512],
                                 st=(hc == 0), sp=(hc == 7):
                          t.matmul(o, l, r, start=st, stop=sp),
                          wait=wt, inc=[(c.s_pe, 1)] if hc == 7 else ())
            pe_done = c.s_pe.n
            bx = st4 % 2
            t_ap = tt.ap()
            P.add("vector", lambda v, o=t_ap[:, 0:512], i=c.psF[:, 0, :],
                           r=x2r.ap()[:, 0:512]:
                  v.tensor_add(o, i, r),
                  wait=[(c.s_pe, pe_done), (c.s_res, c.s_res.n),
                        (c.s_act, c.s_act.n)])
            P.add("vector", lambda v, o=t_ap[:, 512:1024], i=c.psF[:, 1, :],
                           r=x2r.ap()[:, 512:1024]:
                  v.tensor_add(o, i, r), inc=[(sd, 1)])
            bankO1 = sd.n
            last_adds1 = sd.n
            if "l1attn" in DBG and qc == 1 and st4 == 0:
                dbg_dump(c, "t1", t_ap, [128, D], F32)
            emit_finalize_tile(c, t_ap, sqs.ap(), stats.ap(), xn[bx].ap(),
                               row0, c.x3f, c.x3b)
        if "l1attn" in DBG and qc == 1:
            dbg_dump(c, "ynT1", ynT.ap(), [128, 8, SCW], BF16)

    for t in (wo1p, trim, Pp, ynT, stat, tcap, ones64, x2r, tt, xn[0], xn[1],
              stats, sqs, c.q1T, c.k1T, c.v1):
        sb.freeit(t)
    P.barrier("l1attn_done")


def emit_l1ffnA(c):
    P, sb = c.P, c.sb
    w1T = sb.alloc("w1T1", [128, 8, HFF], BF16)
    w2T = sb.alloc("w2T1", [128, 8, HFF], BF16)
    s1t = sb.alloc("s1t", [128, 32], F32)
    s2t = sb.alloc("s2t", [128, 32], F32)
    xt = [sb.alloc(f"xta{i}", [128, 8, SCW], BF16) for i in range(2)]
    u_sb = [sb.alloc(f"usb{i}", [128, SCW], F32) for i in range(2)]
    ha = sb.alloc("ha", [128, 32, SCW], BF16)

    sd = c.s_dve
    wait_stg = ((c.s_stg_f1, c.s_stg_f1.n),)
    load_wT(c, w1T, c.st_w1w, 8, wait_stg)
    load_wT(c, w2T, c.st_w2w, 8, wait_stg)
    P.dma("sync", s1t.ap(), c.w1s, c.s_ldw)
    P.dma("sync", s2t.ap(), c.w2s, c.s_ldw)
    ldw_done = c.s_ldw.n

    xt_free = [None, None]
    ha_store_done = 0
    bankU = [0, 0]
    bankG = [0, 0]
    ustt = [0, 0]
    for sc8 in range(NSC):
        b = sc8 % 2
        w = [] if xt_free[b] is None else [(c.s_pe, xt_free[b])]
        load_xT(c, xt[b].ap(), c.x3b, sc8 * SCW, SCW, w)
        ldx_done = c.s_ldx.n
        for hc in range(32):
            ub = hc % 2
            gb = 2 + hc % 2
            for dc in range(8):
                wt = []
                if hc == 0 and dc == 0:
                    wt = [(c.s_ldx, ldx_done), (c.s_ldw, ldw_done)]
                if dc == 0:
                    wt.append((c.s_act, bankU[ub]))
                P.add("tensor",
                      lambda t, o=c.psF[:, ub, :], l=w1T.ap()[:, dc, hc * 128:(hc + 1) * 128],
                             r=xt[b].ap()[:, dc, :], st=(dc == 0), sp=(dc == 7):
                      t.matmul(o, l, r, start=st, stop=sp),
                      wait=wt, inc=[(c.s_pe, 1)] if dc == 7 else ())
            for dc in range(8):
                wt = []
                if dc == 0:
                    wt.append((sd, bankG[gb - 2]))
                P.add("tensor",
                      lambda t, o=c.psF[:, gb, :], l=w2T.ap()[:, dc, hc * 128:(hc + 1) * 128],
                             r=xt[b].ap()[:, dc, :], st=(dc == 0), sp=(dc == 7):
                      t.matmul(o, l, r, start=st, stop=sp),
                      wait=wt, inc=[(c.s_pe, 1)] if dc == 7 else ())
            P.add("scalar",
                  lambda s, o=u_sb[hc % 2].ap(), i=c.psF[:, ub, :],
                         sc_=s1t.ap()[:, hc:hc + 1]:
                  s.activation(o, i, AF.Silu, scale=sc_),
                  wait=[(c.s_pe, c.s_pe.n), (sd, ustt[hc % 2])],
                  inc=[(c.s_act, 1)])
            bankU[ub] = c.s_act.n
            stw = [(c.s_act, c.s_act.n)]
            if hc == 0:
                stw.append((c.s_st, ha_store_done))
            P.add("vector",
                  lambda v, o=ha.ap()[:, hc, :], i=c.psF[:, gb, :],
                         sc_=s2t.ap()[:, hc:hc + 1], u=u_sb[hc % 2].ap():
                  v.scalar_tensor_tensor(o, i, sc_, u, ALU.mult, ALU.mult),
                  wait=stw, inc=[(sd, 1)])
            bankG[gb - 2] = sd.n
            ustt[hc % 2] = sd.n
        xt_free[b] = c.s_pe.n
        if "l1ffnA" in DBG and sc8 == 0:
            dbg_dump(c, "xta", xt[b].ap(), [128, 8, SCW], BF16)
            dbg_dump(c, "w1Tc", w1T.ap()[:, :, 0:512], [128, 8, 512], BF16)
            dbg_dump(c, "hasc", ha.ap(), [128, 32, SCW], BF16)
            dbg_dump(c, "s1td", s1t.ap(), [128, 32], F32)
        P.dma("scalar",
              c.haT.rearrange("(a p) s -> p a s", p=128)[:, :, sc8 * SCW:(sc8 + 1) * SCW],
              ha.ap(), c.s_st, wait=[(sd, sd.n)])
        ha_store_done = c.s_st.n

    for t in (w1T, w2T, s1t, s2t, xt[0], xt[1], u_sb[0], u_sb[1], ha):
        sb.freeit(t)
    P.barrier("l1ffnA_done")


def emit_l1ffnBC(c, st_w, scales_in, in_d, out_d, tag):
    P, sb = c.P, c.sb
    hw = [sb.alloc(f"hw{tag}{i}", [128, 32, 1024], BF16) for i in range(2)]
    hst = sb.alloc(f"hst{tag}", [128, 32], F32)
    hin = [sb.alloc(f"hin{tag}{i}", [128, 32, SCW], BF16) for i in range(2)]
    ho = sb.alloc(f"ho{tag}", [128, 8, SCW], BF16)

    P.dma("sync", hst.ap(), scales_in, c.s_ldw)
    in_v = in_d.rearrange("(a p) s -> p a s", p=128)
    out_v = out_d.rearrange("(a p) s -> p a s", p=128)

    ldw_after = {}
    hw_free = [None, None]
    s_bc = [P.sem(f"bc{tag}0"), P.sem(f"bc{tag}1")]

    def load_stripe(s, extra_wait):
        for ic in range(32):
            P.dma("scalar", hw[s % 2].ap()[:, ic, :],
                  st_w[s * 1024:(s + 1) * 1024, ic * 128:(ic + 1) * 128],
                  s_bc[s % 2], wait=extra_wait if ic == 0 else (), transpose=True)
        ldw_after[s] = (s_bc[s % 2], s_bc[s % 2].n)

    load_stripe(0, [(c.s_stg_f1, c.s_stg_f1.n)])
    hin_free = [None, None]
    ho_store_done = 0
    bankB = [0, 0]
    for stripe in range(4):
        if stripe + 1 < 4:
            w = [(c.s_stg_f1, c.s_stg_f1.n)]
            if hw_free[(stripe + 1) % 2] is not None:
                w.append((c.s_pe, hw_free[(stripe + 1) % 2]))
            load_stripe(stripe + 1, w)
        for sc8 in range(NSC):
            b = sc8 % 2
            w = []
            if hin_free[b] is not None:
                w.append((c.s_pe, hin_free[b]))
            P.dma("sync", hin[b].ap(), in_v[:, :, sc8 * SCW:(sc8 + 1) * SCW],
                  c.s_ldx, wait=w)
            ldx_done = c.s_ldx.n
            for oc in range(8):
                bank = oc % 2
                for ic in range(32):
                    wt = []
                    if oc == 0 and ic == 0:
                        wt = [(c.s_ldx, ldx_done), ldw_after[stripe]]
                    if ic == 0:
                        wt.append((c.s_act, bankB[bank]))
                    P.add("tensor",
                          lambda t, o=c.psF[:, bank, :],
                                 l=hw[stripe % 2].ap()[:, ic, oc * 128:(oc + 1) * 128],
                                 r=hin[b].ap()[:, ic, :], st=(ic == 0), sp=(ic == 31):
                          t.matmul(o, l, r, start=st, stop=sp),
                          wait=wt, inc=[(c.s_pe, 1)] if ic == 31 else ())
                gw = [(c.s_pe, c.s_pe.n)]
                if oc == 0:
                    gw.append((c.s_st, ho_store_done))
                P.add("scalar",
                      lambda s, o=ho.ap()[:, oc, :], i=c.psF[:, bank, :],
                             sc_=hst.ap()[:, stripe * 8 + oc:stripe * 8 + oc + 1]:
                      s.activation(o, i, AF.Gelu, scale=sc_),
                      wait=gw, inc=[(c.s_act, 1)])
                bankB[bank] = c.s_act.n
            hin_free[b] = c.s_pe.n
            P.dma("scalar",
                  out_v[:, stripe * 8:(stripe + 1) * 8, sc8 * SCW:(sc8 + 1) * SCW],
                  ho.ap(), c.s_st, wait=[(c.s_act, c.s_act.n)])
            ho_store_done = c.s_st.n
        hw_free[stripe % 2] = c.s_pe.n

    for t in (hw[0], hw[1], hst, hin[0], hin[1], ho):
        sb.freeit(t)
    P.barrier(f"l1ffn{tag}_done")


def emit_l1ffnB(c):
    emit_l1ffnBC(c, c.st_h1, c.h1s, c.haT, c.hbT, "B")


def emit_l1ffnC(c):
    emit_l1ffnBC(c, c.st_h2, c.h2s, c.hbT, c.haT, "C")


def emit_l1ffnD(c):
    P, sb = c.P, c.sb
    w3T = sb.alloc("w3T1", [128, 32, D], BF16)
    s3c = sb.alloc("s3c", [128, D], F32)
    hin = [sb.alloc(f"hinD{i}", [128, 32, SCW], BF16) for i in range(2)]
    x3r = sb.alloc("x3r", [128, D], F32)
    tt = sb.alloc("ttD", [128, D], F32)
    xn = [sb.alloc(f"xnD{i}", [128, D], F32) for i in range(2)]
    stats = sb.alloc("statsD", [128, 12], F32)
    sqs = sb.alloc("sqsD", [128, 512], F32)

    sd = c.s_dve
    wait_stg = ((c.s_stg_f1, c.s_stg_f1.n),)
    load_wT(c, w3T, c.st_w3, 32, wait_stg)
    P.dma("sync", s3c.ap(), c.w3s[None, :].broadcast_to([128, D]), c.s_ldw)
    ldw_done = c.s_ldw.n
    in_v = c.haT.rearrange("(a p) s -> p a s", p=128)

    hin_free = [None, None]
    bankD = 0
    for sc8 in range(NSC):
        b = sc8 % 2
        w = []
        if hin_free[b] is not None:
            w.append((c.s_pe, hin_free[b]))
        P.dma("sync", hin[b].ap(), in_v[:, :, sc8 * SCW:(sc8 + 1) * SCW],
              c.s_ldx, wait=w)
        ldx_done = c.s_ldx.n
        for st4 in range(4):
            row0 = sc8 * SCW + st4 * 128
            P.dma("sync", x3r.ap(), c.x3f[row0:row0 + 128, :], c.s_res,
                  wait=[(sd, max(0, sd.n - 1))])
            for half in range(2):
                bank = 2 + half
                for ic in range(32):
                    wt = []
                    if st4 == 0 and half == 0 and ic == 0:
                        wt = [(c.s_ldx, ldx_done), (c.s_ldw, ldw_done)]
                    if ic == 0:
                        wt.append((sd, bankD))
                    P.add("tensor",
                          lambda t, o=c.psF[:, bank, :], l=hin[b].ap()[:, ic, st4 * 128:(st4 + 1) * 128],
                                 r=w3T.ap()[:, ic, half * 512:(half + 1) * 512],
                                 st=(ic == 0), sp=(ic == 31):
                          t.matmul(o, l, r, start=st, stop=sp),
                          wait=wt, inc=[(c.s_pe, 1)] if ic == 31 else ())
            pe_done = c.s_pe.n
            bx = st4 % 2
            t_ap = tt.ap()
            P.add("vector", lambda v, o=t_ap[:, 0:512], i=c.psF[:, 2, :],
                           s3=s3c.ap()[:, 0:512]:
                  v.tensor_mul(o, i, s3),
                  wait=[(c.s_pe, pe_done), (c.s_res, c.s_res.n),
                        (c.s_act, c.s_act.n)])
            P.add("vector", lambda v, o=t_ap[:, 512:1024], i=c.psF[:, 3, :],
                           s3=s3c.ap()[:, 512:1024]:
                  v.tensor_mul(o, i, s3), inc=[(sd, 1)])
            bankD = sd.n
            P.add("vector", lambda v, o=t_ap, i=x3r.ap():
                  v.tensor_add(o, o, i),
                  wait=[(sd, sd.n)], inc=[(sd, 1)])
            emit_finalize_tile(c, t_ap, sqs.ap(), stats.ap(), xn[bx].ap(),
                               row0, c.y, None)
        hin_free[b] = c.s_pe.n

    for t in (w3T, s3c, hin[0], hin[1], x3r, tt, xn[0], xn[1], stats, sqs):
        sb.freeit(t)
    P.barrier("l1ffnD_done")


# version 07
# speedup vs baseline: 1.1493x; 1.0189x over previous
import math
import numpy as np
import ml_dtypes
import sys
if '/opt/trn_rl_repo' not in sys.path:
    sys.path.insert(0, '/opt/trn_rl_repo')
import concourse.bass as bass
import concourse.mybir as mybir
from concourse.bass_utils import run_bass_kernel_spmd

F32 = mybir.dt.float32
BF16 = mybir.dt.bfloat16
AF = mybir.ActivationFunctionType
ALU = mybir.AluOpType
AX = mybir.AxisListType

S = 4096
D = 1024
HFF = 4096
NH = 16
DK = 64
NKV = 4
SCW = 512
NSC = S // SCW
NST = S // 128
EPS = 1e-5
CAP = 50.0
THETA = 1e4

ENGINES = ("tensor", "vector", "scalar", "gpsimd", "sync")

PERM1 = [0, 4, 1, 5, 2, 6, 3, 7, 8, 12, 9, 13, 10, 14, 11, 15]


class Sem:
    __slots__ = ("h", "n", "name")
    def __init__(self, h, name):
        self.h = h
        self.n = 0
        self.name = name


class Planner:
    def __init__(self, nc):
        self.nc = nc
        self.items = {e: [] for e in ENGINES}
        self.pending = {e: [] for e in ENGINES}
        self.bar_idx = {e: 0 for e in ENGINES}
        self.nsem = 0

    def sem(self, name=None):
        name = name or f"s{self.nsem}"
        self.nsem += 1
        h = self.nc.alloc_semaphore(name)
        return Sem(h, name)

    def add(self, e, fn=None, wait=(), inc=(), is_dma=False):
        waits = list(self.pending[e]) + list(wait)
        self.pending[e] = []
        for s, n in inc:
            s.n += n
        self.items[e].append({"waits": waits, "fn": fn, "incs": list(inc),
                              "is_dma": is_dma})

    def dma(self, e, out, in_, sem, wait=(), transpose=False, inc=()):
        if transpose:
            fn = lambda eng, o=out, i=in_: eng.dma_start_transpose(out=o, in_=i)
        else:
            fn = lambda eng, o=out, i=in_: eng.dma_start(out=o, in_=i)
        self.add(e, fn, wait, [(sem, 16), *inc], is_dma=True)

    def barrier(self, name="bar"):
        b = self.sem(name)
        total = 0
        for e in ENGINES:
            its = self.items[e]
            if len(its) > self.bar_idx[e]:
                last = its[-1]
                n = 16 if (last["is_dma"] and last["fn"] is not None) else 1
                if last["fn"] is not None and not last["incs"]:
                    last["incs"].append((b, n))
                else:
                    self.items[e].append({"waits": [], "fn": None,
                                          "incs": [(b, 1)], "is_dma": False})
                    n = 1
                b.n += n
                total += n
            self.bar_idx[e] = len(self.items[e])
        for e in ENGINES:
            self.pending[e].append((b, total))
        return b

    def replay(self, ename, eng):
        for it in self.items[ename]:
            for (s, v) in it["waits"]:
                eng.wait_ge(s.h, v)
            ins = it["fn"](eng) if it["fn"] is not None else None
            incs = it["incs"]
            if ins is not None and incs:
                ins.then_inc(incs[0][0].h, incs[0][1])
                for s, n in incs[1:]:
                    eng.sem_inc(s.h, n)
            else:
                for s, n in incs:
                    eng.sem_inc(s.h, n)


class _SBWrap:
    __slots__ = ("t", "_sb_range")
    def __init__(self, t, rng):
        self.t = t
        self._sb_range = rng
    def ap(self):
        return self.t.ap()


class SB:
    def __init__(self, nc, base=16640, top=229376):
        self.nc = nc
        self.free = [(base, top)]
        self.cnt = 0

    @staticmethod
    def _sz(dt):
        return {F32: 4, BF16: 2, mybir.dt.int32: 4, mybir.dt.uint8: 1}[dt]

    def alloc(self, name, shape, dtype, align=128):
        bpp = int(np.prod(shape[1:])) * self._sz(dtype)
        for i, (a, b) in enumerate(self.free):
            start = (a + align - 1) // align * align
            if start + bpp <= b:
                pre = (a, start)
                post = (start + bpp, b)
                repl = []
                if pre[1] > pre[0]:
                    repl.append(pre)
                if post[1] > post[0]:
                    repl.append(post)
                self.free[i:i + 1] = repl
                self.cnt += 1
                t = self.nc.alloc_sbuf_tensor_at(f"{name}_{self.cnt}", list(shape),
                                                 dtype, offset=start)
                return _SBWrap(t, (start, start + bpp))
        raise MemoryError(f"SBUF alloc failed for {name} {shape} ({bpp} B/part); "
                          f"free={self.free}")

    def freeit(self, t):
        a, b = t._sb_range
        self.free.append((a, b))
        self.free.sort()
        merged = []
        for seg in self.free:
            if merged and seg[0] <= merged[-1][1]:
                merged[-1] = (merged[-1][0], max(merged[-1][1], seg[1]))
            else:
                merged.append(seg)
        self.free = merged


_CONST_CACHE = {}

def host_consts():
    if _CONST_CACHE:
        return _CONST_CACHE
    inv = 1.0 / THETA ** (np.arange(0, DK, 2, dtype=np.float64) / DK)
    freqs = np.arange(S, dtype=np.float64)[:, None] * inv[None, :]
    _CONST_CACHE["cos"] = np.cos(freqs).astype(np.float32)
    _CONST_CACHE["sin"] = np.sin(freqs).astype(np.float32)
    jk = np.arange(128)[:, None]
    jq = np.arange(128)[None, :]
    _CONST_CACHE["tri"] = (jq >= jk).astype(ml_dtypes.bfloat16)
    _CONST_CACHE["eye"] = np.eye(128, dtype=ml_dtypes.bfloat16)
    return _CONST_CACHE


class Ctx:
    pass


def make_ctx(nc, taps=False):
    c = Ctx()
    c.nc = nc
    c.P = Planner(nc)
    c.sb = SB(nc)
    c.taps = taps
    T = lambda name, shape, dt, kind: nc.dram_tensor(name, shape, dt, kind=kind).ap()
    I = lambda name, shape, dt=F32: T(name, shape, dt, "ExternalInput")
    scratch_kind = "ExternalOutput" if taps else "Internal"
    N = lambda name, shape, dt: T(name, shape, dt, scratch_kind)

    c.x = I("x", [S, D])
    c.wq0 = I("wq0", [D, D]); c.wk0 = I("wk0", [D, D])
    c.wv0 = I("wv0", [D, D]); c.wo0 = I("wo0", [D, D])
    c.bo0 = I("bo0", [D])
    c.W1 = I("W1", [HFF, D]); c.b1 = I("b1", [128, HFF // 128])
    c.W2 = I("W2", [D, HFF]); c.b2 = I("b2", [D])
    c.scale0 = I("scale0", [1, 1])
    c.wq1 = I("wq1", [D, D]); c.wk1 = I("wk1", [D, 256])
    c.wv1 = I("wv1", [D, 256]); c.wo1 = I("wo1", [D, D])
    c.w1w = I("w1w", [HFF, D]); c.w1s = I("w1s", [128, 32])
    c.w2w = I("w2w", [HFF, D]); c.w2s = I("w2s", [128, 32])
    c.h1w = I("h1w", [HFF, HFF]); c.h1s = I("h1s", [128, 32])
    c.h2w = I("h2w", [HFF, HFF]); c.h2s = I("h2s", [128, 32])
    c.w3w = I("w3w", [D, HFF]); c.w3s = I("w3s", [D])
    c.cos = I("cos", [S, 32]); c.sin = I("sin", [S, 32])
    c.tri = I("tri", [128, 128], BF16)
    c.eye = I("eye", [128, 128], BF16)

    c.y = T("y", [S, D], F32, "ExternalOutput")

    ST = lambda name, shape: T(name, shape, BF16, "Internal")
    c.st_wq0 = ST("st_wq0", [D, D]); c.st_wk0 = ST("st_wk0", [D, D])
    c.st_wv0 = ST("st_wv0", [D, D]); c.st_wo0 = ST("st_wo0", [D, D])
    c.st_W1 = ST("st_W1", [HFF, D]); c.st_W2 = ST("st_W2", [D, HFF])
    c.st_w1w = ST("st_w1w", [HFF, D]); c.st_w2w = ST("st_w2w", [HFF, D])
    c.st_h1 = ST("st_h1", [HFF, HFF]); c.st_h2 = ST("st_h2", [HFF, HFF])
    c.st_w3 = ST("st_w3", [D, HFF])

    c.x0b = T("x0b", [S, D], BF16, "Internal")
    c.x1f = N("x1f", [S, D], F32); c.x1b = T("x1b", [S, D], BF16, "Internal")
    c.x2f = N("x2f", [S, D], F32); c.x2b = T("x2b", [S, D], BF16, "Internal")
    c.x3f = N("x3f", [S, D], F32); c.x3b = T("x3b", [S, D], BF16, "Internal")
    c.haT = T("haT", [HFF, S], BF16, scratch_kind)
    c.hbT = T("hbT", [HFF, S], BF16, scratch_kind)

    P = c.P
    c.s_stg_x = P.sem("stg_x")
    c.s_stg_a0 = P.sem("stg_a0")
    c.s_stg_f0 = P.sem("stg_f0")
    c.s_stg_f1 = P.sem("stg_f1")
    c.s_ldw = P.sem("ldw")
    c.s_ldx = P.sem("ldx")
    c.s_res = P.sem("res")
    c.s_st = P.sem("st")
    c.s_stg_o = P.sem("stg_o")
    c.s_pe = P.sem("pe")
    c.s_act = P.sem("act")
    c.s_dve = P.sem("dve")

    c.psF_h = nc.alloc_psum_tensor("psF", [128, 7, 512], F32)
    c.psT_h = nc.alloc_psum_tensor("psT", [128, 1, 1024], BF16)
    c.psF = c.psF_h.ap()
    c.psT = c.psT_h.ap()
    return c


def tview(ap):
    return ap.rearrange("(a p) d -> p a d", p=128)


def emit_stage(c, groups=("x", "a0", "f0", "f1")):
    P = c.P
    if "x" in groups:
        P.dma("gpsimd", c.x0b, c.x, c.s_stg_x)
    if "a0" in groups:
        for dst, src in ((c.st_wq0, c.wq0), (c.st_wk0, c.wk0),
                         (c.st_wv0, c.wv0), (c.st_wo0, c.wo0)):
            P.dma("gpsimd", dst, src, c.s_stg_a0)
    if "f0" in groups:
        P.dma("gpsimd", c.st_W1, c.W1, c.s_stg_f0)
        P.dma("gpsimd", c.st_W2, c.W2, c.s_stg_f0)
    if "f1" in groups:
        for dst, src in ((c.st_w1w, c.w1w), (c.st_w2w, c.w2w), (c.st_w3, c.w3w)):
            P.dma("gpsimd", dst, src, c.s_stg_f1)
        for dst, src in ((c.st_h1, c.h1w), (c.st_h2, c.h2w)):
            for r in range(4):
                P.dma("gpsimd", dst[r * 1024:(r + 1) * 1024, :],
                      src[r * 1024:(r + 1) * 1024, :], c.s_stg_f1)


def load_wT(c, dst_sb, st_ap, nch, wait):
    P = c.P
    for dc in range(nch):
        P.dma("sync", dst_sb.ap()[:, dc, :], st_ap[:, dc * 128:(dc + 1) * 128],
              c.s_ldw, wait=wait if dc == 0 else (), transpose=True)


def load_xT(c, dst_ap, src_bf, s0, width, wait):
    P = c.P
    for dc in range(8):
        P.dma("sync", dst_ap[:, dc, :], src_bf[s0:s0 + width, dc * 128:(dc + 1) * 128],
              c.s_ldx, wait=wait if dc == 0 else (), transpose=True)


def emit_l0kv(c):
    P, sb = c.P, c.sb
    c.kT = sb.alloc("kT", [128, 8, S], BF16)
    c.v0 = sb.alloc("v0", [128, NST, NH, 65], BF16)
    wkT = sb.alloc("wkT", [128, 8, D], BF16)
    wvT = sb.alloc("wvT", [128, 8, D], BF16)
    xt = [sb.alloc(f"xt{i}", [128, 8, SCW], BF16) for i in range(2)]

    wait_stg = ((c.s_stg_a0, c.s_stg_a0.n),)
    load_wT(c, wkT, c.st_wk0, 8, wait_stg)
    load_wT(c, wvT, c.st_wv0, 8, wait_stg)
    ldw_done = c.s_ldw.n

    P.add("vector", lambda v: v.memset(c.v0.ap()[:, :, :, 64:65], 1.0),
          inc=[(c.s_dve, 1)])

    xt_free_at = [None, None]
    bankK = [0, 0]
    bankV = [0, 0]
    for sc in range(NSC):
        b = sc % 2
        w = [(c.s_stg_x, c.s_stg_x.n)] if sc == 0 else ()
        if xt_free_at[b] is not None:
            w = ((c.s_pe, xt_free_at[b]),)
        load_xT(c, xt[b].ap(), c.x0b, sc * SCW, SCW, w)
        ldx_done = c.s_ldx.n

        first_wait = [(c.s_ldx, ldx_done), (c.s_ldw, ldw_done)]
        for ec in range(8):
            bank = ec % 2
            for dc in range(8):
                wt = list(first_wait) if (ec == 0 and dc == 0) else []
                if dc == 0:
                    wt.append((c.s_act, bankK[bank]))
                P.add("tensor",
                      lambda t, o=c.psF[:, bank, :], l=wkT.ap()[:, dc, ec * 128:(ec + 1) * 128],
                             r=xt[b].ap()[:, dc, :], st=(dc == 0), sp=(dc == 7):
                      t.matmul(o, l, r, start=st, stop=sp),
                      wait=wt, inc=[(c.s_pe, 1)] if dc == 7 else ())
            P.add("scalar",
                  lambda s, o=c.kT.ap()[:, ec, sc * SCW:(sc + 1) * SCW], i=c.psF[:, bank, :]:
                  s.activation(o, i, AF.Copy),
                  wait=[(c.s_pe, c.s_pe.n)], inc=[(c.s_act, 1)])
            bankK[bank] = c.s_act.n
        for st4 in range(4):
            for half in range(2):
                bank = 2 + (st4 * 2 + half) % 2
                for dc in range(8):
                    wt = []
                    if dc == 0:
                        wt.append((c.s_dve, bankV[bank - 2]))
                    P.add("tensor",
                          lambda t, o=c.psF[:, bank, :],
                                 l=xt[b].ap()[:, dc, st4 * 128:(st4 + 1) * 128],
                                 r=wvT.ap()[:, dc, half * 512:(half + 1) * 512],
                                 st=(dc == 0), sp=(dc == 7):
                          t.matmul(o, l, r, start=st, stop=sp),
                          wait=wt, inc=[(c.s_pe, 1)] if dc == 7 else ())
                src = c.psF[:, bank, :].rearrange("p (h d) -> p h d", d=64)
                dst = c.v0.ap()[:, sc * 4 + st4, half * 8:(half + 1) * 8, 0:64]
                P.add("vector",
                      lambda v, o=dst, i=src: v.tensor_copy(o, i),
                      wait=[(c.s_pe, c.s_pe.n)], inc=[(c.s_dve, 1)])
                bankV[bank - 2] = c.s_dve.n
        xt_free_at[b] = c.s_pe.n

    if "l0kv" in DBG:
        dbg_dump(c, "kT", c.kT.ap(), [128, 8, S], BF16)
        dbg_dump(c, "v0", c.v0.ap(), [128, NST, NH, 65], BF16)
    for t in (wkT, wvT, xt[0], xt[1]):
        sb.freeit(t)
    P.barrier("l0kv_done")

def emit_finalize_tile(c, t_ap, scratch, stats, xn, row0, outf, outb,
                       xn_single=False):
    P = c.P
    sumsq = stats[:, 0:1]; sumsq2 = stats[:, 1:2]; ssum = stats[:, 2:3]
    m = stats[:, 3:4]; e2 = stats[:, 4:5]; veps = stats[:, 5:6]
    rstd = stats[:, 6:7]; nmr = stats[:, 7:8]; lnv = stats[:, 8:9]
    sqw = [(c.s_dve, c.s_dve.n)]
    P.add("scalar", lambda s: s.activation(scratch, t_ap[:, 0:512], AF.Square,
                                           accum_out=sumsq),
          wait=sqw)
    P.add("scalar", lambda s: s.activation(scratch, t_ap[:, 512:1024], AF.Square,
                                           accum_out=sumsq2),
          inc=[(c.s_act, 1)])
    sd = c.s_dve
    P.add("vector", lambda v: v.reduce_sum(ssum, t_ap, axis=AX.X), inc=[(sd, 1)])
    P.add("vector", lambda v: v.tensor_scalar(m, ssum, 1.0 / D, None, ALU.mult),
          wait=[(sd, sd.n)], inc=[(sd, 1)])
    P.add("vector", lambda v: v.tensor_add(sumsq, sumsq, sumsq2),
          wait=[(c.s_act, c.s_act.n)], inc=[(sd, 1)])
    P.add("vector", lambda v: v.tensor_scalar(e2, sumsq, 1.0 / D, EPS, ALU.mult, ALU.add),
          wait=[(sd, sd.n)], inc=[(sd, 1)])
    P.add("vector", lambda v: v.scalar_tensor_tensor(veps, m, -1.0, m, ALU.mult, ALU.mult),
          wait=[(sd, sd.n)], inc=[(sd, 1)])
    P.add("vector", lambda v: v.tensor_add(veps, veps, e2),
          wait=[(sd, sd.n)], inc=[(sd, 1)])
    P.add("scalar", lambda s: s.activation(lnv, veps, AF.Ln),
          wait=[(sd, sd.n)])
    P.add("scalar", lambda s: s.activation(rstd, lnv, AF.Exp, scale=-0.5),
          inc=[(c.s_act, 1)])
    P.add("vector", lambda v: v.scalar_tensor_tensor(nmr, m, -1.0, rstd, ALU.mult, ALU.mult),
          wait=[(c.s_act, c.s_act.n)], inc=[(sd, 1)])
    back = 0 if xn_single else 16
    aw = [(sd, sd.n), (c.s_st, max(0, c.s_st.n - back))]
    if outb is not None:
        aw.append((c.s_stg_o, max(0, c.s_stg_o.n - back)))
    P.add("vector", lambda v: v.tensor_scalar(xn, t_ap, rstd, nmr, ALU.mult, ALU.add),
          wait=aw, inc=[(sd, 1)])
    w = [(c.s_dve, c.s_dve.n)]
    if "fin" in DBG and getattr(c, "_fin_n", 0) == 0:
        c._fin_n = 1
        dbg_dump(c, "stats", stats, [128, 12], F32)
        dbg_dump(c, "xn", xn, [128, D], F32)
    P.dma("scalar", outf[row0:row0 + 128, :], xn, c.s_st, wait=w)
    if outb is not None:
        P.dma("gpsimd", outb[row0:row0 + 128, :], xn, c.s_stg_o, wait=w)


def emit_l0attn(c):
    P, sb = c.P, c.sb
    wqT = sb.alloc("wqT", [128, 8, D], BF16)
    woT = sb.alloc("woT", [128, 8, D], BF16)
    bocast = sb.alloc("bocast", [128, D], BF16)
    ones64 = sb.alloc("ones64", [128, 64], F32)
    qT = sb.alloc("qT", [128, 8, SCW], BF16)
    xtq = sb.alloc("xtq", [128, 8, SCW], BF16)
    Pp = sb.alloc("Pp", [128, 4, SCW], BF16)
    ynT = sb.alloc("ynT", [128, 8, SCW], BF16)
    stat = sb.alloc("stat", [128, SCW], F32)
    x0r = sb.alloc("x0r", [128, D], F32)
    tt = sb.alloc("tt", [128, D], F32)
    xn1b = sb.alloc("xn", [128, D], F32)
    xn = [xn1b, xn1b]
    stats = sb.alloc("stats", [128, 12], F32)

    wait_stg = ((c.s_stg_a0, c.s_stg_a0.n),)
    load_wT(c, wqT, c.st_wq0, 8, wait_stg)
    load_wT(c, woT, c.st_wo0, 8, wait_stg)
    P.dma("gpsimd", bocast.ap(), c.bo0[None, :].broadcast_to([128, D]), c.s_ldw)
    ldw_done = c.s_ldw.n
    P.add("vector", lambda v: v.memset(ones64.ap(), 1.0), inc=[(c.s_dve, 1)])

    xtq_free = None
    bankQ = [0, 0]
    bankS = [0, 0]
    last_adds = 0
    for qc in range(NSC):
        w = ((c.s_stg_x, c.s_stg_x.n),) if qc == 0 else ((c.s_pe, xtq_free),)
        load_xT(c, xtq.ap(), c.x0b, qc * SCW, SCW, w)
        ldx_done = c.s_ldx.n
        for ec in range(8):
            qb = 4 + ec % 2
            for dc in range(8):
                wt = []
                if ec == 0 and dc == 0:
                    wt = [(c.s_ldx, ldx_done), (c.s_ldw, ldw_done),
                          (c.s_act, getattr(c, "_exps0", 0))]
                if dc == 0:
                    wt.append((c.s_act, bankQ[ec % 2]))
                P.add("tensor",
                      lambda t, o=c.psF[:, qb, :], l=wqT.ap()[:, dc, ec * 128:(ec + 1) * 128],
                             r=xtq.ap()[:, dc, :], st=(dc == 0), sp=(dc == 7):
                      t.matmul(o, l, r, start=st, stop=sp),
                      wait=wt, inc=[(c.s_pe, 1)] if dc == 7 else ())
            P.add("scalar",
                  lambda s, o=qT.ap()[:, ec, :], i=c.psF[:, qb, :]:
                  s.activation(o, i, AF.Copy),
                  wait=[(c.s_pe, c.s_pe.n)], inc=[(c.s_act, 1)])
            bankQ[ec % 2] = c.s_act.n
        xtq_free = c.s_pe.n
        qT_ready = c.s_act.n

        ynd = getattr(c, "_yn0", None)
        if ynd is None:
            ynd = {0: 0, 1: 0}
            c._yn0 = ynd
            c._pb0 = 0

        def emit_norm0(avb_, p0_, chunk_, av_stop, act_after):
            P.add("vector", lambda v, o=stat.ap()[64:65, :], i=c.psF[64:65, avb_, :]:
                  v.reciprocal(o, i),
                  wait=[(c.s_pe, av_stop), (c.s_act, act_after)], inc=[(c.s_dve, 1)])
            P.add("tensor", lambda t, o=c.psF[0:64, 6, :], l=ones64.ap()[64:65, 0:64],
                           r=stat.ap()[64:65, :]:
                  t.matmul(o, l, r, start=True, stop=True),
                  wait=[(c.s_dve, c.s_dve.n), (c.s_dve, c._pb0)],
                  inc=[(c.s_pe, 1)])
            P.add("vector", lambda v, o=stat.ap()[0:64, :], i=c.psF[0:64, 6, :]:
                  v.tensor_copy(o, i),
                  wait=[(c.s_pe, c.s_pe.n)], inc=[(c.s_dve, 1)])
            c._pb0 = c.s_dve.n
            P.add("vector", lambda v, o=ynT.ap()[p0_:p0_ + 64, chunk_, :],
                           a=c.psF[0:64, avb_, :], b2=stat.ap()[0:64, :]:
                  v.tensor_mul(o, a, b2),
                  wait=[(c.s_dve, c.s_dve.n)], inc=[(c.s_dve, 1)])
            ynd[avb_ - 2] = c.s_dve.n

        SBPAIR = ((0, 1), (4, 5))
        pending_norms = []
        exps_done = 0
        for hp in range(8):
            av_q = []
            for kt in range(NST):
                pair = SBPAIR[kt % 2]
                for hpar in range(2):
                    p0 = hpar * 64
                    sbank = pair[hpar]
                    wt = []
                    if kt == 0 and hp == 0 and hpar == 0:
                        wt.append((c.s_act, qT_ready))
                        wt.append((c.s_dve, last_adds))
                    if hpar == 0:
                        wt.append((c.s_act, bankS[kt % 2]))
                    P.add("tensor",
                          lambda t, o=c.psF[:, sbank, :],
                                 l=c.kT.ap()[p0:p0 + 64, hp, kt * 128:(kt + 1) * 128],
                                 r=qT.ap()[p0:p0 + 64, hp, :]:
                          t.matmul(o, l, r, start=True, stop=True),
                          wait=wt, inc=[(c.s_pe, 1)])
                P.add("scalar",
                      lambda s, o=Pp.ap()[:, 2 * (kt % 2):2 * (kt % 2) + 2, :].rearrange("p a b -> p (a b)"),
                             i=c.psF[:, pair[0]:pair[0] + 2, :].rearrange("p a b -> p (a b)"):
                      s.activation(o, i, AF.Exp, scale=0.125),
                      wait=[(c.s_pe, c.s_pe.n)], inc=[(c.s_act, 1)])
                bankS[kt % 2] = c.s_act.n
                exp_done = c.s_act.n
                if kt == 1 and pending_norms:
                    for pn in pending_norms:
                        emit_norm0(*pn)
                    pending_norms = []
                for hpar in range(2):
                    avb = 2 + hpar
                    avw = [(c.s_act, exp_done)]
                    av_q.append((
                        (lambda t, o=c.psF[0:65, avb, :],
                                l=c.v0.ap()[:, kt, 2 * hp + hpar, :],
                                r=Pp.ap()[:, 2 * (kt % 2) + hpar, :],
                                st=(kt == 0), sp=(kt == NST - 1):
                         t.matmul(o, l, r, start=st, stop=sp)),
                        avw, [(c.s_pe, 1)] if kt == NST - 1 else (),
                        hpar if kt == 0 else None))
                while len(av_q) > 2:
                    a = av_q.pop(0)
                    w = list(a[1])
                    if a[3] is not None:
                        w.append((c.s_dve, ynd[a[3]]))
                    P.add("tensor", a[0], wait=w, inc=a[2])
            for a in av_q:
                w = list(a[1])
                if a[3] is not None:
                    w.append((c.s_dve, ynd[a[3]]))
                P.add("tensor", a[0], wait=w, inc=a[2])
            exps_done = c.s_act.n
            pending_norms = [(2, 0, hp, c.s_pe.n, c.s_act.n),
                             (3, 64, hp, c.s_pe.n, c.s_act.n)]
        for pn in pending_norms:
            emit_norm0(*pn)
        pending_norms = []
        c._exps0 = exps_done
        heads_done = c.s_dve.n

        for st4 in range(4):
            row0 = qc * SCW + st4 * 128
            P.dma("sync", x0r.ap(), c.x[row0:row0 + 128, :], c.s_res,
                  wait=[(c.s_dve, max(0, c.s_dve.n - 1))])
            for half in range(2):
                bank = half
                for hc in range(8):
                    wt = []
                    if hc == 0:
                        wt.append((c.s_dve, heads_done))
                        wt.append((c.s_dve, max(0, c.s_dve.n - 2)))
                    P.add("tensor",
                          lambda t, o=c.psF[:, bank, :],
                                 l=ynT.ap()[:, hc, st4 * 128:(st4 + 1) * 128],
                                 r=woT.ap()[:, hc, half * 512:(half + 1) * 512],
                                 st=(hc == 0), sp=(hc == 7):
                          t.matmul(o, l, r, start=st, stop=sp),
                          wait=wt, inc=[(c.s_pe, 1)] if hc == 7 else ())
            pe_done = c.s_pe.n
            b = st4 % 2
            t_ap = tt.ap()
            P.add("vector", lambda v, o=t_ap[:, 0:512], i=c.psF[:, 0, :],
                           r=x0r.ap()[:, 0:512]:
                  v.tensor_add(o, i, r),
                  wait=[(c.s_pe, pe_done), (c.s_res, c.s_res.n),
                        (c.s_act, c.s_act.n)])
            P.add("vector", lambda v, o=t_ap[:, 512:1024], i=c.psF[:, 1, :],
                           r=x0r.ap()[:, 512:1024]:
                  v.tensor_add(o, i, r), inc=[(c.s_dve, 1)])
            bankO = c.s_dve.n
            last_adds = c.s_dve.n
            P.add("vector", lambda v, o=t_ap, i=bocast.ap():
                  v.tensor_add(o, o, i),
                  wait=[(c.s_dve, c.s_dve.n)], inc=[(c.s_dve, 1)])
            if "l0attn" in DBG and qc == 0 and st4 == 0:
                dbg_dump(c, "t0", t_ap, [128, D], F32)
            emit_finalize_tile(c, t_ap, stat.ap(), stats.ap(), xn[b].ap(),
                               row0, c.x1f, c.x1b, xn_single=True)
        if "l0attn" in DBG and qc == 0:
            dbg_dump(c, "qT", qT.ap(), [128, 8, SCW], BF16)
            dbg_dump(c, "ynT", ynT.ap(), [128, 8, SCW], BF16)

    for t in (wqT, woT, bocast, ones64, qT, xtq, Pp, ynT, stat, x0r,
              tt, xn1b, stats, c.kT, c.v0):
        sb.freeit(t)
    P.barrier("l0attn_done")


def emit_finish(c):
    P = c.P
    P.add("scalar", wait=[(c.s_st, c.s_st.n)])
    P.add("gpsimd", wait=[(c.s_stg_o, c.s_stg_o.n)])
    P.add("sync", wait=[(c.s_st, c.s_st.n), (c.s_stg_o, c.s_stg_o.n)])


def build_nc(phases, taps=False):
    nc = bass.Bass("TRN2", target_bir_lowering=False, debug=False)
    c = make_ctx(nc, taps=taps)
    for ph in phases:
        ph(c)
    emit_finish(c)
    with nc.Block() as block:
        @block.tensor
        def _(eng):
            c.P.replay("tensor", eng)
        @block.vector
        def _(eng):
            c.P.replay("vector", eng)
        @block.scalar
        def _(eng):
            c.P.replay("scalar", eng)
        @block.gpsimd
        def _(eng):
            c.P.replay("gpsimd", eng)
        @block.sync
        def _(eng):
            c.P.replay("sync", eng)
    return nc, c


def make_inmap(inputs, core):
    hc = host_consts()
    a = lambda k: np.ascontiguousarray(inputs[k])
    rs = lambda v: np.ascontiguousarray(np.asarray(v).reshape(-1, 128).T)
    m = dict(
        x=np.ascontiguousarray(inputs["L0_attn_x"][core]),
        wq0=a("L0_attn_w_q"), wk0=a("L0_attn_w_k"), wv0=a("L0_attn_w_v"),
        wo0=a("L0_attn_w_o"), bo0=a("L0_attn_b_o"),
        W1=a("L0_ffn_W1"),
        b1=np.ascontiguousarray(np.asarray(inputs["L0_ffn_b1"]).reshape(32, 128).T),
        W2=a("L0_ffn_W2"), b2=a("L0_ffn_b2"),
        scale0=np.ascontiguousarray(inputs["L0_ffn_scale"]).reshape(1, 1),
        wq1=a("L1_attn_w_q"), wk1=a("L1_attn_w_k"), wv1=a("L1_attn_w_v"),
        wo1=a("L1_attn_w_o"),
        w1w=a("L1_ffn_w1_w"), w1s=rs(inputs["L1_ffn_w1_s"]),
        w2w=a("L1_ffn_w2_w"), w2s=rs(inputs["L1_ffn_w2_s"]),
        h1w=a("L1_ffn_h1_w"), h1s=rs(inputs["L1_ffn_h1_s"]),
        h2w=a("L1_ffn_h2_w"), h2s=rs(inputs["L1_ffn_h2_s"]),
        w3w=a("L1_ffn_w3_w"), w3s=a("L1_ffn_w3_s"),
        cos=hc["cos"], sin=hc["sin"], tri=hc["tri"], eye=hc["eye"],
    )
    return m


DBG = set()


def dbg_dump(c, name, ap, shape, dtype):
    t = c.nc.dram_tensor("dbg_" + name, list(shape), dtype, kind="ExternalOutput").ap()
    P = c.P
    P.dma("sync", t, ap, c.s_ldx,
          wait=[(c.s_pe, c.s_pe.n), (c.s_act, c.s_act.n), (c.s_dve, c.s_dve.n)])
    for e in ENGINES:
        P.pending[e].append((c.s_ldx, c.s_ldx.n))


def kernel(**inputs):
    phases = [emit_stage, emit_l0kv, emit_l0attn, emit_l0ffnw, emit_l0ffn,
              emit_l1prep, emit_l1attn, emit_l1ffnA, emit_l1ffnB, emit_l1ffnC,
              emit_l1ffnD]
    nc, c = build_nc(phases, taps=False)
    in_maps = [make_inmap(inputs, core) for core in range(8)]
    res = run_bass_kernel_spmd(nc, in_maps, core_ids=list(range(8)))
    out = np.stack([res.results[i]["y"] for i in range(8)], axis=0)
    return out.astype(np.float32)


def emit_l0ffnw(c):
    P, sb = c.P, c.sb
    c.W1T = sb.alloc("W1T", [128, 8, HFF], BF16)
    c.W2T = sb.alloc("W2T", [128, 32, D], BF16)
    c.b1t = sb.alloc("b1t", [128, 32], F32)
    c.b2cast = sb.alloc("b2cast", [128, D], F32)
    xre = sb.alloc("xre", [1, D], F32)
    xro = sb.alloc("xro", [1, D], F32)
    obf = sb.alloc("obf", [1, D], BF16)
    sev = sb.alloc("sev", [1, D], BF16)
    ev4 = sb.alloc("ev4", [1, 256], BF16)
    scs = sb.alloc("scs", [1, 1], F32)

    wait_stg = ((c.s_stg_f0, c.s_stg_f0.n),)
    load_wT(c, c.W1T, c.st_W1, 8, wait_stg)
    load_wT(c, c.W2T, c.st_W2, 32, wait_stg)
    ldw_done = c.s_ldw.n
    P.dma("sync", c.b1t.ap(), c.b1, c.s_ldw)
    P.dma("sync", c.b2cast.ap(), c.b2[None, :].broadcast_to([128, D]), c.s_ldw)
    P.dma("sync", scs.ap(), c.scale0, c.s_ldw)
    sd = c.s_dve

    def rows_prep(tok_e, tok_o):
        P.dma("sync", xre.ap(), c.x1f[tok_e:tok_e + 1, :], c.s_res,
              wait=[(sd, sd.n)])
        P.dma("sync", xro.ap(), c.x1f[tok_o:tok_o + 1, :], c.s_res)
        P.add("vector", lambda v: v.tensor_copy(obf.ap(), xro.ap()),
              wait=[(c.s_res, c.s_res.n), (c.s_ldw, ldw_done + 48),
                    (c.s_pe, c.s_pe.n)], inc=[(sd, 1)])
        P.add("vector", lambda v: v.tensor_scalar(sev.ap(), xre.ap(),
                                                  scs.ap()[0:1, 0:1], None, ALU.mult),
              inc=[(sd, 1)])

    for p in range(4):
        rows_prep(1 + 2 * p, 2 + 2 * p)
        rows_ready = sd.n
        for dc in range(8):
            for half in range(2):
                bank = (dc * 2 + half) % 2
                wt = [(sd, rows_ready)]
                wt.append((sd, max(0, sd.n - 1)))
                P.add("tensor",
                      lambda t, o=c.psF[:, bank, :], l=obf.ap()[0:1, dc * 128:(dc + 1) * 128],
                             r=sev.ap()[0:1, half * 512:(half + 1) * 512]:
                      t.matmul(o, l, r, start=True, stop=True),
                      wait=wt, inc=[(c.s_pe, 1)])
                dst = c.W1T.ap()[:, dc, p * 1024 + half * 512: p * 1024 + (half + 1) * 512]
                P.add("vector",
                      lambda v, o=dst, i=c.psF[:, bank, :]:
                      v.tensor_add(o, o, i),
                      wait=[(c.s_pe, c.s_pe.n)], inc=[(sd, 1)])
    for p in range(4):
        rows_prep(9 + 2 * p, 10 + 2 * p)
        for k in range(4):
            P.add("vector",
                  lambda v, o=ev4.ap(), i=sev.ap().rearrange("a (q f) -> a q f", f=4)[0:1, :, k]:
                  v.tensor_copy(o, i),
                  wait=[(sd, sd.n), (c.s_pe, c.s_pe.n)], inc=[(sd, 1)])
            ev_ready = sd.n
            for jc in range(8):
                bank = jc % 2
                wt = [(sd, ev_ready), (sd, max(0, sd.n - 1))]
                P.add("tensor",
                      lambda t, o=c.psF[:, bank, 0:256], l=obf.ap()[0:1, jc * 128:(jc + 1) * 128],
                             r=ev4.ap():
                      t.matmul(o, l, r, start=True, stop=True),
                      wait=wt, inc=[(c.s_pe, 1)])
                dst = c.W2T.ap()[:, 8 * k + jc, 256 * p: 256 * (p + 1)]
                P.add("vector",
                      lambda v, o=dst, i=c.psF[:, bank, 0:256]:
                      v.tensor_add(o, o, i),
                      wait=[(c.s_pe, c.s_pe.n)], inc=[(sd, 1)])

    for t in (xre, xro, obf, sev, ev4, scs):
        sb.freeit(t)
    P.barrier("l0ffnw_done")


def emit_l0ffn(c):
    P, sb = c.P, c.sb
    h_sb = sb.alloc("h0", [128, 32, SCW], BF16)
    xt = [sb.alloc(f"xtf{i}", [128, 8, SCW], BF16) for i in range(2)]
    x1r = sb.alloc("x1r", [128, D], F32)
    tt = sb.alloc("ttf", [128, D], F32)
    xn = [sb.alloc(f"xnf{i}", [128, D], F32) for i in range(2)]
    stats = sb.alloc("statsf", [128, 12], F32)
    sqs = sb.alloc("sqsf", [128, 512], F32)

    xt_free = [None, None]
    w2_done = None
    bankH = [0, 0]
    bankO2 = 0
    for sc8 in range(NSC):
        b = sc8 % 2
        w = [] if xt_free[b] is None else [(c.s_pe, xt_free[b])]
        load_xT(c, xt[b].ap(), c.x1b, sc8 * SCW, SCW, w)
        ldx_done = c.s_ldx.n
        for hc in range(32):
            bank = hc % 2
            for dc in range(8):
                wt = []
                if hc == 0 and dc == 0:
                    wt.append((c.s_ldx, ldx_done))
                if dc == 0:
                    wt.append((c.s_act, bankH[bank]))
                P.add("tensor",
                      lambda t, o=c.psF[:, bank, :], l=c.W1T.ap()[:, dc, hc * 128:(hc + 1) * 128],
                             r=xt[b].ap()[:, dc, :], st=(dc == 0), sp=(dc == 7):
                      t.matmul(o, l, r, start=st, stop=sp),
                      wait=wt, inc=[(c.s_pe, 1)] if dc == 7 else ())
            gw = [(c.s_pe, c.s_pe.n)]
            if w2_done is not None:
                gw.append((c.s_pe, w2_done))
            P.add("scalar",
                  lambda s, o=h_sb.ap()[:, hc, :], i=c.psF[:, bank, :],
                         bb=c.b1t.ap()[:, hc:hc + 1]:
                  s.activation(o, i, AF.Gelu, bias=bb),
                  wait=gw, inc=[(c.s_act, 1)])
            bankH[bank] = c.s_act.n
        h_done = c.s_act.n
        xt_free[b] = c.s_pe.n
        for st4 in range(4):
            row0 = sc8 * SCW + st4 * 128
            P.dma("sync", x1r.ap(), c.x1f[row0:row0 + 128, :], c.s_res,
                  wait=[(c.s_dve, max(0, c.s_dve.n - 1))])
            for half in range(2):
                bank = 2 + half
                for hc in range(32):
                    wt = []
                    if hc == 0:
                        wt.append((c.s_act, h_done))
                        wt.append((c.s_dve, bankO2))
                    P.add("tensor",
                          lambda t, o=c.psF[:, bank, :],
                                 l=h_sb.ap()[:, hc, st4 * 128:(st4 + 1) * 128],
                                 r=c.W2T.ap()[:, hc, half * 512:(half + 1) * 512],
                                 st=(hc == 0), sp=(hc == 31):
                          t.matmul(o, l, r, start=st, stop=sp),
                          wait=wt, inc=[(c.s_pe, 1)] if hc == 31 else ())
            pe_done = c.s_pe.n
            bx = st4 % 2
            t_ap = tt.ap()
            P.add("vector", lambda v, o=t_ap[:, 0:512], i=c.psF[:, 2, :],
                           r=x1r.ap()[:, 0:512]:
                  v.tensor_add(o, i, r),
                  wait=[(c.s_pe, pe_done), (c.s_res, c.s_res.n),
                        (c.s_act, c.s_act.n)])
            P.add("vector", lambda v, o=t_ap[:, 512:1024], i=c.psF[:, 3, :],
                           r=x1r.ap()[:, 512:1024]:
                  v.tensor_add(o, i, r), inc=[(c.s_dve, 1)])
            bankO2 = c.s_dve.n
            P.add("vector", lambda v, o=t_ap, i=c.b2cast.ap():
                  v.tensor_add(o, o, i),
                  wait=[(c.s_dve, c.s_dve.n)], inc=[(c.s_dve, 1)])
            emit_finalize_tile(c, t_ap, sqs.ap(), stats.ap(), xn[bx].ap(),
                               row0, c.x2f, c.x2b)
        w2_done = c.s_pe.n

    for t in (h_sb, xt[0], xt[1], x1r, tt, xn[0], xn[1], stats, sqs,
              c.W1T, c.W2T, c.b1t, c.b2cast):
        sb.freeit(t)
    P.barrier("l0ffn_done")


def emit_l1prep(c):
    P, sb = c.P, c.sb
    c.q1T = sb.alloc("q1T", [128, 8, S], BF16)
    c.k1T = sb.alloc("k1T", [128, 2, S], BF16)
    c.v1 = sb.alloc("v1", [128, NST, NKV, 65], BF16)
    wq1s = sb.alloc("wq1s", [128, 8, D], BF16)
    wk1s = sb.alloc("wk1s", [128, 8, 256], BF16)
    wv1s = sb.alloc("wv1s", [128, 8, 256], BF16)
    eyes = sb.alloc("eyes", [128, 128], BF16)
    xt = [sb.alloc(f"xtp{i}", [128, 8, SCW], BF16) for i in range(2)]
    qf = sb.alloc("qf", [128, NH, DK], F32)
    kf = sb.alloc("kf", [128, NKV, DK], F32)
    qsq = sb.alloc("qsq", [128, D], F32)
    qn = sb.alloc("qn", [128, NH, DK], F32)
    kn = sb.alloc("kn", [128, NKV, DK], F32)
    qrot = sb.alloc("qrot", [128, D], BF16)
    krot = sb.alloc("krot", [128, 256], BF16)
    cosb = [sb.alloc(f"cosb{i}", [128, 32], F32) for i in range(2)]
    sinb = [sb.alloc(f"sinb{i}", [128, 32], F32) for i in range(2)]
    rstat = sb.alloc("rstat", [128, NH * 4], F32)

    sd = c.s_dve
    P.dma("gpsimd", wq1s.ap(), c.wq1.rearrange("(a p) n -> p a n", p=128), c.s_ldw)
    P.dma("gpsimd", wk1s.ap(), c.wk1.rearrange("(a p) n -> p a n", p=128), c.s_ldw)
    P.dma("gpsimd", wv1s.ap(), c.wv1.rearrange("(a p) n -> p a n", p=128), c.s_ldw)
    P.dma("sync", eyes.ap(), c.eye, c.s_ldw)
    ldw_done = c.s_ldw.n
    P.add("vector", lambda v: v.memset(c.v1.ap()[:, :, :, 64:65], 1.0), inc=[(sd, 1)])

    sumq = rstat.ap()[:, 0:16]; vq = rstat.ap()[:, 16:32]
    sumk = rstat.ap()[:, 32:36]; vk = rstat.ap()[:, 36:40]
    rq = rstat.ap()[:, 40:56]; rk = rstat.ap()[:, 56:60]

    xt_free = [None, None]
    prev_tr_evac = 0
    for st in range(NST):
        sc8, off = st // 4, (st % 4) * 128
        cb = st % 2
        if st % 4 == 0:
            fa = xt_free[(st // 4) % 2]
            w = [(c.s_ldw, ldw_done)] if st == 0 else                 ([(c.s_pe, fa)] if fa is not None else [])
            load_xT(c, xt[(st // 4) % 2].ap(), c.x2b, sc8 * SCW, SCW, w)
        xb = xt[(st // 4) % 2]
        ldx_done = c.s_ldx.n
        P.dma("sync", cosb[cb].ap(), c.cos[st * 128:(st + 1) * 128, :], c.s_res,
              wait=[(sd, sd.n)])
        P.dma("sync", sinb[cb].ap(), c.sin[st * 128:(st + 1) * 128, :], c.s_res)
        res_done = c.s_res.n

        for half in range(2):
            for dc in range(8):
                wt = []
                if half == 0 and dc == 0:
                    wt = [(c.s_ldx, ldx_done), (sd, max(0, sd.n - 2))]
                P.add("tensor",
                      lambda t, o=c.psF[:, half, :], l=xb.ap()[:, dc, off:off + 128],
                             r=wq1s.ap()[:, dc, half * 512:(half + 1) * 512],
                             st_=(dc == 0), sp=(dc == 7):
                      t.matmul(o, l, r, start=st_, stop=sp),
                      wait=wt, inc=[(c.s_pe, 1)] if dc == 7 else ())
        for (bank, wsb) in ((2, wk1s), (3, wv1s)):
            for dc in range(8):
                P.add("tensor",
                      lambda t, o=c.psF[:, bank, 0:256], l=xb.ap()[:, dc, off:off + 128],
                             r=wsb.ap()[:, dc, :], st_=(dc == 0), sp=(dc == 7):
                      t.matmul(o, l, r, start=st_, stop=sp),
                      wait=(), inc=[(c.s_pe, 1)] if dc == 7 else ())
        proj_done = c.s_pe.n
        qv = qf.ap().rearrange("p (a b) d -> p a b d", b=2)
        pv0 = c.psF[:, 0, :].rearrange("p (h d) -> p h d", d=DK)
        pv1 = c.psF[:, 1, :].rearrange("p (h d) -> p h d", d=DK)
        P.add("vector", lambda v, o=qv[:, 0:4, 0, :], i=pv0[:, 0:4, :]: v.tensor_copy(o, i),
              wait=[(c.s_pe, proj_done)])
        P.add("vector", lambda v, o=qv[:, 0:4, 1, :], i=pv0[:, 4:8, :]: v.tensor_copy(o, i))
        P.add("vector", lambda v, o=qv[:, 4:8, 0, :], i=pv1[:, 0:4, :]: v.tensor_copy(o, i))
        P.add("vector", lambda v, o=qv[:, 4:8, 1, :], i=pv1[:, 4:8, :]: v.tensor_copy(o, i))
        P.add("vector", lambda v, o=kf.ap(), i=c.psF[:, 2, 0:256].rearrange("p (h d) -> p h d", d=DK):
              v.tensor_copy(o, i))
        P.add("vector", lambda v, o=c.v1.ap()[:, st, :, 0:64],
                       i=c.psF[:, 3, 0:256].rearrange("p (h d) -> p h d", d=DK):
              v.tensor_copy(o, i), inc=[(sd, 1)])
        evac_done = sd.n

        P.add("scalar", lambda s: s.activation(qsq.ap(), qf.ap().rearrange("p a b -> p (a b)"),
                                               AF.Square),
              wait=[(sd, evac_done)], inc=[(c.s_act, 1)])
        P.add("vector", lambda v: v.reduce_sum(sumq.rearrange("p (a b) -> p a b", b=1),
                                               qsq.ap().rearrange("p (a b) -> p a b", b=DK),
                                               axis=AX.X),
              wait=[(c.s_act, c.s_act.n)], inc=[(sd, 1)])
        P.add("scalar", lambda s: s.activation(qsq.ap()[:, 0:256],
                                               kf.ap().rearrange("p a b -> p (a b)"),
                                               AF.Square),
              wait=[(sd, sd.n)], inc=[(c.s_act, 1)])
        P.add("vector", lambda v: v.reduce_sum(sumk.rearrange("p (a b) -> p a b", b=1),
                                               qsq.ap()[:, 0:256].rearrange("p (a b) -> p a b", b=DK),
                                               axis=AX.X),
              wait=[(c.s_act, c.s_act.n)], inc=[(sd, 1)])
        P.add("vector", lambda v: v.tensor_scalar(vq, sumq, 1.0 / DK, EPS, ALU.mult, ALU.add),
              wait=[(sd, sd.n)])
        P.add("vector", lambda v: v.tensor_scalar(vk, sumk, 1.0 / DK, EPS, ALU.mult, ALU.add),
              inc=[(sd, 1)])
        P.add("scalar", lambda s: s.activation(vq, vq, AF.Ln), wait=[(sd, sd.n)])
        P.add("scalar", lambda s: s.activation(rq, vq, AF.Exp, scale=-0.5))
        P.add("scalar", lambda s: s.activation(vk, vk, AF.Ln))
        P.add("scalar", lambda s: s.activation(rk, vk, AF.Exp, scale=-0.5),
              inc=[(c.s_act, 1)])
        P.add("vector", lambda v: v.tensor_tensor(
              qn.ap(), qf.ap(),
              rq.rearrange("p (a b) -> p a b", b=1).broadcast_to([128, NH, DK]), ALU.mult),
              wait=[(c.s_act, c.s_act.n)])
        P.add("vector", lambda v: v.tensor_tensor(
              kn.ap(), kf.ap(),
              rk.rearrange("p (a b) -> p a b", b=1).broadcast_to([128, NKV, DK]), ALU.mult),
              inc=[(sd, 1)])
        t1q = qsq.ap()[:, 0:512].rearrange("p (a b) -> p a b", b=32)
        t2q = qsq.ap()[:, 512:1024].rearrange("p (a b) -> p a b", b=32)
        cq = cosb[cb].ap().rearrange("p (a b) -> p a b", a=1).broadcast_to([128, NH, 32])
        sq_ = sinb[cb].ap().rearrange("p (a b) -> p a b", a=1).broadcast_to([128, NH, 32])
        ck = cosb[cb].ap().rearrange("p (a b) -> p a b", a=1).broadcast_to([128, NKV, 32])
        sk = sinb[cb].ap().rearrange("p (a b) -> p a b", a=1).broadcast_to([128, NKV, 32])
        qnv = qn.ap(); knv = kn.ap()
        qrv = qrot.ap().rearrange("p (a b) -> p a b", b=DK)
        krv = krot.ap().rearrange("p (a b) -> p a b", b=DK)
        P.add("vector", lambda v, a=t1q, b=cq: v.tensor_tensor(a, qnv[:, :, 0:32], b, ALU.mult),
              wait=[(sd, sd.n), (c.s_res, res_done)])
        P.add("vector", lambda v, a=t2q, b=sq_: v.tensor_tensor(a, qnv[:, :, 32:64], b, ALU.mult),
              inc=[(sd, 1)])
        P.add("vector", lambda v, a=t1q, b=t2q: v.tensor_sub(qrv[:, :, 0:32], a, b),
              wait=[(sd, sd.n)])
        P.add("vector", lambda v, a=t1q, b=cq: v.tensor_tensor(a, qnv[:, :, 32:64], b, ALU.mult))
        P.add("vector", lambda v, a=t2q, b=sq_: v.tensor_tensor(a, qnv[:, :, 0:32], b, ALU.mult),
              inc=[(sd, 1)])
        P.add("vector", lambda v, a=t1q, b=t2q: v.tensor_add(qrv[:, :, 32:64], a, b),
              wait=[(sd, sd.n)])
        t1k = qsq.ap()[:, 0:128].rearrange("p (a b) -> p a b", b=32)
        t2k = qsq.ap()[:, 128:256].rearrange("p (a b) -> p a b", b=32)
        P.add("vector", lambda v, a=t1k, b=ck: v.tensor_tensor(a, knv[:, :, 0:32], b, ALU.mult))
        P.add("vector", lambda v, a=t2k, b=sk: v.tensor_tensor(a, knv[:, :, 32:64], b, ALU.mult),
              inc=[(sd, 1)])
        P.add("vector", lambda v, a=t1k, b=t2k: v.tensor_sub(krv[:, :, 0:32], a, b),
              wait=[(sd, sd.n)])
        P.add("vector", lambda v, a=t1k, b=ck: v.tensor_tensor(a, knv[:, :, 32:64], b, ALU.mult))
        P.add("vector", lambda v, a=t2k, b=sk: v.tensor_tensor(a, knv[:, :, 0:32], b, ALU.mult),
              inc=[(sd, 1)])
        P.add("vector", lambda v, a=t1k, b=t2k: v.tensor_add(krv[:, :, 32:64], a, b),
              wait=[(sd, sd.n)], inc=[(sd, 1)])
        rope_done = sd.n

        for ec in range(8):
            wt = [(sd, rope_done)] if ec == 0 else []
            if ec == 0:
                wt.append((sd, prev_tr_evac))
            P.add("tensor",
                  lambda t, o=c.psT[:, 0, ec * 128:(ec + 1) * 128],
                         i=qrot.ap()[:, ec * 128:(ec + 1) * 128], e=eyes.ap():
                  t.transpose(o, i, e),
                  wait=wt, inc=[(c.s_pe, 1)] if ec == 7 else ())
        P.add("vector",
              lambda v, o=c.q1T.ap()[:, :, st * 128:(st + 1) * 128],
                     i=c.psT[:, 0, :].rearrange("p (a b) -> p a b", b=128):
              v.tensor_copy(o, i),
              wait=[(c.s_pe, c.s_pe.n)], inc=[(sd, 1)])
        qev = sd.n
        for ec in range(2):
            wt = [(sd, qev)] if ec == 0 else []
            P.add("tensor",
                  lambda t, o=c.psT[:, 0, ec * 128:(ec + 1) * 128],
                         i=krot.ap()[:, ec * 128:(ec + 1) * 128], e=eyes.ap():
                  t.transpose(o, i, e),
                  wait=wt, inc=[(c.s_pe, 1)] if ec == 1 else ())
        P.add("vector",
              lambda v, o=c.k1T.ap()[:, :, st * 128:(st + 1) * 128],
                     i=c.psT[:, 0, 0:256].rearrange("p (a b) -> p a b", b=128):
              v.tensor_copy(o, i),
              wait=[(c.s_pe, c.s_pe.n)], inc=[(sd, 1)])
        if "l1prepd" in DBG and st == 0:
            dbg_dump(c, "qf", qf.ap(), [128, NH, DK], F32)
            dbg_dump(c, "qn", qn.ap(), [128, NH, DK], F32)
            dbg_dump(c, "qrot", qrot.ap(), [128, D], BF16)
            dbg_dump(c, "kf", kf.ap(), [128, NKV, DK], F32)
            dbg_dump(c, "krot", krot.ap(), [128, 256], BF16)
            dbg_dump(c, "rstat", rstat.ap(), [128, NH * 4], F32)
        prev_tr_evac = sd.n
        xt_free[(st // 4) % 2] = c.s_pe.n

    if "l1prep" in DBG:
        dbg_dump(c, "q1T", c.q1T.ap(), [128, 8, S], BF16)
        dbg_dump(c, "k1T", c.k1T.ap(), [128, 2, S], BF16)
        dbg_dump(c, "v1", c.v1.ap(), [128, NST, NKV, 65], BF16)
    for t in (wq1s, wk1s, wv1s, eyes, xt[0], xt[1], qf, kf, qsq, qn, kn,
              qrot, krot, cosb[0], cosb[1], sinb[0], sinb[1], rstat):
        sb.freeit(t)
    P.barrier("l1prep_done")


def emit_l1attn(c):
    P, sb = c.P, c.sb
    wo1p = sb.alloc("wo1p", [128, 8, D], BF16)
    trim = sb.alloc("trim", [128, 128], BF16)
    Pp = sb.alloc("Pp1", [128, 4, SCW], BF16)
    ynT = sb.alloc("ynT1", [128, 8, SCW], BF16)
    stat = sb.alloc("stat1", [128, SCW], F32)
    tcap = sb.alloc("tcap", [128, 2 * SCW], F32)
    ones64 = sb.alloc("ones64b", [128, 64], F32)
    x2r = sb.alloc("x2r", [128, D], F32)
    tt = sb.alloc("tt1", [128, D], F32)
    xn = [sb.alloc(f"xn1{i}", [128, D], F32) for i in range(2)]
    stats = sb.alloc("stats1", [128, 12], F32)
    sqs = sb.alloc("sqs1", [128, 512], F32)

    sd = c.s_dve
    for p in range(NH):
        h = PERM1[p]
        b0 = (p % 2) * 64
        P.dma("gpsimd", wo1p.ap()[b0:b0 + 64, p // 2, :],
              c.wo1[h * 64:(h + 1) * 64, :], c.s_ldw)
    P.dma("sync", trim.ap(), c.tri, c.s_ldw)
    ldw_done = c.s_ldw.n
    P.add("vector", lambda v: v.memset(ones64.ap(), 1.0), inc=[(sd, 1)])

    ynmul_done = {0: 0, 1: 0}
    c._pb1 = 0
    bankS1 = [0, 0]
    bankO1 = 0
    last_adds1 = 0

    def emit_norm1(avb_, p0_, chunk_, av_stop, act_after):
        P.add("vector", lambda v, o=stat.ap()[64:65, :], i=c.psF[64:65, avb_, :]:
              v.reciprocal(o, i),
              wait=[(c.s_pe, av_stop), (c.s_act, act_after)], inc=[(sd, 1)])
        P.add("tensor", lambda t, o=c.psF[0:64, 6, :], l=ones64.ap()[64:65, 0:64],
                       r=stat.ap()[64:65, :]:
              t.matmul(o, l, r, start=True, stop=True),
              wait=[(sd, sd.n), (sd, c._pb1)],
              inc=[(c.s_pe, 1)])
        P.add("vector", lambda v, o=stat.ap()[0:64, :], i=c.psF[0:64, 6, :]:
              v.tensor_copy(o, i),
              wait=[(c.s_pe, c.s_pe.n)], inc=[(sd, 1)])
        c._pb1 = sd.n
        P.add("vector", lambda v, o=ynT.ap()[p0_:p0_ + 64, chunk_, :],
                       a=c.psF[0:64, avb_, :], b2=stat.ap()[0:64, :]:
              v.tensor_mul(o, a, b2),
              wait=[(sd, sd.n)], inc=[(sd, 1)])
        ynmul_done[avb_ - 2] = sd.n

    SBP1 = ((0, 1), (4, 5))
    for qc in range(NSC):
        pending_norms = []
        for pp in range(8):
            kt_last = min(NST - 1, 4 * qc + 3)
            av_q = []
            for kt in range(kt_last + 1):
                m = kt - 4 * qc
                col0 = 128 * m if m > 0 else 0
                pair = SBP1[kt % 2]
                for hpar in range(2):
                    pos = 2 * pp + hpar
                    h = PERM1[pos]
                    p0 = hpar * 64
                    jchunk = (h // 4) // 2
                    sbank = pair[hpar]
                    wt = []
                    if kt == 0 and pp == 0 and hpar == 0:
                        if qc == 0:
                            wt.append((c.s_ldw, ldw_done))
                        wt.append((sd, last_adds1))
                    if hpar == 0:
                        wt.append((c.s_act, bankS1[kt % 2]))
                    P.add("tensor",
                          lambda t, o=c.psF[:, sbank, col0:512],
                                 l=c.k1T.ap()[p0:p0 + 64, jchunk, kt * 128:(kt + 1) * 128],
                                 r=c.q1T.ap()[p0:p0 + 64, pp, qc * 512 + col0: (qc + 1) * 512]:
                          t.matmul(o, l, r, start=True, stop=True),
                          wait=wt, inc=[(c.s_pe, 1)])
                sl = 2 * (kt % 2)
                P.add("scalar",
                      lambda s, o=tcap.ap(),
                             i=c.psF[:, pair[0]:pair[0] + 2, :].rearrange("p a b -> p (a b)"):
                      s.activation(o, i, AF.Tanh, scale=1.0 / (8.0 * CAP)),
                      wait=[(c.s_pe, c.s_pe.n)], inc=[(c.s_act, 1)])
                P.add("scalar",
                      lambda s, o=Pp.ap()[:, sl:sl + 2, :].rearrange("p a b -> p (a b)"),
                             i=tcap.ap():
                      s.activation(o, i, AF.Exp, scale=CAP),
                      inc=[(c.s_act, 1)])
                bankS1[kt % 2] = c.s_act.n
                exp_done = c.s_act.n
                if kt == 1 and pending_norms:
                    for pn in pending_norms:
                        emit_norm1(*pn)
                    pending_norms = []
                masked = 0 <= m <= 3
                for hpar in range(2):
                    slot = sl + hpar
                    if masked:
                        P.add("vector",
                              lambda v, o=Pp.ap()[:, slot, col0:col0 + 128], tm=trim.ap():
                              v.tensor_mul(o, o, tm),
                              wait=[(c.s_act, exp_done)], inc=[(sd, 1)])
                for hpar in range(2):
                    pos = 2 * pp + hpar
                    j = PERM1[pos] // 4
                    avb = 2 + hpar
                    avw = [(sd, sd.n)] if masked else [(c.s_act, exp_done)]
                    av_q.append((
                        (lambda t, o=c.psF[0:65, avb, col0:512],
                                l=c.v1.ap()[:, kt, j, :],
                                r=Pp.ap()[:, slot if False else sl + hpar, col0:512],
                                st=(kt == 0), sp=(kt == kt_last):
                         t.matmul(o, l, r, start=st, stop=sp)),
                        avw, [(c.s_pe, 1)] if kt == kt_last else (),
                        hpar if kt == 0 else None))
                while len(av_q) > 2:
                    a = av_q.pop(0)
                    w = list(a[1])
                    if a[3] is not None:
                        w.append((sd, ynmul_done[a[3]]))
                    P.add("tensor", a[0], wait=w, inc=a[2])
            for a in av_q:
                w = list(a[1])
                if a[3] is not None:
                    w.append((sd, ynmul_done[a[3]]))
                P.add("tensor", a[0], wait=w, inc=a[2])
            pending_norms = [(2, 0, pp, c.s_pe.n, c.s_act.n),
                             (3, 64, pp, c.s_pe.n, c.s_act.n)]
        for pn in pending_norms:
            emit_norm1(*pn)
        pending_norms = []
        heads_done = sd.n

        for st4 in range(4):
            row0 = qc * SCW + st4 * 128
            P.dma("sync", x2r.ap(), c.x2f[row0:row0 + 128, :], c.s_res,
                  wait=[(sd, max(0, sd.n - 1))])
            for half in range(2):
                bank = half
                for hc in range(8):
                    wt = []
                    if hc == 0:
                        wt.append((sd, heads_done))
                        wt.append((sd, bankO1))
                    P.add("tensor",
                          lambda t, o=c.psF[:, bank, :],
                                 l=ynT.ap()[:, hc, st4 * 128:(st4 + 1) * 128],
                                 r=wo1p.ap()[:, hc, half * 512:(half + 1) * 512],
                                 st=(hc == 0), sp=(hc == 7):
                          t.matmul(o, l, r, start=st, stop=sp),
                          wait=wt, inc=[(c.s_pe, 1)] if hc == 7 else ())
            pe_done = c.s_pe.n
            bx = st4 % 2
            t_ap = tt.ap()
            P.add("vector", lambda v, o=t_ap[:, 0:512], i=c.psF[:, 0, :],
                           r=x2r.ap()[:, 0:512]:
                  v.tensor_add(o, i, r),
                  wait=[(c.s_pe, pe_done), (c.s_res, c.s_res.n),
                        (c.s_act, c.s_act.n)])
            P.add("vector", lambda v, o=t_ap[:, 512:1024], i=c.psF[:, 1, :],
                           r=x2r.ap()[:, 512:1024]:
                  v.tensor_add(o, i, r), inc=[(sd, 1)])
            bankO1 = sd.n
            last_adds1 = sd.n
            if "l1attn" in DBG and qc == 1 and st4 == 0:
                dbg_dump(c, "t1", t_ap, [128, D], F32)
            emit_finalize_tile(c, t_ap, sqs.ap(), stats.ap(), xn[bx].ap(),
                               row0, c.x3f, c.x3b)
        if "l1attn" in DBG and qc == 1:
            dbg_dump(c, "ynT1", ynT.ap(), [128, 8, SCW], BF16)

    for t in (wo1p, trim, Pp, ynT, stat, tcap, ones64, x2r, tt, xn[0], xn[1],
              stats, sqs, c.q1T, c.k1T, c.v1):
        sb.freeit(t)
    P.barrier("l1attn_done")


def emit_l1ffnA(c):
    P, sb = c.P, c.sb
    w1T = sb.alloc("w1T1", [128, 8, HFF], BF16)
    w2T = sb.alloc("w2T1", [128, 8, HFF], BF16)
    s1t = sb.alloc("s1t", [128, 32], F32)
    s2t = sb.alloc("s2t", [128, 32], F32)
    xt = [sb.alloc(f"xta{i}", [128, 8, SCW], BF16) for i in range(2)]
    u_sb = [sb.alloc(f"usb{i}", [128, SCW], F32) for i in range(2)]
    ha = sb.alloc("ha", [128, 32, SCW], BF16)

    sd = c.s_dve
    wait_stg = ((c.s_stg_f1, c.s_stg_f1.n),)
    load_wT(c, w1T, c.st_w1w, 8, wait_stg)
    load_wT(c, w2T, c.st_w2w, 8, wait_stg)
    P.dma("sync", s1t.ap(), c.w1s, c.s_ldw)
    P.dma("sync", s2t.ap(), c.w2s, c.s_ldw)
    ldw_done = c.s_ldw.n

    xt_free = [None, None]
    ha_store_done = 0
    bankU = [0, 0]
    bankG = [0, 0]
    ustt = [0, 0]
    for sc8 in range(NSC):
        b = sc8 % 2
        w = [] if xt_free[b] is None else [(c.s_pe, xt_free[b])]
        load_xT(c, xt[b].ap(), c.x3b, sc8 * SCW, SCW, w)
        ldx_done = c.s_ldx.n
        for hc in range(32):
            ub = hc % 2
            gb = 2 + hc % 2
            for dc in range(8):
                wt = []
                if hc == 0 and dc == 0:
                    wt = [(c.s_ldx, ldx_done), (c.s_ldw, ldw_done)]
                if dc == 0:
                    wt.append((c.s_act, bankU[ub]))
                P.add("tensor",
                      lambda t, o=c.psF[:, ub, :], l=w1T.ap()[:, dc, hc * 128:(hc + 1) * 128],
                             r=xt[b].ap()[:, dc, :], st=(dc == 0), sp=(dc == 7):
                      t.matmul(o, l, r, start=st, stop=sp),
                      wait=wt, inc=[(c.s_pe, 1)] if dc == 7 else ())
            for dc in range(8):
                wt = []
                if dc == 0:
                    wt.append((sd, bankG[gb - 2]))
                P.add("tensor",
                      lambda t, o=c.psF[:, gb, :], l=w2T.ap()[:, dc, hc * 128:(hc + 1) * 128],
                             r=xt[b].ap()[:, dc, :], st=(dc == 0), sp=(dc == 7):
                      t.matmul(o, l, r, start=st, stop=sp),
                      wait=wt, inc=[(c.s_pe, 1)] if dc == 7 else ())
            P.add("scalar",
                  lambda s, o=u_sb[hc % 2].ap(), i=c.psF[:, ub, :],
                         sc_=s1t.ap()[:, hc:hc + 1]:
                  s.activation(o, i, AF.Silu, scale=sc_),
                  wait=[(c.s_pe, c.s_pe.n), (sd, ustt[hc % 2])],
                  inc=[(c.s_act, 1)])
            bankU[ub] = c.s_act.n
            stw = [(c.s_act, c.s_act.n)]
            if hc == 0:
                stw.append((c.s_st, ha_store_done))
            P.add("vector",
                  lambda v, o=ha.ap()[:, hc, :], i=c.psF[:, gb, :],
                         sc_=s2t.ap()[:, hc:hc + 1], u=u_sb[hc % 2].ap():
                  v.scalar_tensor_tensor(o, i, sc_, u, ALU.mult, ALU.mult),
                  wait=stw, inc=[(sd, 1)])
            bankG[gb - 2] = sd.n
            ustt[hc % 2] = sd.n
        xt_free[b] = c.s_pe.n
        if "l1ffnA" in DBG and sc8 == 0:
            dbg_dump(c, "xta", xt[b].ap(), [128, 8, SCW], BF16)
            dbg_dump(c, "w1Tc", w1T.ap()[:, :, 0:512], [128, 8, 512], BF16)
            dbg_dump(c, "hasc", ha.ap(), [128, 32, SCW], BF16)
            dbg_dump(c, "s1td", s1t.ap(), [128, 32], F32)
        P.dma("scalar",
              c.haT.rearrange("(a p) s -> p a s", p=128)[:, :, sc8 * SCW:(sc8 + 1) * SCW],
              ha.ap(), c.s_st, wait=[(sd, sd.n)])
        ha_store_done = c.s_st.n

    for t in (w1T, w2T, s1t, s2t, xt[0], xt[1], u_sb[0], u_sb[1], ha):
        sb.freeit(t)
    P.barrier("l1ffnA_done")


def emit_l1ffnBC(c, st_w, scales_in, in_d, out_d, tag):
    P, sb = c.P, c.sb
    hw = [sb.alloc(f"hw{tag}{i}", [128, 32, 1024], BF16) for i in range(2)]
    hst = sb.alloc(f"hst{tag}", [128, 32], F32)
    hin = [sb.alloc(f"hin{tag}{i}", [128, 32, SCW], BF16) for i in range(2)]
    ho = sb.alloc(f"ho{tag}", [128, 8, SCW], BF16)

    P.dma("sync", hst.ap(), scales_in, c.s_ldw)
    in_v = in_d.rearrange("(a p) s -> p a s", p=128)
    out_v = out_d.rearrange("(a p) s -> p a s", p=128)

    ldw_after = {}
    hw_free = [None, None]
    s_bc = [P.sem(f"bc{tag}0"), P.sem(f"bc{tag}1")]

    def load_stripe(s, extra_wait):
        for ic in range(32):
            P.dma("scalar", hw[s % 2].ap()[:, ic, :],
                  st_w[s * 1024:(s + 1) * 1024, ic * 128:(ic + 1) * 128],
                  s_bc[s % 2], wait=extra_wait if ic == 0 else (), transpose=True)
        ldw_after[s] = (s_bc[s % 2], s_bc[s % 2].n)

    load_stripe(0, [(c.s_stg_f1, c.s_stg_f1.n)])
    hin_free = [None, None]
    ho_store_done = 0
    bankB = [0, 0]
    for stripe in range(4):
        if stripe + 1 < 4:
            w = [(c.s_stg_f1, c.s_stg_f1.n)]
            if hw_free[(stripe + 1) % 2] is not None:
                w.append((c.s_pe, hw_free[(stripe + 1) % 2]))
            load_stripe(stripe + 1, w)
        for sc8 in range(NSC):
            b = sc8 % 2
            w = []
            if hin_free[b] is not None:
                w.append((c.s_pe, hin_free[b]))
            P.dma("sync", hin[b].ap(), in_v[:, :, sc8 * SCW:(sc8 + 1) * SCW],
                  c.s_ldx, wait=w)
            ldx_done = c.s_ldx.n
            for oc in range(8):
                bank = oc % 2
                for ic in range(32):
                    wt = []
                    if oc == 0 and ic == 0:
                        wt = [(c.s_ldx, ldx_done), ldw_after[stripe]]
                    if ic == 0:
                        wt.append((c.s_act, bankB[bank]))
                    P.add("tensor",
                          lambda t, o=c.psF[:, bank, :],
                                 l=hw[stripe % 2].ap()[:, ic, oc * 128:(oc + 1) * 128],
                                 r=hin[b].ap()[:, ic, :], st=(ic == 0), sp=(ic == 31):
                          t.matmul(o, l, r, start=st, stop=sp),
                          wait=wt, inc=[(c.s_pe, 1)] if ic == 31 else ())
                gw = [(c.s_pe, c.s_pe.n)]
                if oc == 0:
                    gw.append((c.s_st, ho_store_done))
                P.add("scalar",
                      lambda s, o=ho.ap()[:, oc, :], i=c.psF[:, bank, :],
                             sc_=hst.ap()[:, stripe * 8 + oc:stripe * 8 + oc + 1]:
                      s.activation(o, i, AF.Gelu, scale=sc_),
                      wait=gw, inc=[(c.s_act, 1)])
                bankB[bank] = c.s_act.n
            hin_free[b] = c.s_pe.n
            P.dma("scalar",
                  out_v[:, stripe * 8:(stripe + 1) * 8, sc8 * SCW:(sc8 + 1) * SCW],
                  ho.ap(), c.s_st, wait=[(c.s_act, c.s_act.n)])
            ho_store_done = c.s_st.n
        hw_free[stripe % 2] = c.s_pe.n

    for t in (hw[0], hw[1], hst, hin[0], hin[1], ho):
        sb.freeit(t)
    P.barrier(f"l1ffn{tag}_done")


def emit_l1ffnB(c):
    emit_l1ffnBC(c, c.st_h1, c.h1s, c.haT, c.hbT, "B")


def emit_l1ffnC(c):
    emit_l1ffnBC(c, c.st_h2, c.h2s, c.hbT, c.haT, "C")


def emit_l1ffnD(c):
    P, sb = c.P, c.sb
    w3T = sb.alloc("w3T1", [128, 32, D], BF16)
    s3c = sb.alloc("s3c", [128, D], F32)
    hin = [sb.alloc(f"hinD{i}", [128, 32, SCW], BF16) for i in range(2)]
    x3r = sb.alloc("x3r", [128, D], F32)
    tt = sb.alloc("ttD", [128, D], F32)
    xn = [sb.alloc(f"xnD{i}", [128, D], F32) for i in range(2)]
    stats = sb.alloc("statsD", [128, 12], F32)
    sqs = sb.alloc("sqsD", [128, 512], F32)

    sd = c.s_dve
    wait_stg = ((c.s_stg_f1, c.s_stg_f1.n),)
    load_wT(c, w3T, c.st_w3, 32, wait_stg)
    P.dma("sync", s3c.ap(), c.w3s[None, :].broadcast_to([128, D]), c.s_ldw)
    ldw_done = c.s_ldw.n
    in_v = c.haT.rearrange("(a p) s -> p a s", p=128)

    hin_free = [None, None]
    bankD = 0
    for sc8 in range(NSC):
        b = sc8 % 2
        w = []
        if hin_free[b] is not None:
            w.append((c.s_pe, hin_free[b]))
        P.dma("sync", hin[b].ap(), in_v[:, :, sc8 * SCW:(sc8 + 1) * SCW],
              c.s_ldx, wait=w)
        ldx_done = c.s_ldx.n
        for st4 in range(4):
            row0 = sc8 * SCW + st4 * 128
            P.dma("sync", x3r.ap(), c.x3f[row0:row0 + 128, :], c.s_res,
                  wait=[(sd, max(0, sd.n - 1))])
            for half in range(2):
                bank = 2 + half
                for ic in range(32):
                    wt = []
                    if st4 == 0 and half == 0 and ic == 0:
                        wt = [(c.s_ldx, ldx_done), (c.s_ldw, ldw_done)]
                    if ic == 0:
                        wt.append((sd, bankD))
                    P.add("tensor",
                          lambda t, o=c.psF[:, bank, :], l=hin[b].ap()[:, ic, st4 * 128:(st4 + 1) * 128],
                                 r=w3T.ap()[:, ic, half * 512:(half + 1) * 512],
                                 st=(ic == 0), sp=(ic == 31):
                          t.matmul(o, l, r, start=st, stop=sp),
                          wait=wt, inc=[(c.s_pe, 1)] if ic == 31 else ())
            pe_done = c.s_pe.n
            bx = st4 % 2
            t_ap = tt.ap()
            P.add("vector", lambda v, o=t_ap[:, 0:512], i=c.psF[:, 2, :],
                           s3=s3c.ap()[:, 0:512]:
                  v.tensor_mul(o, i, s3),
                  wait=[(c.s_pe, pe_done), (c.s_res, c.s_res.n),
                        (c.s_act, c.s_act.n)])
            P.add("vector", lambda v, o=t_ap[:, 512:1024], i=c.psF[:, 3, :],
                           s3=s3c.ap()[:, 512:1024]:
                  v.tensor_mul(o, i, s3), inc=[(sd, 1)])
            bankD = sd.n
            P.add("vector", lambda v, o=t_ap, i=x3r.ap():
                  v.tensor_add(o, o, i),
                  wait=[(sd, sd.n)], inc=[(sd, 1)])
            emit_finalize_tile(c, t_ap, sqs.ap(), stats.ap(), xn[bx].ap(),
                               row0, c.y, None)
        hin_free[b] = c.s_pe.n

    for t in (w3T, s3c, hin[0], hin[1], x3r, tt, xn[0], xn[1], stats, sqs):
        sb.freeit(t)
    P.barrier("l1ffnD_done")


# version 08
# speedup vs baseline: 1.1611x; 1.0103x over previous
import math
import numpy as np
import ml_dtypes
import sys
if '/opt/trn_rl_repo' not in sys.path:
    sys.path.insert(0, '/opt/trn_rl_repo')
import concourse.bass as bass
import concourse.mybir as mybir
from concourse.bass_utils import run_bass_kernel_spmd

F32 = mybir.dt.float32
BF16 = mybir.dt.bfloat16
AF = mybir.ActivationFunctionType
ALU = mybir.AluOpType
AX = mybir.AxisListType

S = 4096
D = 1024
HFF = 4096
NH = 16
DK = 64
NKV = 4
SCW = 512
NSC = S // SCW
NST = S // 128
EPS = 1e-5
CAP = 50.0
THETA = 1e4

ENGINES = ("tensor", "vector", "scalar", "gpsimd", "sync")

PERM1 = [0, 4, 1, 5, 2, 6, 3, 7, 8, 12, 9, 13, 10, 14, 11, 15]


class Sem:
    __slots__ = ("h", "n", "name")
    def __init__(self, h, name):
        self.h = h
        self.n = 0
        self.name = name


class Planner:
    def __init__(self, nc):
        self.nc = nc
        self.items = {e: [] for e in ENGINES}
        self.pending = {e: [] for e in ENGINES}
        self.bar_idx = {e: 0 for e in ENGINES}
        self.nsem = 0

    def sem(self, name=None):
        name = name or f"s{self.nsem}"
        self.nsem += 1
        h = self.nc.alloc_semaphore(name)
        return Sem(h, name)

    def add(self, e, fn=None, wait=(), inc=(), is_dma=False):
        waits = list(self.pending[e]) + list(wait)
        self.pending[e] = []
        for s, n in inc:
            s.n += n
        self.items[e].append({"waits": waits, "fn": fn, "incs": list(inc),
                              "is_dma": is_dma})

    def dma(self, e, out, in_, sem, wait=(), transpose=False, inc=()):
        if transpose:
            fn = lambda eng, o=out, i=in_: eng.dma_start_transpose(out=o, in_=i)
        else:
            fn = lambda eng, o=out, i=in_: eng.dma_start(out=o, in_=i)
        self.add(e, fn, wait, [(sem, 16), *inc], is_dma=True)

    def barrier(self, name="bar"):
        b = self.sem(name)
        total = 0
        for e in ENGINES:
            its = self.items[e]
            if len(its) > self.bar_idx[e]:
                last = its[-1]
                n = 16 if (last["is_dma"] and last["fn"] is not None) else 1
                if last["fn"] is not None and not last["incs"]:
                    last["incs"].append((b, n))
                else:
                    self.items[e].append({"waits": [], "fn": None,
                                          "incs": [(b, 1)], "is_dma": False})
                    n = 1
                b.n += n
                total += n
            self.bar_idx[e] = len(self.items[e])
        for e in ENGINES:
            self.pending[e].append((b, total))
        return b

    def replay(self, ename, eng):
        for it in self.items[ename]:
            for (s, v) in it["waits"]:
                eng.wait_ge(s.h, v)
            ins = it["fn"](eng) if it["fn"] is not None else None
            incs = it["incs"]
            if ins is not None and incs:
                ins.then_inc(incs[0][0].h, incs[0][1])
                for s, n in incs[1:]:
                    eng.sem_inc(s.h, n)
            else:
                for s, n in incs:
                    eng.sem_inc(s.h, n)


class _SBWrap:
    __slots__ = ("t", "_sb_range")
    def __init__(self, t, rng):
        self.t = t
        self._sb_range = rng
    def ap(self):
        return self.t.ap()


class SB:
    def __init__(self, nc, base=16640, top=229376):
        self.nc = nc
        self.free = [(base, top)]
        self.cnt = 0

    @staticmethod
    def _sz(dt):
        return {F32: 4, BF16: 2, mybir.dt.int32: 4, mybir.dt.uint8: 1}[dt]

    def alloc(self, name, shape, dtype, align=128):
        bpp = int(np.prod(shape[1:])) * self._sz(dtype)
        for i, (a, b) in enumerate(self.free):
            start = (a + align - 1) // align * align
            if start + bpp <= b:
                pre = (a, start)
                post = (start + bpp, b)
                repl = []
                if pre[1] > pre[0]:
                    repl.append(pre)
                if post[1] > post[0]:
                    repl.append(post)
                self.free[i:i + 1] = repl
                self.cnt += 1
                t = self.nc.alloc_sbuf_tensor_at(f"{name}_{self.cnt}", list(shape),
                                                 dtype, offset=start)
                return _SBWrap(t, (start, start + bpp))
        raise MemoryError(f"SBUF alloc failed for {name} {shape} ({bpp} B/part); "
                          f"free={self.free}")

    def freeit(self, t):
        a, b = t._sb_range
        self.free.append((a, b))
        self.free.sort()
        merged = []
        for seg in self.free:
            if merged and seg[0] <= merged[-1][1]:
                merged[-1] = (merged[-1][0], max(merged[-1][1], seg[1]))
            else:
                merged.append(seg)
        self.free = merged


_CONST_CACHE = {}

def host_consts():
    if _CONST_CACHE:
        return _CONST_CACHE
    inv = 1.0 / THETA ** (np.arange(0, DK, 2, dtype=np.float64) / DK)
    freqs = np.arange(S, dtype=np.float64)[:, None] * inv[None, :]
    _CONST_CACHE["cos"] = np.cos(freqs).astype(np.float32)
    _CONST_CACHE["sin"] = np.sin(freqs).astype(np.float32)
    jk = np.arange(128)[:, None]
    jq = np.arange(128)[None, :]
    _CONST_CACHE["tri"] = (jq >= jk).astype(ml_dtypes.bfloat16)
    _CONST_CACHE["eye"] = np.eye(128, dtype=ml_dtypes.bfloat16)
    return _CONST_CACHE


class Ctx:
    pass


def make_ctx(nc, taps=False):
    c = Ctx()
    c.nc = nc
    c.P = Planner(nc)
    c.sb = SB(nc)
    c.taps = taps
    T = lambda name, shape, dt, kind: nc.dram_tensor(name, shape, dt, kind=kind).ap()
    I = lambda name, shape, dt=F32: T(name, shape, dt, "ExternalInput")
    scratch_kind = "ExternalOutput" if taps else "Internal"
    N = lambda name, shape, dt: T(name, shape, dt, scratch_kind)

    c.x = I("x", [S, D])
    c.wq0 = I("wq0", [D, D]); c.wk0 = I("wk0", [D, D])
    c.wv0 = I("wv0", [D, D]); c.wo0 = I("wo0", [D, D])
    c.bo0 = I("bo0", [D])
    c.W1 = I("W1", [HFF, D]); c.b1 = I("b1", [128, HFF // 128])
    c.W2 = I("W2", [D, HFF]); c.b2 = I("b2", [D])
    c.scale0 = I("scale0", [1, 1])
    c.wq1 = I("wq1", [D, D]); c.wk1 = I("wk1", [D, 256])
    c.wv1 = I("wv1", [D, 256]); c.wo1 = I("wo1", [D, D])
    c.w1w = I("w1w", [HFF, D]); c.w1s = I("w1s", [128, 32])
    c.w2w = I("w2w", [HFF, D]); c.w2s = I("w2s", [128, 32])
    c.h1w = I("h1w", [HFF, HFF]); c.h1s = I("h1s", [128, 32])
    c.h2w = I("h2w", [HFF, HFF]); c.h2s = I("h2s", [128, 32])
    c.w3w = I("w3w", [D, HFF]); c.w3s = I("w3s", [D])
    c.cos = I("cos", [S, 32]); c.sin = I("sin", [S, 32])
    c.tri = I("tri", [128, 128], BF16)
    c.eye = I("eye", [128, 128], BF16)

    c.y = T("y", [S, D], F32, "ExternalOutput")

    ST = lambda name, shape: T(name, shape, BF16, "Internal")
    c.st_wq0 = ST("st_wq0", [D, D]); c.st_wk0 = ST("st_wk0", [D, D])
    c.st_wv0 = ST("st_wv0", [D, D]); c.st_wo0 = ST("st_wo0", [D, D])
    c.st_W1 = ST("st_W1", [HFF, D]); c.st_W2 = ST("st_W2", [D, HFF])
    c.st_w1w = ST("st_w1w", [HFF, D]); c.st_w2w = ST("st_w2w", [HFF, D])
    c.st_h1 = ST("st_h1", [HFF, HFF]); c.st_h2 = ST("st_h2", [HFF, HFF])
    c.st_w3 = ST("st_w3", [D, HFF])

    c.x0b = T("x0b", [S, D], BF16, "Internal")
    c.x1f = N("x1f", [S, D], F32); c.x1b = T("x1b", [S, D], BF16, "Internal")
    c.x2f = N("x2f", [S, D], F32); c.x2b = T("x2b", [S, D], BF16, "Internal")
    c.x3f = N("x3f", [S, D], F32); c.x3b = T("x3b", [S, D], BF16, "Internal")
    c.haT = T("haT", [HFF, S], BF16, scratch_kind)
    c.hbT = T("hbT", [HFF, S], BF16, scratch_kind)

    P = c.P
    c.s_stg_x = P.sem("stg_x")
    c.s_stg_a0 = P.sem("stg_a0")
    c.s_stg_f0 = P.sem("stg_f0")
    c.s_stg_f1 = P.sem("stg_f1")
    c.s_ldw = P.sem("ldw")
    c.s_ldx = P.sem("ldx")
    c.s_res = P.sem("res")
    c.s_st = P.sem("st")
    c.s_stg_o = P.sem("stg_o")
    c.s_pe = P.sem("pe")
    c.s_act = P.sem("act")
    c.s_dve = P.sem("dve")

    c.psF_h = nc.alloc_psum_tensor("psF", [128, 7, 512], F32)
    c.psT_h = nc.alloc_psum_tensor("psT", [128, 1, 1024], BF16)
    c.psF = c.psF_h.ap()
    c.psT = c.psT_h.ap()
    return c


def tview(ap):
    return ap.rearrange("(a p) d -> p a d", p=128)


def emit_stage(c, groups=("x", "a0", "f0", "f1")):
    P = c.P
    if "x" in groups:
        P.dma("gpsimd", c.x0b, c.x, c.s_stg_x)
    if "a0" in groups:
        for dst, src in ((c.st_wq0, c.wq0), (c.st_wk0, c.wk0),
                         (c.st_wv0, c.wv0), (c.st_wo0, c.wo0)):
            P.dma("gpsimd", dst, src, c.s_stg_a0)
    if "f0" in groups:
        P.dma("gpsimd", c.st_W1, c.W1, c.s_stg_f0)
        P.dma("gpsimd", c.st_W2, c.W2, c.s_stg_f0)
    if "f1" in groups:
        for dst, src in ((c.st_w1w, c.w1w), (c.st_w2w, c.w2w), (c.st_w3, c.w3w)):
            P.dma("gpsimd", dst, src, c.s_stg_f1)
        for dst, src in ((c.st_h1, c.h1w), (c.st_h2, c.h2w)):
            for r in range(4):
                P.dma("gpsimd", dst[r * 1024:(r + 1) * 1024, :],
                      src[r * 1024:(r + 1) * 1024, :], c.s_stg_f1)


def load_wT(c, dst_sb, st_ap, nch, wait):
    P = c.P
    for dc in range(nch):
        P.dma("sync", dst_sb.ap()[:, dc, :], st_ap[:, dc * 128:(dc + 1) * 128],
              c.s_ldw, wait=wait if dc == 0 else (), transpose=True)


def load_xT(c, dst_ap, src_bf, s0, width, wait):
    P = c.P
    for dc in range(8):
        P.dma("sync", dst_ap[:, dc, :], src_bf[s0:s0 + width, dc * 128:(dc + 1) * 128],
              c.s_ldx, wait=wait if dc == 0 else (), transpose=True)


def emit_l0kv(c):
    P, sb = c.P, c.sb
    c.kT = sb.alloc("kT", [128, 8, S], BF16)
    c.v0 = sb.alloc("v0", [128, NST, NH, 65], BF16)
    wkT = sb.alloc("wkT", [128, 8, D], BF16)
    wvT = sb.alloc("wvT", [128, 8, D], BF16)
    xt = [sb.alloc(f"xt{i}", [128, 8, SCW], BF16) for i in range(2)]

    wait_stg = ((c.s_stg_a0, c.s_stg_a0.n),)
    load_wT(c, wkT, c.st_wk0, 8, wait_stg)
    load_wT(c, wvT, c.st_wv0, 8, wait_stg)
    ldw_done = c.s_ldw.n

    P.add("vector", lambda v: v.memset(c.v0.ap()[:, :, :, 64:65], 1.0),
          inc=[(c.s_dve, 1)])

    xt_free_at = [None, None]
    bankK = [0, 0]
    bankV = [0, 0]
    for sc in range(NSC):
        b = sc % 2
        w = [(c.s_stg_x, c.s_stg_x.n)] if sc == 0 else ()
        if xt_free_at[b] is not None:
            w = ((c.s_pe, xt_free_at[b]),)
        load_xT(c, xt[b].ap(), c.x0b, sc * SCW, SCW, w)
        ldx_done = c.s_ldx.n

        first_wait = [(c.s_ldx, ldx_done), (c.s_ldw, ldw_done)]
        for ec in range(8):
            bank = ec % 2
            for dc in range(8):
                wt = list(first_wait) if (ec == 0 and dc == 0) else []
                if dc == 0:
                    wt.append((c.s_act, bankK[bank]))
                P.add("tensor",
                      lambda t, o=c.psF[:, bank, :], l=wkT.ap()[:, dc, ec * 128:(ec + 1) * 128],
                             r=xt[b].ap()[:, dc, :], st=(dc == 0), sp=(dc == 7):
                      t.matmul(o, l, r, start=st, stop=sp),
                      wait=wt, inc=[(c.s_pe, 1)] if dc == 7 else ())
            P.add("scalar",
                  lambda s, o=c.kT.ap()[:, ec, sc * SCW:(sc + 1) * SCW], i=c.psF[:, bank, :]:
                  s.activation(o, i, AF.Copy),
                  wait=[(c.s_pe, c.s_pe.n)], inc=[(c.s_act, 1)])
            bankK[bank] = c.s_act.n
        for st4 in range(4):
            for half in range(2):
                bank = 2 + (st4 * 2 + half) % 2
                for dc in range(8):
                    wt = []
                    if dc == 0:
                        wt.append((c.s_dve, bankV[bank - 2]))
                    P.add("tensor",
                          lambda t, o=c.psF[:, bank, :],
                                 l=xt[b].ap()[:, dc, st4 * 128:(st4 + 1) * 128],
                                 r=wvT.ap()[:, dc, half * 512:(half + 1) * 512],
                                 st=(dc == 0), sp=(dc == 7):
                          t.matmul(o, l, r, start=st, stop=sp),
                          wait=wt, inc=[(c.s_pe, 1)] if dc == 7 else ())
                src = c.psF[:, bank, :].rearrange("p (h d) -> p h d", d=64)
                dst = c.v0.ap()[:, sc * 4 + st4, half * 8:(half + 1) * 8, 0:64]
                P.add("vector",
                      lambda v, o=dst, i=src: v.tensor_copy(o, i),
                      wait=[(c.s_pe, c.s_pe.n)], inc=[(c.s_dve, 1)])
                bankV[bank - 2] = c.s_dve.n
        xt_free_at[b] = c.s_pe.n

    if "l0kv" in DBG:
        dbg_dump(c, "kT", c.kT.ap(), [128, 8, S], BF16)
        dbg_dump(c, "v0", c.v0.ap(), [128, NST, NH, 65], BF16)
    for t in (wkT, wvT, xt[0], xt[1]):
        sb.freeit(t)
    P.barrier("l0kv_done")

def emit_finalize_tile(c, t_ap, scratch, stats, xn, row0, outf, outb,
                       xn_single=False):
    P = c.P
    sumsq = stats[:, 0:1]; sumsq2 = stats[:, 1:2]; ssum = stats[:, 2:3]
    m = stats[:, 3:4]; e2 = stats[:, 4:5]; veps = stats[:, 5:6]
    rstd = stats[:, 6:7]; nmr = stats[:, 7:8]; lnv = stats[:, 8:9]
    sqw = [(c.s_dve, c.s_dve.n)]
    P.add("scalar", lambda s: s.activation(scratch, t_ap[:, 0:512], AF.Square,
                                           accum_out=sumsq),
          wait=sqw)
    P.add("scalar", lambda s: s.activation(scratch, t_ap[:, 512:1024], AF.Square,
                                           accum_out=sumsq2),
          inc=[(c.s_act, 1)])
    sd = c.s_dve
    P.add("vector", lambda v: v.reduce_sum(ssum, t_ap, axis=AX.X), inc=[(sd, 1)])
    P.add("vector", lambda v: v.tensor_scalar(m, ssum, 1.0 / D, None, ALU.mult),
          wait=[(sd, sd.n)], inc=[(sd, 1)])
    P.add("vector", lambda v: v.tensor_add(sumsq, sumsq, sumsq2),
          wait=[(c.s_act, c.s_act.n)], inc=[(sd, 1)])
    P.add("vector", lambda v: v.tensor_scalar(e2, sumsq, 1.0 / D, EPS, ALU.mult, ALU.add),
          wait=[(sd, sd.n)], inc=[(sd, 1)])
    P.add("vector", lambda v: v.scalar_tensor_tensor(veps, m, -1.0, m, ALU.mult, ALU.mult),
          wait=[(sd, sd.n)], inc=[(sd, 1)])
    P.add("vector", lambda v: v.tensor_add(veps, veps, e2),
          wait=[(sd, sd.n)], inc=[(sd, 1)])
    P.add("scalar", lambda s: s.activation(lnv, veps, AF.Ln),
          wait=[(sd, sd.n)])
    P.add("scalar", lambda s: s.activation(rstd, lnv, AF.Exp, scale=-0.5),
          inc=[(c.s_act, 1)])
    P.add("vector", lambda v: v.scalar_tensor_tensor(nmr, m, -1.0, rstd, ALU.mult, ALU.mult),
          wait=[(c.s_act, c.s_act.n)], inc=[(sd, 1)])
    back = 0 if xn_single else 16
    aw = [(sd, sd.n), (c.s_st, max(0, c.s_st.n - back))]
    if outb is not None:
        aw.append((c.s_stg_o, max(0, c.s_stg_o.n - back)))
    P.add("vector", lambda v: v.tensor_scalar(xn, t_ap, rstd, nmr, ALU.mult, ALU.add),
          wait=aw, inc=[(sd, 1)])
    w = [(c.s_dve, c.s_dve.n)]
    if "fin" in DBG and getattr(c, "_fin_n", 0) == 0:
        c._fin_n = 1
        dbg_dump(c, "stats", stats, [128, 12], F32)
        dbg_dump(c, "xn", xn, [128, D], F32)
    P.dma("scalar", outf[row0:row0 + 128, :], xn, c.s_st, wait=w)
    if outb is not None:
        P.dma("gpsimd", outb[row0:row0 + 128, :], xn, c.s_stg_o, wait=w)


def emit_l0attn(c):
    P, sb = c.P, c.sb
    wqT = sb.alloc("wqT", [128, 8, D], BF16)
    woT = sb.alloc("woT", [128, 8, D], BF16)
    bocast = sb.alloc("bocast", [128, D], BF16)
    ones64 = sb.alloc("ones64", [128, 64], F32)
    qT = sb.alloc("qT", [128, 8, SCW], BF16)
    xtq = sb.alloc("xtq", [128, 8, SCW], BF16)
    Pp = sb.alloc("Pp", [128, 4, SCW], BF16)
    ynT = sb.alloc("ynT", [128, 8, SCW], BF16)
    stat = sb.alloc("stat", [128, SCW], F32)
    x0r = sb.alloc("x0r", [128, D], F32)
    tt = sb.alloc("tt", [128, D], F32)
    xn1b = sb.alloc("xn", [128, D], F32)
    xn = [xn1b, xn1b]
    stats = sb.alloc("stats", [128, 12], F32)

    wait_stg = ((c.s_stg_a0, c.s_stg_a0.n),)
    load_wT(c, wqT, c.st_wq0, 8, wait_stg)
    load_wT(c, woT, c.st_wo0, 8, wait_stg)
    P.dma("gpsimd", bocast.ap(), c.bo0[None, :].broadcast_to([128, D]), c.s_ldw)
    ldw_done = c.s_ldw.n
    P.add("vector", lambda v: v.memset(ones64.ap(), 1.0), inc=[(c.s_dve, 1)])

    xtq_free = None
    bankQ = [0, 0]
    bankS = [0, 0]
    last_adds = 0
    for qc in range(NSC):
        w = ((c.s_stg_x, c.s_stg_x.n),) if qc == 0 else ((c.s_pe, xtq_free),)
        load_xT(c, xtq.ap(), c.x0b, qc * SCW, SCW, w)
        ldx_done = c.s_ldx.n
        for ec in range(8):
            qb = 4 + ec % 2
            for dc in range(8):
                wt = []
                if ec == 0 and dc == 0:
                    wt = [(c.s_ldx, ldx_done), (c.s_ldw, ldw_done),
                          (c.s_act, getattr(c, "_exps0", 0))]
                if dc == 0:
                    wt.append((c.s_act, bankQ[ec % 2]))
                P.add("tensor",
                      lambda t, o=c.psF[:, qb, :], l=wqT.ap()[:, dc, ec * 128:(ec + 1) * 128],
                             r=xtq.ap()[:, dc, :], st=(dc == 0), sp=(dc == 7):
                      t.matmul(o, l, r, start=st, stop=sp),
                      wait=wt, inc=[(c.s_pe, 1)] if dc == 7 else ())
            P.add("scalar",
                  lambda s, o=qT.ap()[:, ec, :], i=c.psF[:, qb, :]:
                  s.activation(o, i, AF.Copy),
                  wait=[(c.s_pe, c.s_pe.n)], inc=[(c.s_act, 1)])
            bankQ[ec % 2] = c.s_act.n
        xtq_free = c.s_pe.n
        qT_ready = c.s_act.n

        ynd = getattr(c, "_yn0", None)
        if ynd is None:
            ynd = {0: 0, 1: 0}
            c._yn0 = ynd
            c._pb0 = 0

        def emit_norm0(avb_, p0_, chunk_, av_stop, act_after):
            P.add("vector", lambda v, o=stat.ap()[64:65, :], i=c.psF[64:65, avb_, :]:
                  v.reciprocal(o, i),
                  wait=[(c.s_pe, av_stop), (c.s_act, act_after)], inc=[(c.s_dve, 1)])
            P.add("tensor", lambda t, o=c.psF[0:64, 6, :], l=ones64.ap()[64:65, 0:64],
                           r=stat.ap()[64:65, :]:
                  t.matmul(o, l, r, start=True, stop=True),
                  wait=[(c.s_dve, c.s_dve.n), (c.s_dve, c._pb0)],
                  inc=[(c.s_pe, 1)])
            P.add("vector", lambda v, o=stat.ap()[0:64, :], i=c.psF[0:64, 6, :]:
                  v.tensor_copy(o, i),
                  wait=[(c.s_pe, c.s_pe.n)], inc=[(c.s_dve, 1)])
            c._pb0 = c.s_dve.n
            P.add("vector", lambda v, o=ynT.ap()[p0_:p0_ + 64, chunk_, :],
                           a=c.psF[0:64, avb_, :], b2=stat.ap()[0:64, :]:
                  v.tensor_mul(o, a, b2),
                  wait=[(c.s_dve, c.s_dve.n)], inc=[(c.s_dve, 1)])
            ynd[avb_ - 2] = c.s_dve.n

        SBPAIR = ((0, 1), (4, 5))
        pending_norms = []
        exps_done = 0
        for hp in range(8):
            av_q = []
            for kt in range(NST):
                pair = SBPAIR[kt % 2]
                for hpar in range(2):
                    p0 = hpar * 64
                    sbank = pair[hpar]
                    wt = []
                    if kt == 0 and hp == 0 and hpar == 0:
                        wt.append((c.s_act, qT_ready))
                        wt.append((c.s_dve, last_adds))
                    if hpar == 0:
                        wt.append((c.s_act, bankS[kt % 2]))
                    P.add("tensor",
                          lambda t, o=c.psF[:, sbank, :],
                                 l=c.kT.ap()[p0:p0 + 64, hp, kt * 128:(kt + 1) * 128],
                                 r=qT.ap()[p0:p0 + 64, hp, :]:
                          t.matmul(o, l, r, start=True, stop=True),
                          wait=wt, inc=[(c.s_pe, 1)])
                P.add("scalar",
                      lambda s, o=Pp.ap()[:, 2 * (kt % 2):2 * (kt % 2) + 2, :].rearrange("p a b -> p (a b)"),
                             i=c.psF[:, pair[0]:pair[0] + 2, :].rearrange("p a b -> p (a b)"):
                      s.activation(o, i, AF.Exp, scale=0.125),
                      wait=[(c.s_pe, c.s_pe.n)], inc=[(c.s_act, 1)])
                bankS[kt % 2] = c.s_act.n
                exp_done = c.s_act.n
                if kt == 1 and pending_norms:
                    for pn in pending_norms:
                        emit_norm0(*pn)
                    pending_norms = []
                for hpar in range(2):
                    avb = 2 + hpar
                    avw = [(c.s_act, exp_done)]
                    av_q.append((
                        (lambda t, o=c.psF[0:65, avb, :],
                                l=c.v0.ap()[:, kt, 2 * hp + hpar, :],
                                r=Pp.ap()[:, 2 * (kt % 2) + hpar, :],
                                st=(kt == 0), sp=(kt == NST - 1):
                         t.matmul(o, l, r, start=st, stop=sp)),
                        avw, [(c.s_pe, 1)] if kt == NST - 1 else (),
                        hpar if kt == 0 else None))
                while len(av_q) > 2:
                    a = av_q.pop(0)
                    w = list(a[1])
                    if a[3] is not None:
                        w.append((c.s_dve, ynd[a[3]]))
                    P.add("tensor", a[0], wait=w, inc=a[2])
            for a in av_q:
                w = list(a[1])
                if a[3] is not None:
                    w.append((c.s_dve, ynd[a[3]]))
                P.add("tensor", a[0], wait=w, inc=a[2])
            exps_done = c.s_act.n
            pending_norms = [(2, 0, hp, c.s_pe.n, c.s_act.n),
                             (3, 64, hp, c.s_pe.n, c.s_act.n)]
        for pn in pending_norms:
            emit_norm0(*pn)
        pending_norms = []
        c._exps0 = exps_done
        heads_done = c.s_dve.n

        for st4 in range(4):
            row0 = qc * SCW + st4 * 128
            P.dma("sync", x0r.ap(), c.x[row0:row0 + 128, :], c.s_res,
                  wait=[(c.s_dve, max(0, c.s_dve.n - 1))])
            for half in range(2):
                bank = half
                for hc in range(8):
                    wt = []
                    if hc == 0:
                        wt.append((c.s_dve, heads_done))
                        wt.append((c.s_dve, max(0, c.s_dve.n - 2)))
                    P.add("tensor",
                          lambda t, o=c.psF[:, bank, :],
                                 l=ynT.ap()[:, hc, st4 * 128:(st4 + 1) * 128],
                                 r=woT.ap()[:, hc, half * 512:(half + 1) * 512],
                                 st=(hc == 0), sp=(hc == 7):
                          t.matmul(o, l, r, start=st, stop=sp),
                          wait=wt, inc=[(c.s_pe, 1)] if hc == 7 else ())
            pe_done = c.s_pe.n
            b = st4 % 2
            t_ap = tt.ap()
            P.add("vector", lambda v, o=t_ap[:, 0:512], i=c.psF[:, 0, :],
                           r=x0r.ap()[:, 0:512]:
                  v.tensor_add(o, i, r),
                  wait=[(c.s_pe, pe_done), (c.s_res, c.s_res.n),
                        (c.s_act, c.s_act.n)])
            P.add("vector", lambda v, o=t_ap[:, 512:1024], i=c.psF[:, 1, :],
                           r=x0r.ap()[:, 512:1024]:
                  v.tensor_add(o, i, r), inc=[(c.s_dve, 1)])
            bankO = c.s_dve.n
            last_adds = c.s_dve.n
            P.add("vector", lambda v, o=t_ap, i=bocast.ap():
                  v.tensor_add(o, o, i),
                  wait=[(c.s_dve, c.s_dve.n)], inc=[(c.s_dve, 1)])
            if "l0attn" in DBG and qc == 0 and st4 == 0:
                dbg_dump(c, "t0", t_ap, [128, D], F32)
            emit_finalize_tile(c, t_ap, stat.ap(), stats.ap(), xn[b].ap(),
                               row0, c.x1f, c.x1b, xn_single=True)
        if "l0attn" in DBG and qc == 0:
            dbg_dump(c, "qT", qT.ap(), [128, 8, SCW], BF16)
            dbg_dump(c, "ynT", ynT.ap(), [128, 8, SCW], BF16)

    for t in (wqT, woT, bocast, ones64, qT, xtq, Pp, ynT, stat, x0r,
              tt, xn1b, stats, c.kT, c.v0):
        sb.freeit(t)
    P.barrier("l0attn_done")


def emit_finish(c):
    P = c.P
    P.add("scalar", wait=[(c.s_st, c.s_st.n)])
    P.add("gpsimd", wait=[(c.s_stg_o, c.s_stg_o.n)])
    P.add("sync", wait=[(c.s_st, c.s_st.n), (c.s_stg_o, c.s_stg_o.n)])


def build_nc(phases, taps=False):
    nc = bass.Bass("TRN2", target_bir_lowering=False, debug=False)
    c = make_ctx(nc, taps=taps)
    for ph in phases:
        ph(c)
    emit_finish(c)
    with nc.Block() as block:
        @block.tensor
        def _(eng):
            c.P.replay("tensor", eng)
        @block.vector
        def _(eng):
            c.P.replay("vector", eng)
        @block.scalar
        def _(eng):
            c.P.replay("scalar", eng)
        @block.gpsimd
        def _(eng):
            c.P.replay("gpsimd", eng)
        @block.sync
        def _(eng):
            c.P.replay("sync", eng)
    return nc, c


def make_inmap(inputs, core):
    hc = host_consts()
    a = lambda k: np.ascontiguousarray(inputs[k])
    rs = lambda v: np.ascontiguousarray(np.asarray(v).reshape(-1, 128).T)
    m = dict(
        x=np.ascontiguousarray(inputs["L0_attn_x"][core]),
        wq0=a("L0_attn_w_q"), wk0=a("L0_attn_w_k"), wv0=a("L0_attn_w_v"),
        wo0=a("L0_attn_w_o"), bo0=a("L0_attn_b_o"),
        W1=a("L0_ffn_W1"),
        b1=np.ascontiguousarray(np.asarray(inputs["L0_ffn_b1"]).reshape(32, 128).T),
        W2=a("L0_ffn_W2"), b2=a("L0_ffn_b2"),
        scale0=np.ascontiguousarray(inputs["L0_ffn_scale"]).reshape(1, 1),
        wq1=a("L1_attn_w_q"), wk1=a("L1_attn_w_k"), wv1=a("L1_attn_w_v"),
        wo1=a("L1_attn_w_o"),
        w1w=a("L1_ffn_w1_w"), w1s=rs(inputs["L1_ffn_w1_s"]),
        w2w=a("L1_ffn_w2_w"), w2s=rs(inputs["L1_ffn_w2_s"]),
        h1w=a("L1_ffn_h1_w"), h1s=rs(inputs["L1_ffn_h1_s"]),
        h2w=a("L1_ffn_h2_w"), h2s=rs(inputs["L1_ffn_h2_s"]),
        w3w=a("L1_ffn_w3_w"), w3s=a("L1_ffn_w3_s"),
        cos=hc["cos"], sin=hc["sin"], tri=hc["tri"], eye=hc["eye"],
    )
    return m


DBG = set()


def dbg_dump(c, name, ap, shape, dtype):
    t = c.nc.dram_tensor("dbg_" + name, list(shape), dtype, kind="ExternalOutput").ap()
    P = c.P
    P.dma("sync", t, ap, c.s_ldx,
          wait=[(c.s_pe, c.s_pe.n), (c.s_act, c.s_act.n), (c.s_dve, c.s_dve.n)])
    for e in ENGINES:
        P.pending[e].append((c.s_ldx, c.s_ldx.n))


def kernel(**inputs):
    phases = [emit_stage, emit_l0kv, emit_l0attn, emit_l0ffnw, emit_l0ffn,
              emit_l1prep, emit_l1attn, emit_l1ffnA, emit_l1ffnB, emit_l1ffnC,
              emit_l1ffnD]
    nc, c = build_nc(phases, taps=False)
    in_maps = [make_inmap(inputs, core) for core in range(8)]
    res = run_bass_kernel_spmd(nc, in_maps, core_ids=list(range(8)))
    out = np.stack([res.results[i]["y"] for i in range(8)], axis=0)
    return out.astype(np.float32)


def emit_l0ffnw(c):
    P, sb = c.P, c.sb
    c.W1T = sb.alloc("W1T", [128, 8, HFF], BF16)
    c.W2T = sb.alloc("W2T", [128, 32, D], BF16)
    c.b1t = sb.alloc("b1t", [128, 32], F32)
    c.b2cast = sb.alloc("b2cast", [128, D], F32)
    xre = sb.alloc("xre", [1, D], F32)
    xro = sb.alloc("xro", [1, D], F32)
    obf = sb.alloc("obf", [1, D], BF16)
    sev = sb.alloc("sev", [1, D], BF16)
    ev4 = sb.alloc("ev4", [1, 256], BF16)
    scs = sb.alloc("scs", [1, 1], F32)

    wait_stg = ((c.s_stg_f0, c.s_stg_f0.n),)
    load_wT(c, c.W1T, c.st_W1, 8, wait_stg)
    load_wT(c, c.W2T, c.st_W2, 32, wait_stg)
    ldw_done = c.s_ldw.n
    P.dma("sync", c.b1t.ap(), c.b1, c.s_ldw)
    P.dma("sync", c.b2cast.ap(), c.b2[None, :].broadcast_to([128, D]), c.s_ldw)
    P.dma("sync", scs.ap(), c.scale0, c.s_ldw)
    sd = c.s_dve

    def rows_prep(tok_e, tok_o):
        P.dma("sync", xre.ap(), c.x1f[tok_e:tok_e + 1, :], c.s_res,
              wait=[(sd, sd.n)])
        P.dma("sync", xro.ap(), c.x1f[tok_o:tok_o + 1, :], c.s_res)
        P.add("vector", lambda v: v.tensor_copy(obf.ap(), xro.ap()),
              wait=[(c.s_res, c.s_res.n), (c.s_ldw, ldw_done + 48),
                    (c.s_pe, c.s_pe.n)], inc=[(sd, 1)])
        P.add("vector", lambda v: v.tensor_scalar(sev.ap(), xre.ap(),
                                                  scs.ap()[0:1, 0:1], None, ALU.mult),
              inc=[(sd, 1)])

    for p in range(4):
        rows_prep(1 + 2 * p, 2 + 2 * p)
        rows_ready = sd.n
        for dc in range(8):
            for half in range(2):
                bank = (dc * 2 + half) % 2
                wt = [(sd, rows_ready)]
                wt.append((sd, max(0, sd.n - 1)))
                P.add("tensor",
                      lambda t, o=c.psF[:, bank, :], l=obf.ap()[0:1, dc * 128:(dc + 1) * 128],
                             r=sev.ap()[0:1, half * 512:(half + 1) * 512]:
                      t.matmul(o, l, r, start=True, stop=True),
                      wait=wt, inc=[(c.s_pe, 1)])
                dst = c.W1T.ap()[:, dc, p * 1024 + half * 512: p * 1024 + (half + 1) * 512]
                P.add("vector",
                      lambda v, o=dst, i=c.psF[:, bank, :]:
                      v.tensor_add(o, o, i),
                      wait=[(c.s_pe, c.s_pe.n)], inc=[(sd, 1)])
    for p in range(4):
        rows_prep(9 + 2 * p, 10 + 2 * p)
        for k in range(4):
            P.add("vector",
                  lambda v, o=ev4.ap(), i=sev.ap().rearrange("a (q f) -> a q f", f=4)[0:1, :, k]:
                  v.tensor_copy(o, i),
                  wait=[(sd, sd.n), (c.s_pe, c.s_pe.n)], inc=[(sd, 1)])
            ev_ready = sd.n
            for jc in range(8):
                bank = jc % 2
                wt = [(sd, ev_ready), (sd, max(0, sd.n - 1))]
                P.add("tensor",
                      lambda t, o=c.psF[:, bank, 0:256], l=obf.ap()[0:1, jc * 128:(jc + 1) * 128],
                             r=ev4.ap():
                      t.matmul(o, l, r, start=True, stop=True),
                      wait=wt, inc=[(c.s_pe, 1)])
                dst = c.W2T.ap()[:, 8 * k + jc, 256 * p: 256 * (p + 1)]
                P.add("vector",
                      lambda v, o=dst, i=c.psF[:, bank, 0:256]:
                      v.tensor_add(o, o, i),
                      wait=[(c.s_pe, c.s_pe.n)], inc=[(sd, 1)])

    for t in (xre, xro, obf, sev, ev4, scs):
        sb.freeit(t)
    P.barrier("l0ffnw_done")


def emit_l0ffn(c):
    P, sb = c.P, c.sb
    h_sb = sb.alloc("h0", [128, 32, SCW], BF16)
    xt = [sb.alloc(f"xtf{i}", [128, 8, SCW], BF16) for i in range(2)]
    x1r = sb.alloc("x1r", [128, D], F32)
    tt = sb.alloc("ttf", [128, D], F32)
    xn = [sb.alloc(f"xnf{i}", [128, D], F32) for i in range(2)]
    stats = sb.alloc("statsf", [128, 12], F32)
    sqs = sb.alloc("sqsf", [128, 512], F32)

    xt_free = [None, None]
    w2_done = None
    bankH = [0, 0]
    bankO2 = 0
    for sc8 in range(NSC):
        b = sc8 % 2
        w = [] if xt_free[b] is None else [(c.s_pe, xt_free[b])]
        load_xT(c, xt[b].ap(), c.x1b, sc8 * SCW, SCW, w)
        ldx_done = c.s_ldx.n
        for hc in range(32):
            bank = hc % 2
            for dc in range(8):
                wt = []
                if hc == 0 and dc == 0:
                    wt.append((c.s_ldx, ldx_done))
                if dc == 0:
                    wt.append((c.s_act, bankH[bank]))
                P.add("tensor",
                      lambda t, o=c.psF[:, bank, :], l=c.W1T.ap()[:, dc, hc * 128:(hc + 1) * 128],
                             r=xt[b].ap()[:, dc, :], st=(dc == 0), sp=(dc == 7):
                      t.matmul(o, l, r, start=st, stop=sp),
                      wait=wt, inc=[(c.s_pe, 1)] if dc == 7 else ())
            gw = [(c.s_pe, c.s_pe.n)]
            if w2_done is not None:
                gw.append((c.s_pe, w2_done))
            P.add("scalar",
                  lambda s, o=h_sb.ap()[:, hc, :], i=c.psF[:, bank, :],
                         bb=c.b1t.ap()[:, hc:hc + 1]:
                  s.activation(o, i, AF.Gelu, bias=bb),
                  wait=gw, inc=[(c.s_act, 1)])
            bankH[bank] = c.s_act.n
        h_done = c.s_act.n
        xt_free[b] = c.s_pe.n
        for st4 in range(4):
            row0 = sc8 * SCW + st4 * 128
            P.dma("sync", x1r.ap(), c.x1f[row0:row0 + 128, :], c.s_res,
                  wait=[(c.s_dve, max(0, c.s_dve.n - 1))])
            for half in range(2):
                bank = 2 + half
                for hc in range(32):
                    wt = []
                    if hc == 0:
                        wt.append((c.s_act, h_done))
                        wt.append((c.s_dve, bankO2))
                    P.add("tensor",
                          lambda t, o=c.psF[:, bank, :],
                                 l=h_sb.ap()[:, hc, st4 * 128:(st4 + 1) * 128],
                                 r=c.W2T.ap()[:, hc, half * 512:(half + 1) * 512],
                                 st=(hc == 0), sp=(hc == 31):
                          t.matmul(o, l, r, start=st, stop=sp),
                          wait=wt, inc=[(c.s_pe, 1)] if hc == 31 else ())
            pe_done = c.s_pe.n
            bx = st4 % 2
            t_ap = tt.ap()
            P.add("vector", lambda v, o=t_ap[:, 0:512], i=c.psF[:, 2, :],
                           r=x1r.ap()[:, 0:512]:
                  v.tensor_add(o, i, r),
                  wait=[(c.s_pe, pe_done), (c.s_res, c.s_res.n),
                        (c.s_act, c.s_act.n)])
            P.add("vector", lambda v, o=t_ap[:, 512:1024], i=c.psF[:, 3, :],
                           r=x1r.ap()[:, 512:1024]:
                  v.tensor_add(o, i, r), inc=[(c.s_dve, 1)])
            bankO2 = c.s_dve.n
            P.add("vector", lambda v, o=t_ap, i=c.b2cast.ap():
                  v.tensor_add(o, o, i),
                  wait=[(c.s_dve, c.s_dve.n)], inc=[(c.s_dve, 1)])
            emit_finalize_tile(c, t_ap, sqs.ap(), stats.ap(), xn[bx].ap(),
                               row0, c.x2f, c.x2b)
        w2_done = c.s_pe.n

    for t in (h_sb, xt[0], xt[1], x1r, tt, xn[0], xn[1], stats, sqs,
              c.W1T, c.W2T, c.b1t, c.b2cast):
        sb.freeit(t)
    P.barrier("l0ffn_done")


def emit_l1prep(c):
    P, sb = c.P, c.sb
    c.q1T = sb.alloc("q1T", [128, 8, S], BF16)
    c.k1T = sb.alloc("k1T", [128, 2, S], BF16)
    c.v1 = sb.alloc("v1", [128, NST, NKV, 65], BF16)
    wq1s = sb.alloc("wq1s", [128, 8, D], BF16)
    wk1s = sb.alloc("wk1s", [128, 8, 256], BF16)
    wv1s = sb.alloc("wv1s", [128, 8, 256], BF16)
    eyes = sb.alloc("eyes", [128, 128], BF16)
    xt = [sb.alloc(f"xtp{i}", [128, 8, SCW], BF16) for i in range(2)]
    qf = sb.alloc("qf", [128, NH, DK], F32)
    kf = sb.alloc("kf", [128, NKV, DK], F32)
    qsq = sb.alloc("qsq", [128, D], F32)
    qn = sb.alloc("qn", [128, NH, DK], F32)
    kn = sb.alloc("kn", [128, NKV, DK], F32)
    qrot = [sb.alloc(f"qrot{i}", [128, D], BF16) for i in range(2)]
    krot = [sb.alloc(f"krot{i}", [128, 256], BF16) for i in range(2)]
    cosb = [sb.alloc(f"cosb{i}", [128, 32], F32) for i in range(2)]
    sinb = [sb.alloc(f"sinb{i}", [128, 32], F32) for i in range(2)]
    rstat = sb.alloc("rstat", [128, NH * 4], F32)

    sd = c.s_dve
    P.dma("gpsimd", wq1s.ap(), c.wq1.rearrange("(a p) n -> p a n", p=128), c.s_ldw)
    P.dma("gpsimd", wk1s.ap(), c.wk1.rearrange("(a p) n -> p a n", p=128), c.s_ldw)
    P.dma("gpsimd", wv1s.ap(), c.wv1.rearrange("(a p) n -> p a n", p=128), c.s_ldw)
    P.dma("sync", eyes.ap(), c.eye, c.s_ldw)
    ldw_done = c.s_ldw.n
    P.add("vector", lambda v: v.memset(c.v1.ap()[:, :, :, 64:65], 1.0), inc=[(sd, 1)])

    sumq = rstat.ap()[:, 0:16]; vq = rstat.ap()[:, 16:32]
    sumk = rstat.ap()[:, 32:36]; vk = rstat.ap()[:, 36:40]
    rq = rstat.ap()[:, 40:56]; rk = rstat.ap()[:, 56:60]

    xt_free = [None, None]
    prev_tr_evac = 0
    pending_tr = [None]
    tr_pe = {0: 0, 1: 0}

    def emit_tr(stv, buf, rope_val):
        for ec in range(8):
            wt = [(sd, rope_val), (sd, prev_tr_evac)] if ec == 0 else []
            P.add("tensor",
                  lambda t, o=c.psT[:, 0, ec * 128:(ec + 1) * 128],
                         i=qrot[buf].ap()[:, ec * 128:(ec + 1) * 128], e=eyes.ap():
                  t.transpose(o, i, e),
                  wait=wt, inc=[(c.s_pe, 1)] if ec == 7 else ())
        P.add("vector",
              lambda v, o=c.q1T.ap()[:, :, stv * 128:(stv + 1) * 128],
                     i=c.psT[:, 0, :].rearrange("p (a b) -> p a b", b=128):
              v.tensor_copy(o, i),
              wait=[(c.s_pe, c.s_pe.n)], inc=[(sd, 1)])
        qev = sd.n
        for ec in range(2):
            wt = [(sd, qev)] if ec == 0 else []
            P.add("tensor",
                  lambda t, o=c.psT[:, 0, ec * 128:(ec + 1) * 128],
                         i=krot[buf].ap()[:, ec * 128:(ec + 1) * 128], e=eyes.ap():
                  t.transpose(o, i, e),
                  wait=wt, inc=[(c.s_pe, 1)] if ec == 1 else ())
        tr_pe[buf] = c.s_pe.n
        P.add("vector",
              lambda v, o=c.k1T.ap()[:, :, stv * 128:(stv + 1) * 128],
                     i=c.psT[:, 0, 0:256].rearrange("p (a b) -> p a b", b=128):
              v.tensor_copy(o, i),
              wait=[(c.s_pe, c.s_pe.n)], inc=[(sd, 1)])
        return sd.n

    for st in range(NST):
        sc8, off = st // 4, (st % 4) * 128
        cb = st % 2
        if st % 4 == 0:
            fa = xt_free[(st // 4) % 2]
            w = [(c.s_ldw, ldw_done)] if st == 0 else                 ([(c.s_pe, fa)] if fa is not None else [])
            load_xT(c, xt[(st // 4) % 2].ap(), c.x2b, sc8 * SCW, SCW, w)
        xb = xt[(st // 4) % 2]
        ldx_done = c.s_ldx.n
        P.dma("sync", cosb[cb].ap(), c.cos[st * 128:(st + 1) * 128, :], c.s_res,
              wait=[(sd, sd.n)])
        P.dma("sync", sinb[cb].ap(), c.sin[st * 128:(st + 1) * 128, :], c.s_res)
        res_done = c.s_res.n

        for half in range(2):
            for dc in range(8):
                wt = []
                if half == 0 and dc == 0:
                    wt = [(c.s_ldx, ldx_done), (sd, max(0, sd.n - 2))]
                P.add("tensor",
                      lambda t, o=c.psF[:, half, :], l=xb.ap()[:, dc, off:off + 128],
                             r=wq1s.ap()[:, dc, half * 512:(half + 1) * 512],
                             st_=(dc == 0), sp=(dc == 7):
                      t.matmul(o, l, r, start=st_, stop=sp),
                      wait=wt, inc=[(c.s_pe, 1)] if dc == 7 else ())
        for (bank, wsb) in ((2, wk1s), (3, wv1s)):
            for dc in range(8):
                P.add("tensor",
                      lambda t, o=c.psF[:, bank, 0:256], l=xb.ap()[:, dc, off:off + 128],
                             r=wsb.ap()[:, dc, :], st_=(dc == 0), sp=(dc == 7):
                      t.matmul(o, l, r, start=st_, stop=sp),
                      wait=(), inc=[(c.s_pe, 1)] if dc == 7 else ())
        proj_done = c.s_pe.n
        if pending_tr[0] is not None:
            prev_tr_evac = emit_tr(*pending_tr[0])
            pending_tr[0] = None
        qv = qf.ap().rearrange("p (a b) d -> p a b d", b=2)
        pv0 = c.psF[:, 0, :].rearrange("p (h d) -> p h d", d=DK)
        pv1 = c.psF[:, 1, :].rearrange("p (h d) -> p h d", d=DK)
        P.add("vector", lambda v, o=qv[:, 0:4, 0, :], i=pv0[:, 0:4, :]: v.tensor_copy(o, i),
              wait=[(c.s_pe, proj_done)])
        P.add("vector", lambda v, o=qv[:, 0:4, 1, :], i=pv0[:, 4:8, :]: v.tensor_copy(o, i))
        P.add("vector", lambda v, o=qv[:, 4:8, 0, :], i=pv1[:, 0:4, :]: v.tensor_copy(o, i))
        P.add("vector", lambda v, o=qv[:, 4:8, 1, :], i=pv1[:, 4:8, :]: v.tensor_copy(o, i))
        P.add("vector", lambda v, o=kf.ap(), i=c.psF[:, 2, 0:256].rearrange("p (h d) -> p h d", d=DK):
              v.tensor_copy(o, i))
        P.add("vector", lambda v, o=c.v1.ap()[:, st, :, 0:64],
                       i=c.psF[:, 3, 0:256].rearrange("p (h d) -> p h d", d=DK):
              v.tensor_copy(o, i), inc=[(sd, 1)])
        evac_done = sd.n

        P.add("scalar", lambda s: s.activation(qsq.ap(), qf.ap().rearrange("p a b -> p (a b)"),
                                               AF.Square),
              wait=[(sd, evac_done)], inc=[(c.s_act, 1)])
        P.add("vector", lambda v: v.reduce_sum(sumq.rearrange("p (a b) -> p a b", b=1),
                                               qsq.ap().rearrange("p (a b) -> p a b", b=DK),
                                               axis=AX.X),
              wait=[(c.s_act, c.s_act.n)], inc=[(sd, 1)])
        P.add("scalar", lambda s: s.activation(qsq.ap()[:, 0:256],
                                               kf.ap().rearrange("p a b -> p (a b)"),
                                               AF.Square),
              wait=[(sd, sd.n)], inc=[(c.s_act, 1)])
        P.add("vector", lambda v: v.reduce_sum(sumk.rearrange("p (a b) -> p a b", b=1),
                                               qsq.ap()[:, 0:256].rearrange("p (a b) -> p a b", b=DK),
                                               axis=AX.X),
              wait=[(c.s_act, c.s_act.n)], inc=[(sd, 1)])
        P.add("vector", lambda v: v.tensor_scalar(vq, sumq, 1.0 / DK, EPS, ALU.mult, ALU.add),
              wait=[(sd, sd.n)])
        P.add("vector", lambda v: v.tensor_scalar(vk, sumk, 1.0 / DK, EPS, ALU.mult, ALU.add),
              inc=[(sd, 1)])
        P.add("scalar", lambda s: s.activation(vq, vq, AF.Ln), wait=[(sd, sd.n)])
        P.add("scalar", lambda s: s.activation(rq, vq, AF.Exp, scale=-0.5))
        P.add("scalar", lambda s: s.activation(vk, vk, AF.Ln))
        P.add("scalar", lambda s: s.activation(rk, vk, AF.Exp, scale=-0.5),
              inc=[(c.s_act, 1)])
        P.add("vector", lambda v: v.tensor_tensor(
              qn.ap(), qf.ap(),
              rq.rearrange("p (a b) -> p a b", b=1).broadcast_to([128, NH, DK]), ALU.mult),
              wait=[(c.s_act, c.s_act.n)])
        P.add("vector", lambda v: v.tensor_tensor(
              kn.ap(), kf.ap(),
              rk.rearrange("p (a b) -> p a b", b=1).broadcast_to([128, NKV, DK]), ALU.mult),
              inc=[(sd, 1)])
        t1q = qsq.ap()[:, 0:512].rearrange("p (a b) -> p a b", b=32)
        t2q = qsq.ap()[:, 512:1024].rearrange("p (a b) -> p a b", b=32)
        cq = cosb[cb].ap().rearrange("p (a b) -> p a b", a=1).broadcast_to([128, NH, 32])
        sq_ = sinb[cb].ap().rearrange("p (a b) -> p a b", a=1).broadcast_to([128, NH, 32])
        ck = cosb[cb].ap().rearrange("p (a b) -> p a b", a=1).broadcast_to([128, NKV, 32])
        sk = sinb[cb].ap().rearrange("p (a b) -> p a b", a=1).broadcast_to([128, NKV, 32])
        qnv = qn.ap(); knv = kn.ap()
        qrv = qrot[st % 2].ap().rearrange("p (a b) -> p a b", b=DK)
        krv = krot[st % 2].ap().rearrange("p (a b) -> p a b", b=DK)
        P.add("vector", lambda v, a=t1q, b=cq: v.tensor_tensor(a, qnv[:, :, 0:32], b, ALU.mult),
              wait=[(sd, sd.n), (c.s_res, res_done), (c.s_pe, tr_pe[st % 2])])
        P.add("vector", lambda v, a=t2q, b=sq_: v.tensor_tensor(a, qnv[:, :, 32:64], b, ALU.mult),
              inc=[(sd, 1)])
        P.add("vector", lambda v, a=t1q, b=t2q, o=qrv[:, :, 0:32]: v.tensor_sub(o, a, b),
              wait=[(sd, sd.n)])
        P.add("vector", lambda v, a=t1q, b=cq: v.tensor_tensor(a, qnv[:, :, 32:64], b, ALU.mult))
        P.add("vector", lambda v, a=t2q, b=sq_: v.tensor_tensor(a, qnv[:, :, 0:32], b, ALU.mult),
              inc=[(sd, 1)])
        P.add("vector", lambda v, a=t1q, b=t2q, o=qrv[:, :, 32:64]: v.tensor_add(o, a, b),
              wait=[(sd, sd.n)])
        t1k = qsq.ap()[:, 0:128].rearrange("p (a b) -> p a b", b=32)
        t2k = qsq.ap()[:, 128:256].rearrange("p (a b) -> p a b", b=32)
        P.add("vector", lambda v, a=t1k, b=ck: v.tensor_tensor(a, knv[:, :, 0:32], b, ALU.mult))
        P.add("vector", lambda v, a=t2k, b=sk: v.tensor_tensor(a, knv[:, :, 32:64], b, ALU.mult),
              inc=[(sd, 1)])
        P.add("vector", lambda v, a=t1k, b=t2k, o=krv[:, :, 0:32]: v.tensor_sub(o, a, b),
              wait=[(sd, sd.n)])
        P.add("vector", lambda v, a=t1k, b=ck: v.tensor_tensor(a, knv[:, :, 32:64], b, ALU.mult))
        P.add("vector", lambda v, a=t2k, b=sk: v.tensor_tensor(a, knv[:, :, 0:32], b, ALU.mult),
              inc=[(sd, 1)])
        P.add("vector", lambda v, a=t1k, b=t2k, o=krv[:, :, 32:64]: v.tensor_add(o, a, b),
              wait=[(sd, sd.n)], inc=[(sd, 1)])
        rope_done = sd.n
        pending_tr[0] = (st, st % 2, rope_done)
        xt_free[(st // 4) % 2] = c.s_pe.n
        if "l1prepd" in DBG and st == 0:
            dbg_dump(c, "qf", qf.ap(), [128, NH, DK], F32)
            dbg_dump(c, "qn", qn.ap(), [128, NH, DK], F32)
            dbg_dump(c, "kf", kf.ap(), [128, NKV, DK], F32)
            dbg_dump(c, "rstat", rstat.ap(), [128, NH * 4], F32)

    prev_tr_evac = emit_tr(*pending_tr[0])
    pending_tr[0] = None
    if "l1prep" in DBG:
        dbg_dump(c, "q1T", c.q1T.ap(), [128, 8, S], BF16)
        dbg_dump(c, "k1T", c.k1T.ap(), [128, 2, S], BF16)
        dbg_dump(c, "v1", c.v1.ap(), [128, NST, NKV, 65], BF16)
    for t in (wq1s, wk1s, wv1s, eyes, xt[0], xt[1], qf, kf, qsq, qn, kn,
              qrot[0], qrot[1], krot[0], krot[1], cosb[0], cosb[1],
              sinb[0], sinb[1], rstat):
        sb.freeit(t)
    P.barrier("l1prep_done")


def emit_l1attn(c):
    P, sb = c.P, c.sb
    wo1p = sb.alloc("wo1p", [128, 8, D], BF16)
    trim = sb.alloc("trim", [128, 128], BF16)
    Pp = sb.alloc("Pp1", [128, 4, SCW], BF16)
    ynT = sb.alloc("ynT1", [128, 8, SCW], BF16)
    stat = sb.alloc("stat1", [128, SCW], F32)
    tcap = sb.alloc("tcap", [128, 2 * SCW], F32)
    ones64 = sb.alloc("ones64b", [128, 64], F32)
    x2r = sb.alloc("x2r", [128, D], F32)
    tt = sb.alloc("tt1", [128, D], F32)
    xn = [sb.alloc(f"xn1{i}", [128, D], F32) for i in range(2)]
    stats = sb.alloc("stats1", [128, 12], F32)
    sqs = sb.alloc("sqs1", [128, 512], F32)

    sd = c.s_dve
    for p in range(NH):
        h = PERM1[p]
        b0 = (p % 2) * 64
        P.dma("gpsimd", wo1p.ap()[b0:b0 + 64, p // 2, :],
              c.wo1[h * 64:(h + 1) * 64, :], c.s_ldw)
    P.dma("sync", trim.ap(), c.tri, c.s_ldw)
    ldw_done = c.s_ldw.n
    P.add("vector", lambda v: v.memset(ones64.ap(), 1.0), inc=[(sd, 1)])

    ynmul_done = {0: 0, 1: 0}
    c._pb1 = 0
    bankS1 = [0]
    bankO1 = 0
    last_adds1 = 0

    def emit_norm1(avb_, p0_, chunk_, av_stop, act_after):
        P.add("vector", lambda v, o=stat.ap()[64:65, :], i=c.psF[64:65, avb_, :]:
              v.reciprocal(o, i),
              wait=[(c.s_pe, av_stop), (c.s_act, act_after)], inc=[(sd, 1)])
        P.add("tensor", lambda t, o=c.psF[0:64, 5, :], l=ones64.ap()[64:65, 0:64],
                       r=stat.ap()[64:65, :]:
              t.matmul(o, l, r, start=True, stop=True),
              wait=[(sd, sd.n), (sd, c._pb1)],
              inc=[(c.s_pe, 1)])
        P.add("vector", lambda v, o=stat.ap()[0:64, :], i=c.psF[0:64, 5, :]:
              v.tensor_copy(o, i),
              wait=[(c.s_pe, c.s_pe.n)], inc=[(sd, 1)])
        c._pb1 = sd.n
        P.add("vector", lambda v, o=ynT.ap()[p0_:p0_ + 64, chunk_, :],
                       a=c.psF[0:64, avb_, :], b2=stat.ap()[0:64, :]:
              v.tensor_mul(o, a, b2),
              wait=[(sd, sd.n)], inc=[(sd, 1)])
        ynmul_done[avb_ - 2] = sd.n

    for qc in range(NSC):
        pending_norm = None
        for pos in range(NH):
            h = PERM1[pos]
            p0 = (pos % 2) * 64
            chunk = pos // 2
            j = h // 4
            jchunk = j // 2
            avb = 2 + pos % 2
            kt_last = min(NST - 1, 4 * qc + 3)
            av_q = []
            for pr in range((kt_last + 1) // 2):
                for sbank in range(2):
                    kt = 2 * pr + sbank
                    m = kt - 4 * qc
                    col0 = 128 * m if m > 0 else 0
                    wt = []
                    if kt == 0 and pos == 0 and qc == 0:
                        wt.append((c.s_ldw, ldw_done))
                    if kt == 0 and pos == 0:
                        wt.append((sd, last_adds1))
                    if sbank == 0:
                        wt.append((c.s_act, bankS1[0]))
                    P.add("tensor",
                          lambda t, o=c.psF[:, sbank, col0:512],
                                 l=c.k1T.ap()[p0:p0 + 64, jchunk, kt * 128:(kt + 1) * 128],
                                 r=c.q1T.ap()[p0:p0 + 64, chunk, qc * 512 + col0: (qc + 1) * 512]:
                          t.matmul(o, l, r, start=True, stop=True),
                          wait=wt, inc=[(c.s_pe, 1)])
                sl = 2 * (pr % 2)
                P.add("scalar",
                      lambda s, o=tcap.ap(), i=c.psF[:, 0:2, :].rearrange("p a b -> p (a b)"):
                      s.activation(o, i, AF.Tanh, scale=1.0 / (8.0 * CAP)),
                      wait=[(c.s_pe, c.s_pe.n)], inc=[(c.s_act, 1)])
                bankS1[0] = c.s_act.n
                P.add("scalar",
                      lambda s, o=Pp.ap()[:, sl:sl + 2, :].rearrange("p a b -> p (a b)"),
                             i=tcap.ap():
                      s.activation(o, i, AF.Exp, scale=CAP),
                      inc=[(c.s_act, 1)])
                exp_done = c.s_act.n
                if pr == 1 and pending_norm is not None:
                    emit_norm1(*pending_norm)
                    pending_norm = None
                for kt in (2 * pr, 2 * pr + 1):
                    m = kt - 4 * qc
                    col0 = 128 * m if m > 0 else 0
                    masked = 0 <= m <= 3
                    slot = sl + kt % 2
                    if masked:
                        P.add("vector",
                              lambda v, o=Pp.ap()[:, slot, col0:col0 + 128], tm=trim.ap():
                              v.tensor_mul(o, o, tm),
                              wait=[(c.s_act, exp_done)], inc=[(sd, 1)])
                    avw = [(sd, sd.n)] if masked else [(c.s_act, exp_done)]
                    if kt == 0:
                        avw.append((sd, ynmul_done[avb - 2]))
                    av_q.append((
                        (lambda t, o=c.psF[0:65, avb, col0:512], l=c.v1.ap()[:, kt, j, :],
                                r=Pp.ap()[:, slot, col0:512], st=(kt == 0), sp=(kt == kt_last):
                         t.matmul(o, l, r, start=st, stop=sp)),
                        avw, [(c.s_pe, 1)] if kt == kt_last else ()))
                while len(av_q) > 2:
                    a = av_q.pop(0)
                    P.add("tensor", a[0], wait=a[1], inc=a[2])
            for a in av_q:
                P.add("tensor", a[0], wait=a[1], inc=a[2])
            pending_norm = (avb, p0, chunk, c.s_pe.n, c.s_act.n)
        emit_norm1(*pending_norm)
        pending_norm = None
        heads_done = sd.n

        for st4 in range(4):
            row0 = qc * SCW + st4 * 128
            P.dma("sync", x2r.ap(), c.x2f[row0:row0 + 128, :], c.s_res,
                  wait=[(sd, max(0, sd.n - 1))])
            for half in range(2):
                bank = half
                for hc in range(8):
                    wt = []
                    if hc == 0:
                        wt.append((sd, heads_done))
                        wt.append((sd, bankO1))
                    P.add("tensor",
                          lambda t, o=c.psF[:, bank, :],
                                 l=ynT.ap()[:, hc, st4 * 128:(st4 + 1) * 128],
                                 r=wo1p.ap()[:, hc, half * 512:(half + 1) * 512],
                                 st=(hc == 0), sp=(hc == 7):
                          t.matmul(o, l, r, start=st, stop=sp),
                          wait=wt, inc=[(c.s_pe, 1)] if hc == 7 else ())
            pe_done = c.s_pe.n
            bx = st4 % 2
            t_ap = tt.ap()
            P.add("vector", lambda v, o=t_ap[:, 0:512], i=c.psF[:, 0, :],
                           r=x2r.ap()[:, 0:512]:
                  v.tensor_add(o, i, r),
                  wait=[(c.s_pe, pe_done), (c.s_res, c.s_res.n),
                        (c.s_act, c.s_act.n)])
            P.add("vector", lambda v, o=t_ap[:, 512:1024], i=c.psF[:, 1, :],
                           r=x2r.ap()[:, 512:1024]:
                  v.tensor_add(o, i, r), inc=[(sd, 1)])
            bankO1 = sd.n
            last_adds1 = sd.n
            if "l1attn" in DBG and qc == 1 and st4 == 0:
                dbg_dump(c, "t1", t_ap, [128, D], F32)
            emit_finalize_tile(c, t_ap, sqs.ap(), stats.ap(), xn[bx].ap(),
                               row0, c.x3f, c.x3b)
        if "l1attn" in DBG and qc == 1:
            dbg_dump(c, "ynT1", ynT.ap(), [128, 8, SCW], BF16)

    for t in (wo1p, trim, Pp, ynT, stat, tcap, ones64, x2r, tt, xn[0], xn[1],
              stats, sqs, c.q1T, c.k1T, c.v1):
        sb.freeit(t)
    P.barrier("l1attn_done")


def emit_l1ffnA(c):
    P, sb = c.P, c.sb
    w1T = sb.alloc("w1T1", [128, 8, HFF], BF16)
    w2T = sb.alloc("w2T1", [128, 8, HFF], BF16)
    s1t = sb.alloc("s1t", [128, 32], F32)
    s2t = sb.alloc("s2t", [128, 32], F32)
    xt = [sb.alloc(f"xta{i}", [128, 8, SCW], BF16) for i in range(2)]
    u_sb = [sb.alloc(f"usb{i}", [128, SCW], F32) for i in range(2)]
    ha = sb.alloc("ha", [128, 32, SCW], BF16)

    sd = c.s_dve
    wait_stg = ((c.s_stg_f1, c.s_stg_f1.n),)
    load_wT(c, w1T, c.st_w1w, 8, wait_stg)
    load_wT(c, w2T, c.st_w2w, 8, wait_stg)
    P.dma("sync", s1t.ap(), c.w1s, c.s_ldw)
    P.dma("sync", s2t.ap(), c.w2s, c.s_ldw)
    ldw_done = c.s_ldw.n

    xt_free = [None, None]
    ha_store_done = 0
    bankU = [0, 0]
    bankG = [0, 0]
    ustt = [0, 0]
    for sc8 in range(NSC):
        b = sc8 % 2
        w = [] if xt_free[b] is None else [(c.s_pe, xt_free[b])]
        load_xT(c, xt[b].ap(), c.x3b, sc8 * SCW, SCW, w)
        ldx_done = c.s_ldx.n
        for hc in range(32):
            ub = hc % 2
            gb = 2 + hc % 2
            for dc in range(8):
                wt = []
                if hc == 0 and dc == 0:
                    wt = [(c.s_ldx, ldx_done), (c.s_ldw, ldw_done)]
                if dc == 0:
                    wt.append((c.s_act, bankU[ub]))
                P.add("tensor",
                      lambda t, o=c.psF[:, ub, :], l=w1T.ap()[:, dc, hc * 128:(hc + 1) * 128],
                             r=xt[b].ap()[:, dc, :], st=(dc == 0), sp=(dc == 7):
                      t.matmul(o, l, r, start=st, stop=sp),
                      wait=wt, inc=[(c.s_pe, 1)] if dc == 7 else ())
            for dc in range(8):
                wt = []
                if dc == 0:
                    wt.append((sd, bankG[gb - 2]))
                P.add("tensor",
                      lambda t, o=c.psF[:, gb, :], l=w2T.ap()[:, dc, hc * 128:(hc + 1) * 128],
                             r=xt[b].ap()[:, dc, :], st=(dc == 0), sp=(dc == 7):
                      t.matmul(o, l, r, start=st, stop=sp),
                      wait=wt, inc=[(c.s_pe, 1)] if dc == 7 else ())
            P.add("scalar",
                  lambda s, o=u_sb[hc % 2].ap(), i=c.psF[:, ub, :],
                         sc_=s1t.ap()[:, hc:hc + 1]:
                  s.activation(o, i, AF.Silu, scale=sc_),
                  wait=[(c.s_pe, c.s_pe.n), (sd, ustt[hc % 2])],
                  inc=[(c.s_act, 1)])
            bankU[ub] = c.s_act.n
            stw = [(c.s_act, c.s_act.n)]
            if hc == 0:
                stw.append((c.s_st, ha_store_done))
            P.add("vector",
                  lambda v, o=ha.ap()[:, hc, :], i=c.psF[:, gb, :],
                         sc_=s2t.ap()[:, hc:hc + 1], u=u_sb[hc % 2].ap():
                  v.scalar_tensor_tensor(o, i, sc_, u, ALU.mult, ALU.mult),
                  wait=stw, inc=[(sd, 1)])
            bankG[gb - 2] = sd.n
            ustt[hc % 2] = sd.n
        xt_free[b] = c.s_pe.n
        if "l1ffnA" in DBG and sc8 == 0:
            dbg_dump(c, "xta", xt[b].ap(), [128, 8, SCW], BF16)
            dbg_dump(c, "w1Tc", w1T.ap()[:, :, 0:512], [128, 8, 512], BF16)
            dbg_dump(c, "hasc", ha.ap(), [128, 32, SCW], BF16)
            dbg_dump(c, "s1td", s1t.ap(), [128, 32], F32)
        P.dma("scalar",
              c.haT.rearrange("(a p) s -> p a s", p=128)[:, :, sc8 * SCW:(sc8 + 1) * SCW],
              ha.ap(), c.s_st, wait=[(sd, sd.n)])
        ha_store_done = c.s_st.n

    for t in (w1T, w2T, s1t, s2t, xt[0], xt[1], u_sb[0], u_sb[1], ha):
        sb.freeit(t)
    P.barrier("l1ffnA_done")


def emit_l1ffnBC(c, st_w, scales_in, in_d, out_d, tag):
    P, sb = c.P, c.sb
    hw = [sb.alloc(f"hw{tag}{i}", [128, 32, 1024], BF16) for i in range(2)]
    hst = sb.alloc(f"hst{tag}", [128, 32], F32)
    hin = [sb.alloc(f"hin{tag}{i}", [128, 32, SCW], BF16) for i in range(2)]
    ho = sb.alloc(f"ho{tag}", [128, 8, SCW], BF16)

    P.dma("sync", hst.ap(), scales_in, c.s_ldw)
    in_v = in_d.rearrange("(a p) s -> p a s", p=128)
    out_v = out_d.rearrange("(a p) s -> p a s", p=128)

    ldw_after = {}
    hw_free = [None, None]
    s_bc = [P.sem(f"bc{tag}0"), P.sem(f"bc{tag}1")]

    def load_stripe(s, extra_wait):
        for ic in range(32):
            P.dma("scalar", hw[s % 2].ap()[:, ic, :],
                  st_w[s * 1024:(s + 1) * 1024, ic * 128:(ic + 1) * 128],
                  s_bc[s % 2], wait=extra_wait if ic == 0 else (), transpose=True)
        ldw_after[s] = (s_bc[s % 2], s_bc[s % 2].n)

    load_stripe(0, [(c.s_stg_f1, c.s_stg_f1.n)])
    hin_free = [None, None]
    ho_store_done = 0
    bankB = [0, 0]
    for stripe in range(4):
        if stripe + 1 < 4:
            w = [(c.s_stg_f1, c.s_stg_f1.n)]
            if hw_free[(stripe + 1) % 2] is not None:
                w.append((c.s_pe, hw_free[(stripe + 1) % 2]))
            load_stripe(stripe + 1, w)
        for sc8 in range(NSC):
            b = sc8 % 2
            w = []
            if hin_free[b] is not None:
                w.append((c.s_pe, hin_free[b]))
            P.dma("sync", hin[b].ap(), in_v[:, :, sc8 * SCW:(sc8 + 1) * SCW],
                  c.s_ldx, wait=w)
            ldx_done = c.s_ldx.n
            for oc in range(8):
                bank = oc % 2
                for ic in range(32):
                    wt = []
                    if oc == 0 and ic == 0:
                        wt = [(c.s_ldx, ldx_done), ldw_after[stripe]]
                    if ic == 0:
                        wt.append((c.s_act, bankB[bank]))
                    P.add("tensor",
                          lambda t, o=c.psF[:, bank, :],
                                 l=hw[stripe % 2].ap()[:, ic, oc * 128:(oc + 1) * 128],
                                 r=hin[b].ap()[:, ic, :], st=(ic == 0), sp=(ic == 31):
                          t.matmul(o, l, r, start=st, stop=sp),
                          wait=wt, inc=[(c.s_pe, 1)] if ic == 31 else ())
                gw = [(c.s_pe, c.s_pe.n)]
                if oc == 0:
                    gw.append((c.s_st, ho_store_done))
                P.add("scalar",
                      lambda s, o=ho.ap()[:, oc, :], i=c.psF[:, bank, :],
                             sc_=hst.ap()[:, stripe * 8 + oc:stripe * 8 + oc + 1]:
                      s.activation(o, i, AF.Gelu, scale=sc_),
                      wait=gw, inc=[(c.s_act, 1)])
                bankB[bank] = c.s_act.n
            hin_free[b] = c.s_pe.n
            P.dma("scalar",
                  out_v[:, stripe * 8:(stripe + 1) * 8, sc8 * SCW:(sc8 + 1) * SCW],
                  ho.ap(), c.s_st, wait=[(c.s_act, c.s_act.n)])
            ho_store_done = c.s_st.n
        hw_free[stripe % 2] = c.s_pe.n

    for t in (hw[0], hw[1], hst, hin[0], hin[1], ho):
        sb.freeit(t)
    P.barrier(f"l1ffn{tag}_done")


def emit_l1ffnB(c):
    emit_l1ffnBC(c, c.st_h1, c.h1s, c.haT, c.hbT, "B")


def emit_l1ffnC(c):
    emit_l1ffnBC(c, c.st_h2, c.h2s, c.hbT, c.haT, "C")


def emit_l1ffnD(c):
    P, sb = c.P, c.sb
    w3T = sb.alloc("w3T1", [128, 32, D], BF16)
    s3c = sb.alloc("s3c", [128, D], F32)
    hin = [sb.alloc(f"hinD{i}", [128, 32, SCW], BF16) for i in range(2)]
    x3r = sb.alloc("x3r", [128, D], F32)
    tt = sb.alloc("ttD", [128, D], F32)
    xn = [sb.alloc(f"xnD{i}", [128, D], F32) for i in range(2)]
    stats = sb.alloc("statsD", [128, 12], F32)
    sqs = sb.alloc("sqsD", [128, 512], F32)

    sd = c.s_dve
    wait_stg = ((c.s_stg_f1, c.s_stg_f1.n),)
    load_wT(c, w3T, c.st_w3, 32, wait_stg)
    P.dma("sync", s3c.ap(), c.w3s[None, :].broadcast_to([128, D]), c.s_ldw)
    ldw_done = c.s_ldw.n
    in_v = c.haT.rearrange("(a p) s -> p a s", p=128)

    hin_free = [None, None]
    bankD = 0
    for sc8 in range(NSC):
        b = sc8 % 2
        w = []
        if hin_free[b] is not None:
            w.append((c.s_pe, hin_free[b]))
        P.dma("sync", hin[b].ap(), in_v[:, :, sc8 * SCW:(sc8 + 1) * SCW],
              c.s_ldx, wait=w)
        ldx_done = c.s_ldx.n
        for st4 in range(4):
            row0 = sc8 * SCW + st4 * 128
            P.dma("sync", x3r.ap(), c.x3f[row0:row0 + 128, :], c.s_res,
                  wait=[(sd, max(0, sd.n - 1))])
            for half in range(2):
                bank = 2 + half
                for ic in range(32):
                    wt = []
                    if st4 == 0 and half == 0 and ic == 0:
                        wt = [(c.s_ldx, ldx_done), (c.s_ldw, ldw_done)]
                    if ic == 0:
                        wt.append((sd, bankD))
                    P.add("tensor",
                          lambda t, o=c.psF[:, bank, :], l=hin[b].ap()[:, ic, st4 * 128:(st4 + 1) * 128],
                                 r=w3T.ap()[:, ic, half * 512:(half + 1) * 512],
                                 st=(ic == 0), sp=(ic == 31):
                          t.matmul(o, l, r, start=st, stop=sp),
                          wait=wt, inc=[(c.s_pe, 1)] if ic == 31 else ())
            pe_done = c.s_pe.n
            bx = st4 % 2
            t_ap = tt.ap()
            P.add("vector", lambda v, o=t_ap[:, 0:512], i=c.psF[:, 2, :],
                           s3=s3c.ap()[:, 0:512]:
                  v.tensor_mul(o, i, s3),
                  wait=[(c.s_pe, pe_done), (c.s_res, c.s_res.n),
                        (c.s_act, c.s_act.n)])
            P.add("vector", lambda v, o=t_ap[:, 512:1024], i=c.psF[:, 3, :],
                           s3=s3c.ap()[:, 512:1024]:
                  v.tensor_mul(o, i, s3), inc=[(sd, 1)])
            bankD = sd.n
            P.add("vector", lambda v, o=t_ap, i=x3r.ap():
                  v.tensor_add(o, o, i),
                  wait=[(sd, sd.n)], inc=[(sd, 1)])
            emit_finalize_tile(c, t_ap, sqs.ap(), stats.ap(), xn[bx].ap(),
                               row0, c.y, None)
        hin_free[b] = c.s_pe.n

    for t in (w3T, s3c, hin[0], hin[1], x3r, tt, xn[0], xn[1], stats, sqs):
        sb.freeit(t)
    P.barrier("l1ffnD_done")
